# version 37
# baseline (speedup 1.0000x reference)
import contextlib
import numpy as np
import ml_dtypes
import concourse.bass as bass
import concourse.mybir as mybir
from concourse.bass_utils import run_bass_kernel_spmd

F32 = mybir.dt.float32
BF16 = mybir.dt.bfloat16
U32 = mybir.dt.uint32
AF = mybir.ActivationFunctionType
ALU = mybir.AluOpType
AX = mybir.AxisListType

S = 4096
D = 1024
H = 16
NE = 16
NT = 32
NTILE = 96
NG = 48
BLOBW = 14336
BO_W1, BO_W3, BO_W2 = 0, 4096, 8192
BO_B1, BO_B2, BO_B3 = 12288, 12800, 13824
BIGOOB = float(1 << 22)
EPS_HYP = 1e-6
BF = ml_dtypes.bfloat16


class Sched:

    ENGINES = ("pe", "dve", "act", "pool", "sp")

    def __init__(self):
        self.q = {e: [] for e in self.ENGINES}
        self.count = {e: 0 for e in self.ENGINES}
        self.dma_fams = set()
        self.hiw = {}
        self.writer = {}
        self.readers = {}
        self.sem = {}

    def _wait(self, eng, tgt, cnt):
        if eng == "pe" and tgt == "pe":
            return
        if tgt in self.dma_fams:
            cnt = self.count[tgt]
        if cnt <= 0:
            return
        k = (eng, tgt)
        if self.hiw.get(k, 0) >= cnt:
            return
        self.hiw[k] = cnt
        sem = self.sem
        self.q[eng].append(lambda e, t=tgt, c=cnt: e.wait_ge(sem[t], c))

    def op(self, eng, fn, reads=(), writes=(), war=(), inc=1, tag="", sem=None):
        fam = sem or eng
        if sem is not None:
            self.dma_fams.add(fam)
            self.count.setdefault(fam, 0)
        for kk in reads:
            w = self.writer.get(kk)
            if w is not None:
                self._wait(eng, w[0], w[1])
        for kk in tuple(writes) + tuple(war):
            for reng, rcnt in self.readers.get(kk, {}).items():
                self._wait(eng, reng, rcnt)
            w = self.writer.get(kk)
            if w is not None:
                self._wait(eng, w[0], w[1])
        newcnt = self.count[fam] + inc
        semd = self.sem

        def thunk(e, fn=fn, n=inc, tag=tag, fam=fam):
            ins = fn(e)
            if ins is None:
                raise RuntimeError(f"op returned no instruction (tag={tag})")
            ins.then_inc(semd[fam], n)
        self.q[eng].append(thunk)
        self.count[fam] = newcnt
        for kk in writes:
            self.writer[kk] = (fam, newcnt)
            self.readers[kk] = {}
        for kk in reads:
            self.readers.setdefault(kk, {})[fam] = newcnt


def _bf(x):
    return np.asarray(x).astype(BF)


def host_prep(inputs):
    g = {k: np.asarray(v) for k, v in inputs.items()}
    p = {}
    for nm, key in (("wq", "L0_attn_Wq"), ("wk", "L0_attn_Wk"),
                    ("wv", "L0_attn_Wv"), ("wo", "L0_attn_Wo"),
                    ("ws1a", "L0_attn_Ws1"), ("ws2a", "L0_attn_Ws2")):
        p[nm] = np.ascontiguousarray(g[key], np.float32)
    p["gatewt"] = np.ascontiguousarray(g["L0_ffn_gate_w"].T, np.float32)
    p["we1t"] = np.ascontiguousarray(g["L0_ffn_We1"].transpose(0, 2, 1), np.float32)
    p["we3t"] = np.ascontiguousarray(g["L0_ffn_We3"].transpose(0, 2, 1), np.float32)
    we2t = g["L0_ffn_We2"].transpose(0, 2, 1)
    p["we2p"] = np.ascontiguousarray(
        np.concatenate([we2t[:, 1:, :], we2t[:, 0:1, :]], axis=1), np.float32)
    p["be1r"] = np.ascontiguousarray(np.asarray(g["L0_ffn_be1"], np.float32))
    p["be3r"] = np.ascontiguousarray(np.asarray(g["L0_ffn_be3"], np.float32))
    p["be2r"] = np.ascontiguousarray(np.asarray(g["L0_ffn_be2"], np.float32))
    p["sw1t"] = np.ascontiguousarray(g["L0_ffn_Ws1"].T, np.float32)
    p["sw3t"] = np.ascontiguousarray(g["L0_ffn_Ws3"].T, np.float32)
    sw2t = g["L0_ffn_Ws2"].T
    sw2ts = np.zeros((1024, 1023), np.float32)
    sw2ts[0:1023] = sw2t[1:, :]
    p["sw2ts"] = sw2ts
    p["sw2t0"] = np.ascontiguousarray(sw2t[0:1, :], np.float32)

    def colsT(bv, m):
        return np.asarray(bv, np.float32).reshape(m, 128).T

    def pad_to(bv, n):
        out = np.zeros((n,), np.float32)
        out[: bv.shape[0]] = bv
        return out

    p["bqt"] = np.ascontiguousarray(colsT(g["L0_attn_bq"], 8))
    p["bkt"] = np.ascontiguousarray(colsT(g["L0_attn_bk"], 8))
    p["bvt"] = np.ascontiguousarray(colsT(g["L0_attn_bv"], 8))
    p["borow"] = np.ascontiguousarray(
        np.asarray(g["L0_attn_bo"], np.float32).reshape(1, 1024))
    p["bs1at"] = np.ascontiguousarray(colsT(g["L0_attn_bs1"], 4))
    p["bs2ar"] = np.ascontiguousarray(np.broadcast_to(
        np.asarray(g["L0_attn_bs2"], np.float32).reshape(1, 1), (128, 1)))
    p["bs1st"] = np.ascontiguousarray(colsT(pad_to(np.asarray(g["L0_ffn_bs1"], np.float32), 1024), 8))
    p["bs3st"] = np.ascontiguousarray(colsT(pad_to(np.asarray(g["L0_ffn_bs3"], np.float32), 1024), 8))
    p["bs2st"] = np.ascontiguousarray(colsT(pad_to(np.asarray(g["L0_ffn_bs2"], np.float32), 1024), 8))

    p["ident"] = _bf(np.eye(128, dtype=np.float32))
    p["strictu128"] = _bf(np.triu(np.ones((128, 128), np.float32), 1))
    p["strictu32"] = _bf(np.triu(np.ones((32, 32), np.float32), 1))
    p["iotaf"] = np.ascontiguousarray(
        np.arange(S, dtype=np.float32).reshape(NT, 128).T)
    p["iota16"] = np.ascontiguousarray(
        np.broadcast_to(np.arange(16, dtype=np.float32), (128, 16)))
    p["onesrow"] = np.ones((1, 128), np.float32)
    p["onesrowb"] = _bf(np.ones((1, 128), np.float32))
    p["tb80"] = np.ascontiguousarray(
        (np.arange(NG, dtype=np.float32) * 256).reshape(1, NG))
    p["onescol"] = _bf(np.ones((128, 1), np.float32))
    return p


WEIGHT_SPECS = [
    ("wq", [1024, 1024], F32), ("wk", [1024, 1024], F32),
    ("wv", [1024, 1024], F32), ("wo", [1024, 1024], F32),
    ("ws1a", [1024, 512], F32), ("ws2a", [512, 1], F32),
    ("gatewt", [1024, 16], F32),
    ("we1t", [NE, 1024, 511], F32), ("we3t", [NE, 1024, 511], F32),
    ("we2p", [NE, 512, 1023], F32), ("be1r", [NE, 511], F32),
    ("be3r", [NE, 511], F32), ("be2r", [NE, 1023], F32),
    ("sw1t", [1024, 1023], F32), ("sw3t", [1024, 1023], F32),
    ("sw2ts", [1024, 1023], F32), ("sw2t0", [1, 1023], F32),
    ("bqt", [128, 8], F32), ("bkt", [128, 8], F32), ("bvt", [128, 8], F32),
    ("borow", [1, 1024], F32), ("bs1at", [128, 4], F32), ("bs2ar", [128, 1], F32),
    ("bs1st", [128, 8], F32), ("bs3st", [128, 8], F32), ("bs2st", [128, 8], F32),
    ("ident", [128, 128], BF16), ("strictu128", [128, 128], BF16),
    ("strictu32", [32, 32], BF16), ("iotaf", [128, NT], F32),
    ("iota16", [128, 16], F32), ("onesrow", [1, 128], F32), ("onescol", [128, 1], BF16),
    ("onesrowb", [1, 128], BF16), ("tb80", [1, NG], F32),
]

CB_NAMES = {"bqt", "bkt", "bvt", "bs1at", "bs2ar",
            "bs1st", "bs3st", "bs2st", "ident", "strictu128",
            "strictu32", "iotaf", "iota16", "onesrow", "onescol",
            "onesrowb", "tb80"}

DUMP_SHAPES = {
    "h1": [S, D], "imp": [128, NT], "sel": [128, NT], "rank": [128, NT],
    "h1sel": [16, D], "kself": [128, 128], "qself": [128, 128],
    "vselt": [16, D], "ctxb": [16, D], "x2": [S, D], "h2": [S, D],
    "expg": [128, NT * 16], "hit": [128, NT * 16], "dest0": [128, NT],
    "dest1": [128, NT], "w0": [128, NT], "w1": [128, NT],
    "ein": [NTILE * 128, D], "zsh": [S, D], "eout": [NTILE * 128, D],
    "expt": [128, 1024], "rdenb": [16, 16], "d0u": [128, NT], "d1u": [128, NT],
}

PHASES = ["init", "A", "imp", "topk", "selproj", "p1", "p2", "route",
          "scatter", "shared", "experts", "final"]


class MK:
    def __init__(self, dumps=(), upto="final", p2tau=NT):
        self.nc = bass.Bass("TRN2", target_bir_lowering=False)
        self.S = Sched()
        self.dumps = tuple(dumps)
        self.upto = PHASES.index(upto)
        self.p2tau = p2tau

    def op(self, *a, **k):
        self.S.op(*a, **k)

    def build(self):
        nc = self.nc
        self.d_in = {"x": nc.dram_tensor("x", [S, D], F32, kind="ExternalInput").ap()}
        for name, shape, dt in WEIGHT_SPECS:
            self.d_in[name] = nc.dram_tensor(name, shape, dt, kind="ExternalInput").ap()
        self.out_d = nc.dram_tensor("out", [S, D], F32, kind="ExternalOutput").ap()
        self.h1_dram = nc.dram_tensor("h1_dram", [S, D], BF16, kind="Internal").ap()
        self.h2_dram = nc.dram_tensor("h2_dram", [S, D], BF16, kind="Internal").ap()
        self.x2_dram = nc.dram_tensor("x2_dram", [S, D], F32, kind="Internal").ap()
        self.zsh_dram = nc.dram_tensor("zsh_dram", [S, D], BF16, kind="Internal").ap()
        self.impflat_dram = nc.dram_tensor("impflat_dram", [1, S], F32, kind="Internal").ap()
        self.ein_dram = nc.dram_tensor("ein_dram", [NTILE * 128, D], BF16, kind="Internal").ap()
        self.eout_dram = nc.dram_tensor("eout_dram", [NTILE * 128, D], BF16, kind="Internal").ap()
        self.wblob = nc.dram_tensor("wblob", [NE * 128, BLOBW], BF16, kind="Internal").ap()
        self.dump_d = {nm: nc.dram_tensor("dump_" + nm, DUMP_SHAPES[nm], F32,
                                          kind="ExternalOutput").ap()
                       for nm in self.dumps}

        ctx = contextlib.ExitStack()
        sbuf = lambda name, shape, dt: ctx.enter_context(nc.sbuf_tensor(name, shape, dt))
        psum = lambda name, shape, dt: ctx.enter_context(nc.psum_tensor(name, shape, dt))
        with ctx:
            self.A1 = sbuf("A1", [128, 32768], BF16)
            self.A2 = sbuf("A2", [128, 16384], BF16)
            self.A3 = sbuf("A3", [128, 16384], BF16)
            self.WA = sbuf("WA", [128, 24576], BF16)
            self.SA = sbuf("SA", [128, 4096], F32)
            self.ctxf = self.WA[:, 0:1024]
            self.cb = {}
            for name, shape, dt in WEIGHT_SPECS:
                if name in CB_NAMES:
                    self.cb[name] = sbuf("c_" + name, shape, dt)
            b = {}
            B16 = sbuf("B16", [128, 5120], BF16)
            self.B16 = B16
            b["gatew"] = self.A3[:, 0:128]
            b["ws2a"] = self.A3[:, 1540:1544]
            b["PT"] = B16[0:16, 0:4096]
            b["ctxbT"] = B16[0:16, 4096:5120]
            b["vselT"] = B16[32:48, 0:1024]
            b["wTst"] = B16[32:48, 1024:1536]
            b["h1selT"] = B16[32:48, 1536:2560]
            b["borowb"] = B16[32:33, 2560:3584]
            b["onesb32"] = B16[32:33, 3584:3712]
            b["timerow"] = B16[64:65, 0:1024]
            b["sw2t0"] = B16[64:65, 3072:4095]
            b["h1selF"] = sbuf("h1selF", [128, 128], BF16)
            b["qselF"] = sbuf("qselF", [128, 128], BF16)
            b["kselF"] = sbuf("kselF", [128, 128], BF16)
            b["vselF"] = sbuf("vselF", [128, 128], BF16)
            b["denb8"] = sbuf("denb8", [16, 8], F32)
            b["rdenb"] = sbuf("rdenb", [16, 16], F32)
            b["imp_tm"] = sbuf("imp_tm", [128, NT], F32)
            b["topv"] = sbuf("topv", [1, 16], F32)
            b["t16b"] = sbuf("t16b", [128, 1], F32)
            b["sel"] = sbuf("sel", [128, NT], F32)
            b["notsel"] = sbuf("notsel", [128, NT], F32)
            b["hitbf"] = sbuf("hitbf", [128, NT], BF16)
            b["selT"] = sbuf("selT_", [32, 128], BF16)
            b["rfree"] = sbuf("rfree", [128, NT], F32)
            b["rowtot"] = sbuf("rowtot", [128, 1], F32)
            b["rowtotb"] = sbuf("rowtotb", [128, 1], BF16)
            b["offs"] = sbuf("offs", [128, 1], F32)
            b["rank16"] = sbuf("rank16", [128, NT], F32)
            b["Pbuf"] = self.A3[:, 1024:1536]
            b["Pf32"] = sbuf("Pf32", [128, 16], F32)
            b["idx16"] = sbuf("idx16", [16, 1], U32)
            b["idx16f"] = sbuf("idx16f", [16, 1], F32)
            b["expg"] = self.SA[:, 512:1024].bitcast(F32) if False else self.SA[:, 512:1024]
            b["hit"] = self.A2[:, 0:512]
            b["rall"] = self.SA[:, 0:512]
            b["rden"] = sbuf("rden", [128, NT], F32)
            b["w0"] = sbuf("w0", [128, NT], F32)
            b["w1"] = sbuf("w1", [128, NT], F32)
            b["dest0f"] = self.SA[:, 1024:1056]
            b["dest1f"] = self.SA[:, 1056:1088]
            b["dest0"] = sbuf("dest0", [128, NT], U32)
            b["dest1"] = sbuf("dest1", [128, NT], U32)
            b["pofsb"] = self.SA[:, 1184:1200]
            for i, nm in enumerate(("m1v", "m2v", "eav", "ebv", "ddv",
                                    "flagv", "wlov", "whiv")):
                b[nm] = self.SA[:, 1264 + i * 32:1296 + i * 32]
            b["reqall"] = self.SA[:, 0:512]

            b["req2all"] = self.SA[:, 3072:3584]
            b["cntrow"] = self.SA[0:1, 1200:1216]
            b["pcrow"] = self.SA[0:1, 1216:1232]
            b["pcs"] = self.SA[0:1, 1232:1248]
            b["pofsrow"] = self.SA[0:1, 1248:1264]
            b["texp"] = B16[32:33, 3712:3712 + NG]
            b["t80"] = B16[32:33, 3800:3800 + NG]

            b["idxmat"] = sbuf("idxmat", [128, NG], U32)
            b["pc32"] = b["dest0"][0:1, 0:16]
            b["elo"] = self.SA[:, 1120:1152]
            b["ehi"] = self.SA[:, 1152:1184]
            rs16 = sbuf("rs16", [128, 24], F32)
            for i, nm in enumerate(("m1", "m2", "dd", "ea", "eb", "flag", "wlo",
                                    "whi", "r0", "r1", "msk")):
                b[nm] = rs16[:, i:i + 1]
            b["req"] = self.SA[:, 1088:1104]
            b["req2"] = self.SA[:, 1104:1120]
            b["exps"] = self.WA[:, 1536:1792]
            b["wsc"] = self.WA[:, 1792:2048]
            b["kselP"] = self.WA[:, 6144:6400]
            b["qselP"] = self.WA[:, 24320:24576]
            b["aden"] = sbuf("aden", [128, 16], F32)
            b["arden"] = sbuf("arden", [128, 16], F32)
            b["mean"] = rs16[:, 11:12]
            b["ssq"] = rs16[:, 14:15]
            b["rstd"] = rs16[:, 12:13]
            b["s0"] = rs16[:, 13:14]
            b["c1"] = rs16[:, 15:16]
            b["ceps"] = rs16[:, 16:17]
            for i, nm in enumerate(("s0b", "ssqb", "mean0", "mean1",
                                    "rstd0", "rstd1", "ssqA0", "ssqA1")):
                b[nm] = rs16[:, i:i + 1] if i < 7 else rs16[:, 17 + i - 7:18]
            b["hitbfB"] = sbuf("hitbfB", [128, NT], BF16)
            b["selTB"] = sbuf("selTB2", [32, 128], BF16)
            b["rowtotbB"] = sbuf("rowtotbB", [128, 1], BF16)
            b["rfreeB"] = b["imp_tm"]
            b["rowtotB"] = b["r0"]
            b["offsB"] = b["r1"]

            b["zt"] = self.A3[:, 15360:16384]
            b["bt"] = [self.WA[:, 3072:4096], self.WA[:, 4096:5120]]
            self.b = b

            self.pA = psum("pA", [128, 512], F32)
            self.pB = psum("pB", [128, 512], F32)
            self.pBig = psum("pBig", [128, 1024], F32)
            self.pS = psum("pS", [128, 512], F32)
            self.pX = psum("pX", [128, 512], F32)
            self.pT = [psum("pT0", [128, 1024], BF16), psum("pT1", [128, 1024], BF16)]

            FAMS = ["blb", "cbld", "xin0", "xin1", "h1st", "impq", "sx0", "sx1",
                    "sh2", "zst", "el0", "el1", "eW", "fx0", "fx1", "fz0",
                    "fz1", "fs", "wld", "zein", "gsel", "sc", "sc0", "sc1",
                    "fo00", "fo01",
                    "fo10", "fo11", "ewa", "ewb", "st0", "st1", "dmpp", "dmps"]
            for e in list(Sched.ENGINES) + FAMS:
                self.S.sem[e] = ctx.enter_context(nc.semaphore("sem_" + e))

            import os as _os
            skip = set(_os.environ.get("SKIPPH", "").split(","))
            emits = [self.emit_init, self.emit_phaseA, self.emit_imp,
                     self.emit_topk, self.emit_selproj, self.emit_p1,
                     self.emit_p2, self.emit_route, self.emit_scatter,
                     self.emit_shared, self.emit_experts, self.emit_final]
            for i, fn in enumerate(emits):
                if i <= self.upto and PHASES[i] not in skip:
                    fn()
            self.emit_dump()
            self.emit_tail()

            with nc.Block() as block:
                qq = self.S.q

                @block.tensor
                def _(eng):
                    for t in qq["pe"]:
                        t(eng)

                @block.vector
                def _(eng):
                    for t in qq["dve"]:
                        t(eng)

                @block.scalar
                def _(eng):
                    for t in qq["act"]:
                        t(eng)

                @block.gpsimd
                def _(eng):
                    self.bcreg = eng.alloc_register("bcreg")
                    eng.reg_mov(self.bcreg, NTILE * 128 - 1)
                    for t in qq["pool"]:
                        t(eng)

                @block.sync
                def _(eng):
                    for t in qq["sp"]:
                        t(eng)
        return self.nc

    def load_w(self, name, wofs, cols, nk=8, dram_slice=None, key=None, war=(),
               sem="wld", eng="pool"):
        src = dram_slice if dram_slice is not None else self.d_in[name]
        dst = self.WA[:, wofs:wofs + nk * cols].rearrange("p (k f) -> p k f", k=nk)

        def fn(e, src=src, dst=dst):
            return e.dma_start(out=dst, in_=src.rearrange("(k p) f -> p k f", p=128))
        self.op(eng, fn, writes=[key or ("w", name)], war=war, inc=16,
                tag="ldw_" + (name if isinstance(name, str) else str(name)),
                sem=sem)

    def wck(self, wofs, cols, k, sub=None):
        c = self.WA[:, wofs + k * cols: wofs + (k + 1) * cols]
        if sub is not None:
            c = c[:, sub[0]:sub[1]]
        return c

    def tbatch(self, srcs, outw, evac_fn, bank, rkeys, wkeys, tag, war=(),
               base=0):
        pT = self.pT[bank]
        ident = self.cb["ident"]

        def tf(e, srcs=srcs, pT=pT, outw=outw, base=base):
            ins = None
            for i, sap in enumerate(srcs):
                r = sap.shape[0]
                ins = e.transpose(pT[0:sap.shape[1], i * outw:i * outw + r],
                                  sap, ident[base:base + r, base:base + r])
            return ins
        self.op("pe", tf, reads=list(rkeys) + [("c", "ident")],
                writes=[("pt", bank)], tag=tag)
        self.op("dve", evac_fn, reads=[("pt", bank)], writes=list(wkeys),
                war=war, tag=tag + "e")

    H1F_KEYS = [("h1F", tau, half) for tau in range(NT) for half in range(2)]
    H2F_KEYS = [("h2F", tau, half) for tau in range(NT) for half in range(2)]

    def emit_init(self):
        d_in, cb, b = self.d_in, self.cb, self.b
        for name in cb:
            self.op("act", lambda e, n=name: e.dma_start(out=cb[n][:], in_=d_in[n]),
                    writes=[("c", name)], inc=16, tag="ld_" + name, sem="cbld")
        self.op("dve", lambda e: e.memset(b["zt"][:], 0.0), writes=["zt"], tag="zt")
        self.op("dve", lambda e: e.memset(b["c1"], 1.0), writes=["c1"], tag="c1")
        self.op("dve", lambda e: e.memset(b["ceps"], 1e-5), writes=["ceps"], tag="ceps")
        self.op("dve", lambda e: e.memset(b["onesb32"], 1.0), writes=["onesb32"], tag="o32")
        self.op("pool", lambda e: e.dma_start(
            out=b["ws2a"], in_=d_in["ws2a"].rearrange("(m p) o -> p (m o)", p=128),
            allow_slow_non_contiguous=True),
            writes=["ws2a"], inc=16, tag="ld_ws2a", sem="wld")
        self.load_w("ws1a", 16384, 512)
        self.load_w("wk", 0, 1024)
        self.load_w("wv", 8192, 1024)
        self.op("pool", lambda e: e.dma_start(out=b["borowb"], in_=d_in["borow"]),
                writes=["borowb"], inc=16, tag="ld_borow", sem="wld")

    def emit_phaseA(self):
        b, SA, A1 = self.b, self.SA, self.A1
        h1F = A1[:].rearrange("p (m t) -> p m t", m=8)
        xin = [SA[:, 0:1024], SA[:, 1024:2048]]
        h1t = [self.A2[:, i * 1024:(i + 1) * 1024] for i in range(4)]
        sqj = self.A2[:, 4096:5120]
        xd = self.d_in["x"].rearrange("(T p) d -> T p d", p=128)
        h1d = self.h1_dram.rearrange("(T p) d -> T p d", p=128)
        def a_front(tau):
            bb = tau % 2
            return [
                lambda: self.op("sp", lambda e: e.dma_start(
                    out=xin[bb], in_=xd[tau]),
                    writes=[("xin", bb)], inc=16, tag=f"ldx{tau}",
                    sem=f"xin{bb}"),
                lambda: self.ln_head(xin[bb], [("xin", bb)], f"ln1_{tau}",
                                     mk=f"mean{bb}"),
            ]

        def a_back(tau):
            bb = tau % 2
            s4 = tau % 4
            st = [
                lambda: self.ln_tail(
                    xin[bb], [("xin", bb)], h1t[s4], [("h1t", s4)], sqj,
                    f"ln1_{tau}", mk=f"mean{bb}", rk=f"rstd{bb}",
                    qk=f"ssqA{bb}"),
                lambda: self.op("sp", lambda e: e.dma_start(
                    out=h1d[tau], in_=h1t[s4]),
                    reads=[("h1t", s4)], writes=[("h1_dram", tau)], inc=16,
                    tag=f"sth1{tau}", sem="h1st"),
            ]
            for half in range(2):
                bank = (tau * 2 + half) % 2
                srcs = [h1t[s4][:, (half * 4 + j) * 128:(half * 4 + j + 1) * 128]
                        for j in range(4)]

                def ev(e, tau=tau, half=half, bank=bank):
                    return e.tensor_copy(
                        h1F[:, half * 4:half * 4 + 4, tau * 128:(tau + 1) * 128],
                        self.pT[bank][:, 0:512].rearrange("p (m t) -> p m t", m=4))
                st.append(lambda half=half, bank=bank, srcs=srcs, ev=ev, tau=tau:
                          self.tbatch(srcs, 128, ev, bank, [("h1t", s4)],
                                      [("h1F", tau, half)], f"trA{tau}_{half}"))
            return st

        relu = [self.A3[:, 0:512], self.A3[:, 512:1024]]
        cb = self.cb
        pS = self.pS

        def imp_piece(n, m):
            rb = n % 2
            if True:
                bank = [self.pA, self.pB][(n * 4 + m) % 2]
                bkey = ["pA", "pB"][(n * 4 + m) % 2]

                def mm(e, m=m, n=n, bank=bank):
                    ins = None
                    for k in range(8):
                        ins = e.matmul(bank[:, 0:512],
                                       self.wck(16384, 512, k,
                                                (m * 128, (m + 1) * 128)),
                                       h1F[:, k, n * 512:(n + 1) * 512],
                                       start=(k == 0), stop=(k == 7))
                    return ins
                self.op("pe", mm, reads=[("w", "ws1a")] +
                        [("h1F", 4 * n + j, hh)
                         for j in range(4) for hh in range(2)],
                        writes=[bkey], tag=f"is1_{m}_{n}")
                self.op("act", lambda e, m=m, rb=rb, bank=bank: e.activation(
                    relu[rb], bank[:, 0:512], AF.Relu,
                    bias=cb["bs1at"][:, m:m + 1]),
                    reads=[bkey, ("c", "bs1at")], writes=[("relu", rb)],
                    tag=f"ir_{m}_{n}")

                def mm2(e, m=m, n=n, rb=rb):
                    ins = None
                    for j in range(4):
                        tau = n * 4 + j
                        ins = e.matmul(pS[:, tau:tau + 1],
                                       relu[rb][:, j * 128:(j + 1) * 128],
                                       b["ws2a"][:, m:m + 1],
                                       start=(n == 0 and m == 0 and j == 0),
                                       stop=(n == 7 and m == 3 and j == 3))
                    return ins
                self.op("pe", mm2, reads=[("relu", rb), "ws2a"], writes=["pS"],
                        tag=f"is2_{m}_{n}")

        for i in range(NT + 4):
            A = a_front(i) if i < NT else []
            B = a_back(i - 1) if 1 <= i <= NT else []
            na, nb = len(A), len(B)
            ia = ib = 0
            while ia < na or ib < nb:
                if ia < na and (ib >= nb or ia * (nb + 1) <= ib * (na + 1)):
                    A[ia]()
                    ia += 1
                else:
                    B[ib]()
                    ib += 1
            k = i - 4
            if 0 <= k < NT:
                imp_piece(k // 4, k % 4)

    def ln_head(self, x_ap, x_keys, tag, mk="mean"):
        b = self.b
        self.op("dve", lambda e, mk=mk: e.tensor_reduce(
            b[mk], x_ap, axis=AX.X, op=ALU.add, negate=True),
            reads=x_keys, writes=[mk], tag=tag + "m")
        self.op("dve", lambda e, mk=mk: e.tensor_scalar(
            b[mk], b[mk], 1.0 / D, None, op0=ALU.mult),
            reads=[mk], writes=[mk], tag=tag + "s")

    def ln_tail(self, x_ap, x_keys, out_ap, out_keys, sqj_ap, tag,
                mk="mean", rk="rstd", qk="ssq"):
        b = self.b
        self.op("act", lambda e, mk=mk, qk=qk: e.activation(
            sqj_ap, x_ap, AF.Square, bias=b[mk], scale=1.0,
            accum_out=b[qk]),
            reads=x_keys + [mk], writes=[qk, "sqj"], tag=tag + "q")
        self.op("act", lambda e, rk=rk, qk=qk: e.activation(
            b[rk], b[qk], AF.Sqrt, scale=1.0 / D, bias=b["ceps"]),
            reads=[qk, "ceps"], writes=[rk], tag=tag + "r")
        self.op("dve", lambda e, rk=rk: e.reciprocal(b[rk], b[rk]),
                reads=[rk], writes=[rk], tag=tag + "i")
        self.op("dve", lambda e, mk=mk, rk=rk: e.tensor_scalar(
            out_ap, x_ap, b[mk], b[rk], op0=ALU.add, op1=ALU.mult),
            reads=x_keys + [mk, rk], writes=out_keys, tag=tag + "o")

    def ln_tile(self, x_ap, x_keys, out_ap, out_keys, sqj_ap, tag):
        self.ln_head(x_ap, x_keys, tag)
        self.ln_tail(x_ap, x_keys, out_ap, out_keys, sqj_ap, tag)

    def emit_bulk(self):
        d_in, b = self.d_in, self.b
        eind = self.ein_dram.rearrange("(n p) d -> n p d", p=128)
        for i in range(NTILE):
            gate = [("h1F", NT - 1, 0), ("h1F", NT - 1, 1)] if i == 0 else []
            self.op("pool", lambda e, i=i: e.dma_start(out=eind[i], in_=b["zt"][:]),
                    reads=["zt"] + gate, writes=[("eind_z", i)], inc=16,
                    tag=f"zein{i}", sem="zein")
        wb = self.wblob.rearrange("(n p) w -> n p w", p=128)
        for ex in range(NE):
            self.op("pool", lambda e, ex=ex: e.dma_start(
                out=wb[ex][:, BO_W1:BO_W1 + 4096].rearrange(
                    "p (k f) -> p k f", k=8)[:, :, 0:511],
                in_=d_in["we1t"][ex].rearrange("(k p) f -> p k f", p=128)),
                writes=[("wblob", ex, 0)], inc=16, tag=f"wb1_{ex}", sem="blb")
            self.op("pool", lambda e, ex=ex: e.dma_start(
                out=wb[ex][:, BO_W3:BO_W3 + 4096].rearrange(
                    "p (k f) -> p k f", k=8)[:, :, 0:511],
                in_=d_in["we3t"][ex].rearrange("(k p) f -> p k f", p=128)),
                writes=[("wblob", ex, 1)], inc=16, tag=f"wb3_{ex}", sem="blb")
            self.op("pool", lambda e, ex=ex: e.dma_start(
                out=wb[ex][:, BO_W2:BO_W2 + 4096].rearrange(
                    "p (k f) -> p k f", k=4)[:, :, 0:1023],
                in_=d_in["we2p"][ex].rearrange("(k p) f -> p k f", p=128)),
                writes=[("wblob", ex, 2)], inc=16, tag=f"wb2_{ex}", sem="blb")
            self.op("pool", lambda e, ex=ex: e.dma_start(
                out=wb[ex][0:1, BO_B1:BO_B1 + 511],
                in_=d_in["be1r"][ex:ex + 1, :]),
                writes=[("wblob", ex, 3)], inc=16, tag=f"wbb1_{ex}", sem="blb")
            self.op("pool", lambda e, ex=ex: e.dma_start(
                out=wb[ex][0:1, BO_B2:BO_B2 + 1023],
                in_=d_in["be2r"][ex:ex + 1, :]),
                writes=[("wblob", ex, 4)], inc=16, tag=f"wbb2_{ex}", sem="blb")
            self.op("pool", lambda e, ex=ex: e.dma_start(
                out=wb[ex][0:1, BO_B3:BO_B3 + 511],
                in_=d_in["be3r"][ex:ex + 1, :]),
                writes=[("wblob", ex, 5)], inc=16, tag=f"wbb3_{ex}", sem="blb")

    def emit_imp(self):
        b, cb = self.b, self.cb
        pS = self.pS
        self.op("act", lambda e: e.activation(
            b["imp_tm"][:], pS[:, 0:NT], AF.Sigmoid, bias=cb["bs2ar"][:, 0:1]),
            reads=["pS", ("c", "bs2ar")], writes=["imp_tm"], tag="isig")

    def emit_topk(self):
        b, cb, SA = self.b, self.cb, self.SA
        impf = SA[0:1, 0:4096]
        scr = SA[32:33, 0:4096]
        self.op("sp", lambda e: e.dma_start(
            out=self.impflat_dram.rearrange("o (p T) -> (o p) T", p=128),
            in_=b["imp_tm"][:]),
            reads=["imp_tm"], writes=["impflat"], inc=16, tag="impo", sem="impq")
        self.op("sp", lambda e: e.dma_start(out=impf, in_=self.impflat_dram),
                reads=["impflat"], writes=["impf"], war=[("xin", 0), ("xin", 1)],
                inc=16, tag="impi", sem="impq")
        self.op("dve", lambda e: e.max(out=b["topv"][:, 0:8], in_=impf),
                reads=["impf"], writes=["topv"], tag="mx8a")
        self.op("dve", lambda e: e.match_replace(
            out=scr, in_to_replace=b["topv"][:, 0:8], in_values=impf, imm_value=-1.0),
            reads=["impf", "topv"], writes=["scr"], tag="mra")
        self.op("dve", lambda e: e.max(out=b["topv"][:, 8:16], in_=scr),
                reads=["scr", "topv"], writes=["topv"], tag="mx8b")
        self.op("pe", lambda e: e.matmul(self.pX[:, 0:1], cb["onesrow"][0:1, :],
                                         b["topv"][0:1, 15:16], start=True, stop=True),
                reads=[("c", "onesrow"), "topv"], writes=["pX"], tag="bc16")
        self.op("dve", lambda e: e.tensor_copy(b["t16b"][:], self.pX[:, 0:1]),
                reads=["pX"], writes=["t16b"], tag="t16")
        self.op("dve", lambda e: e.tensor_scalar(
            b["sel"][:], b["imp_tm"][:], b["t16b"][:, 0:1], None, op0=ALU.is_ge),
            reads=["imp_tm", "t16b"], writes=["sel"], tag="sel")
        self.op("dve", lambda e: e.tensor_scalar(
            b["notsel"][:], b["sel"][:], -1.0, 1.0, op0=ALU.mult, op1=ALU.add),
            reads=["sel"], writes=["notsel"], tag="nsel")
        self.op("dve", lambda e: e.tensor_copy(b["hitbf"][:], b["sel"][:]),
                reads=["sel"], writes=["hitbf"], tag="hbf")
        self.emit_rank(b["sel"][:], ["sel"], b["rank16"][:], [("rank16",)], "rs")

        first = [True]
        for tau in range(NT):
            self.op("dve", lambda e, tau=tau: e.tensor_scalar(
                b["Pf32"][:], cb["iota16"][:], b["rank16"][:, tau:tau + 1], None,
                op0=ALU.is_equal),
                reads=[("rank16",), ("c", "iota16")], writes=["Pf32"], tag=f"pq{tau}")
            self.op("dve", lambda e, tau=tau: e.tensor_scalar(
                b["Pf32"][:], b["Pf32"][:], b["sel"][:, tau:tau + 1], None,
                op0=ALU.mult),
                reads=["Pf32", "sel"], writes=["Pf32"], tag=f"pm{tau}")
            self.op("dve", lambda e, tau=tau: e.tensor_copy(
                b["Pbuf"][:, tau * 16:(tau + 1) * 16], b["Pf32"][:]),
                reads=["Pf32"], writes=[("Pbuf", tau)], tag=f"pb{tau}")

            def mmi(e, tau=tau, first=first):
                ins = e.matmul(self.pS[0:16, 48:49], b["Pf32"][:],
                               cb["iotaf"][:, tau:tau + 1],
                               start=first[0], stop=(tau == NT - 1))
                first[0] = False
                return ins
            self.op("pe", mmi, reads=["Pf32", ("c", "iotaf")], writes=["pS"],
                    tag=f"px{tau}")
            bank = tau % 2
            self.op("pe", lambda e, tau=tau, bank=bank: e.transpose(
                self.pT[bank][0:16, 0:128], b["Pbuf"][:, tau * 16:(tau + 1) * 16],
                cb["ident"][:]),
                reads=[("Pbuf", tau), ("c", "ident")], writes=[("pt", bank)],
                tag=f"pt{tau}")
            self.op("dve", lambda e, tau=tau, bank=bank: e.tensor_copy(
                b["PT"][:, tau * 128:(tau + 1) * 128], self.pT[bank][0:16, 0:128]),
                reads=[("pt", bank)], writes=[("PT", tau)], tag=f"pe{tau}")
        self.op("dve", lambda e: e.tensor_copy(b["idx16f"][0:16, :], self.pS[0:16, 48:49]),
                reads=["pS"], writes=["idx16f"], tag="i16f")
        self.op("dve", lambda e: e.tensor_copy(b["idx16"][0:16, :], b["idx16f"][0:16, :]),
                reads=["idx16f"], writes=["idx16"], tag="i16")
        hkeys = [("h1_dram", tau) for tau in range(NT)]
        self.op("pool", lambda e: e.indirect_dma_start(
            out=b["h1selT"], out_offset=None, in_=self.h1_dram,
            in_offset=bass.IndirectOffsetOnAxis(ap=b["idx16"][0:16, :], axis=0)),
            reads=["idx16"] + hkeys, writes=["h1selT"], inc=16, tag="gsel",
            sem="gsel")
        self.load_w("wq", 16384, 1024, key=("w", "wqsel"), war=[("w", "ws1a")])
        self.emit_bulk()

    def rank_ops(self, hit_ap, hit_keys, rank_ap, rank_keys, tag, war=(),
                 ps=0):
        b, cb = self.b, self.cb
        sfx = "B" if ps else ""
        hb, sT, rf = "hitbf" + sfx, "selT" + sfx, "rfree" + sfx
        rt, rtb, of = "rowtot" + sfx, "rowtotb" + sfx, "offs" + sfx
        pS_, kS = [(self.pS, "pS"), (self.pA, "pA")][ps]
        pX_, kX = [(self.pX, "pX"), (self.pB, "pB")][ps]
        pT_ = self.pT[ps]
        return [
            lambda: self.op("pe", lambda e: e.transpose(
                pT_[0:32, 0:128], b[hb][:, 0:NT], cb["ident"][:]),
                reads=[hb, ("c", "ident")], writes=[("pt", ps)], tag=tag + "t"),
            lambda: self.op("dve", lambda e: e.tensor_copy(
                b[sT][0:32, :], pT_[0:32, 0:128]),
                reads=[("pt", ps)], writes=[sT], tag=tag + "e"),
            lambda: self.op("pe", lambda e: e.matmul(
                pS_[:, 0:32], b[sT][0:32, :],
                cb["strictu32"][0:32, :], start=True, stop=True),
                reads=[sT, ("c", "strictu32")], writes=[kS], tag=tag + "f"),
            lambda: self.op("dve", lambda e: e.tensor_copy(
                b[rf][:], pS_[:, 0:32]),
                reads=[kS], writes=[rf], tag=tag + "g"),
            lambda: self.op("dve", lambda e: e.tensor_reduce(
                b[rt][:], hit_ap, axis=AX.X, op=ALU.add),
                reads=hit_keys, writes=[rt], tag=tag + "h"),
            lambda: self.op("dve", lambda e: e.tensor_copy(b[rtb][:], b[rt][:]),
                    reads=[rt], writes=[rtb], tag=tag + "i"),
            lambda: self.op("pe", lambda e: e.matmul(
                pX_[:, 0:1], cb["strictu128"][:],
                b[rtb][:, 0:1], start=True, stop=True),
                reads=[rtb, ("c", "strictu128")], writes=[kX], tag=tag + "j"),
            lambda: self.op("dve", lambda e: e.tensor_copy(b[of][:], pX_[:, 0:1]),
                    reads=[kX], writes=[of], tag=tag + "k"),
            lambda: self.op("dve", lambda e: e.tensor_scalar(
                rank_ap, b[rf][:], b[of][:, 0:1], None, op0=ALU.add),
                reads=[rf, of], writes=rank_keys, war=war, tag=tag + "l"),
        ]

    def emit_rank(self, hit_ap, hit_keys, rank_ap, rank_keys, tag, war=()):
        for f in self.rank_ops(hit_ap, hit_keys, rank_ap, rank_keys, tag, war):
            f()

    def emit_selproj(self):
        b, cb = self.b, self.cb
        h1selF = b["h1selF"][:].rearrange("p (m j) -> p m j", m=8)
        vselF = b["vselF"][:].rearrange("p (m j) -> p m j", m=8)
        qselF = b["qselF"][:].rearrange("p (m j) -> p m j", m=8)
        kselF = b["kselF"][:].rearrange("p (m j) -> p m j", m=8)
        for half in range(2):
            bank = half % 2
            srcs = [b["h1selT"][:, (half * 4 + j) * 128:(half * 4 + j + 1) * 128]
                    for j in range(4)]

            def ev(e, half=half, bank=bank):
                return e.tensor_copy(
                    h1selF[:, half * 4:half * 4 + 4, :],
                    self.pT[bank][:, 0:64].rearrange("p (m j) -> p m j", m=4))
            self.tbatch(srcs, 16, ev, bank, ["h1selT"], ["h1selF"], f"st{half}",
                        base=32)
        for wname, wofs, bias, dst, dkey in (
                ("wqsel", 16384, "bqt", qselF, "qselF"),
                ("wk", 0, "bkt", kselF, "kselF"),
                ("wv", 8192, "bvt", vselF, "vselF")):
            for m in range(8):
                bank = [self.pA, self.pB][m % 2]
                bkey = ["pA", "pB"][m % 2]

                def mm(e, wofs=wofs, m=m, bank=bank):
                    ins = None
                    for k in range(8):
                        ins = e.matmul(bank[:, 0:16],
                                       self.wck(wofs, 1024, k, (m * 128, (m + 1) * 128)),
                                       h1selF[:, k, :],
                                       start=(k == 0), stop=(k == 7))
                    return ins
                self.op("pe", mm, reads=[("w", wname), "h1selF"], writes=[bkey],
                        tag=f"sp_{wname}_{m}")
                scl = 0.125 if wname == "wqsel" else None
                op1 = ALU.mult if scl is not None else ALU.bypass
                self.op("dve", lambda e, m=m, dst=dst, bias=bias, bank=bank,
                        scl=scl, op1=op1: e.tensor_scalar(
                            dst[:, m, :], bank[:, 0:16], cb[bias][:, m:m + 1],
                            scl, op0=ALU.add, op1=op1),
                        reads=[bkey, ("c", bias)], writes=[dkey],
                        tag=f"se_{wname}_{m}")
        for half in range(2):
            bank = half % 2
            srcs = [vselF[:, half * 4 + j, :] for j in range(4)]

            def ev(e, half=half, bank=bank):
                return e.tensor_copy(
                    b["vselT"][:, half * 512:(half + 1) * 512],
                    self.pT[bank][0:16, 0:512])
            self.tbatch(srcs, 128, ev, bank, ["vselF"], ["vselT"], f"vt{half}")
        self.emit_selP("qselP", qselF, war=[("w", "wqsel")])

    def emit_selP(self, nm, srcf, war=()):
        b = self.b
        self.op("dve", lambda e, nm=nm: e.memset(b[nm], 0.0), writes=[nm],
                war=war, tag=nm + "z")
        for h in range(H):
            m, lo = h // 2, (h % 2) * 64
            self.op("dve", lambda e, m=m, lo=lo, h=h, nm=nm, srcf=srcf:
                    e.tensor_copy(
                        b[nm][lo:lo + 64, h * 16:(h + 1) * 16],
                        srcf[lo:lo + 64, m, :]),
                    reads=["kselF" if nm == "kselP" else "qselF"],
                    writes=[nm], tag=f"{nm}{h}")

    def emit_p1(self):
        b, cb = self.b, self.cb
        h1F = self.A1[:].rearrange("p (m t) -> p m t", m=8)
        Kg = [self.A3[:, 0:4096], self.A3[:, 4096:8192]]
        Vg = [self.A3[:, 8192:12288].rearrange("p (T d) -> p T d", T=NT),
              self.A3[:, 12288:16384].rearrange("p (T d) -> p T d", T=NT)]
        Vfm = self.A2[:, 0:4096]
        expb = [self.A2[0:16, 4096:8192], self.A2[32:48, 4096:8192]]
        expT = self.A2[:, 8192:9216].rearrange("p (T j) -> p T j", T=NT)
        qselF = b["qselF"][:].rearrange("p (m j) -> p m j", m=8)
        pBig = self.pBig
        first_ctx = [True, True]
        VFM_KEYS = [("Vfm", n) for n in range(8)]

        for g in range(8):
            gb = g % 2
            for wname, wofs, bias, dstf in (("wk", 0, "bkt", None),
                                            ("wv", 8192, "bvt", Vfm)):
                for n in range(8):
                    bank = [self.pA, self.pB][n % 2]
                    bkey = ["pA", "pB"][n % 2]

                    def mm(e, wofs=wofs, g=g, n=n, bank=bank):
                        ins = None
                        for k in range(8):
                            ins = e.matmul(bank[:, 0:512],
                                           self.wck(wofs, 1024, k,
                                                    (g * 128, (g + 1) * 128)),
                                           h1F[:, k, n * 512:(n + 1) * 512],
                                           start=(k == 0), stop=(k == 7))
                        return ins
                    self.op("pe", mm, reads=[("w", wname)] + self.H1F_KEYS,
                            writes=[bkey], tag=f"{wname}p{g}_{n}")
                    if dstf is None:
                        war = [("h1t", i) for i in range(4)] if (g == 0 and n == 0) else []
                        self.op("dve", lambda e, g=g, n=n, gb=gb, bank=bank:
                                e.tensor_scalar(
                                    Kg[gb][:, n * 512:(n + 1) * 512], bank[:, 0:512],
                                    cb["bkt"][:, g:g + 1], None, op0=ALU.add),
                                reads=[bkey, ("c", "bkt")], writes=[("Kg", gb, n)],
                                war=[("relu", 0), ("relu", 1)] if (g < 2 and n == 0) else [],
                                tag=f"kpe{g}_{n}")
                    else:
                        war = ([("h1t", i) for i in range(4)] + ["sqj"]) \
                            if (g == 0 and n == 0) else []
                        self.op("dve", lambda e, g=g, n=n, bank=bank: e.tensor_scalar(
                            Vfm[:, n * 512:(n + 1) * 512], bank[:, 0:512],
                            cb["bvt"][:, g:g + 1], None, op0=ALU.add),
                            reads=[bkey, ("c", "bvt")], writes=[("Vfm", n)],
                            war=war, tag=f"vpe{g}_{n}")
            for q in range(8):
                bank = q % 2
                srcs = [Vfm[:, (q * 4 + j) * 128:(q * 4 + j + 1) * 128]
                        for j in range(4)]

                def ev(e, gb=gb, q=q, bank=bank):
                    return e.tensor_copy(
                        Vg[gb][:, q * 4:q * 4 + 4, :],
                        self.pT[bank][:, 0:512].rearrange("p (T d) -> p T d", T=4))
                self.tbatch(srcs, 128, ev, bank, VFM_KEYS, [("Vg", gb, q)],
                            f"vt{g}_{q}",
                            war=["zt"] if (g == 1 and q >= 6) else [])
            for hp in range(2):
                h = 2 * g + hp
                sbk = [self.pS, self.pX][hp]
                sbkey = ["pS", "pX"][hp]
                for n in range(8):
                    def mm(e, g=g, hp=hp, n=n, sbk=sbk, gb=gb, h=h):
                        return e.matmul(sbk[0:16, 0:512],
                                        b["qselP"][:, h * 16:(h + 1) * 16],
                                        Kg[gb][:, n * 512:(n + 1) * 512],
                                        start=True, stop=True)
                    self.op("pe", mm, reads=["qselP", ("Kg", gb, n)],
                            writes=[sbkey], tag=f"sb{h}_{n}")
                    self.op("act", lambda e, hp=hp, n=n, sbk=sbk: e.activation(
                        expb[hp][:, n * 512:(n + 1) * 512], sbk[0:16, 0:512], AF.Exp,
                        accum_out=b["denb8"][0:16, n:n + 1]),
                        reads=[sbkey], writes=[("expb", hp, n), "denb8"],
                        war=["sqj"] if (g == 0 and n < 2) else [],
                        tag=f"se{h}_{n}")
                self.op("dve", lambda e, h=h: e.tensor_reduce(
                    b["rdenb"][0:16, h:h + 1], b["denb8"][0:16, :], axis=AX.X,
                    op=ALU.add),
                    reads=["denb8"], writes=[("rdt", h)], tag=f"dn{h}")
                self.op("dve", lambda e, h=h: e.reciprocal(
                    b["rdenb"][0:16, h:h + 1], b["rdenb"][0:16, h:h + 1]),
                    reads=[("rdt", h)], writes=[("rdenb", h), ("rdt", h)],
                    tag=f"dr{h}")
            for hp in range(2):
                ebkeys = [("expb", hp, n) for n in range(8)]
                for q in range(8):
                    bank = q % 2
                    srcs = [expb[hp][:, (q * 4 + j) * 128:(q * 4 + j + 1) * 128]
                            for j in range(4)]

                    def ev(e, hp=hp, q=q, bank=bank):
                        return e.tensor_copy(
                            expT[:, q * 4:q * 4 + 4, hp * 16:(hp + 1) * 16],
                            self.pT[bank][:, 0:64].rearrange("p (T j) -> p T j", T=4))
                    self.tbatch(srcs, 16, ev, bank, ebkeys, [("expT", hp, q)],
                                f"bt{g}_{hp}_{q}", base=32 * hp)
            for hp in range(2):
                h = 2 * g + hp
                bk01 = 0 if h < 8 else 1

                def mmc(e, gb=gb, hp=hp, h=h, first_ctx=first_ctx, bk01=bk01):
                    ins = None
                    for kt in range(NT):
                        ins = e.matmul(pBig[0:16, h * 64:(h + 1) * 64],
                                       expT[:, kt, hp * 16:(hp + 1) * 16],
                                       Vg[gb][:, kt, 64 * hp:64 * hp + 64],
                                       start=(first_ctx[bk01] and kt == 0),
                                       stop=(h in (7, 15) and kt == NT - 1))
                    first_ctx[bk01] = False
                    return ins
                self.op("pe", mmc,
                        reads=[("expT", hp, q) for q in range(8)] +
                              [("Vg", gb, q) for q in range(8)],
                        writes=[("pBig", bk01)], tag=f"cb{h}")
        for h in range(H):
            bk01 = 0 if h < 8 else 1
            self.op("dve", lambda e, h=h: e.tensor_scalar(
                b["ctxbT"][:, h * 64:(h + 1) * 64],
                self.pBig[0:16, h * 64:(h + 1) * 64],
                b["rdenb"][0:16, h:h + 1], None, op0=ALU.mult),
                reads=[("pBig", bk01), ("rdenb", h)], writes=[("ctxbT", h)],
                tag=f"ce{h}")

    def load_w_half(self, name, wofs, h, key, war, tag):
        src = self.d_in[name].rearrange("(k p) f -> p k f", p=128)[
            :, :, h * 512:(h + 1) * 512]
        dst = self.WA[:, wofs:wofs + 8192].rearrange("p (k f) -> p k f", k=8)[
            :, :, h * 512:(h + 1) * 512]

        def fn(e, src=src, dst=dst):
            return e.dma_start(out=dst, in_=src)
        self.op("pool", fn, writes=[key], war=war, inc=16, tag=tag, sem="wld")

    def emit_p2(self):
        import os as _os
        b, cb = self.b, self.cb
        h1F = self.A1[:].rearrange("p (m t) -> p m t", m=8)
        h2F = h1F
        self.load_w("wo", 8192, 1024, key=("w", "wo"), war=[("w", "wv")])
        self.load_w("wq", 16384, 1024, key=("w", "wqsel"), war=["qselP"])
        kselF = b["kselF"][:].rearrange("p (m j) -> p m j", m=8)
        self.emit_selP("kselP", kselF, war=[("w", "wk")])

        def qch(m):
            if m < 4:
                return self.A2[:, m * 4096:(m + 1) * 4096]
            return self.A3[:, (m - 4) * 4096:(m - 3) * 4096]

        for m in range(8):
            for n in range(8):
                bank = [self.pA, self.pB][(m * 8 + n) % 2]
                bkey = ["pA", "pB"][(m * 8 + n) % 2]

                def mm(e, m=m, n=n, bank=bank):
                    ins = None
                    for k in range(8):
                        ins = e.matmul(bank[:, 0:512],
                                       self.wck(16384, 1024, k, (m * 128, (m + 1) * 128)),
                                       h1F[:, k, n * 512:(n + 1) * 512],
                                       start=(k == 0), stop=(k == 7))
                    return ins
                self.op("pe", mm, reads=[("w", "wqsel")] + self.H1F_KEYS,
                        writes=[bkey], tag=f"qp{m}_{n}")
                self.op("dve", lambda e, m=m, n=n, bank=bank: e.tensor_scalar(
                    qch(m)[:, n * 512:(n + 1) * 512], bank[:, 0:512],
                    cb["bqt"][:, m:m + 1], 0.125, op0=ALU.add, op1=ALU.mult),
                    reads=[bkey, ("c", "bqt")], writes=[("Qf", m, n)],
                    tag=f"qe{m}_{n}")
        QF_KEYS = [("Qf", m, n) for m in range(8) for n in range(8)]

        kselF = b["kselF"][:].rearrange("p (m j) -> p m j", m=8)
        xin = [self.SA[:, 0:1024], self.SA[:, 1024:2048]]
        x2t = [self.SA[:, 2048:3072], self.SA[:, 3072:4096]]
        xd = self.d_in["x"].rearrange("(T p) d -> T p d", p=128)
        x2d = self.x2_dram.rearrange("(T p) d -> T p d", p=128)
        h2d = self.h2_dram.rearrange("(T p) d -> T p d", p=128)

        def p2F(tau):
            bb = tau % 2
            bank = [self.pA, self.pB][tau % 2]
            bkey = ["pA", "pB"][tau % 2]
            st = [lambda: self.op("sp", lambda e: e.dma_start(
                out=xin[bb], in_=xd[tau]),
                writes=[("xin", bb)], war=["impf", "scr"] if tau < 2 else [],
                inc=16, tag=f"px{tau}", sem=f"xin{bb}")]

            def mma(e, tau=tau, bank=bank):
                ins = None
                for h in range(H):
                    m = h // 2
                    ins = e.matmul(bank[:, h * 16:(h + 1) * 16],
                                   qch(m)[:, tau * 128:(tau + 1) * 128],
                                   b["kselP"][:, h * 16:(h + 1) * 16],
                                   start=(h == 0), stop=(h == H - 1))
                return ins
            st.append(lambda: self.op("pe", mma, reads=QF_KEYS + ["kselP"],
                                      writes=[bkey], tag=f"sa{tau}"))
            st.append(lambda: self.op("act", lambda e: e.activation(
                b["exps"][:], bank[:, 0:256], AF.Exp),
                reads=[bkey], writes=["exps"], tag=f"se{tau}"))
            st.append(lambda: self.op("dve", lambda e: e.tensor_reduce(
                b["aden"][:], b["exps"][:].rearrange("p (h j) -> p h j", h=16),
                axis=AX.X, op=ALU.add),
                reads=["exps"], writes=["aden"], tag=f"sd{tau}"))
            st.append(lambda: self.op("dve", lambda e: e.reciprocal(
                b["arden"][:], b["aden"][:]),
                reads=["aden"], writes=["arden"], tag=f"sr{tau}"))
            st.append(lambda: self.op("dve", lambda e: e.tensor_tensor(
                b["wsc"][:].rearrange("p (h j) -> p h j", h=16),
                b["exps"][:].rearrange("p (h j) -> p h j", h=16),
                b["arden"][:].rearrange("p (h j) -> p h j", h=16, j=1)
                .broadcast_to([128, 16, 16]), op=ALU.mult),
                reads=["exps", "arden"], writes=["wsc"], tag=f"sw{tau}"))
            for q in range(4):
                tb = q % 2
                srcs = [b["wsc"][:, (q * 4 + j) * 16:(q * 4 + j + 1) * 16]
                        for j in range(4)]

                def ev(e, tb=tb):
                    return e.tensor_copy(b["wTst"][:, 0:512],
                                         self.pT[tb][0:16, 0:512])
                st.append(lambda q=q, tb=tb, srcs=srcs, ev=ev:
                          self.tbatch(srcs, 128, ev, tb, ["wsc"], ["wTst"],
                                      f"wt{tau}_{q}"))

                def mmc(e, tau=tau, q=q):
                    ins = None
                    for j in range(4):
                        h = q * 4 + j
                        ins = e.matmul(self.pBig[:, h * 64:(h + 1) * 64],
                                       b["wTst"][:, j * 128:(j + 1) * 128],
                                       b["vselT"][:, h * 64:(h + 1) * 64],
                                       start=(h in (0, 8)), stop=False)
                    return ins
                st.append(lambda mmc=mmc, q=q: self.op(
                    "pe", mmc, reads=["wTst", "vselT"],
                    writes=[("pBig", q // 2)], tag=f"cx{tau}_{q}"))

            def mmp(e, tau=tau):
                e.matmul(self.pBig[:, 0:512],
                         b["PT"][:, tau * 128:(tau + 1) * 128],
                         b["ctxbT"][:, 0:512], start=False, stop=True)
                return e.matmul(self.pBig[:, 512:1024],
                                b["PT"][:, tau * 128:(tau + 1) * 128],
                                b["ctxbT"][:, 512:1024], start=False, stop=True)
            st.append(lambda: self.op("pe", mmp,
                      reads=[("PT", tau)] + [("ctxbT", h) for h in range(H)],
                      writes=[("pBig", 0), ("pBig", 1)], tag=f"cp{tau}"))
            return st

        def p2B(tau):
            bb = tau % 2
            bt = b["bt"][bb]
            sh = self.WA[:, 2048:3072]
            st = []
            for half in range(2):
                st.append(lambda half=half: self.op(
                    "dve", lambda e: e.tensor_scalar(
                        bt[:, half * 512:(half + 1) * 512],
                        self.pBig[:, half * 512:(half + 1) * 512],
                        b["notsel"][:, tau:tau + 1], None, op0=ALU.mult),
                    reads=[("pBig", half), "notsel"],
                    writes=[("bt", bb, half)], tag=f"cv{tau}_{half}"))
            for half in range(2):
                srcs = [bt[:, (half * 4 + j) * 128:(half * 4 + j + 1) * 128]
                        for j in range(4)]

                def ev(e, half=half):
                    return e.tensor_copy(
                        self.ctxf[:, half * 512:(half + 1) * 512],
                        self.pT[half][:, 0:512])
                st.append(lambda half=half, srcs=srcs, ev=ev: self.tbatch(
                    srcs, 128, ev, half, [("bt", bb, 0), ("bt", bb, 1)],
                    [("ctxf", half)], f"cf{tau}_{half}"))
            for half, pw, pwkey in ((0, self.pS, "pS"), (1, self.pX, "pX")):
                def mmw(e, half=half, pw=pw):
                    for k in range(8):
                        e.matmul(pw[:, 0:512],
                                 self.ctxf[:, k * 128:(k + 1) * 128],
                                 self.wck(8192, 1024, k,
                                          (half * 512, (half + 1) * 512)),
                                 start=(k == 0), stop=False)
                    return e.matmul(pw[:, 0:512], b["onesb32"],
                                    b["borowb"][:, half * 512:(half + 1) * 512],
                                    start=False, stop=True)
                st.append(lambda mmw=mmw, pwkey=pwkey: self.op(
                    "pe", mmw, reads=[("ctxf", 0), ("ctxf", 1), ("w", "wo"),
                                      "borowb", "onesb32"],
                    writes=[pwkey], tag=f"wo{tau}_{half}"))
                st.append(lambda half=half, pw=pw, pwkey=pwkey: self.op(
                    "dve", lambda e: e.tensor_tensor(
                        x2t[bb][:, half * 512:(half + 1) * 512], pw[:, 0:512],
                        xin[bb][:, half * 512:(half + 1) * 512], op=ALU.add),
                    reads=[pwkey, ("xin", bb)], writes=[("x2t", bb, half)],
                    tag=f"we{tau}_{half}"))
            st.append(lambda: self.op("sp", lambda e: e.dma_start(
                out=x2d[tau], in_=x2t[bb]),
                reads=[("x2t", bb, 0), ("x2t", bb, 1)],
                writes=[("x2_dram", tau)],
                inc=16, tag=f"sx{tau}", sem=f"sx{bb}"))
            st.append(lambda: self.ln_head(
                x2t[bb], [("x2t", bb, 0), ("x2t", bb, 1)], f"l2_{tau}",
                mk=f"mean{bb}"))
            st.append(lambda: self.ln_tail(
                x2t[bb], [("x2t", bb, 0), ("x2t", bb, 1)], bt,
                [("bt", bb, 0), ("bt", bb, 1)], sh, f"l2_{tau}",
                mk=f"mean{bb}", rk=f"rstd{bb}", qk=f"ssqA{bb}"))
            st.append(lambda: self.op("sp", lambda e: e.dma_start(
                out=h2d[tau], in_=bt),
                reads=[("bt", bb, 0), ("bt", bb, 1)],
                writes=[("h2_dram", tau)],
                inc=16, tag=f"sh{tau}", sem="sh2"))
            for half in range(2):
                srcs = [bt[:, (half * 4 + j) * 128:(half * 4 + j + 1) * 128]
                        for j in range(4)]

                def ev(e, tau=tau, half=half):
                    return e.tensor_copy(
                        h2F[:, half * 4:half * 4 + 4,
                            tau * 128:(tau + 1) * 128],
                        self.pT[half][:, 0:512].rearrange(
                            "p (m t) -> p m t", m=4))
                st.append(lambda half=half, srcs=srcs, ev=ev, tau=tau:
                          self.tbatch(
                              srcs, 128, ev, half,
                              [("bt", bb, 0), ("bt", bb, 1)],
                              [("h2F", tau, half)], f"t2{tau}_{half}",
                              war=[("h1F", tau, hh) for hh in range(2)]))
            return st

        for i in range(self.p2tau + 1):
            A = p2F(i) if i < self.p2tau else []
            B = p2B(i - 1) if i >= 1 else []
            na, nb = len(A), len(B)
            ia = ib = 0
            while ia < na or ib < nb:
                if ia < na and (ib >= nb or ia * (nb + 1) <= ib * (na + 1)):
                    A[ia]()
                    ia += 1
                else:
                    B[ib]()
                    ib += 1

    def emit_route(self):
        b, cb = self.b, self.cb
        d_in = self.d_in
        h2F = self.A1[:].rearrange("p (m t) -> p m t", m=8)
        gw = b["gatew"].rearrange("p (m n) -> p m n", m=8)
        self.op("pool", lambda e: e.dma_start(
            out=gw, in_=d_in["gatewt"].rearrange("(m p) n -> p m n", p=128)),
            writes=["gatew"],
            war=[("Qf", 4, n) for n in range(8)] + [("Kg", 0, n) for n in range(8)],
            inc=16, tag="ld_gatew", sem="wld")

        def mmg(e):
            ins = None
            for tau in range(NT):
                for k in range(8):
                    ins = e.matmul(self.pA[:, tau * 16:(tau + 1) * 16],
                                   h2F[:, k, tau * 128:(tau + 1) * 128],
                                   gw[:, k, :],
                                   start=(tau == 0 and k == 0),
                                   stop=(tau == NT - 1 and k == 7))
            return ins
        self.op("pe", mmg, reads=self.H2F_KEYS + ["gatew"], writes=["pA"], tag="gt")
        self.op("act", lambda e: e.activation(b["expg"], self.pA[:, 0:512], AF.Exp),
                reads=["pA"], writes=["expg"], war=[("xin", 0), ("xin", 1)],
                tag="ge")
        self.op("dve", lambda e: e.tensor_reduce(
            b["rden"][:], b["expg"].rearrange("p (T j) -> p T j", T=NT),
            axis=AX.X, op=ALU.add),
            reads=["expg"], writes=["rden"], tag="gd")
        self.op("dve", lambda e: e.reciprocal(b["rden"][:], b["rden"][:]),
                reads=["rden"], writes=["rden"], tag="gr")
        XW = [("xin", 0), ("xin", 1)]
        r3 = b["reqall"].rearrange("p (T j) -> p T j", T=NT)
        q3 = b["req2all"].rearrange("p (T j) -> p T j", T=NT)
        expg3 = b["expg"].rearrange("p (T j) -> p T j", T=NT)
        iotam = cb["iota16"][:].unsqueeze(1).broadcast_to((128, NT, 16))
        HIT_W = [("hit", tau) for tau in range(NT)]
        self.op("dve", lambda e: e.tensor_reduce(
            b["m1v"], expg3, axis=AX.X, op=ALU.max),
            reads=["expg"], writes=["m1v"], war=XW, tag="v1")
        self.op("dve", lambda e: e.tensor_tensor(
            r3, expg3, b["m1v"].unsqueeze(2).broadcast_to((128, NT, 16)),
            op=ALU.is_ge),
            reads=["expg", "m1v"], writes=["reqall"], war=XW, tag="v2")
        self.op("dve", lambda e: e.scalar_tensor_tensor(
            q3, r3, -1e30, expg3, op0=ALU.mult, op1=ALU.add),
            reads=["reqall", "expg"], writes=["req2all"],
            war=[("x2t", 1, 0), ("x2t", 1, 1)], tag="v3")
        self.op("dve", lambda e: e.tensor_reduce(
            b["m2v"], q3, axis=AX.X, op=ALU.max),
            reads=["req2all"], writes=["m2v"], war=XW, tag="v4")
        self.op("dve", lambda e: e.tensor_tensor(
            q3, q3, b["m2v"].unsqueeze(2).broadcast_to((128, NT, 16)),
            op=ALU.is_ge),
            reads=["req2all", "m2v"], writes=["req2all"], tag="v5")
        self.op("dve", lambda e: e.tensor_add(
            b["hit"][:, :], b["reqall"], b["req2all"]),
            reads=["reqall", "req2all"], writes=HIT_W,
            war=[("Qf", m, n) for m in range(4) for n in range(8)], tag="v6")
        self.op("dve", lambda e: e.tensor_tensor(
            r3, r3, iotam, op=ALU.mult),
            reads=["reqall", ("c", "iota16")], writes=["reqall"], tag="v7")
        self.op("dve", lambda e: e.tensor_reduce(
            b["eav"], r3, axis=AX.X, op=ALU.add),
            reads=["reqall"], writes=["eav"], war=XW, tag="v8")
        self.op("dve", lambda e: e.tensor_tensor(
            q3, q3, iotam, op=ALU.mult),
            reads=["req2all", ("c", "iota16")], writes=["req2all"], tag="v9")
        self.op("dve", lambda e: e.tensor_reduce(
            b["ebv"], q3, axis=AX.X, op=ALU.add),
            reads=["req2all"], writes=["ebv"], war=XW, tag="va")
        self.op("dve", lambda e: e.tensor_tensor(
            b["elo"][:, :], b["eav"], b["ebv"], op=ALU.min),
            reads=["eav", "ebv"], war=XW,
            writes=[("elo", t) for t in range(NT)], tag="vb")
        self.op("dve", lambda e: e.tensor_tensor(
            b["ehi"][:, :], b["eav"], b["ebv"], op=ALU.max),
            reads=["eav", "ebv"],
            writes=[("ehi", t) for t in range(NT)], tag="vc")
        self.op("dve", lambda e: e.tensor_tensor(
            b["flagv"], b["eav"], b["ebv"], op=ALU.is_lt),
            reads=["eav", "ebv"], writes=["flagv"], war=XW, tag="vd")
        self.op("dve", lambda e: e.tensor_tensor(
            b["ddv"], b["m1v"], b["m2v"], op=ALU.subtract),
            reads=["m1v", "m2v"], writes=["ddv"], war=XW, tag="ve")
        self.op("dve", lambda e: e.tensor_tensor(
            b["ddv"], b["ddv"], b["flagv"], op=ALU.mult),
            reads=["ddv", "flagv"], writes=["ddv"], tag="vf")
        self.op("dve", lambda e: e.tensor_tensor(
            b["wlov"], b["m2v"], b["ddv"], op=ALU.add),
            reads=["m2v", "ddv"], writes=["wlov"], war=XW, tag="vg")
        self.op("dve", lambda e: e.tensor_tensor(
            b["whiv"], b["m1v"], b["m2v"], op=ALU.add),
            reads=["m1v", "m2v"], writes=["whiv"], war=XW, tag="vh")
        self.op("dve", lambda e: e.tensor_tensor(
            b["whiv"], b["whiv"], b["wlov"], op=ALU.subtract),
            reads=["whiv", "wlov"], writes=["whiv"], tag="vi")
        self.op("dve", lambda e: e.tensor_tensor(
            b["w0"][:, :], b["wlov"], b["rden"][:], op=ALU.mult),
            reads=["wlov", "rden"], writes=[("w0", t) for t in range(NT)],
            tag="vj")
        self.op("dve", lambda e: e.tensor_tensor(
            b["w1"][:, :], b["whiv"], b["rden"][:], op=ALU.mult),
            reads=["whiv", "rden"], writes=[("w1", t) for t in range(NT)],
            tag="vk")
        HIT_KEYS = [("hit", tau) for tau in range(NT)]
        hit3 = b["hit"].rearrange("p (T j) -> p T j", T=NT)
        rall3 = b["rall"].rearrange("p (T j) -> p T j", T=NT)
        def rank_chain(ex):
            ps = ex % 2
            sfx = "B" if ps else ""
            ops = [lambda: self.op(
                "dve", lambda e, ex=ex, sfx=sfx: e.tensor_copy(
                    b["hitbf" + sfx][:], hit3[:, :, ex]),
                reads=HIT_KEYS, writes=["hitbf" + sfx], tag=f"rh{ex}")]
            ops += self.rank_ops(hit3[:, :, ex], HIT_KEYS, rall3[:, :, ex],
                                 [("rall", ex)], f"re{ex}",
                                 war=[("xin", 0), ("xin", 1), "reqall"]
                                 if ex < 2 else [], ps=ps)
            ops.append(lambda: self.op(
                "pe", lambda e, ex=ex, sfx=sfx: e.matmul(
                    self.pBig[0:1, ex:ex + 1], b["rowtotb" + sfx][:, 0:1],
                    cb["onescol"][:], start=True, stop=True),
                reads=["rowtotb" + sfx, ("c", "onescol")],
                writes=[("pBig", 0)], tag=f"rc{ex}"))
            return ops

        for p0 in range(0, NE, 2):
            for a, bl in zip(rank_chain(p0), rank_chain(p0 + 1)):
                a()
                bl()
        RALL_KEYS = [("rall", ex) for ex in range(NE)]
        self.op("dve", lambda e: e.tensor_copy(b["cntrow"][:], self.pBig[0:1, 0:16]),
                reads=[("pBig", 0)], writes=["cntrow"],
                war=[("xin", 0), ("xin", 1)], tag="pc0")
        self.op("dve", lambda e: e.tensor_scalar(
            b["pcs"][:], b["cntrow"][:], 127.49, 1.0 / 256.0,
            op0=ALU.add, op1=ALU.mult),
            reads=["cntrow"], writes=["pcs"], tag="pc1")
        self.op("dve", lambda e: e.tensor_copy(b["pc32"][:], b["pcs"][:]),
                reads=["pcs"], writes=["pc32"], tag="pc2")
        self.op("dve", lambda e: e.tensor_copy(b["pcs"][:], b["pc32"][:]),
                reads=["pc32"], writes=["pcs"], tag="pc3")
        self.op("dve", lambda e: e.tensor_scalar(
            b["pcrow"][:], b["pcs"][:], 256.0, None, op0=ALU.mult),
            reads=["pcs"], writes=["pcrow"], tag="pc4")
        self.op("dve", lambda e: e.memset(b["pofsrow"][:, 0:1], 0.0),
                writes=["pofsrow"], war=[("xin", 0), ("xin", 1)], tag="po0")
        for ex in range(1, NE):
            self.op("dve", lambda e, ex=ex: e.tensor_tensor(
                b["pofsrow"][:, ex:ex + 1], b["pofsrow"][:, ex - 1:ex],
                b["pcrow"][:, ex - 1:ex], op=ALU.add),
                reads=["pofsrow", "pcrow"], writes=["pofsrow"], tag=f"po{ex}")
        self.op("pe", lambda e: e.matmul(
            self.pS[:, 0:16], cb["onesrow"][:], b["pofsrow"][:],
            start=True, stop=True),
            reads=["pofsrow", ("c", "onesrow")], writes=["pS"], tag="pob")
        self.op("dve", lambda e: e.tensor_copy(b["pofsb"][:], self.pS[:, 0:16]),
                reads=["pS"], writes=["pofsb"],
                war=[("xin", 0), ("xin", 1)], tag="poc")
        for ex in range(NE):
            self.op("dve", lambda e, ex=ex: e.tensor_scalar(
                rall3[:, :, ex], rall3[:, :, ex], b["pofsb"][:, ex:ex + 1], None,
                op0=ALU.add),
                reads=[("rall", ex), "pofsb"], writes=[("rall", ex)], tag=f"pr{ex}")
        self.op("dve", lambda e: e.memset(b["texp"][:], -1.0),
                writes=["texp"], tag="tx_0")
        for ex in range(NE):
            self.op("dve", lambda e, ex=ex: e.tensor_scalar(
                b["t80"][:], cb["tb80"][:], b["pofsrow"][:, ex:ex + 1], None,
                op0=ALU.is_ge),
                reads=[("c", "tb80"), "pofsrow"], writes=["t80"], tag=f"tx{ex}a")
            self.op("dve", lambda e: e.tensor_tensor(
                b["texp"][:], b["texp"][:], b["t80"][:], op=ALU.add),
                reads=["texp", "t80"], writes=["texp"], tag=f"tx{ex}b")
        self.op("pe", lambda e: e.matmul(
            self.pX[:, 0:NG], b["onesb32"][:], b["texp"][:],
            start=True, stop=True),
            reads=["texp", "onesb32"], writes=["pX"], tag="txb")
        self.op("dve", lambda e: e.tensor_scalar(
            self.pX[:, 0:NG], self.pX[:, 0:NG], 128.0,
            cb["iotaf"][:, 0:1], op0=ALU.mult, op1=ALU.add),
            reads=["pX", ("c", "iotaf")], writes=["pX"], tag="txd")
        self.op("dve", lambda e: e.tensor_copy(b["idxmat"][:], self.pX[:, 0:NG]),
                reads=["pX"], writes=["idxmat"], tag="txf")
        for ekey, dkey in (("elo", "dest0f"), ("ehi", "dest1f")):
            self.op("dve", lambda e, ekey=ekey: e.tensor_tensor(
                q3, iotam,
                b[ekey][:, :].unsqueeze(2).broadcast_to((128, NT, 16)),
                op=ALU.is_equal),
                reads=[(ekey, t) for t in range(NT)] + [("c", "iota16")],
                writes=["req2all"], tag=f"dw1{dkey}")
            self.op("dve", lambda e: e.tensor_tensor(
                q3, q3, rall3, op=ALU.mult),
                reads=["req2all"] + RALL_KEYS, writes=["req2all"],
                tag=f"dw2{dkey}")
            self.op("dve", lambda e, dkey=dkey: e.tensor_reduce(
                b[dkey][:, :], q3, axis=AX.X, op=ALU.add),
                reads=["req2all"], writes=[(dkey, t) for t in range(NT)],
                tag=f"dw3{dkey}")
        self.op("dve", lambda e: e.tensor_copy(b["dest0"][:], b["dest0f"]),
                reads=[("dest0f", tau) for tau in range(NT)], writes=["dest0"],
                war=["pc32"], tag="dc0")
        self.op("dve", lambda e: e.tensor_copy(b["dest1"][:], b["dest1f"]),
                reads=[("dest1f", tau) for tau in range(NT)], writes=["dest1"],
                tag="dc1")

    def emit_scatter(self):
        import os as _os_sc
        b = self.b
        self.load_w("sw1t", 0, 1023, key=("w", "sw1t"),
                    war=[("w", "wk"), "kselP", ("ctxf", 0), ("ctxf", 1), "exps",
                         "wsc", "sqj"] +
                        [("bt", i, h) for i in range(2) for h in range(2)])
        self.load_w("sw3t", 8192, 1023, key=("w", "sw3t"), war=[("w", "wo")])
        self.load_w("sw2ts", 16384, 1023, key=("w", "sw2ts"),
                    war=[("w", "wqsel"), "qselP"])
        self.op("pool", lambda e: e.dma_start(
            out=b["sw2t0"], in_=self.d_in["sw2t0"]),
            writes=["sw2t0"], inc=16, tag="ld_sw2t0", sem="wld")
        h2d = self.h2_dram.rearrange("(T p) d -> T p d", p=128)
        zkeys = [("eind_z", i) for i in range(NTILE)]
        sst = [self.A2[:, 12288:13312], self.A2[:, 13312:14336]]
        QF_ALL = [("Qf", m, n) for m in range(8) for n in range(8)]
        for tau in range(NT):
            bb = tau % 2
            stg = sst[bb]
            self.op("sp", lambda e, tau=tau, stg=stg: e.dma_start(
                out=stg, in_=h2d[tau]),
                reads=[("h2_dram", tau)] +
                      ([("w", "sw2ts")] if tau == 0 else []),
                writes=[("sst", bb)],
                war=QF_ALL if tau < 2 else [], inc=16, tag=f"sl{tau}",
                sem=f"st{bb}")
            for slot, dest in ((0, b["dest0"]), (1, b["dest1"])):
                self.op("pool", lambda e, tau=tau, stg=stg, dest=dest:
                        e.indirect_dma_start(
                            out=self.ein_dram,
                            out_offset=bass.IndirectOffsetOnAxis(
                                ap=dest[:, tau:tau + 1], axis=0),
                            in_=stg, in_offset=None,
                            bounds_check=self.bcreg, oob_is_err=False),
                        reads=[("sst", bb),
                               "dest0" if slot == 0 else "dest1"] + zkeys,
                        writes=[("ein", tau, slot)] + (
                            ["sc_chain"] if _os_sc.environ.get("SIMSAFE")
                            else []),
                        inc=16, tag=f"sc{tau}_{slot}", sem=f"sc{tau % 2}")

    def emit_shared(self):
        b, cb = self.b, self.cb
        h2F = self.A1[:].rearrange("p (m t) -> p m t", m=8)
        QF_WAR = [("Qf", m, n) for m in range(8) for n in range(8)]
        a_sh = self.A3[:, 0:8192].rearrange("p (m t) -> p m t", m=8)
        s_sh = self.A3[:, 8192:16384].rearrange("p (m t) -> p m t", m=8)
        z_sh = self.A2[:, 0:8192].rearrange("p (m t) -> p m t", m=8)
        zsq = self.A2[:, 8192:9216]
        sqz = self.A2[:, 9216:10240]
        zshd = self.zsh_dram.rearrange("(T p) d -> T p d", p=128)

        for blk in range(4):
            t0 = blk * 1024
            for m in range(8):
                rows = 128 if m < 7 else 127
                for nh in range(2):
                    bank = [self.pA, self.pB][nh]
                    bkey = ["pA", "pB"][nh]

                    def mm1(e, m=m, nh=nh, bank=bank, t0=t0, rows=rows):
                        ins = None
                        for k in range(8):
                            ins = e.matmul(bank[0:rows, 0:512],
                                           self.wck(0, 1023, k, (m * 128, m * 128 + rows)),
                                           h2F[:, k, t0 + nh * 512:t0 + (nh + 1) * 512],
                                           start=(k == 0), stop=(k == 7))
                        return ins
                    self.op("pe", mm1, reads=[("w", "sw1t")] + self.H2F_KEYS,
                            writes=[bkey], tag=f"s1_{blk}_{m}_{nh}")
                    self.op("act", lambda e, m=m, nh=nh, bank=bank, rows=rows:
                            e.activation(
                                a_sh[0:rows, m, nh * 512:(nh + 1) * 512],
                                bank[0:rows, 0:512], AF.Silu,
                                bias=cb["bs1st"][0:rows, m:m + 1]),
                            reads=[bkey, ("c", "bs1st")], writes=[("ash", m, nh)],
                            war=QF_WAR if (blk == 0 and m == 4 and nh == 0) else [],
                            tag=f"s1e_{blk}_{m}_{nh}")

                    def mm3(e, m=m, nh=nh, bank=bank, t0=t0, rows=rows):
                        ins = None
                        for k in range(8):
                            ins = e.matmul(bank[0:rows, 0:512],
                                           self.wck(8192, 1023, k,
                                                    (m * 128, m * 128 + rows)),
                                           h2F[:, k, t0 + nh * 512:t0 + (nh + 1) * 512],
                                           start=(k == 0), stop=(k == 7))
                        return ins
                    self.op("pe", mm3, reads=[("w", "sw3t")] + self.H2F_KEYS,
                            writes=[bkey], tag=f"s3_{blk}_{m}_{nh}")
                    self.op("dve", lambda e, m=m, nh=nh, bank=bank, rows=rows:
                            e.scalar_tensor_tensor(
                                s_sh[0:rows, m, nh * 512:(nh + 1) * 512],
                                bank[0:rows, 0:512], cb["bs3st"][0:rows, m:m + 1],
                                a_sh[0:rows, m, nh * 512:(nh + 1) * 512],
                                op0=ALU.add, op1=ALU.mult),
                            reads=[bkey, ("c", "bs3st"), ("ash", m, nh)],
                            writes=[("ssh", m, nh)], tag=f"s3e_{blk}_{m}_{nh}")
            for m in range(8):
                rows = 128 if m < 7 else 127
                self.op("act", lambda e, m=m, rows=rows: e.activation(
                    zsq[0:rows, :], s_sh[0:rows, m, :], AF.Square),
                    reads=[("ssh", m, 0), ("ssh", m, 1)], writes=["zsq"],
                    war=QF_WAR if (blk == 0 and m == 0) else [],
                    tag=f"sq_{blk}_{m}")
                for nh, pw, pwkey in ((0, self.pS, "pS"), (1, self.pX, "pX")):
                    self.op("pe", lambda e, m=m, nh=nh, pw=pw, rows=rows: e.matmul(
                        pw[0:1, 0:512], cb["onescol"][0:rows, :],
                        zsq[0:rows, nh * 512:(nh + 1) * 512],
                        start=(m == 0), stop=(m == 7)),
                        reads=["zsq", ("c", "onescol")], writes=[pwkey],
                        tag=f"ts_{blk}_{m}_{nh}")
            for nh, pw, pwkey in ((0, self.pS, "pS"), (1, self.pX, "pX")):
                self.op("act", lambda e, nh=nh, pw=pw: e.activation(
                    b["timerow"][:, nh * 512:(nh + 1) * 512], pw[0:1, 0:512],
                    AF.Sqrt, bias=b["c1"][0:1, :]),
                    reads=[pwkey, "c1"], writes=[("time", nh)],
                    war=["borowb"] if blk == 0 else [], tag=f"tt_{blk}_{nh}")
            for m in range(8):
                rows_out = 128 if m < 7 else 127
                for nh in range(2):
                    def mmz(e, m=m, nh=nh, mc=rows_out):
                        ins = None
                        for k in range(8):
                            rows = 128 if k < 7 else 127
                            ins = e.matmul(self.pBig[0:mc, nh * 512:(nh + 1) * 512],
                                           self.wck(16384, 1023, k,
                                                    (m * 128, m * 128 + mc))[0:rows, :],
                                           s_sh[0:rows, k, nh * 512:(nh + 1) * 512],
                                           start=(k == 0), stop=False)
                        return e.matmul(self.pBig[0:mc, nh * 512:(nh + 1) * 512],
                                        b["sw2t0"][:, m * 128:m * 128 + mc],
                                        b["timerow"][:, nh * 512:(nh + 1) * 512],
                                        start=False, stop=True)
                    self.op("pe", mmz,
                            reads=[("w", "sw2ts"), "sw2t0", ("time", nh)] +
                                  [("ssh", mm_, nn) for mm_ in range(8)
                                   for nn in range(2)],
                            writes=[("pBig", nh)], tag=f"z_{blk}_{m}_{nh}")
                    self.op("dve", lambda e, m=m, nh=nh, ro=rows_out: e.tensor_scalar(
                        z_sh[0:ro, m, nh * 512:(nh + 1) * 512],
                        self.pBig[0:ro, nh * 512:(nh + 1) * 512],
                        cb["bs2st"][0:ro, m:m + 1], None, op0=ALU.add),
                        reads=[("pBig", nh), ("c", "bs2st")],
                        writes=[("zsh", m, nh)], tag=f"ze_{blk}_{m}_{nh}")
            for tc in range(8):
                tau = blk * 8 + tc
                sbb = tc % 2
                stg = self.A2[:, 10240 + sbb * 1024:10240 + (sbb + 1) * 1024][:]
                for half in range(2):
                    if half == 0:
                        srcs = [z_sh[:, j, tc * 128:(tc + 1) * 128] for j in range(4)]
                        dst = stg[:, 1:513]
                        pws = 512
                    else:
                        srcs = ([z_sh[:, 4 + j, tc * 128:(tc + 1) * 128]
                                 for j in range(3)] +
                                [z_sh[0:127, 7, tc * 128:(tc + 1) * 128]])
                        dst = stg[:, 513:1024]
                        pws = 511

                    def ev(e, dst=dst, half=half, pws=pws):
                        return e.tensor_copy(dst, self.pT[half][:, 0:pws])
                    self.tbatch(srcs, 128, ev, half,
                                [("zsh", half * 4 + j, nn)
                                 for j in range(4) for nn in range(2)],
                                [("shs", sbb, half)], f"zt{tau}_{half}")
                self.op("act", lambda e, stg=stg: e.activation(
                    sqz[:, 0:1023], stg[:, 1:1024], AF.Square, accum_out=b["ssq"]),
                    reads=[("shs", sbb, 0), ("shs", sbb, 1)], writes=["ssq", "sqz"],
                    tag=f"zs{tau}")
                self.op("act", lambda e, stg=stg: e.activation(
                    stg[:, 0:1], b["ssq"], AF.Sqrt, bias=b["c1"]),
                    reads=["ssq", "c1"], writes=[("shs", sbb, "t")], tag=f"zq{tau}")
                self.op("sp", lambda e, tau=tau, stg=stg: e.dma_start(
                    out=zshd[tau], in_=stg[:, 0:1024]),
                    reads=[("shs", sbb, 0), ("shs", sbb, 1), ("shs", sbb, "t")],
                    writes=[("zsh_dram", tau)], inc=16, tag=f"zst{tau}", sem="zst")

    def emit_experts(self):
        b, cb = self.b, self.cb
        A1, WA = self.A1, self.WA
        wslot = [A1[:, 0:BLOBW], A1[:, BLOBW:2 * BLOBW]]
        in_tm = [A1[:, 28672 + i * 1024:28672 + (i + 1) * 1024] for i in range(4)]
        in_fm = [WA[:, 8192:9216].rearrange("p (k c) -> p k c", k=8),
                 WA[:, 9216:10240].rearrange("p (k c) -> p k c", k=8)]
        a_tm = [WA[:, 0:512], WA[:, 512:1024]]
        s_tm = [WA[:, 1024:1536], WA[:, 1536:2048]]
        out_tm = [WA[:, 2048:3072], WA[:, 3072:4096]]
        sqs = WA[:, 4096:4608]
        sqz = WA[:, 4608:5632]
        s_fm = [WA[:, 5632:6144].rearrange("p (k c) -> p k c", k=4),
                WA[:, 6144:6656].rearrange("p (k c) -> p k c", k=4)]
        eind = self.ein_dram.rearrange("(n p) d -> n p d", p=128)
        eoutd = self.eout_dram.rearrange("(n p) d -> n p d", p=128)
        EIN_KEYS = [("ein", tau, slot) for tau in range(NT) for slot in (0, 1)]
        WB_KEYS = [("wblob", ex, j) for ex in range(NE) for j in range(6)]

        def ld_in(k):
            s4 = k % 4
            self.op("sp", lambda e, k=k, s4=s4: e.dma_start(
                out=in_tm[s4], in_=eind[k]),
                reads=EIN_KEYS + [("eind_z", k)], writes=[("etm", s4)],
                war=self.H2F_KEYS if k < 4 else [], inc=16, tag=f"el{k}",
                sem=f"el{s4 % 2}")

        def gather_w(g):
            sb = g % 2
            war = self.H2F_KEYS if g < 2 else [("ewt", g - 2)]
            self.op("pool", lambda e, sb=sb, g=g: e.indirect_dma_start(
                out=wslot[sb], out_offset=None, in_=self.wblob,
                in_offset=bass.IndirectOffsetOnAxis(
                    ap=b["idxmat"][:, g:g + 1], axis=0),
                bounds_check=self.bcreg, oob_is_err=False),
                reads=["idxmat"] + WB_KEYS, writes=[("ewt", g)], war=war,
                inc=16, tag=f"gw{g}", sem="ewa" if sb == 0 else "ewb")

        def et_t(k):
            sb, s4 = k % 2, k % 4
            f2 = k < 2
            for hf in range(2):
                srcs = [in_tm[s4][:, (hf * 4 + j) * 128:(hf * 4 + j + 1) * 128]
                        for j in range(4)]

                def ev(e, sb=sb, hf=hf):
                    return e.tensor_copy(
                        in_fm[sb][:, hf * 4:hf * 4 + 4, :],
                        self.pT[hf][:, 0:512].rearrange("p (m t) -> p m t", m=4))
                self.tbatch(srcs, 128, ev, hf, [("etm", s4)],
                            [("efm", sb, hf)], f"et{k}_{hf}",
                            war=[("w", "sw3t")] if f2 else [])

        def front(k):
            sb = k % 2
            ws = wslot[(k // 2) % 2]
            pS1 = [self.pA, self.pS][k % 2]
            pS3 = [self.pB, self.pX][k % 2]
            kS1 = ["pA", "pS"][k % 2]
            kS3 = ["pB", "pX"][k % 2]
            f2 = k < 2

            def mm13(e, sb=sb, ws=ws, pS1=pS1, pS3=pS3):
                ins = None
                for pp, ofs, bofs in ((pS1, BO_W1, BO_B1), (pS3, BO_W3, BO_B3)):
                    for kc in range(8):
                        ins = e.matmul(pp[:, 0:511], in_fm[sb][:, kc, :],
                                       ws[:, ofs + kc * 512:ofs + kc * 512 + 511],
                                       start=(kc == 0), stop=False)
                    ins = e.matmul(pp[:, 0:511], cb["onesrowb"][:],
                                   ws[0:1, bofs:bofs + 511],
                                   start=False, stop=True)
                return ins
            self.op("pe", mm13,
                    reads=[("efm", sb, 0), ("efm", sb, 1), ("ewt", k // 2),
                           ("c", "onesrowb")],
                    writes=[kS1, kS3], tag=f"e13_{k}")
        def front_act(k):
            sb = k % 2
            pS1 = [self.pA, self.pS][k % 2]
            pS3 = [self.pB, self.pX][k % 2]
            kS1 = ["pA", "pS"][k % 2]
            kS3 = ["pB", "pX"][k % 2]
            f2 = k < 2
            self.op("act", lambda e, sb=sb, pS1=pS1: e.activation(
                a_tm[sb][:, 0:511], pS1[:, 0:511], AF.Silu),
                reads=[kS1], writes=[("ea", sb)],
                war=[("w", "sw1t")] if f2 else [], tag=f"e1e_{k}")
            self.op("dve", lambda e, sb=sb, pS3=pS3: e.tensor_tensor(
                s_tm[sb][:, 0:511], pS3[:, 0:511], a_tm[sb][:, 0:511],
                op=ALU.mult),
                reads=[kS3, ("ea", sb)], writes=[("es", sb)],
                war=([("hit", t) for t in range(NT)] + ["exps", "wsc"])
                if f2 else [], tag=f"e3e_{k}")
            self.op("act", lambda e, sb=sb: e.activation(
                sqs[:, 0:511], s_tm[sb][:, 0:511], AF.Square, accum_out=b["ssq"]),
                reads=[("es", sb)], writes=["ssq", "sqs"],
                war=[("bt", 1, 0), ("bt", 1, 1)] if f2 else [], tag=f"eq_{k}")
            self.op("act", lambda e, sb=sb: e.activation(
                s_tm[sb][:, 511:512], b["ssq"][:], AF.Sqrt, bias=b["c1"][:]),
                reads=["ssq", "c1"], writes=[("es", sb, "t")], tag=f"eT_{k}")

        def back_st(k):
            sb = k % 2
            f2 = k < 2
            for q in range(2):
                srcs = [s_tm[sb][:, (q * 2 + j) * 128:(q * 2 + j + 1) * 128]
                        for j in range(2)]
                rk = [("es", sb), ("es", sb, "t")] if q == 1 else [("es", sb)]

                def ev(e, sb=sb, q=q):
                    return e.tensor_copy(
                        s_fm[sb][:, q * 2:q * 2 + 2, :],
                        self.pT[q][:, 0:256].rearrange("p (m t) -> p m t", m=2))
                self.tbatch(srcs, 128, ev, q, rk, [("esf", sb, q)],
                            f"st_{k}_{q}", war=["kselP"] if f2 else [])

        def back_mm(k):
            sb = k % 2
            ws = wslot[(k // 2) % 2]
            f2 = k < 2
            for half, (c0, cw) in enumerate(((0, 512), (512, 511))):
                def mmz(e, sb=sb, ws=ws, c0=c0, cw=cw):
                    ins = None
                    for kc in range(4):
                        ins = e.matmul(
                            self.pBig[:, c0:c0 + cw], s_fm[sb][:, kc, :],
                            ws[:, BO_W2 + kc * 1024 + c0:
                               BO_W2 + kc * 1024 + c0 + cw],
                            start=(kc == 0), stop=False)
                    return e.matmul(
                        self.pBig[:, c0:c0 + cw], cb["onesrowb"][:],
                        ws[0:1, BO_B2 + c0:BO_B2 + c0 + cw],
                        start=False, stop=True)
                self.op("pe", mmz,
                        reads=[("esf", sb, 0), ("esf", sb, 1), ("ewt", k // 2),
                               ("c", "onesrowb")],
                        writes=[("pBig", half)], tag=f"ez_{k}_{half}")
                self.op("dve", lambda e, sb=sb, c0=c0, cw=cw, half=half:
                        e.tensor_copy(out_tm[sb][:, 1 + c0:1 + c0 + cw],
                                      self.pBig[:, c0:c0 + cw]),
                        reads=[("pBig", half)], writes=[("ez", sb, half)],
                        war=(["sqj", ("bt", 0, 0), ("bt", 0, 1)] if f2 else []),
                        tag=f"eze_{k}_{half}")

        def back_fin(k):
            sb = k % 2
            self.op("act", lambda e, sb=sb: e.activation(
                sqz[:, 0:1023], out_tm[sb][:, 1:1024], AF.Square,
                accum_out=b["ssq"]),
                reads=[("ez", sb, 0), ("ez", sb, 1)], writes=["ssq", "sqz"],
                tag=f"eS_{k}")
            self.op("act", lambda e, sb=sb: e.activation(
                out_tm[sb][:, 0:1], b["ssq"][:], AF.Sqrt, bias=b["c1"][:]),
                reads=["ssq", "c1"], writes=[("ez", sb, "t")], tag=f"eQ_{k}")
            self.op("sp", lambda e, k=k, sb=sb: e.dma_start(
                out=eoutd[k], in_=out_tm[sb]),
                reads=[("ez", sb, 0), ("ez", sb, 1), ("ez", sb, "t")],
                writes=[("eout", k)], inc=16, tag=f"eW{k}", sem="eW")
            if k % 2 == 1 and k // 2 + 2 < NG:
                gather_w(k // 2 + 2)

        ld_in(0)
        ld_in(1)
        ld_in(2)
        gather_w(0)
        gather_w(1)
        for i in range(NTILE + 1):
            if i >= 1:
                back_st(i - 1)
            if i < NTILE:
                if i + 3 < NTILE:
                    ld_in(i + 3)
                et_t(i)
                front(i)
                front_act(i)
            if i >= 1:
                back_mm(i - 1)
                back_fin(i - 1)

    def emit_final(self):
        b = self.b
        A3 = self.A3
        O0 = [A3[:, 0:1024], A3[:, 1024:2048]]
        O1 = [A3[:, 2048:3072], A3[:, 3072:4096]]
        ZS = [A3[:, 4096:5120], A3[:, 5120:6144]]
        sqb = [A3[:, 6144:7168], A3[:, 7168:8192]]
        xin = [self.SA[:, 0:1024], self.SA[:, 1024:2048]]
        yb = [self.SA[:, 2048:3072], self.SA[:, 3072:4096]]
        x2d = self.x2_dram.rearrange("(T p) d -> T p d", p=128)
        zshd = self.zsh_dram.rearrange("(T p) d -> T p d", p=128)
        outd = self.out_d.rearrange("(T p) d -> T p d", p=128)
        EOUT_KEYS = [("eout", k) for k in range(NTILE)]
        ASH_WAR = ([("ash", m, nh) for m in range(8) for nh in range(2)] +
                   [("ssh", m, nh) for m in range(8) for nh in range(2)])

        def chainA(tau):
            bb = tau % 2
            fwar = ([("rall", ex) for ex in range(NE)] + ["expg"] +
                    [("dest0f", t) for t in range(NT)] +
                    [("dest1f", t) for t in range(NT)] +
                    [("elo", t) for t in range(NT)] +
                    [("ehi", t) for t in range(NT)] +
                    ["pofsb", "cntrow", "pcrow", "pcs", "pofsrow", "m1v",
                     "m2v", "eav", "ebv", "ddv", "flagv", "wlov", "whiv",
                     "reqall"]) \
                if tau < 2 else []
            ywar = (["req2all", ("x2t", 1, 0), ("x2t", 1, 1)] if bb == 1 else
                    [("x2t", 0, 0), ("x2t", 0, 1)]) if tau < 2 else []
            st = [
                lambda: self.op("sp", lambda e: e.dma_start(
                    out=xin[bb], in_=x2d[tau]),
                    reads=[("x2_dram", tau)], writes=[("xin", bb)], war=fwar,
                    inc=16, tag=f"fx{tau}", sem=f"fx{bb}"),
                lambda: self.op("sp", lambda e: e.dma_start(
                    out=ZS[bb], in_=zshd[tau]),
                    reads=[("zsh_dram", tau)], writes=[("ZS", bb)],
                    war=ASH_WAR if tau < 2 else [], inc=16, tag=f"fz{tau}",
                    sem=f"fz{bb}"),
            ]
            for slot, Obuf, dkey in ((0, O0, "dest0"), (1, O1, "dest1")):
                st.append(lambda slot=slot, Obuf=Obuf, dkey=dkey:
                          self.op("pool", lambda e: e.indirect_dma_start(
                              out=Obuf[bb], out_offset=None,
                              in_=self.eout_dram,
                              in_offset=bass.IndirectOffsetOnAxis(
                                  ap=b[dkey][:, tau:tau + 1], axis=0),
                              bounds_check=self.bcreg, oob_is_err=False),
                              reads=[dkey] + EOUT_KEYS,
                              writes=[("O", slot, bb)],
                              war=ASH_WAR if tau < 2 else [], inc=16,
                              tag=f"fo{tau}_{slot}", sem=f"fo{slot}{bb}"))
            st.append(lambda: self.op("dve", lambda e: e.tensor_scalar(
                yb[bb], O0[bb], b["w0"][:, tau:tau + 1], None, op0=ALU.mult),
                reads=[("O", 0, bb), ("w0", tau)],
                writes=[("y", bb)], war=ywar, tag=f"f1{tau}"))
            st.append(lambda: self.op("dve", lambda e: e.tensor_scalar(
                yb[bb][:, 0:1], yb[bb][:, 0:1], 1.0, None, op0=ALU.add),
                reads=[("y", bb)], writes=[("y0", bb)], tag=f"f2{tau}"))
            st += self.hyp_s(tau, bb, sqb[bb], "a", y0k=("y0", bb))
            s0 = b[["s0", "s0b"][bb]]
            s0k = ["s0", "s0b"][bb]
            w1s = b[["mean0", "mean1"][bb]]
            w1sk = ["mean0", "mean1"][bb]
            st.append(lambda: self.op("dve", lambda e: e.tensor_tensor(
                w1s, b["w1"][:, tau:tau + 1], s0, op=ALU.mult),
                reads=[("w1", tau), s0k], writes=[w1sk], tag=f"fw{tau}"))
            st.append(lambda: self.op("dve", lambda e: e.scalar_tensor_tensor(
                yb[bb], O1[bb], w1s, yb[bb],
                op0=ALU.mult, op1=ALU.add),
                reads=[("O", 1, bb), w1sk, ("y", bb), ("y0", bb)],
                writes=[("y", bb), ("y0", bb)], tag=f"f3{tau}"))
            st += self.hyp_s(tau, bb, sqb[bb], "b")
            return st

        def chainB(tau):
            bb = tau % 2
            s0 = b[["s0", "s0b"][bb]]
            s0k = ["s0", "s0b"][bb]
            st = []
            st.append(lambda: self.op("dve", lambda e: e.scalar_tensor_tensor(
                yb[bb], ZS[bb], s0, yb[bb], op0=ALU.mult, op1=ALU.add),
                reads=[("ZS", bb), s0k, ("y", bb)], writes=[("y", bb)],
                tag=f"f4{tau}"))
            st += self.hyp_s(tau, bb, sqb[bb], "c")
            st.append(lambda: self.op("dve", lambda e: e.reciprocal(s0, s0),
                    reads=[s0k], writes=[s0k], tag=f"fr{tau}"))
            st.append(lambda: self.op("dve", lambda e: e.scalar_tensor_tensor(
                xin[bb], yb[bb], s0, xin[bb], op0=ALU.mult, op1=ALU.add),
                reads=[("xin", bb), s0k, ("y", bb)], writes=[("xin", bb)],
                tag=f"f5{tau}"))
            st.append(lambda: self.op("sp", lambda e: e.dma_start(
                out=outd[tau], in_=xin[bb]),
                reads=[("xin", bb)], writes=[("out", tau)], inc=16,
                tag=f"fs{tau}", sem="fs"))
            return st

        for i in range(NT + 1):
            A = chainA(i) if i < NT else []
            B = chainB(i - 1) if i >= 1 else []
            na, nb = len(A), len(B)
            ia = ib = 0
            while ia < na or ib < nb:
                if ia < na and (ib >= nb or ia * (nb + 1) <= ib * (na + 1)):
                    A[ia]()
                    ia += 1
                else:
                    B[ib]()
                    ib += 1

    def hyp_s(self, tau, bb, sqb, sub, y0k=None):
        b = self.b
        y = self.SA[:, 2048 + bb * 1024:3072 + bb * 1024]
        yk = ("y", bb)
        s0 = b[["s0", "s0b"][bb]]
        s0k = ["s0", "s0b"][bb]
        sq = b[["ssq", "ssqb"][bb]]
        sqk = ["ssq", "ssqb"][bb]
        sbwar = ([("bt", 1, 0), ("bt", 1, 1)]
                 if (tau < 2 and sub == "a") else [])
        y0read = [y0k] if y0k else []
        return [
            lambda: self.op("act", lambda e: e.activation(
                sqb[:, 0:1023], y[:, 1:1024], AF.Square, accum_out=sq),
                reads=[yk], writes=[sqk, ("sqb", bb)], war=sbwar,
                tag=f"h{sub}{tau}s"),
            lambda: self.op("dve", lambda e: e.tensor_tensor(
                s0, y[:, 0:1], y[:, 0:1], op=ALU.mult),
                reads=[yk] + y0read, writes=[s0k], tag=f"h{sub}{tau}z"),
            lambda: self.op("dve", lambda e: e.tensor_scalar(
                s0, s0, sq, EPS_HYP, op0=ALU.subtract, op1=ALU.max),
                reads=[s0k, sqk], writes=[s0k], tag=f"h{sub}{tau}d"),
            lambda: self.op("act", lambda e: e.activation(
                s0, s0, AF.Sqrt),
                reads=[s0k], writes=[s0k], tag=f"h{sub}{tau}q"),
        ]

    def emit_dump(self):
        dd = self.dump_d
        if not dd:
            return
        b = self.b

        def dram_dump(nm, src, rkeys):
            self.op("pool", lambda e, nm=nm, src=src: e.dma_start(out=dd[nm], in_=src),
                    reads=rkeys, inc=16, tag="D" + nm, sem="dmpp")

        if "h1" in dd:
            dram_dump("h1", self.h1_dram, [("h1_dram", t) for t in range(NT)])
        if "x2" in dd:
            dram_dump("x2", self.x2_dram, [("x2_dram", t) for t in range(NT)])
        if "h2" in dd:
            dram_dump("h2", self.h2_dram, [("h2_dram", t) for t in range(NT)])
        if "zsh" in dd:
            dram_dump("zsh", self.zsh_dram, [("zsh_dram", t) for t in range(NT)])
        if "ein" in dd:
            dram_dump("ein", self.ein_dram,
                      [("ein", t, s) for t in range(NT) for s in (0, 1)])
        if "eout" in dd:
            dram_dump("eout", self.eout_dram,
                      [("eout", k) for k in range(NTILE)])
        sd = {
            "imp": (b["imp_tm"][:], ["imp_tm"]),
            "sel": (b["sel"][:], ["sel"]),
            "rank": (b["rank16"][:], [("rank16",)]),
            "w0": (b["w0"][:], [("w0", t) for t in range(NT)]),
            "w1": (b["w1"][:], [("w1", t) for t in range(NT)]),
            "dest0": (b["dest0f"], [("dest0f", t) for t in range(NT)]),
            "dest1": (b["dest1f"], [("dest1f", t) for t in range(NT)]),
            "expg": (b["expg"], ["expg"]),
        }
        for nm, (src, rkeys) in sd.items():
            if nm in dd:
                self.op("sp", lambda e, nm=nm, src=src: e.dma_start(out=dd[nm], in_=src),
                        reads=rkeys, inc=16, tag="D" + nm, sem="dmps")
        if "expt" in dd:
            self.op("pool", lambda e: e.dma_start(
                out=dd["expt"], in_=self.A2[:, 8192:9216]),
                reads=[("expT", hp, q) for hp in range(2) for q in range(8)],
                inc=16, tag="Dexpt", sem="dmpp")
        if "rdenb" in dd:
            self.op("sp", lambda e: e.dma_start(out=dd["rdenb"], in_=b["rdenb"][0:16, :]),
                    reads=[("rdenb", h) for h in range(H)], inc=16, tag="Drdenb",
                    sem="dmps")
        bd = {
            "d0u": (b["dest0"][:], ["dest0"]),
            "d1u": (b["dest1"][:], ["dest1"]),
            "hit": (b["hit"], [("hit", t) for t in range(NT)]),
            "h1sel": (b["h1selT"], ["h1selT"]),
            "kself": (b["kselF"][:], ["kselF"]),
            "qself": (b["qselF"][:], ["qselF"]),
            "vselt": (b["vselT"][0:16, :], ["vselT"]),
            "ctxb": (b["ctxbT"][0:16, :], [("ctxbT", h) for h in range(H)]),
        }
        for nm, (src, rkeys) in bd.items():
            if nm in dd:
                self.op("pool", lambda e, nm=nm, src=src: e.dma_start(out=dd[nm], in_=src),
                        reads=rkeys, inc=16, tag="D" + nm, sem="dmpp")

    def emit_tail(self):
        final = dict(self.S.count)
        for eng in Sched.ENGINES:
            for tgt, cnt in final.items():
                if tgt != eng and cnt > 0:
                    self.S._wait(eng, tgt, cnt)


VFM_ALL = [("Vfm", n) for n in range(8)]
EXPB_ALL = [("expb", hp, n) for hp in range(2) for n in range(8)]
EXPT_ALL = [("expT", hp, q) for hp in range(2) for q in range(8)]


def build(dumps=(), upto="final", p2tau=NT):
    return MK(dumps, upto, p2tau).build()


def kernel(**inputs):
    prep = host_prep(inputs)
    x = np.asarray(inputs["L0_attn_x"], np.float32)
    nc = build()
    in_maps = []
    for c in range(8):
        m = {"x": np.ascontiguousarray(x[c])}
        m.update(prep)
        in_maps.append(m)
    res = run_bass_kernel_spmd(nc, in_maps, core_ids=list(range(8)))
    return np.stack([np.asarray(r["out"], np.float32) for r in res.results])


# version 38
# speedup vs baseline: 1.0245x; 1.0245x over previous
import contextlib
import numpy as np
import ml_dtypes
import concourse.bass as bass
import concourse.mybir as mybir
from concourse.bass_utils import run_bass_kernel_spmd

F32 = mybir.dt.float32
BF16 = mybir.dt.bfloat16
U32 = mybir.dt.uint32
AF = mybir.ActivationFunctionType
ALU = mybir.AluOpType
AX = mybir.AxisListType

S = 4096
D = 1024
H = 16
NE = 16
NT = 32
NTILE = 96
NG = 48
BLOBW = 14336
BO_W1, BO_W3, BO_W2 = 0, 4096, 8192
BO_B1, BO_B2, BO_B3 = 12288, 12800, 13824
BIGOOB = float(1 << 22)
EPS_HYP = 1e-6
BF = ml_dtypes.bfloat16


class Sched:

    ENGINES = ("pe", "dve", "act", "pool", "sp")

    def __init__(self):
        self.q = {e: [] for e in self.ENGINES}
        self.count = {e: 0 for e in self.ENGINES}
        self.dma_fams = set()
        self.hiw = {}
        self.writer = {}
        self.readers = {}
        self.sem = {}

    def _wait(self, eng, tgt, cnt):
        if eng == "pe" and tgt == "pe":
            return
        if tgt in self.dma_fams:
            cnt = self.count[tgt]
        if cnt <= 0:
            return
        k = (eng, tgt)
        if self.hiw.get(k, 0) >= cnt:
            return
        self.hiw[k] = cnt
        sem = self.sem
        self.q[eng].append(lambda e, t=tgt, c=cnt: e.wait_ge(sem[t], c))

    def op(self, eng, fn, reads=(), writes=(), war=(), inc=1, tag="", sem=None):
        fam = sem or eng
        if sem is not None:
            self.dma_fams.add(fam)
            self.count.setdefault(fam, 0)
        for kk in reads:
            w = self.writer.get(kk)
            if w is not None:
                self._wait(eng, w[0], w[1])
        for kk in tuple(writes) + tuple(war):
            for reng, rcnt in self.readers.get(kk, {}).items():
                self._wait(eng, reng, rcnt)
            w = self.writer.get(kk)
            if w is not None:
                self._wait(eng, w[0], w[1])
        newcnt = self.count[fam] + inc
        semd = self.sem

        def thunk(e, fn=fn, n=inc, tag=tag, fam=fam):
            ins = fn(e)
            if ins is None:
                raise RuntimeError(f"op returned no instruction (tag={tag})")
            ins.then_inc(semd[fam], n)
        self.q[eng].append(thunk)
        self.count[fam] = newcnt
        for kk in writes:
            self.writer[kk] = (fam, newcnt)
            self.readers[kk] = {}
        for kk in reads:
            self.readers.setdefault(kk, {})[fam] = newcnt


def _bf(x):
    return np.asarray(x).astype(BF)


def host_prep(inputs):
    g = {k: np.asarray(v) for k, v in inputs.items()}
    p = {}
    for nm, key in (("wq", "L0_attn_Wq"), ("wk", "L0_attn_Wk"),
                    ("wv", "L0_attn_Wv"), ("wo", "L0_attn_Wo"),
                    ("ws1a", "L0_attn_Ws1"), ("ws2a", "L0_attn_Ws2")):
        p[nm] = np.ascontiguousarray(g[key], np.float32)
    p["gatewt"] = np.ascontiguousarray(g["L0_ffn_gate_w"].T, np.float32)
    p["we1t"] = np.ascontiguousarray(g["L0_ffn_We1"].transpose(0, 2, 1), np.float32)
    p["we3t"] = np.ascontiguousarray(g["L0_ffn_We3"].transpose(0, 2, 1), np.float32)
    we2t = g["L0_ffn_We2"].transpose(0, 2, 1)
    p["we2p"] = np.ascontiguousarray(
        np.concatenate([we2t[:, 1:, :], we2t[:, 0:1, :]], axis=1), np.float32)
    p["be1r"] = np.ascontiguousarray(np.asarray(g["L0_ffn_be1"], np.float32))
    p["be3r"] = np.ascontiguousarray(np.asarray(g["L0_ffn_be3"], np.float32))
    p["be2r"] = np.ascontiguousarray(np.asarray(g["L0_ffn_be2"], np.float32))
    p["sw1t"] = np.ascontiguousarray(g["L0_ffn_Ws1"].T, np.float32)
    p["sw3t"] = np.ascontiguousarray(g["L0_ffn_Ws3"].T, np.float32)
    sw2t = g["L0_ffn_Ws2"].T
    sw2ts = np.zeros((1024, 1023), np.float32)
    sw2ts[0:1023] = sw2t[1:, :]
    p["sw2ts"] = sw2ts
    p["sw2t0"] = np.ascontiguousarray(sw2t[0:1, :], np.float32)

    def colsT(bv, m):
        return np.asarray(bv, np.float32).reshape(m, 128).T

    def pad_to(bv, n):
        out = np.zeros((n,), np.float32)
        out[: bv.shape[0]] = bv
        return out

    p["bqt"] = np.ascontiguousarray(colsT(g["L0_attn_bq"], 8))
    p["bkt"] = np.ascontiguousarray(colsT(g["L0_attn_bk"], 8))
    p["bvt"] = np.ascontiguousarray(colsT(g["L0_attn_bv"], 8))
    p["borow"] = np.ascontiguousarray(
        np.asarray(g["L0_attn_bo"], np.float32).reshape(1, 1024))
    p["bs1at"] = np.ascontiguousarray(colsT(g["L0_attn_bs1"], 4))
    p["bs2ar"] = np.ascontiguousarray(np.broadcast_to(
        np.asarray(g["L0_attn_bs2"], np.float32).reshape(1, 1), (128, 1)))
    p["bs1st"] = np.ascontiguousarray(colsT(pad_to(np.asarray(g["L0_ffn_bs1"], np.float32), 1024), 8))
    p["bs3st"] = np.ascontiguousarray(colsT(pad_to(np.asarray(g["L0_ffn_bs3"], np.float32), 1024), 8))
    p["bs2st"] = np.ascontiguousarray(colsT(pad_to(np.asarray(g["L0_ffn_bs2"], np.float32), 1024), 8))

    p["ident"] = _bf(np.eye(128, dtype=np.float32))
    p["strictu128"] = _bf(np.triu(np.ones((128, 128), np.float32), 1))
    p["strictu32"] = _bf(np.triu(np.ones((32, 32), np.float32), 1))
    p["iotaf"] = np.ascontiguousarray(
        np.arange(S, dtype=np.float32).reshape(NT, 128).T)
    p["iota16"] = np.ascontiguousarray(
        np.broadcast_to(np.arange(16, dtype=np.float32), (128, 16)))
    p["onesrow"] = np.ones((1, 128), np.float32)
    p["onesrowb"] = _bf(np.ones((1, 128), np.float32))
    p["tb80"] = np.ascontiguousarray(
        (np.arange(NG, dtype=np.float32) * 256).reshape(1, NG))
    p["onescol"] = _bf(np.ones((128, 1), np.float32))
    return p


WEIGHT_SPECS = [
    ("wq", [1024, 1024], F32), ("wk", [1024, 1024], F32),
    ("wv", [1024, 1024], F32), ("wo", [1024, 1024], F32),
    ("ws1a", [1024, 512], F32), ("ws2a", [512, 1], F32),
    ("gatewt", [1024, 16], F32),
    ("we1t", [NE, 1024, 511], F32), ("we3t", [NE, 1024, 511], F32),
    ("we2p", [NE, 512, 1023], F32), ("be1r", [NE, 511], F32),
    ("be3r", [NE, 511], F32), ("be2r", [NE, 1023], F32),
    ("sw1t", [1024, 1023], F32), ("sw3t", [1024, 1023], F32),
    ("sw2ts", [1024, 1023], F32), ("sw2t0", [1, 1023], F32),
    ("bqt", [128, 8], F32), ("bkt", [128, 8], F32), ("bvt", [128, 8], F32),
    ("borow", [1, 1024], F32), ("bs1at", [128, 4], F32), ("bs2ar", [128, 1], F32),
    ("bs1st", [128, 8], F32), ("bs3st", [128, 8], F32), ("bs2st", [128, 8], F32),
    ("ident", [128, 128], BF16), ("strictu128", [128, 128], BF16),
    ("strictu32", [32, 32], BF16), ("iotaf", [128, NT], F32),
    ("iota16", [128, 16], F32), ("onesrow", [1, 128], F32), ("onescol", [128, 1], BF16),
    ("onesrowb", [1, 128], BF16), ("tb80", [1, NG], F32),
]

CB_NAMES = {"bqt", "bkt", "bvt", "bs1at", "bs2ar",
            "bs1st", "bs3st", "bs2st", "ident", "strictu128",
            "strictu32", "iotaf", "iota16", "onesrow", "onescol",
            "onesrowb", "tb80"}

DUMP_SHAPES = {
    "h1": [S, D], "imp": [128, NT], "sel": [128, NT], "rank": [128, NT],
    "h1sel": [16, D], "kself": [128, 128], "qself": [128, 128],
    "vselt": [16, D], "ctxb": [16, D], "x2": [S, D], "h2": [S, D],
    "expg": [128, NT * 16], "hit": [128, NT * 16], "dest0": [128, NT],
    "dest1": [128, NT], "w0": [128, NT], "w1": [128, NT],
    "ein": [NTILE * 128, D], "zsh": [S, D], "eout": [NTILE * 128, D],
    "expt": [128, 1024], "rdenb": [16, 16], "d0u": [128, NT], "d1u": [128, NT],
}

PHASES = ["init", "A", "imp", "topk", "selproj", "p1", "p2", "route",
          "scatter", "shared", "experts", "final"]


class MK:
    def __init__(self, dumps=(), upto="final", p2tau=NT):
        self.nc = bass.Bass("TRN2", target_bir_lowering=False)
        self.S = Sched()
        self.dumps = tuple(dumps)
        self.upto = PHASES.index(upto)
        self.p2tau = p2tau

    def op(self, *a, **k):
        self.S.op(*a, **k)

    def build(self):
        nc = self.nc
        self.d_in = {"x": nc.dram_tensor("x", [S, D], F32, kind="ExternalInput").ap()}
        for name, shape, dt in WEIGHT_SPECS:
            self.d_in[name] = nc.dram_tensor(name, shape, dt, kind="ExternalInput").ap()
        self.out_d = nc.dram_tensor("out", [S, D], F32, kind="ExternalOutput").ap()
        self.h1_dram = nc.dram_tensor("h1_dram", [S, D], BF16, kind="Internal").ap()
        self.h2_dram = nc.dram_tensor("h2_dram", [S, D], BF16, kind="Internal").ap()
        self.x2_dram = nc.dram_tensor("x2_dram", [S, D], F32, kind="Internal").ap()
        self.zsh_dram = nc.dram_tensor("zsh_dram", [S, D], BF16, kind="Internal").ap()
        self.impflat_dram = nc.dram_tensor("impflat_dram", [1, S], F32, kind="Internal").ap()
        self.ein_dram = nc.dram_tensor("ein_dram", [NTILE * 128, D], BF16, kind="Internal").ap()
        self.eout_dram = nc.dram_tensor("eout_dram", [NTILE * 128, D], BF16, kind="Internal").ap()
        self.wblob = nc.dram_tensor("wblob", [NE * 128, BLOBW], BF16, kind="Internal").ap()
        self.dump_d = {nm: nc.dram_tensor("dump_" + nm, DUMP_SHAPES[nm], F32,
                                          kind="ExternalOutput").ap()
                       for nm in self.dumps}

        ctx = contextlib.ExitStack()
        sbuf = lambda name, shape, dt: ctx.enter_context(nc.sbuf_tensor(name, shape, dt))
        psum = lambda name, shape, dt: ctx.enter_context(nc.psum_tensor(name, shape, dt))
        with ctx:
            self.A1 = sbuf("A1", [128, 32768], BF16)
            self.A2 = sbuf("A2", [128, 16384], BF16)
            self.A3 = sbuf("A3", [128, 16384], BF16)
            self.WA = sbuf("WA", [128, 24576], BF16)
            self.SA = sbuf("SA", [128, 4096], F32)
            self.ctxf = self.WA[:, 0:1024]
            self.cb = {}
            for name, shape, dt in WEIGHT_SPECS:
                if name in CB_NAMES:
                    self.cb[name] = sbuf("c_" + name, shape, dt)
            b = {}
            B16 = sbuf("B16", [128, 5120], BF16)
            self.B16 = B16
            b["gatew"] = self.A3[:, 0:128]
            b["ws2a"] = self.A3[:, 1540:1544]
            b["PT"] = B16[0:16, 0:4096]
            b["ctxbT"] = B16[0:16, 4096:5120]
            b["vselT"] = B16[32:48, 0:1024]
            b["wTst"] = B16[32:48, 1024:1536]
            b["h1selT"] = B16[32:48, 1536:2560]
            b["borowb"] = B16[32:33, 2560:3584]
            b["onesb32"] = B16[32:33, 3584:3712]
            b["timerow"] = B16[64:65, 0:1024]
            b["sw2t0"] = B16[64:65, 3072:4095]
            b["h1selF"] = sbuf("h1selF", [128, 128], BF16)
            b["qselF"] = sbuf("qselF", [128, 128], BF16)
            b["kselF"] = sbuf("kselF", [128, 128], BF16)
            b["vselF"] = sbuf("vselF", [128, 128], BF16)
            b["denb8"] = sbuf("denb8", [16, 8], F32)
            b["rdenb"] = sbuf("rdenb", [16, 16], F32)
            b["imp_tm"] = sbuf("imp_tm", [128, NT], F32)
            b["topv"] = sbuf("topv", [1, 16], F32)
            b["t16b"] = sbuf("t16b", [128, 1], F32)
            b["sel"] = sbuf("sel", [128, NT], F32)
            b["notsel"] = sbuf("notsel", [128, NT], F32)
            b["hitbf"] = sbuf("hitbf", [128, NT], BF16)
            b["selT"] = sbuf("selT_", [32, 128], BF16)
            b["rfree"] = sbuf("rfree", [128, NT], F32)
            b["rowtot"] = sbuf("rowtot", [128, 1], F32)
            b["rowtotb"] = sbuf("rowtotb", [128, 1], BF16)
            b["offs"] = sbuf("offs", [128, 1], F32)
            b["rank16"] = sbuf("rank16", [128, NT], F32)
            b["Pbuf"] = self.A3[:, 1024:1536]
            b["Pf32"] = sbuf("Pf32", [128, 16], F32)
            b["idx16"] = sbuf("idx16", [16, 1], U32)
            b["idx16f"] = sbuf("idx16f", [16, 1], F32)
            b["expg"] = self.SA[:, 512:1024].bitcast(F32) if False else self.SA[:, 512:1024]
            b["hit"] = self.A2[:, 0:512]
            b["rall"] = self.SA[:, 0:512]
            b["rden"] = sbuf("rden", [128, NT], F32)
            b["w0"] = sbuf("w0", [128, NT], F32)
            b["w1"] = sbuf("w1", [128, NT], F32)
            b["dest0f"] = self.SA[:, 1024:1056]
            b["dest1f"] = self.SA[:, 1056:1088]
            b["dest0"] = sbuf("dest0", [128, NT], U32)
            b["dest1"] = sbuf("dest1", [128, NT], U32)
            b["pofsb"] = self.SA[:, 1184:1200]
            for i, nm in enumerate(("m1v", "m2v", "eav", "ebv", "ddv",
                                    "flagv", "wlov", "whiv")):
                b[nm] = self.SA[:, 1264 + i * 32:1296 + i * 32]
            b["reqall"] = self.SA[:, 0:512]

            b["req2all"] = self.SA[:, 3072:3584]
            b["cntrow"] = self.SA[0:1, 1200:1216]
            b["pcrow"] = self.SA[0:1, 1216:1232]
            b["pcs"] = self.SA[0:1, 1232:1248]
            b["pofsrow"] = self.SA[0:1, 1248:1264]
            b["texp"] = B16[32:33, 3712:3712 + NG]
            b["t80"] = B16[32:33, 3800:3800 + NG]

            b["idxmat"] = sbuf("idxmat", [128, NG], U32)
            b["pc32"] = b["dest0"][0:1, 0:16]
            b["elo"] = self.SA[:, 1120:1152]
            b["ehi"] = self.SA[:, 1152:1184]
            rs16 = sbuf("rs16", [128, 24], F32)
            for i, nm in enumerate(("m1", "m2", "dd", "ea", "eb", "flag", "wlo",
                                    "whi", "r0", "r1", "msk")):
                b[nm] = rs16[:, i:i + 1]
            b["req"] = self.SA[:, 1088:1104]
            b["req2"] = self.SA[:, 1104:1120]
            b["exps"] = self.WA[:, 1536:1792]
            b["wsc"] = self.WA[:, 1792:2048]
            b["kselP"] = self.WA[:, 6144:6400]
            b["qselP"] = self.WA[:, 24320:24576]
            b["aden"] = sbuf("aden", [128, 16], F32)
            b["arden"] = sbuf("arden", [128, 16], F32)
            b["mean"] = rs16[:, 11:12]
            b["ssq"] = rs16[:, 14:15]
            b["rstd"] = rs16[:, 12:13]
            b["s0"] = rs16[:, 13:14]
            b["c1"] = rs16[:, 15:16]
            b["ceps"] = rs16[:, 16:17]
            for i, nm in enumerate(("s0b", "ssqb", "mean0", "mean1",
                                    "rstd0", "rstd1", "ssqA0", "ssqA1")):
                b[nm] = rs16[:, i:i + 1] if i < 7 else rs16[:, 17 + i - 7:18]
            b["hitbfB"] = sbuf("hitbfB", [128, NT], BF16)
            b["selTB"] = sbuf("selTB2", [32, 128], BF16)
            b["rowtotbB"] = sbuf("rowtotbB", [128, 1], BF16)
            b["rfreeB"] = b["imp_tm"]
            b["rowtotB"] = b["r0"]
            b["offsB"] = b["r1"]

            b["zt"] = self.A3[:, 15360:16384]
            b["bt"] = [self.WA[:, 3072:4096], self.WA[:, 4096:5120]]
            self.b = b

            self.pA = psum("pA", [128, 512], F32)
            self.pB = psum("pB", [128, 512], F32)
            self.pBig = psum("pBig", [128, 1024], F32)
            self.pS = psum("pS", [128, 512], F32)
            self.pX = psum("pX", [128, 512], F32)
            self.pT = [psum("pT0", [128, 1024], BF16), psum("pT1", [128, 1024], BF16)]

            FAMS = ["blb", "cbld", "xin0", "xin1", "h1st", "impq", "sx0", "sx1",
                    "sh2", "zst", "el0", "el1", "eW", "fx0", "fx1", "fz0",
                    "fz1", "fs", "wld", "zein", "gsel", "sc", "sc0", "sc1",
                    "fo00", "fo01",
                    "fo10", "fo11", "ewa", "ewb", "st0", "st1", "dmpp", "dmps"]
            for e in list(Sched.ENGINES) + FAMS:
                self.S.sem[e] = ctx.enter_context(nc.semaphore("sem_" + e))

            import os as _os
            skip = set(_os.environ.get("SKIPPH", "").split(","))
            emits = [self.emit_init, self.emit_phaseA, self.emit_imp,
                     self.emit_topk, self.emit_selproj, self.emit_p1,
                     self.emit_p2, self.emit_route, self.emit_scatter,
                     self.emit_shared, self.emit_experts, self.emit_final]
            for i, fn in enumerate(emits):
                if i <= self.upto and PHASES[i] not in skip:
                    fn()
            self.emit_dump()
            self.emit_tail()

            with nc.Block() as block:
                qq = self.S.q

                @block.tensor
                def _(eng):
                    for t in qq["pe"]:
                        t(eng)

                @block.vector
                def _(eng):
                    for t in qq["dve"]:
                        t(eng)

                @block.scalar
                def _(eng):
                    for t in qq["act"]:
                        t(eng)

                @block.gpsimd
                def _(eng):
                    self.bcreg = eng.alloc_register("bcreg")
                    eng.reg_mov(self.bcreg, NTILE * 128 - 1)
                    for t in qq["pool"]:
                        t(eng)

                @block.sync
                def _(eng):
                    for t in qq["sp"]:
                        t(eng)
        return self.nc

    def load_w(self, name, wofs, cols, nk=8, dram_slice=None, key=None, war=(),
               sem="wld", eng="pool"):
        src = dram_slice if dram_slice is not None else self.d_in[name]
        dst = self.WA[:, wofs:wofs + nk * cols].rearrange("p (k f) -> p k f", k=nk)

        def fn(e, src=src, dst=dst):
            return e.dma_start(out=dst, in_=src.rearrange("(k p) f -> p k f", p=128))
        self.op(eng, fn, writes=[key or ("w", name)], war=war, inc=16,
                tag="ldw_" + (name if isinstance(name, str) else str(name)),
                sem=sem)

    def wck(self, wofs, cols, k, sub=None):
        c = self.WA[:, wofs + k * cols: wofs + (k + 1) * cols]
        if sub is not None:
            c = c[:, sub[0]:sub[1]]
        return c

    def tbatch(self, srcs, outw, evac_fn, bank, rkeys, wkeys, tag, war=(),
               base=0):
        pT = self.pT[bank]
        ident = self.cb["ident"]

        def tf(e, srcs=srcs, pT=pT, outw=outw, base=base):
            ins = None
            for i, sap in enumerate(srcs):
                r = sap.shape[0]
                ins = e.transpose(pT[0:sap.shape[1], i * outw:i * outw + r],
                                  sap, ident[base:base + r, base:base + r])
            return ins
        self.op("pe", tf, reads=list(rkeys) + [("c", "ident")],
                writes=[("pt", bank)], tag=tag)
        self.op("dve", evac_fn, reads=[("pt", bank)], writes=list(wkeys),
                war=war, tag=tag + "e")

    H1F_KEYS = [("h1F", tau, half) for tau in range(NT) for half in range(2)]
    H2F_KEYS = [("h2F", tau, half) for tau in range(NT) for half in range(2)]

    def emit_init(self):
        d_in, cb, b = self.d_in, self.cb, self.b
        for name in cb:
            self.op("act", lambda e, n=name: e.dma_start(out=cb[n][:], in_=d_in[n]),
                    writes=[("c", name)], inc=16, tag="ld_" + name, sem="cbld")
        self.op("dve", lambda e: e.memset(b["zt"][:], 0.0), writes=["zt"], tag="zt")
        self.op("dve", lambda e: e.memset(b["c1"], 1.0), writes=["c1"], tag="c1")
        self.op("dve", lambda e: e.memset(b["ceps"], 1e-5), writes=["ceps"], tag="ceps")
        self.op("dve", lambda e: e.memset(b["onesb32"], 1.0), writes=["onesb32"], tag="o32")
        self.op("pool", lambda e: e.dma_start(
            out=b["ws2a"], in_=d_in["ws2a"].rearrange("(m p) o -> p (m o)", p=128),
            allow_slow_non_contiguous=True),
            writes=["ws2a"], inc=16, tag="ld_ws2a", sem="wld")
        self.load_w("ws1a", 16384, 512)
        self.load_w("wk", 0, 1024)
        self.load_w("wv", 8192, 1024)
        self.op("pool", lambda e: e.dma_start(out=b["borowb"], in_=d_in["borow"]),
                writes=["borowb"], inc=16, tag="ld_borow", sem="wld")

    def emit_phaseA(self):
        b, SA, A1 = self.b, self.SA, self.A1
        h1F = A1[:].rearrange("p (m t) -> p m t", m=8)
        xin = [SA[:, 0:1024], SA[:, 1024:2048]]
        h1t = [self.A2[:, i * 1024:(i + 1) * 1024] for i in range(4)]
        sqj = self.A2[:, 4096:5120]
        xd = self.d_in["x"].rearrange("(T p) d -> T p d", p=128)
        h1d = self.h1_dram.rearrange("(T p) d -> T p d", p=128)
        def a_front(tau):
            bb = tau % 2
            return [
                lambda: self.op("sp", lambda e: e.dma_start(
                    out=xin[bb], in_=xd[tau]),
                    writes=[("xin", bb)], inc=16, tag=f"ldx{tau}",
                    sem=f"xin{bb}"),
                lambda: self.ln_head(xin[bb], [("xin", bb)], f"ln1_{tau}",
                                     mk=f"mean{bb}"),
            ]

        def a_back(tau):
            bb = tau % 2
            s4 = tau % 4
            st = [
                lambda: self.ln_tail(
                    xin[bb], [("xin", bb)], h1t[s4], [("h1t", s4)], sqj,
                    f"ln1_{tau}", mk=f"mean{bb}", rk=f"rstd{bb}",
                    qk=f"ssqA{bb}"),
                lambda: self.op("sp", lambda e: e.dma_start(
                    out=h1d[tau], in_=h1t[s4]),
                    reads=[("h1t", s4)], writes=[("h1_dram", tau)], inc=16,
                    tag=f"sth1{tau}", sem="h1st"),
            ]
            for half in range(2):
                bank = (tau * 2 + half) % 2
                srcs = [h1t[s4][:, (half * 4 + j) * 128:(half * 4 + j + 1) * 128]
                        for j in range(4)]

                def ev(e, tau=tau, half=half, bank=bank):
                    return e.tensor_copy(
                        h1F[:, half * 4:half * 4 + 4, tau * 128:(tau + 1) * 128],
                        self.pT[bank][:, 0:512].rearrange("p (m t) -> p m t", m=4))
                st.append(lambda half=half, bank=bank, srcs=srcs, ev=ev, tau=tau:
                          self.tbatch(srcs, 128, ev, bank, [("h1t", s4)],
                                      [("h1F", tau, half)], f"trA{tau}_{half}"))
            return st

        relu = [self.A3[:, 0:512], self.A3[:, 512:1024]]
        cb = self.cb
        pS = self.pS

        def imp_piece(n, m):
            rb = n % 2
            if True:
                bank = [self.pA, self.pB][(n * 4 + m) % 2]
                bkey = ["pA", "pB"][(n * 4 + m) % 2]

                def mm(e, m=m, n=n, bank=bank):
                    ins = None
                    for k in range(8):
                        ins = e.matmul(bank[:, 0:512],
                                       self.wck(16384, 512, k,
                                                (m * 128, (m + 1) * 128)),
                                       h1F[:, k, n * 512:(n + 1) * 512],
                                       start=(k == 0), stop=(k == 7))
                    return ins
                self.op("pe", mm, reads=[("w", "ws1a")] +
                        [("h1F", 4 * n + j, hh)
                         for j in range(4) for hh in range(2)],
                        writes=[bkey], tag=f"is1_{m}_{n}")
                self.op("act", lambda e, m=m, rb=rb, bank=bank: e.activation(
                    relu[rb], bank[:, 0:512], AF.Relu,
                    bias=cb["bs1at"][:, m:m + 1]),
                    reads=[bkey, ("c", "bs1at")], writes=[("relu", rb)],
                    tag=f"ir_{m}_{n}")

                def mm2(e, m=m, n=n, rb=rb):
                    ins = None
                    for j in range(4):
                        tau = n * 4 + j
                        ins = e.matmul(pS[:, tau:tau + 1],
                                       relu[rb][:, j * 128:(j + 1) * 128],
                                       b["ws2a"][:, m:m + 1],
                                       start=(n == 0 and m == 0 and j == 0),
                                       stop=(n == 7 and m == 3 and j == 3))
                    return ins
                self.op("pe", mm2, reads=[("relu", rb), "ws2a"], writes=["pS"],
                        tag=f"is2_{m}_{n}")

        for i in range(NT + 4):
            A = a_front(i) if i < NT else []
            B = a_back(i - 1) if 1 <= i <= NT else []
            na, nb = len(A), len(B)
            ia = ib = 0
            while ia < na or ib < nb:
                if ia < na and (ib >= nb or ia * (nb + 1) <= ib * (na + 1)):
                    A[ia]()
                    ia += 1
                else:
                    B[ib]()
                    ib += 1
            k = i - 4
            if 0 <= k < NT:
                imp_piece(k // 4, k % 4)

    def ln_head(self, x_ap, x_keys, tag, mk="mean"):
        b = self.b
        self.op("dve", lambda e, mk=mk: e.tensor_reduce(
            b[mk], x_ap, axis=AX.X, op=ALU.add, negate=True),
            reads=x_keys, writes=[mk], tag=tag + "m")
        self.op("dve", lambda e, mk=mk: e.tensor_scalar(
            b[mk], b[mk], 1.0 / D, None, op0=ALU.mult),
            reads=[mk], writes=[mk], tag=tag + "s")

    def ln_tail(self, x_ap, x_keys, out_ap, out_keys, sqj_ap, tag,
                mk="mean", rk="rstd", qk="ssq"):
        b = self.b
        self.op("act", lambda e, mk=mk, qk=qk: e.activation(
            sqj_ap, x_ap, AF.Square, bias=b[mk], scale=1.0,
            accum_out=b[qk]),
            reads=x_keys + [mk], writes=[qk, "sqj"], tag=tag + "q")
        self.op("act", lambda e, rk=rk, qk=qk: e.activation(
            b[rk], b[qk], AF.Sqrt, scale=1.0 / D, bias=b["ceps"]),
            reads=[qk, "ceps"], writes=[rk], tag=tag + "r")
        self.op("dve", lambda e, rk=rk: e.reciprocal(b[rk], b[rk]),
                reads=[rk], writes=[rk], tag=tag + "i")
        self.op("dve", lambda e, mk=mk, rk=rk: e.tensor_scalar(
            out_ap, x_ap, b[mk], b[rk], op0=ALU.add, op1=ALU.mult),
            reads=x_keys + [mk, rk], writes=out_keys, tag=tag + "o")

    def ln_tile(self, x_ap, x_keys, out_ap, out_keys, sqj_ap, tag):
        self.ln_head(x_ap, x_keys, tag)
        self.ln_tail(x_ap, x_keys, out_ap, out_keys, sqj_ap, tag)

    def emit_bulk(self):
        d_in, b = self.d_in, self.b
        eind = self.ein_dram.rearrange("(n p) d -> n p d", p=128)
        for i in range(NTILE):
            gate = [("h1F", NT - 1, 0), ("h1F", NT - 1, 1)] if i == 0 else []
            self.op("pool", lambda e, i=i: e.dma_start(out=eind[i], in_=b["zt"][:]),
                    reads=["zt"] + gate, writes=[("eind_z", i)], inc=16,
                    tag=f"zein{i}", sem="zein")
        wb = self.wblob.rearrange("(n p) w -> n p w", p=128)
        for ex in range(NE):
            self.op("pool", lambda e, ex=ex: e.dma_start(
                out=wb[ex][:, BO_W1:BO_W1 + 4096].rearrange(
                    "p (k f) -> p k f", k=8)[:, :, 0:511],
                in_=d_in["we1t"][ex].rearrange("(k p) f -> p k f", p=128)),
                writes=[("wblob", ex, 0)], inc=16, tag=f"wb1_{ex}", sem="blb")
            self.op("pool", lambda e, ex=ex: e.dma_start(
                out=wb[ex][:, BO_W3:BO_W3 + 4096].rearrange(
                    "p (k f) -> p k f", k=8)[:, :, 0:511],
                in_=d_in["we3t"][ex].rearrange("(k p) f -> p k f", p=128)),
                writes=[("wblob", ex, 1)], inc=16, tag=f"wb3_{ex}", sem="blb")
            self.op("pool", lambda e, ex=ex: e.dma_start(
                out=wb[ex][:, BO_W2:BO_W2 + 4096].rearrange(
                    "p (k f) -> p k f", k=4)[:, :, 0:1023],
                in_=d_in["we2p"][ex].rearrange("(k p) f -> p k f", p=128)),
                writes=[("wblob", ex, 2)], inc=16, tag=f"wb2_{ex}", sem="blb")
            self.op("pool", lambda e, ex=ex: e.dma_start(
                out=wb[ex][0:1, BO_B1:BO_B1 + 511],
                in_=d_in["be1r"][ex:ex + 1, :]),
                writes=[("wblob", ex, 3)], inc=16, tag=f"wbb1_{ex}", sem="blb")
            self.op("pool", lambda e, ex=ex: e.dma_start(
                out=wb[ex][0:1, BO_B2:BO_B2 + 1023],
                in_=d_in["be2r"][ex:ex + 1, :]),
                writes=[("wblob", ex, 4)], inc=16, tag=f"wbb2_{ex}", sem="blb")
            self.op("pool", lambda e, ex=ex: e.dma_start(
                out=wb[ex][0:1, BO_B3:BO_B3 + 511],
                in_=d_in["be3r"][ex:ex + 1, :]),
                writes=[("wblob", ex, 5)], inc=16, tag=f"wbb3_{ex}", sem="blb")

    def emit_imp(self):
        b, cb = self.b, self.cb
        pS = self.pS
        self.op("act", lambda e: e.activation(
            b["imp_tm"][:], pS[:, 0:NT], AF.Sigmoid, bias=cb["bs2ar"][:, 0:1]),
            reads=["pS", ("c", "bs2ar")], writes=["imp_tm"], tag="isig")

    def emit_topk(self):
        b, cb, SA = self.b, self.cb, self.SA
        impf = SA[0:1, 0:4096]
        scr = SA[32:33, 0:4096]
        self.op("sp", lambda e: e.dma_start(
            out=self.impflat_dram.rearrange("o (p T) -> (o p) T", p=128),
            in_=b["imp_tm"][:]),
            reads=["imp_tm"], writes=["impflat"], inc=16, tag="impo", sem="impq")
        self.op("sp", lambda e: e.dma_start(out=impf, in_=self.impflat_dram),
                reads=["impflat"], writes=["impf"], war=[("xin", 0), ("xin", 1)],
                inc=16, tag="impi", sem="impq")
        self.op("dve", lambda e: e.max(out=b["topv"][:, 0:8], in_=impf),
                reads=["impf"], writes=["topv"], tag="mx8a")
        self.op("dve", lambda e: e.match_replace(
            out=scr, in_to_replace=b["topv"][:, 0:8], in_values=impf, imm_value=-1.0),
            reads=["impf", "topv"], writes=["scr"], tag="mra")
        self.op("dve", lambda e: e.max(out=b["topv"][:, 8:16], in_=scr),
                reads=["scr", "topv"], writes=["topv"], tag="mx8b")
        self.op("pe", lambda e: e.matmul(self.pX[:, 0:1], cb["onesrow"][0:1, :],
                                         b["topv"][0:1, 15:16], start=True, stop=True),
                reads=[("c", "onesrow"), "topv"], writes=["pX"], tag="bc16")
        self.op("dve", lambda e: e.tensor_copy(b["t16b"][:], self.pX[:, 0:1]),
                reads=["pX"], writes=["t16b"], tag="t16")
        self.op("dve", lambda e: e.tensor_scalar(
            b["sel"][:], b["imp_tm"][:], b["t16b"][:, 0:1], None, op0=ALU.is_ge),
            reads=["imp_tm", "t16b"], writes=["sel"], tag="sel")
        self.op("dve", lambda e: e.tensor_scalar(
            b["notsel"][:], b["sel"][:], -1.0, 1.0, op0=ALU.mult, op1=ALU.add),
            reads=["sel"], writes=["notsel"], tag="nsel")
        self.op("dve", lambda e: e.tensor_copy(b["hitbf"][:], b["sel"][:]),
                reads=["sel"], writes=["hitbf"], tag="hbf")
        self.emit_rank(b["sel"][:], ["sel"], b["rank16"][:], [("rank16",)], "rs")

        first = [True]
        for tau in range(NT):
            self.op("dve", lambda e, tau=tau: e.tensor_scalar(
                b["Pf32"][:], cb["iota16"][:], b["rank16"][:, tau:tau + 1], None,
                op0=ALU.is_equal),
                reads=[("rank16",), ("c", "iota16")], writes=["Pf32"], tag=f"pq{tau}")
            self.op("dve", lambda e, tau=tau: e.tensor_scalar(
                b["Pf32"][:], b["Pf32"][:], b["sel"][:, tau:tau + 1], None,
                op0=ALU.mult),
                reads=["Pf32", "sel"], writes=["Pf32"], tag=f"pm{tau}")
            self.op("dve", lambda e, tau=tau: e.tensor_copy(
                b["Pbuf"][:, tau * 16:(tau + 1) * 16], b["Pf32"][:]),
                reads=["Pf32"], writes=[("Pbuf", tau)], tag=f"pb{tau}")

            def mmi(e, tau=tau, first=first):
                ins = e.matmul(self.pS[0:16, 48:49], b["Pf32"][:],
                               cb["iotaf"][:, tau:tau + 1],
                               start=first[0], stop=(tau == NT - 1))
                first[0] = False
                return ins
            self.op("pe", mmi, reads=["Pf32", ("c", "iotaf")], writes=["pS"],
                    tag=f"px{tau}")
            bank = tau % 2
            self.op("pe", lambda e, tau=tau, bank=bank: e.transpose(
                self.pT[bank][0:16, 0:128], b["Pbuf"][:, tau * 16:(tau + 1) * 16],
                cb["ident"][:]),
                reads=[("Pbuf", tau), ("c", "ident")], writes=[("pt", bank)],
                tag=f"pt{tau}")
            self.op("dve", lambda e, tau=tau, bank=bank: e.tensor_copy(
                b["PT"][:, tau * 128:(tau + 1) * 128], self.pT[bank][0:16, 0:128]),
                reads=[("pt", bank)], writes=[("PT", tau)], tag=f"pe{tau}")
        self.op("dve", lambda e: e.tensor_copy(b["idx16f"][0:16, :], self.pS[0:16, 48:49]),
                reads=["pS"], writes=["idx16f"], tag="i16f")
        self.op("dve", lambda e: e.tensor_copy(b["idx16"][0:16, :], b["idx16f"][0:16, :]),
                reads=["idx16f"], writes=["idx16"], tag="i16")
        hkeys = [("h1_dram", tau) for tau in range(NT)]
        self.op("pool", lambda e: e.indirect_dma_start(
            out=b["h1selT"], out_offset=None, in_=self.h1_dram,
            in_offset=bass.IndirectOffsetOnAxis(ap=b["idx16"][0:16, :], axis=0)),
            reads=["idx16"] + hkeys, writes=["h1selT"], inc=16, tag="gsel",
            sem="gsel")
        self.load_w("wq", 16384, 1024, key=("w", "wqsel"), war=[("w", "ws1a")])
        self.emit_bulk()

    def rank_ops(self, hit_ap, hit_keys, rank_ap, rank_keys, tag, war=(),
                 ps=0):
        b, cb = self.b, self.cb
        sfx = "B" if ps else ""
        hb, sT, rf = "hitbf" + sfx, "selT" + sfx, "rfree" + sfx
        rt, rtb, of = "rowtot" + sfx, "rowtotb" + sfx, "offs" + sfx
        pS_, kS = [(self.pS, "pS"), (self.pA, "pA")][ps]
        pX_, kX = [(self.pX, "pX"), (self.pB, "pB")][ps]
        pT_ = self.pT[ps]
        return [
            lambda: self.op("pe", lambda e: e.transpose(
                pT_[0:32, 0:128], b[hb][:, 0:NT], cb["ident"][:]),
                reads=[hb, ("c", "ident")], writes=[("pt", ps)], tag=tag + "t"),
            lambda: self.op("dve", lambda e: e.tensor_copy(
                b[sT][0:32, :], pT_[0:32, 0:128]),
                reads=[("pt", ps)], writes=[sT], tag=tag + "e"),
            lambda: self.op("pe", lambda e: e.matmul(
                pS_[:, 0:32], b[sT][0:32, :],
                cb["strictu32"][0:32, :], start=True, stop=True),
                reads=[sT, ("c", "strictu32")], writes=[kS], tag=tag + "f"),
            lambda: self.op("dve", lambda e: e.tensor_copy(
                b[rf][:], pS_[:, 0:32]),
                reads=[kS], writes=[rf], tag=tag + "g"),
            lambda: self.op("dve", lambda e: e.tensor_reduce(
                b[rt][:], hit_ap, axis=AX.X, op=ALU.add),
                reads=hit_keys, writes=[rt], tag=tag + "h"),
            lambda: self.op("dve", lambda e: e.tensor_copy(b[rtb][:], b[rt][:]),
                    reads=[rt], writes=[rtb], tag=tag + "i"),
            lambda: self.op("pe", lambda e: e.matmul(
                pX_[:, 0:1], cb["strictu128"][:],
                b[rtb][:, 0:1], start=True, stop=True),
                reads=[rtb, ("c", "strictu128")], writes=[kX], tag=tag + "j"),
            lambda: self.op("dve", lambda e: e.tensor_copy(b[of][:], pX_[:, 0:1]),
                    reads=[kX], writes=[of], tag=tag + "k"),
            lambda: self.op("dve", lambda e: e.tensor_scalar(
                rank_ap, b[rf][:], b[of][:, 0:1], None, op0=ALU.add),
                reads=[rf, of], writes=rank_keys, war=war, tag=tag + "l"),
        ]

    def emit_rank(self, hit_ap, hit_keys, rank_ap, rank_keys, tag, war=()):
        for f in self.rank_ops(hit_ap, hit_keys, rank_ap, rank_keys, tag, war):
            f()

    def emit_selproj(self):
        b, cb = self.b, self.cb
        h1selF = b["h1selF"][:].rearrange("p (m j) -> p m j", m=8)
        vselF = b["vselF"][:].rearrange("p (m j) -> p m j", m=8)
        qselF = b["qselF"][:].rearrange("p (m j) -> p m j", m=8)
        kselF = b["kselF"][:].rearrange("p (m j) -> p m j", m=8)
        for half in range(2):
            bank = half % 2
            srcs = [b["h1selT"][:, (half * 4 + j) * 128:(half * 4 + j + 1) * 128]
                    for j in range(4)]

            def ev(e, half=half, bank=bank):
                return e.tensor_copy(
                    h1selF[:, half * 4:half * 4 + 4, :],
                    self.pT[bank][:, 0:64].rearrange("p (m j) -> p m j", m=4))
            self.tbatch(srcs, 16, ev, bank, ["h1selT"], ["h1selF"], f"st{half}",
                        base=32)
        for wname, wofs, bias, dst, dkey in (
                ("wqsel", 16384, "bqt", qselF, "qselF"),
                ("wk", 0, "bkt", kselF, "kselF"),
                ("wv", 8192, "bvt", vselF, "vselF")):
            for m in range(8):
                bank = [self.pA, self.pB][m % 2]
                bkey = ["pA", "pB"][m % 2]

                def mm(e, wofs=wofs, m=m, bank=bank):
                    ins = None
                    for k in range(8):
                        ins = e.matmul(bank[:, 0:16],
                                       self.wck(wofs, 1024, k, (m * 128, (m + 1) * 128)),
                                       h1selF[:, k, :],
                                       start=(k == 0), stop=(k == 7))
                    return ins
                self.op("pe", mm, reads=[("w", wname), "h1selF"], writes=[bkey],
                        tag=f"sp_{wname}_{m}")
                scl = 0.125 if wname == "wqsel" else None
                op1 = ALU.mult if scl is not None else ALU.bypass
                self.op("dve", lambda e, m=m, dst=dst, bias=bias, bank=bank,
                        scl=scl, op1=op1: e.tensor_scalar(
                            dst[:, m, :], bank[:, 0:16], cb[bias][:, m:m + 1],
                            scl, op0=ALU.add, op1=op1),
                        reads=[bkey, ("c", bias)], writes=[dkey],
                        tag=f"se_{wname}_{m}")
        for half in range(2):
            bank = half % 2
            srcs = [vselF[:, half * 4 + j, :] for j in range(4)]

            def ev(e, half=half, bank=bank):
                return e.tensor_copy(
                    b["vselT"][:, half * 512:(half + 1) * 512],
                    self.pT[bank][0:16, 0:512])
            self.tbatch(srcs, 128, ev, bank, ["vselF"], ["vselT"], f"vt{half}")
        self.emit_selP("qselP", qselF, war=[("w", "wqsel")])

    def emit_selP(self, nm, srcf, war=()):
        b = self.b
        self.op("dve", lambda e, nm=nm: e.memset(b[nm], 0.0), writes=[nm],
                war=war, tag=nm + "z")
        for h in range(H):
            m, lo = h // 2, (h % 2) * 64
            self.op("dve", lambda e, m=m, lo=lo, h=h, nm=nm, srcf=srcf:
                    e.tensor_copy(
                        b[nm][lo:lo + 64, h * 16:(h + 1) * 16],
                        srcf[lo:lo + 64, m, :]),
                    reads=["kselF" if nm == "kselP" else "qselF"],
                    writes=[nm], tag=f"{nm}{h}")

    def emit_p1(self):
        b, cb = self.b, self.cb
        h1F = self.A1[:].rearrange("p (m t) -> p m t", m=8)
        Kg = [self.A3[:, 0:4096], self.A3[:, 4096:8192]]
        Vg = [self.A3[:, 8192:12288].rearrange("p (T d) -> p T d", T=NT),
              self.A3[:, 12288:16384].rearrange("p (T d) -> p T d", T=NT)]
        Vfm = self.A2[:, 0:4096]
        expb = [self.A2[0:16, 4096:8192], self.A2[32:48, 4096:8192]]
        expT = self.A2[:, 8192:9216].rearrange("p (T j) -> p T j", T=NT)
        qselF = b["qselF"][:].rearrange("p (m j) -> p m j", m=8)
        pBig = self.pBig
        first_ctx = [True, True]
        VFM_KEYS = [("Vfm", n) for n in range(8)]

        for g in range(8):
            gb = g % 2
            for wname, wofs, bias, dstf in (("wk", 0, "bkt", None),
                                            ("wv", 8192, "bvt", Vfm)):
                for n in range(8):
                    bank = [self.pA, self.pB][n % 2]
                    bkey = ["pA", "pB"][n % 2]

                    def mm(e, wofs=wofs, g=g, n=n, bank=bank):
                        ins = None
                        for k in range(8):
                            ins = e.matmul(bank[:, 0:512],
                                           self.wck(wofs, 1024, k,
                                                    (g * 128, (g + 1) * 128)),
                                           h1F[:, k, n * 512:(n + 1) * 512],
                                           start=(k == 0), stop=(k == 7))
                        return ins
                    self.op("pe", mm, reads=[("w", wname)] + self.H1F_KEYS,
                            writes=[bkey], tag=f"{wname}p{g}_{n}")
                    if dstf is None:
                        war = [("h1t", i) for i in range(4)] if (g == 0 and n == 0) else []
                        self.op("dve", lambda e, g=g, n=n, gb=gb, bank=bank:
                                e.tensor_scalar(
                                    Kg[gb][:, n * 512:(n + 1) * 512], bank[:, 0:512],
                                    cb["bkt"][:, g:g + 1], None, op0=ALU.add),
                                reads=[bkey, ("c", "bkt")], writes=[("Kg", gb, n)],
                                war=[("relu", 0), ("relu", 1)] if (g < 2 and n == 0) else [],
                                tag=f"kpe{g}_{n}")
                    else:
                        war = ([("h1t", i) for i in range(4)] + ["sqj"]) \
                            if (g == 0 and n == 0) else []
                        self.op("dve", lambda e, g=g, n=n, bank=bank: e.tensor_scalar(
                            Vfm[:, n * 512:(n + 1) * 512], bank[:, 0:512],
                            cb["bvt"][:, g:g + 1], None, op0=ALU.add),
                            reads=[bkey, ("c", "bvt")], writes=[("Vfm", n)],
                            war=war, tag=f"vpe{g}_{n}")
            for q in range(8):
                bank = q % 2
                srcs = [Vfm[:, (q * 4 + j) * 128:(q * 4 + j + 1) * 128]
                        for j in range(4)]

                def ev(e, gb=gb, q=q, bank=bank):
                    return e.tensor_copy(
                        Vg[gb][:, q * 4:q * 4 + 4, :],
                        self.pT[bank][:, 0:512].rearrange("p (T d) -> p T d", T=4))
                self.tbatch(srcs, 128, ev, bank, VFM_KEYS, [("Vg", gb, q)],
                            f"vt{g}_{q}",
                            war=["zt"] if (g == 1 and q >= 6) else [])
            for hp in range(2):
                h = 2 * g + hp
                sbk = [self.pS, self.pX][hp]
                sbkey = ["pS", "pX"][hp]
                for n in range(8):
                    def mm(e, g=g, hp=hp, n=n, sbk=sbk, gb=gb, h=h):
                        return e.matmul(sbk[0:16, 0:512],
                                        b["qselP"][:, h * 16:(h + 1) * 16],
                                        Kg[gb][:, n * 512:(n + 1) * 512],
                                        start=True, stop=True)
                    self.op("pe", mm, reads=["qselP", ("Kg", gb, n)],
                            writes=[sbkey], tag=f"sb{h}_{n}")
                    self.op("act", lambda e, hp=hp, n=n, sbk=sbk: e.activation(
                        expb[hp][:, n * 512:(n + 1) * 512], sbk[0:16, 0:512], AF.Exp,
                        accum_out=b["denb8"][0:16, n:n + 1]),
                        reads=[sbkey], writes=[("expb", hp, n), "denb8"],
                        war=["sqj"] if (g == 0 and n < 2) else [],
                        tag=f"se{h}_{n}")
                self.op("dve", lambda e, h=h: e.tensor_reduce(
                    b["rdenb"][0:16, h:h + 1], b["denb8"][0:16, :], axis=AX.X,
                    op=ALU.add),
                    reads=["denb8"], writes=[("rdt", h)], tag=f"dn{h}")
                self.op("dve", lambda e, h=h: e.reciprocal(
                    b["rdenb"][0:16, h:h + 1], b["rdenb"][0:16, h:h + 1]),
                    reads=[("rdt", h)], writes=[("rdenb", h), ("rdt", h)],
                    tag=f"dr{h}")
            for hp in range(2):
                ebkeys = [("expb", hp, n) for n in range(8)]
                for q in range(8):
                    bank = q % 2
                    srcs = [expb[hp][:, (q * 4 + j) * 128:(q * 4 + j + 1) * 128]
                            for j in range(4)]

                    def ev(e, hp=hp, q=q, bank=bank):
                        return e.tensor_copy(
                            expT[:, q * 4:q * 4 + 4, hp * 16:(hp + 1) * 16],
                            self.pT[bank][:, 0:64].rearrange("p (T j) -> p T j", T=4))
                    self.tbatch(srcs, 16, ev, bank, ebkeys, [("expT", hp, q)],
                                f"bt{g}_{hp}_{q}", base=32 * hp)
            for hp in range(2):
                h = 2 * g + hp
                bk01 = 0 if h < 8 else 1

                def mmc(e, gb=gb, hp=hp, h=h, first_ctx=first_ctx, bk01=bk01):
                    ins = None
                    for kt in range(NT):
                        ins = e.matmul(pBig[0:16, h * 64:(h + 1) * 64],
                                       expT[:, kt, hp * 16:(hp + 1) * 16],
                                       Vg[gb][:, kt, 64 * hp:64 * hp + 64],
                                       start=(first_ctx[bk01] and kt == 0),
                                       stop=(h in (7, 15) and kt == NT - 1))
                    first_ctx[bk01] = False
                    return ins
                self.op("pe", mmc,
                        reads=[("expT", hp, q) for q in range(8)] +
                              [("Vg", gb, q) for q in range(8)],
                        writes=[("pBig", bk01)], tag=f"cb{h}")
        for h in range(H):
            bk01 = 0 if h < 8 else 1
            self.op("dve", lambda e, h=h: e.tensor_scalar(
                b["ctxbT"][:, h * 64:(h + 1) * 64],
                self.pBig[0:16, h * 64:(h + 1) * 64],
                b["rdenb"][0:16, h:h + 1], None, op0=ALU.mult),
                reads=[("pBig", bk01), ("rdenb", h)], writes=[("ctxbT", h)],
                tag=f"ce{h}")

    def load_w_half(self, name, wofs, h, key, war, tag):
        src = self.d_in[name].rearrange("(k p) f -> p k f", p=128)[
            :, :, h * 512:(h + 1) * 512]
        dst = self.WA[:, wofs:wofs + 8192].rearrange("p (k f) -> p k f", k=8)[
            :, :, h * 512:(h + 1) * 512]

        def fn(e, src=src, dst=dst):
            return e.dma_start(out=dst, in_=src)
        self.op("pool", fn, writes=[key], war=war, inc=16, tag=tag, sem="wld")

    def emit_p2(self):
        import os as _os
        b, cb = self.b, self.cb
        h1F = self.A1[:].rearrange("p (m t) -> p m t", m=8)
        h2F = h1F
        self.load_w("wo", 8192, 1024, key=("w", "wo"), war=[("w", "wv")])
        self.load_w("wq", 16384, 1024, key=("w", "wqsel"), war=["qselP"])
        kselF = b["kselF"][:].rearrange("p (m j) -> p m j", m=8)
        self.emit_selP("kselP", kselF, war=[("w", "wk")])

        def qch(m):
            if m < 4:
                return self.A2[:, m * 4096:(m + 1) * 4096]
            return self.A3[:, (m - 4) * 4096:(m - 3) * 4096]

        for m in range(8):
            for n in range(8):
                bank = [self.pA, self.pB][(m * 8 + n) % 2]
                bkey = ["pA", "pB"][(m * 8 + n) % 2]

                def mm(e, m=m, n=n, bank=bank):
                    ins = None
                    for k in range(8):
                        ins = e.matmul(bank[:, 0:512],
                                       self.wck(16384, 1024, k, (m * 128, (m + 1) * 128)),
                                       h1F[:, k, n * 512:(n + 1) * 512],
                                       start=(k == 0), stop=(k == 7))
                    return ins
                self.op("pe", mm, reads=[("w", "wqsel")] + self.H1F_KEYS,
                        writes=[bkey], tag=f"qp{m}_{n}")
                self.op("dve", lambda e, m=m, n=n, bank=bank: e.tensor_scalar(
                    qch(m)[:, n * 512:(n + 1) * 512], bank[:, 0:512],
                    cb["bqt"][:, m:m + 1], 0.125, op0=ALU.add, op1=ALU.mult),
                    reads=[bkey, ("c", "bqt")], writes=[("Qf", m, n)],
                    tag=f"qe{m}_{n}")
        QF_KEYS = [("Qf", m, n) for m in range(8) for n in range(8)]

        kselF = b["kselF"][:].rearrange("p (m j) -> p m j", m=8)
        xin = [self.SA[:, 0:1024], self.SA[:, 1024:2048]]
        x2t = [self.SA[:, 2048:3072], self.SA[:, 3072:4096]]
        xd = self.d_in["x"].rearrange("(T p) d -> T p d", p=128)
        x2d = self.x2_dram.rearrange("(T p) d -> T p d", p=128)
        h2d = self.h2_dram.rearrange("(T p) d -> T p d", p=128)

        def p2F(tau):
            bb = tau % 2
            bank = [self.pA, self.pB][tau % 2]
            bkey = ["pA", "pB"][tau % 2]
            st = [lambda: self.op("sp", lambda e: e.dma_start(
                out=xin[bb], in_=xd[tau]),
                writes=[("xin", bb)], war=["impf", "scr"] if tau < 2 else [],
                inc=16, tag=f"px{tau}", sem=f"xin{bb}")]

            def mma(e, tau=tau, bank=bank):
                ins = None
                for h in range(H):
                    m = h // 2
                    ins = e.matmul(bank[:, h * 16:(h + 1) * 16],
                                   qch(m)[:, tau * 128:(tau + 1) * 128],
                                   b["kselP"][:, h * 16:(h + 1) * 16],
                                   start=(h == 0), stop=(h == H - 1))
                return ins
            st.append(lambda: self.op("pe", mma, reads=QF_KEYS + ["kselP"],
                                      writes=[bkey], tag=f"sa{tau}"))
            st.append(lambda: self.op("act", lambda e: e.activation(
                b["exps"][:], bank[:, 0:256], AF.Exp),
                reads=[bkey], writes=["exps"], tag=f"se{tau}"))
            st.append(lambda: self.op("dve", lambda e: e.tensor_reduce(
                b["aden"][:], b["exps"][:].rearrange("p (h j) -> p h j", h=16),
                axis=AX.X, op=ALU.add),
                reads=["exps"], writes=["aden"], tag=f"sd{tau}"))
            st.append(lambda: self.op("dve", lambda e: e.reciprocal(
                b["arden"][:], b["aden"][:]),
                reads=["aden"], writes=["arden"], tag=f"sr{tau}"))
            st.append(lambda: self.op("dve", lambda e: e.tensor_tensor(
                b["wsc"][:].rearrange("p (h j) -> p h j", h=16),
                b["exps"][:].rearrange("p (h j) -> p h j", h=16),
                b["arden"][:].rearrange("p (h j) -> p h j", h=16, j=1)
                .broadcast_to([128, 16, 16]), op=ALU.mult),
                reads=["exps", "arden"], writes=["wsc"], tag=f"sw{tau}"))
            for q in range(4):
                tb = q % 2
                srcs = [b["wsc"][:, (q * 4 + j) * 16:(q * 4 + j + 1) * 16]
                        for j in range(4)]

                def ev(e, tb=tb):
                    return e.tensor_copy(b["wTst"][:, 0:512],
                                         self.pT[tb][0:16, 0:512])
                st.append(lambda q=q, tb=tb, srcs=srcs, ev=ev:
                          self.tbatch(srcs, 128, ev, tb, ["wsc"], ["wTst"],
                                      f"wt{tau}_{q}"))

                def mmc(e, tau=tau, q=q):
                    ins = None
                    for j in range(4):
                        h = q * 4 + j
                        ins = e.matmul(self.pBig[:, h * 64:(h + 1) * 64],
                                       b["wTst"][:, j * 128:(j + 1) * 128],
                                       b["vselT"][:, h * 64:(h + 1) * 64],
                                       start=(h in (0, 8)), stop=False)
                    return ins
                st.append(lambda mmc=mmc, q=q: self.op(
                    "pe", mmc, reads=["wTst", "vselT"],
                    writes=[("pBig", q // 2)], tag=f"cx{tau}_{q}"))

            def mmp(e, tau=tau):
                e.matmul(self.pBig[:, 0:512],
                         b["PT"][:, tau * 128:(tau + 1) * 128],
                         b["ctxbT"][:, 0:512], start=False, stop=True)
                return e.matmul(self.pBig[:, 512:1024],
                                b["PT"][:, tau * 128:(tau + 1) * 128],
                                b["ctxbT"][:, 512:1024], start=False, stop=True)
            st.append(lambda: self.op("pe", mmp,
                      reads=[("PT", tau)] + [("ctxbT", h) for h in range(H)],
                      writes=[("pBig", 0), ("pBig", 1)], tag=f"cp{tau}"))
            return st

        def p2B(tau):
            bb = tau % 2
            bt = b["bt"][bb]
            sh = self.WA[:, 2048:3072]
            st = []
            for half in range(2):
                st.append(lambda half=half: self.op(
                    "dve", lambda e: e.tensor_scalar(
                        bt[:, half * 512:(half + 1) * 512],
                        self.pBig[:, half * 512:(half + 1) * 512],
                        b["notsel"][:, tau:tau + 1], None, op0=ALU.mult),
                    reads=[("pBig", half), "notsel"],
                    writes=[("bt", bb, half)], tag=f"cv{tau}_{half}"))
            for half in range(2):
                srcs = [bt[:, (half * 4 + j) * 128:(half * 4 + j + 1) * 128]
                        for j in range(4)]

                def ev(e, half=half):
                    return e.tensor_copy(
                        self.ctxf[:, half * 512:(half + 1) * 512],
                        self.pT[half][:, 0:512])
                st.append(lambda half=half, srcs=srcs, ev=ev: self.tbatch(
                    srcs, 128, ev, half, [("bt", bb, 0), ("bt", bb, 1)],
                    [("ctxf", half)], f"cf{tau}_{half}"))
            for half, pw, pwkey in ((0, self.pS, "pS"), (1, self.pX, "pX")):
                def mmw(e, half=half, pw=pw):
                    for k in range(8):
                        e.matmul(pw[:, 0:512],
                                 self.ctxf[:, k * 128:(k + 1) * 128],
                                 self.wck(8192, 1024, k,
                                          (half * 512, (half + 1) * 512)),
                                 start=(k == 0), stop=False)
                    return e.matmul(pw[:, 0:512], b["onesb32"],
                                    b["borowb"][:, half * 512:(half + 1) * 512],
                                    start=False, stop=True)
                st.append(lambda mmw=mmw, pwkey=pwkey: self.op(
                    "pe", mmw, reads=[("ctxf", 0), ("ctxf", 1), ("w", "wo"),
                                      "borowb", "onesb32"],
                    writes=[pwkey], tag=f"wo{tau}_{half}"))
                st.append(lambda half=half, pw=pw, pwkey=pwkey: self.op(
                    "dve", lambda e: e.tensor_tensor(
                        x2t[bb][:, half * 512:(half + 1) * 512], pw[:, 0:512],
                        xin[bb][:, half * 512:(half + 1) * 512], op=ALU.add),
                    reads=[pwkey, ("xin", bb)], writes=[("x2t", bb, half)],
                    tag=f"we{tau}_{half}"))
            st.append(lambda: self.op("sp", lambda e: e.dma_start(
                out=x2d[tau], in_=x2t[bb]),
                reads=[("x2t", bb, 0), ("x2t", bb, 1)],
                writes=[("x2_dram", tau)],
                inc=16, tag=f"sx{tau}", sem=f"sx{bb}"))
            st.append(lambda: self.ln_head(
                x2t[bb], [("x2t", bb, 0), ("x2t", bb, 1)], f"l2_{tau}",
                mk=f"mean{bb}"))
            st.append(lambda: self.ln_tail(
                x2t[bb], [("x2t", bb, 0), ("x2t", bb, 1)], bt,
                [("bt", bb, 0), ("bt", bb, 1)], sh, f"l2_{tau}",
                mk=f"mean{bb}", rk=f"rstd{bb}", qk=f"ssqA{bb}"))
            st.append(lambda: self.op("sp", lambda e: e.dma_start(
                out=h2d[tau], in_=bt),
                reads=[("bt", bb, 0), ("bt", bb, 1)],
                writes=[("h2_dram", tau)],
                inc=16, tag=f"sh{tau}", sem="sh2"))
            for half in range(2):
                srcs = [bt[:, (half * 4 + j) * 128:(half * 4 + j + 1) * 128]
                        for j in range(4)]

                def ev(e, tau=tau, half=half):
                    return e.tensor_copy(
                        h2F[:, half * 4:half * 4 + 4,
                            tau * 128:(tau + 1) * 128],
                        self.pT[half][:, 0:512].rearrange(
                            "p (m t) -> p m t", m=4))
                st.append(lambda half=half, srcs=srcs, ev=ev, tau=tau:
                          self.tbatch(
                              srcs, 128, ev, half,
                              [("bt", bb, 0), ("bt", bb, 1)],
                              [("h2F", tau, half)], f"t2{tau}_{half}",
                              war=[("h1F", tau, hh) for hh in range(2)]))
            return st

        for i in range(self.p2tau + 1):
            A = p2F(i) if i < self.p2tau else []
            B = p2B(i - 1) if i >= 1 else []
            na, nb = len(A), len(B)
            ia = ib = 0
            while ia < na or ib < nb:
                if ia < na and (ib >= nb or ia * (nb + 1) <= ib * (na + 1)):
                    A[ia]()
                    ia += 1
                else:
                    B[ib]()
                    ib += 1

    def emit_route(self):
        b, cb = self.b, self.cb
        d_in = self.d_in
        h2F = self.A1[:].rearrange("p (m t) -> p m t", m=8)
        gw = b["gatew"].rearrange("p (m n) -> p m n", m=8)
        self.op("pool", lambda e: e.dma_start(
            out=gw, in_=d_in["gatewt"].rearrange("(m p) n -> p m n", p=128)),
            writes=["gatew"],
            war=[("Qf", 4, n) for n in range(8)] + [("Kg", 0, n) for n in range(8)],
            inc=16, tag="ld_gatew", sem="wld")

        def mmg(e):
            ins = None
            for tau in range(NT):
                for k in range(8):
                    ins = e.matmul(self.pA[:, tau * 16:(tau + 1) * 16],
                                   h2F[:, k, tau * 128:(tau + 1) * 128],
                                   gw[:, k, :],
                                   start=(tau == 0 and k == 0),
                                   stop=(tau == NT - 1 and k == 7))
            return ins
        self.op("pe", mmg, reads=self.H2F_KEYS + ["gatew"], writes=["pA"], tag="gt")
        self.op("act", lambda e: e.activation(b["expg"], self.pA[:, 0:512], AF.Exp),
                reads=["pA"], writes=["expg"], war=[("xin", 0), ("xin", 1)],
                tag="ge")
        self.op("dve", lambda e: e.tensor_reduce(
            b["rden"][:], b["expg"].rearrange("p (T j) -> p T j", T=NT),
            axis=AX.X, op=ALU.add),
            reads=["expg"], writes=["rden"], tag="gd")
        self.op("dve", lambda e: e.reciprocal(b["rden"][:], b["rden"][:]),
                reads=["rden"], writes=["rden"], tag="gr")
        XW = [("xin", 0), ("xin", 1)]
        r3 = b["reqall"].rearrange("p (T j) -> p T j", T=NT)
        q3 = b["req2all"].rearrange("p (T j) -> p T j", T=NT)
        expg3 = b["expg"].rearrange("p (T j) -> p T j", T=NT)
        iotam = cb["iota16"][:].unsqueeze(1).broadcast_to((128, NT, 16))
        HIT_W = [("hit", tau) for tau in range(NT)]
        self.op("dve", lambda e: e.tensor_reduce(
            b["m1v"], expg3, axis=AX.X, op=ALU.max),
            reads=["expg"], writes=["m1v"], war=XW, tag="v1")
        self.op("dve", lambda e: e.tensor_tensor(
            r3, expg3, b["m1v"].unsqueeze(2).broadcast_to((128, NT, 16)),
            op=ALU.is_ge),
            reads=["expg", "m1v"], writes=["reqall"], war=XW, tag="v2")
        self.op("dve", lambda e: e.scalar_tensor_tensor(
            q3, r3, -1e30, expg3, op0=ALU.mult, op1=ALU.add),
            reads=["reqall", "expg"], writes=["req2all"],
            war=[("x2t", 1, 0), ("x2t", 1, 1)], tag="v3")
        self.op("dve", lambda e: e.tensor_reduce(
            b["m2v"], q3, axis=AX.X, op=ALU.max),
            reads=["req2all"], writes=["m2v"], war=XW, tag="v4")
        self.op("dve", lambda e: e.tensor_tensor(
            q3, q3, b["m2v"].unsqueeze(2).broadcast_to((128, NT, 16)),
            op=ALU.is_ge),
            reads=["req2all", "m2v"], writes=["req2all"], tag="v5")
        self.op("dve", lambda e: e.tensor_add(
            b["hit"][:, :], b["reqall"], b["req2all"]),
            reads=["reqall", "req2all"], writes=HIT_W,
            war=[("Qf", m, n) for m in range(4) for n in range(8)], tag="v6")
        self.op("dve", lambda e: e.tensor_tensor(
            r3, r3, iotam, op=ALU.mult),
            reads=["reqall", ("c", "iota16")], writes=["reqall"], tag="v7")
        self.op("dve", lambda e: e.tensor_reduce(
            b["eav"], r3, axis=AX.X, op=ALU.add),
            reads=["reqall"], writes=["eav"], war=XW, tag="v8")
        self.op("dve", lambda e: e.tensor_tensor(
            q3, q3, iotam, op=ALU.mult),
            reads=["req2all", ("c", "iota16")], writes=["req2all"], tag="v9")
        self.op("dve", lambda e: e.tensor_reduce(
            b["ebv"], q3, axis=AX.X, op=ALU.add),
            reads=["req2all"], writes=["ebv"], war=XW, tag="va")
        self.op("dve", lambda e: e.tensor_tensor(
            b["elo"][:, :], b["eav"], b["ebv"], op=ALU.min),
            reads=["eav", "ebv"], war=XW,
            writes=[("elo", t) for t in range(NT)], tag="vb")
        self.op("dve", lambda e: e.tensor_tensor(
            b["ehi"][:, :], b["eav"], b["ebv"], op=ALU.max),
            reads=["eav", "ebv"],
            writes=[("ehi", t) for t in range(NT)], tag="vc")
        self.op("dve", lambda e: e.tensor_tensor(
            b["flagv"], b["eav"], b["ebv"], op=ALU.is_lt),
            reads=["eav", "ebv"], writes=["flagv"], war=XW, tag="vd")
        self.op("dve", lambda e: e.tensor_tensor(
            b["ddv"], b["m1v"], b["m2v"], op=ALU.subtract),
            reads=["m1v", "m2v"], writes=["ddv"], war=XW, tag="ve")
        self.op("dve", lambda e: e.tensor_tensor(
            b["ddv"], b["ddv"], b["flagv"], op=ALU.mult),
            reads=["ddv", "flagv"], writes=["ddv"], tag="vf")
        self.op("dve", lambda e: e.tensor_tensor(
            b["wlov"], b["m2v"], b["ddv"], op=ALU.add),
            reads=["m2v", "ddv"], writes=["wlov"], war=XW, tag="vg")
        self.op("dve", lambda e: e.tensor_tensor(
            b["whiv"], b["m1v"], b["m2v"], op=ALU.add),
            reads=["m1v", "m2v"], writes=["whiv"], war=XW, tag="vh")
        self.op("dve", lambda e: e.tensor_tensor(
            b["whiv"], b["whiv"], b["wlov"], op=ALU.subtract),
            reads=["whiv", "wlov"], writes=["whiv"], tag="vi")
        self.op("dve", lambda e: e.tensor_tensor(
            b["w0"][:, :], b["wlov"], b["rden"][:], op=ALU.mult),
            reads=["wlov", "rden"], writes=[("w0", t) for t in range(NT)],
            tag="vj")
        self.op("dve", lambda e: e.tensor_tensor(
            b["w1"][:, :], b["whiv"], b["rden"][:], op=ALU.mult),
            reads=["whiv", "rden"], writes=[("w1", t) for t in range(NT)],
            tag="vk")
        HIT_KEYS = [("hit", tau) for tau in range(NT)]
        hit3 = b["hit"].rearrange("p (T j) -> p T j", T=NT)
        rall3 = b["rall"].rearrange("p (T j) -> p T j", T=NT)
        def rank_chain(ex):
            ps = ex % 2
            sfx = "B" if ps else ""
            ops = [lambda: self.op(
                "dve", lambda e, ex=ex, sfx=sfx: e.tensor_copy(
                    b["hitbf" + sfx][:], hit3[:, :, ex]),
                reads=HIT_KEYS, writes=["hitbf" + sfx], tag=f"rh{ex}")]
            ops += self.rank_ops(hit3[:, :, ex], HIT_KEYS, rall3[:, :, ex],
                                 [("rall", ex)], f"re{ex}",
                                 war=[("xin", 0), ("xin", 1), "reqall"]
                                 if ex < 2 else [], ps=ps)
            ops.append(lambda: self.op(
                "pe", lambda e, ex=ex, sfx=sfx: e.matmul(
                    self.pBig[0:1, ex:ex + 1], b["rowtotb" + sfx][:, 0:1],
                    cb["onescol"][:], start=True, stop=True),
                reads=["rowtotb" + sfx, ("c", "onescol")],
                writes=[("pBig", 0)], tag=f"rc{ex}"))
            return ops

        for p0 in range(0, NE, 2):
            for a, bl in zip(rank_chain(p0), rank_chain(p0 + 1)):
                a()
                bl()
        RALL_KEYS = [("rall", ex) for ex in range(NE)]
        self.op("dve", lambda e: e.tensor_copy(b["cntrow"][:], self.pBig[0:1, 0:16]),
                reads=[("pBig", 0)], writes=["cntrow"],
                war=[("xin", 0), ("xin", 1)], tag="pc0")
        self.op("dve", lambda e: e.tensor_scalar(
            b["pcs"][:], b["cntrow"][:], 127.49, 1.0 / 256.0,
            op0=ALU.add, op1=ALU.mult),
            reads=["cntrow"], writes=["pcs"], tag="pc1")
        self.op("dve", lambda e: e.tensor_copy(b["pc32"][:], b["pcs"][:]),
                reads=["pcs"], writes=["pc32"], tag="pc2")
        self.op("dve", lambda e: e.tensor_copy(b["pcs"][:], b["pc32"][:]),
                reads=["pc32"], writes=["pcs"], tag="pc3")
        self.op("dve", lambda e: e.tensor_scalar(
            b["pcrow"][:], b["pcs"][:], 256.0, None, op0=ALU.mult),
            reads=["pcs"], writes=["pcrow"], tag="pc4")
        self.op("dve", lambda e: e.memset(b["pofsrow"][:, 0:1], 0.0),
                writes=["pofsrow"], war=[("xin", 0), ("xin", 1)], tag="po0")
        for ex in range(1, NE):
            self.op("dve", lambda e, ex=ex: e.tensor_tensor(
                b["pofsrow"][:, ex:ex + 1], b["pofsrow"][:, ex - 1:ex],
                b["pcrow"][:, ex - 1:ex], op=ALU.add),
                reads=["pofsrow", "pcrow"], writes=["pofsrow"], tag=f"po{ex}")
        self.op("pe", lambda e: e.matmul(
            self.pS[:, 0:16], cb["onesrow"][:], b["pofsrow"][:],
            start=True, stop=True),
            reads=["pofsrow", ("c", "onesrow")], writes=["pS"], tag="pob")
        self.op("dve", lambda e: e.tensor_copy(b["pofsb"][:], self.pS[:, 0:16]),
                reads=["pS"], writes=["pofsb"],
                war=[("xin", 0), ("xin", 1)], tag="poc")
        for ex in range(NE):
            self.op("dve", lambda e, ex=ex: e.tensor_scalar(
                rall3[:, :, ex], rall3[:, :, ex], b["pofsb"][:, ex:ex + 1], None,
                op0=ALU.add),
                reads=[("rall", ex), "pofsb"], writes=[("rall", ex)], tag=f"pr{ex}")
        self.op("dve", lambda e: e.memset(b["texp"][:], -1.0),
                writes=["texp"], tag="tx_0")
        for ex in range(NE):
            self.op("dve", lambda e, ex=ex: e.tensor_scalar(
                b["t80"][:], cb["tb80"][:], b["pofsrow"][:, ex:ex + 1], None,
                op0=ALU.is_ge),
                reads=[("c", "tb80"), "pofsrow"], writes=["t80"], tag=f"tx{ex}a")
            self.op("dve", lambda e: e.tensor_tensor(
                b["texp"][:], b["texp"][:], b["t80"][:], op=ALU.add),
                reads=["texp", "t80"], writes=["texp"], tag=f"tx{ex}b")
        self.op("pe", lambda e: e.matmul(
            self.pX[:, 0:NG], b["onesb32"][:], b["texp"][:],
            start=True, stop=True),
            reads=["texp", "onesb32"], writes=["pX"], tag="txb")
        self.op("dve", lambda e: e.tensor_scalar(
            self.pX[:, 0:NG], self.pX[:, 0:NG], 128.0,
            cb["iotaf"][:, 0:1], op0=ALU.mult, op1=ALU.add),
            reads=["pX", ("c", "iotaf")], writes=["pX"], tag="txd")
        self.op("dve", lambda e: e.tensor_copy(b["idxmat"][:], self.pX[:, 0:NG]),
                reads=["pX"], writes=["idxmat"], tag="txf")
        for ekey, dkey in (("elo", "dest0f"), ("ehi", "dest1f")):
            self.op("dve", lambda e, ekey=ekey: e.tensor_tensor(
                q3, iotam,
                b[ekey][:, :].unsqueeze(2).broadcast_to((128, NT, 16)),
                op=ALU.is_equal),
                reads=[(ekey, t) for t in range(NT)] + [("c", "iota16")],
                writes=["req2all"], tag=f"dw1{dkey}")
            self.op("dve", lambda e: e.tensor_tensor(
                q3, q3, rall3, op=ALU.mult),
                reads=["req2all"] + RALL_KEYS, writes=["req2all"],
                tag=f"dw2{dkey}")
            self.op("dve", lambda e, dkey=dkey: e.tensor_reduce(
                b[dkey][:, :], q3, axis=AX.X, op=ALU.add),
                reads=["req2all"], writes=[(dkey, t) for t in range(NT)],
                tag=f"dw3{dkey}")
        self.op("dve", lambda e: e.tensor_copy(b["dest0"][:], b["dest0f"]),
                reads=[("dest0f", tau) for tau in range(NT)], writes=["dest0"],
                war=["pc32"], tag="dc0")
        self.op("dve", lambda e: e.tensor_copy(b["dest1"][:], b["dest1f"]),
                reads=[("dest1f", tau) for tau in range(NT)], writes=["dest1"],
                tag="dc1")

    def emit_scatter(self):
        import os as _os_sc
        b = self.b
        self.load_w("sw1t", 0, 1023, key=("w", "sw1t"),
                    war=[("w", "wk"), "kselP", ("ctxf", 0), ("ctxf", 1), "exps",
                         "wsc", "sqj"] +
                        [("bt", i, h) for i in range(2) for h in range(2)])
        self.load_w("sw3t", 8192, 1023, key=("w", "sw3t"), war=[("w", "wo")])
        self.load_w("sw2ts", 16384, 1023, key=("w", "sw2ts"),
                    war=[("w", "wqsel"), "qselP"])
        self.op("pool", lambda e: e.dma_start(
            out=b["sw2t0"], in_=self.d_in["sw2t0"]),
            writes=["sw2t0"], inc=16, tag="ld_sw2t0", sem="wld")
        h2d = self.h2_dram.rearrange("(T p) d -> T p d", p=128)
        zkeys = [("eind_z", i) for i in range(NTILE)]
        sst = [self.A2[:, 12288:13312], self.A2[:, 13312:14336]]
        QF_ALL = [("Qf", m, n) for m in range(8) for n in range(8)]
        for tau in range(NT):
            bb = tau % 2
            stg = sst[bb]
            self.op("sp", lambda e, tau=tau, stg=stg: e.dma_start(
                out=stg, in_=h2d[tau]),
                reads=[("h2_dram", tau)] +
                      ([("w", "sw2ts")] if tau == 0 else []),
                writes=[("sst", bb)],
                war=QF_ALL if tau < 2 else [], inc=16, tag=f"sl{tau}",
                sem=f"st{bb}")
            for slot, dest in ((0, b["dest0"]), (1, b["dest1"])):
                self.op("pool", lambda e, tau=tau, stg=stg, dest=dest:
                        e.indirect_dma_start(
                            out=self.ein_dram,
                            out_offset=bass.IndirectOffsetOnAxis(
                                ap=dest[:, tau:tau + 1], axis=0),
                            in_=stg, in_offset=None,
                            bounds_check=self.bcreg, oob_is_err=False),
                        reads=[("sst", bb),
                               "dest0" if slot == 0 else "dest1"] + zkeys,
                        writes=[("ein", tau, slot)] + (
                            ["sc_chain"] if _os_sc.environ.get("SIMSAFE")
                            else []),
                        inc=16, tag=f"sc{tau}_{slot}", sem=f"sc{tau % 2}")

    def emit_shared(self):
        b, cb = self.b, self.cb
        h2F = self.A1[:].rearrange("p (m t) -> p m t", m=8)
        QF_WAR = [("Qf", m, n) for m in range(8) for n in range(8)]
        a_sh = self.A3[:, 0:8192].rearrange("p (m t) -> p m t", m=8)
        s_sh = self.A3[:, 8192:16384].rearrange("p (m t) -> p m t", m=8)
        z_sh = self.A2[:, 0:8192].rearrange("p (m t) -> p m t", m=8)
        zsq = self.A2[:, 8192:9216]
        sqz = self.A2[:, 9216:10240]
        zshd = self.zsh_dram.rearrange("(T p) d -> T p d", p=128)

        for blk in range(4):
            t0 = blk * 1024
            for m in range(8):
                rows = 128 if m < 7 else 127
                for nh in range(2):
                    bank = [self.pA, self.pB][nh]
                    bkey = ["pA", "pB"][nh]

                    def mm1(e, m=m, nh=nh, bank=bank, t0=t0, rows=rows):
                        ins = None
                        for k in range(8):
                            ins = e.matmul(bank[0:rows, 0:512],
                                           self.wck(0, 1023, k, (m * 128, m * 128 + rows)),
                                           h2F[:, k, t0 + nh * 512:t0 + (nh + 1) * 512],
                                           start=(k == 0), stop=(k == 7))
                        return ins
                    self.op("pe", mm1, reads=[("w", "sw1t")] + self.H2F_KEYS,
                            writes=[bkey], tag=f"s1_{blk}_{m}_{nh}")
                    self.op("act", lambda e, m=m, nh=nh, bank=bank, rows=rows:
                            e.activation(
                                a_sh[0:rows, m, nh * 512:(nh + 1) * 512],
                                bank[0:rows, 0:512], AF.Silu,
                                bias=cb["bs1st"][0:rows, m:m + 1]),
                            reads=[bkey, ("c", "bs1st")], writes=[("ash", m, nh)],
                            war=QF_WAR if (blk == 0 and m == 4 and nh == 0) else [],
                            tag=f"s1e_{blk}_{m}_{nh}")

                    def mm3(e, m=m, nh=nh, bank=bank, t0=t0, rows=rows):
                        ins = None
                        for k in range(8):
                            ins = e.matmul(bank[0:rows, 0:512],
                                           self.wck(8192, 1023, k,
                                                    (m * 128, m * 128 + rows)),
                                           h2F[:, k, t0 + nh * 512:t0 + (nh + 1) * 512],
                                           start=(k == 0), stop=(k == 7))
                        return ins
                    self.op("pe", mm3, reads=[("w", "sw3t")] + self.H2F_KEYS,
                            writes=[bkey], tag=f"s3_{blk}_{m}_{nh}")
                    self.op("dve", lambda e, m=m, nh=nh, bank=bank, rows=rows:
                            e.scalar_tensor_tensor(
                                s_sh[0:rows, m, nh * 512:(nh + 1) * 512],
                                bank[0:rows, 0:512], cb["bs3st"][0:rows, m:m + 1],
                                a_sh[0:rows, m, nh * 512:(nh + 1) * 512],
                                op0=ALU.add, op1=ALU.mult),
                            reads=[bkey, ("c", "bs3st"), ("ash", m, nh)],
                            writes=[("ssh", m, nh)], tag=f"s3e_{blk}_{m}_{nh}")
            for m in range(8):
                rows = 128 if m < 7 else 127
                self.op("act", lambda e, m=m, rows=rows: e.activation(
                    zsq[0:rows, :], s_sh[0:rows, m, :], AF.Square),
                    reads=[("ssh", m, 0), ("ssh", m, 1)], writes=["zsq"],
                    war=QF_WAR if (blk == 0 and m == 0) else [],
                    tag=f"sq_{blk}_{m}")
                for nh, pw, pwkey in ((0, self.pS, "pS"), (1, self.pX, "pX")):
                    self.op("pe", lambda e, m=m, nh=nh, pw=pw, rows=rows: e.matmul(
                        pw[0:1, 0:512], cb["onescol"][0:rows, :],
                        zsq[0:rows, nh * 512:(nh + 1) * 512],
                        start=(m == 0), stop=(m == 7)),
                        reads=["zsq", ("c", "onescol")], writes=[pwkey],
                        tag=f"ts_{blk}_{m}_{nh}")
            for nh, pw, pwkey in ((0, self.pS, "pS"), (1, self.pX, "pX")):
                self.op("act", lambda e, nh=nh, pw=pw: e.activation(
                    b["timerow"][:, nh * 512:(nh + 1) * 512], pw[0:1, 0:512],
                    AF.Sqrt, bias=b["c1"][0:1, :]),
                    reads=[pwkey, "c1"], writes=[("time", nh)],
                    war=["borowb"] if blk == 0 else [], tag=f"tt_{blk}_{nh}")
            for m in range(8):
                rows_out = 128 if m < 7 else 127
                for nh in range(2):
                    def mmz(e, m=m, nh=nh, mc=rows_out):
                        ins = None
                        for k in range(8):
                            rows = 128 if k < 7 else 127
                            ins = e.matmul(self.pBig[0:mc, nh * 512:(nh + 1) * 512],
                                           self.wck(16384, 1023, k,
                                                    (m * 128, m * 128 + mc))[0:rows, :],
                                           s_sh[0:rows, k, nh * 512:(nh + 1) * 512],
                                           start=(k == 0), stop=False)
                        return e.matmul(self.pBig[0:mc, nh * 512:(nh + 1) * 512],
                                        b["sw2t0"][:, m * 128:m * 128 + mc],
                                        b["timerow"][:, nh * 512:(nh + 1) * 512],
                                        start=False, stop=True)
                    self.op("pe", mmz,
                            reads=[("w", "sw2ts"), "sw2t0", ("time", nh)] +
                                  [("ssh", mm_, nn) for mm_ in range(8)
                                   for nn in range(2)],
                            writes=[("pBig", nh)], tag=f"z_{blk}_{m}_{nh}")
                    self.op("dve", lambda e, m=m, nh=nh, ro=rows_out: e.tensor_scalar(
                        z_sh[0:ro, m, nh * 512:(nh + 1) * 512],
                        self.pBig[0:ro, nh * 512:(nh + 1) * 512],
                        cb["bs2st"][0:ro, m:m + 1], None, op0=ALU.add),
                        reads=[("pBig", nh), ("c", "bs2st")],
                        writes=[("zsh", m, nh)], tag=f"ze_{blk}_{m}_{nh}")
            for tc in range(8):
                tau = blk * 8 + tc
                sbb = tc % 2
                stg = self.A2[:, 10240 + sbb * 1024:10240 + (sbb + 1) * 1024][:]
                for half in range(2):
                    if half == 0:
                        srcs = [z_sh[:, j, tc * 128:(tc + 1) * 128] for j in range(4)]
                        dst = stg[:, 1:513]
                        pws = 512
                    else:
                        srcs = ([z_sh[:, 4 + j, tc * 128:(tc + 1) * 128]
                                 for j in range(3)] +
                                [z_sh[0:127, 7, tc * 128:(tc + 1) * 128]])
                        dst = stg[:, 513:1024]
                        pws = 511

                    def ev(e, dst=dst, half=half, pws=pws):
                        return e.tensor_copy(dst, self.pT[half][:, 0:pws])
                    self.tbatch(srcs, 128, ev, half,
                                [("zsh", half * 4 + j, nn)
                                 for j in range(4) for nn in range(2)],
                                [("shs", sbb, half)], f"zt{tau}_{half}")
                self.op("act", lambda e, stg=stg: e.activation(
                    sqz[:, 0:1023], stg[:, 1:1024], AF.Square, accum_out=b["ssq"]),
                    reads=[("shs", sbb, 0), ("shs", sbb, 1)], writes=["ssq", "sqz"],
                    tag=f"zs{tau}")
                self.op("act", lambda e, stg=stg: e.activation(
                    stg[:, 0:1], b["ssq"], AF.Sqrt, bias=b["c1"]),
                    reads=["ssq", "c1"], writes=[("shs", sbb, "t")], tag=f"zq{tau}")
                self.op("sp", lambda e, tau=tau, stg=stg: e.dma_start(
                    out=zshd[tau], in_=stg[:, 0:1024]),
                    reads=[("shs", sbb, 0), ("shs", sbb, 1), ("shs", sbb, "t")],
                    writes=[("zsh_dram", tau)], inc=16, tag=f"zst{tau}", sem="zst")

    def emit_experts(self):
        b, cb = self.b, self.cb
        A1, WA = self.A1, self.WA
        wslot = [A1[:, 0:BLOBW], A1[:, BLOBW:2 * BLOBW]]
        in_tm = [A1[:, 28672 + i * 1024:28672 + (i + 1) * 1024] for i in range(4)]
        in_fm = [WA[:, 8192:9216].rearrange("p (k c) -> p k c", k=8),
                 WA[:, 9216:10240].rearrange("p (k c) -> p k c", k=8)]
        a_tm = [WA[:, 0:512], WA[:, 512:1024]]
        s_tm = [WA[:, 1024:1536], WA[:, 1536:2048]]
        out_tm = [WA[:, 2048:3072], WA[:, 3072:4096]]
        sqs = WA[:, 4096:4608]
        sqz = WA[:, 4608:5632]
        s_fm = [WA[:, 5632:6144].rearrange("p (k c) -> p k c", k=4),
                WA[:, 6144:6656].rearrange("p (k c) -> p k c", k=4)]
        eind = self.ein_dram.rearrange("(n p) d -> n p d", p=128)
        eoutd = self.eout_dram.rearrange("(n p) d -> n p d", p=128)
        EIN_KEYS = [("ein", tau, slot) for tau in range(NT) for slot in (0, 1)]
        WB_KEYS = [("wblob", ex, j) for ex in range(NE) for j in range(6)]

        def ld_in(k):
            s4 = k % 4
            self.op("sp", lambda e, k=k, s4=s4: e.dma_start(
                out=in_tm[s4], in_=eind[k]),
                reads=EIN_KEYS + [("eind_z", k)], writes=[("etm", s4)],
                war=self.H2F_KEYS if k < 4 else [], inc=16, tag=f"el{k}",
                sem=f"el{s4 % 2}")

        def gather_w(g):
            sb = g % 2
            war = self.H2F_KEYS if g < 2 else [("ewt", g - 2)]
            self.op("pool", lambda e, sb=sb, g=g: e.indirect_dma_start(
                out=wslot[sb], out_offset=None, in_=self.wblob,
                in_offset=bass.IndirectOffsetOnAxis(
                    ap=b["idxmat"][:, g:g + 1], axis=0),
                bounds_check=self.bcreg, oob_is_err=False),
                reads=["idxmat"] + WB_KEYS, writes=[("ewt", g)], war=war,
                inc=16, tag=f"gw{g}", sem="ewa" if sb == 0 else "ewb")

        def et_t(k):
            sb, s4 = k % 2, k % 4
            f2 = k < 2
            for hf in range(2):
                srcs = [in_tm[s4][:, (hf * 4 + j) * 128:(hf * 4 + j + 1) * 128]
                        for j in range(4)]

                def ev(e, sb=sb, hf=hf):
                    return e.tensor_copy(
                        in_fm[sb][:, hf * 4:hf * 4 + 4, :],
                        self.pT[hf][:, 0:512].rearrange("p (m t) -> p m t", m=4))
                self.tbatch(srcs, 128, ev, hf, [("etm", s4)],
                            [("efm", sb, hf)], f"et{k}_{hf}",
                            war=[("w", "sw3t")] if f2 else [])

        def front(k):
            sb = k % 2
            ws = wslot[(k // 2) % 2]
            pS1 = [self.pA, self.pS][k % 2]
            pS3 = [self.pB, self.pX][k % 2]
            kS1 = ["pA", "pS"][k % 2]
            kS3 = ["pB", "pX"][k % 2]
            f2 = k < 2

            def mk13(pp, ofs, bofs):
                def mm(e, sb=sb, ws=ws, pp=pp, ofs=ofs, bofs=bofs):
                    ins = None
                    for kc in range(8):
                        ins = e.matmul(pp[:, 0:511], in_fm[sb][:, kc, :],
                                       ws[:, ofs + kc * 512:ofs + kc * 512 + 511],
                                       start=(kc == 0), stop=False)
                    return e.matmul(pp[:, 0:511], cb["onesrowb"][:],
                                    ws[0:1, bofs:bofs + 511],
                                    start=False, stop=True)
                return mm
            self.op("pe", mk13(pS1, BO_W1, BO_B1),
                    reads=[("efm", sb, 0), ("efm", sb, 1), ("ewt", k // 2),
                           ("c", "onesrowb")],
                    writes=[kS1], tag=f"e13a_{k}")
            self.op("pe", mk13(pS3, BO_W3, BO_B3),
                    reads=[("efm", sb, 0), ("efm", sb, 1), ("ewt", k // 2),
                           ("c", "onesrowb")],
                    writes=[kS3], tag=f"e13b_{k}")
        def front_act(k):
            sb = k % 2
            pS1 = [self.pA, self.pS][k % 2]
            pS3 = [self.pB, self.pX][k % 2]
            kS1 = ["pA", "pS"][k % 2]
            kS3 = ["pB", "pX"][k % 2]
            f2 = k < 2
            self.op("act", lambda e, sb=sb, pS1=pS1: e.activation(
                a_tm[sb][:, 0:511], pS1[:, 0:511], AF.Silu),
                reads=[kS1], writes=[("ea", sb)],
                war=[("w", "sw1t")] if f2 else [], tag=f"e1e_{k}")
            self.op("dve", lambda e, sb=sb, pS3=pS3: e.tensor_tensor(
                s_tm[sb][:, 0:511], pS3[:, 0:511], a_tm[sb][:, 0:511],
                op=ALU.mult),
                reads=[kS3, ("ea", sb)], writes=[("es", sb)],
                war=([("hit", t) for t in range(NT)] + ["exps", "wsc"])
                if f2 else [], tag=f"e3e_{k}")
            self.op("act", lambda e, sb=sb: e.activation(
                sqs[:, 0:511], s_tm[sb][:, 0:511], AF.Square, accum_out=b["ssq"]),
                reads=[("es", sb)], writes=["ssq", "sqs"],
                war=[("bt", 1, 0), ("bt", 1, 1)] if f2 else [], tag=f"eq_{k}")
            self.op("act", lambda e, sb=sb: e.activation(
                s_tm[sb][:, 511:512], b["ssq"][:], AF.Sqrt, bias=b["c1"][:]),
                reads=["ssq", "c1"], writes=[("es", sb, "t")], tag=f"eT_{k}")

        def back_st(k):
            sb = k % 2
            f2 = k < 2
            for q in range(2):
                srcs = [s_tm[sb][:, (q * 2 + j) * 128:(q * 2 + j + 1) * 128]
                        for j in range(2)]
                rk = [("es", sb), ("es", sb, "t")] if q == 1 else [("es", sb)]

                def ev(e, sb=sb, q=q):
                    return e.tensor_copy(
                        s_fm[sb][:, q * 2:q * 2 + 2, :],
                        self.pT[q][:, 0:256].rearrange("p (m t) -> p m t", m=2))
                self.tbatch(srcs, 128, ev, q, rk, [("esf", sb, q)],
                            f"st_{k}_{q}", war=["kselP"] if f2 else [])

        def back_mm(k):
            sb = k % 2
            ws = wslot[(k // 2) % 2]
            f2 = k < 2
            for half, (c0, cw) in enumerate(((0, 512), (512, 511))):
                def mmz(e, sb=sb, ws=ws, c0=c0, cw=cw):
                    ins = None
                    for kc in range(4):
                        ins = e.matmul(
                            self.pBig[:, c0:c0 + cw], s_fm[sb][:, kc, :],
                            ws[:, BO_W2 + kc * 1024 + c0:
                               BO_W2 + kc * 1024 + c0 + cw],
                            start=(kc == 0), stop=False)
                    return e.matmul(
                        self.pBig[:, c0:c0 + cw], cb["onesrowb"][:],
                        ws[0:1, BO_B2 + c0:BO_B2 + c0 + cw],
                        start=False, stop=True)
                self.op("pe", mmz,
                        reads=[("esf", sb, 0), ("esf", sb, 1), ("ewt", k // 2),
                               ("c", "onesrowb")],
                        writes=[("pBig", half)], tag=f"ez_{k}_{half}")
                self.op("dve", lambda e, sb=sb, c0=c0, cw=cw, half=half:
                        e.tensor_copy(out_tm[sb][:, 1 + c0:1 + c0 + cw],
                                      self.pBig[:, c0:c0 + cw]),
                        reads=[("pBig", half)], writes=[("ez", sb, half)],
                        war=(["sqj", ("bt", 0, 0), ("bt", 0, 1)] if f2 else []),
                        tag=f"eze_{k}_{half}")

        def back_fin(k):
            sb = k % 2
            self.op("act", lambda e, sb=sb: e.activation(
                sqz[:, 0:1023], out_tm[sb][:, 1:1024], AF.Square,
                accum_out=b["ssq"]),
                reads=[("ez", sb, 0), ("ez", sb, 1)], writes=["ssq", "sqz"],
                tag=f"eS_{k}")
            self.op("act", lambda e, sb=sb: e.activation(
                out_tm[sb][:, 0:1], b["ssq"][:], AF.Sqrt, bias=b["c1"][:]),
                reads=["ssq", "c1"], writes=[("ez", sb, "t")], tag=f"eQ_{k}")
            self.op("sp", lambda e, k=k, sb=sb: e.dma_start(
                out=eoutd[k], in_=out_tm[sb]),
                reads=[("ez", sb, 0), ("ez", sb, 1), ("ez", sb, "t")],
                writes=[("eout", k)], inc=16, tag=f"eW{k}", sem="eW")
            if k % 2 == 1 and k // 2 + 2 < NG:
                gather_w(k // 2 + 2)

        ld_in(0)
        ld_in(1)
        ld_in(2)
        gather_w(0)
        gather_w(1)
        for i in range(NTILE + 1):
            if i >= 1:
                back_st(i - 1)
            if i < NTILE:
                if i + 3 < NTILE:
                    ld_in(i + 3)
                et_t(i)
                front(i)
                front_act(i)
            if i >= 1:
                back_mm(i - 1)
                back_fin(i - 1)

    def emit_final(self):
        b = self.b
        A3 = self.A3
        O0 = [A3[:, 0:1024], A3[:, 1024:2048]]
        O1 = [A3[:, 2048:3072], A3[:, 3072:4096]]
        ZS = [A3[:, 4096:5120], A3[:, 5120:6144]]
        sqb = [A3[:, 6144:7168], A3[:, 7168:8192]]
        xin = [self.SA[:, 0:1024], self.SA[:, 1024:2048]]
        yb = [self.SA[:, 2048:3072], self.SA[:, 3072:4096]]
        x2d = self.x2_dram.rearrange("(T p) d -> T p d", p=128)
        zshd = self.zsh_dram.rearrange("(T p) d -> T p d", p=128)
        outd = self.out_d.rearrange("(T p) d -> T p d", p=128)
        EOUT_KEYS = [("eout", k) for k in range(NTILE)]
        ASH_WAR = ([("ash", m, nh) for m in range(8) for nh in range(2)] +
                   [("ssh", m, nh) for m in range(8) for nh in range(2)])

        def chainA(tau):
            bb = tau % 2
            fwar = ([("rall", ex) for ex in range(NE)] + ["expg"] +
                    [("dest0f", t) for t in range(NT)] +
                    [("dest1f", t) for t in range(NT)] +
                    [("elo", t) for t in range(NT)] +
                    [("ehi", t) for t in range(NT)] +
                    ["pofsb", "cntrow", "pcrow", "pcs", "pofsrow", "m1v",
                     "m2v", "eav", "ebv", "ddv", "flagv", "wlov", "whiv",
                     "reqall"]) \
                if tau < 2 else []
            ywar = (["req2all", ("x2t", 1, 0), ("x2t", 1, 1)] if bb == 1 else
                    [("x2t", 0, 0), ("x2t", 0, 1)]) if tau < 2 else []
            st = [
                lambda: self.op("sp", lambda e: e.dma_start(
                    out=xin[bb], in_=x2d[tau]),
                    reads=[("x2_dram", tau)], writes=[("xin", bb)], war=fwar,
                    inc=16, tag=f"fx{tau}", sem=f"fx{bb}"),
                lambda: self.op("sp", lambda e: e.dma_start(
                    out=ZS[bb], in_=zshd[tau]),
                    reads=[("zsh_dram", tau)], writes=[("ZS", bb)],
                    war=ASH_WAR if tau < 2 else [], inc=16, tag=f"fz{tau}",
                    sem=f"fz{bb}"),
            ]
            for slot, Obuf, dkey in ((0, O0, "dest0"), (1, O1, "dest1")):
                st.append(lambda slot=slot, Obuf=Obuf, dkey=dkey:
                          self.op("pool", lambda e: e.indirect_dma_start(
                              out=Obuf[bb], out_offset=None,
                              in_=self.eout_dram,
                              in_offset=bass.IndirectOffsetOnAxis(
                                  ap=b[dkey][:, tau:tau + 1], axis=0),
                              bounds_check=self.bcreg, oob_is_err=False),
                              reads=[dkey] + EOUT_KEYS,
                              writes=[("O", slot, bb)],
                              war=ASH_WAR if tau < 2 else [], inc=16,
                              tag=f"fo{tau}_{slot}", sem=f"fo{slot}{bb}"))
            st.append(lambda: self.op("dve", lambda e: e.tensor_scalar(
                yb[bb], O0[bb], b["w0"][:, tau:tau + 1], None, op0=ALU.mult),
                reads=[("O", 0, bb), ("w0", tau)],
                writes=[("y", bb)], war=ywar, tag=f"f1{tau}"))
            st.append(lambda: self.op("dve", lambda e: e.tensor_scalar(
                yb[bb][:, 0:1], yb[bb][:, 0:1], 1.0, None, op0=ALU.add),
                reads=[("y", bb)], writes=[("y0", bb)], tag=f"f2{tau}"))
            st += self.hyp_s(tau, bb, sqb[bb], "a", y0k=("y0", bb))
            s0 = b[["s0", "s0b"][bb]]
            s0k = ["s0", "s0b"][bb]
            w1s = b[["mean0", "mean1"][bb]]
            w1sk = ["mean0", "mean1"][bb]
            st.append(lambda: self.op("dve", lambda e: e.tensor_tensor(
                w1s, b["w1"][:, tau:tau + 1], s0, op=ALU.mult),
                reads=[("w1", tau), s0k], writes=[w1sk], tag=f"fw{tau}"))
            st.append(lambda: self.op("dve", lambda e: e.scalar_tensor_tensor(
                yb[bb], O1[bb], w1s, yb[bb],
                op0=ALU.mult, op1=ALU.add),
                reads=[("O", 1, bb), w1sk, ("y", bb), ("y0", bb)],
                writes=[("y", bb), ("y0", bb)], tag=f"f3{tau}"))
            st += self.hyp_s(tau, bb, sqb[bb], "b")
            return st

        def chainB(tau):
            bb = tau % 2
            s0 = b[["s0", "s0b"][bb]]
            s0k = ["s0", "s0b"][bb]
            st = []
            st.append(lambda: self.op("dve", lambda e: e.scalar_tensor_tensor(
                yb[bb], ZS[bb], s0, yb[bb], op0=ALU.mult, op1=ALU.add),
                reads=[("ZS", bb), s0k, ("y", bb)], writes=[("y", bb)],
                tag=f"f4{tau}"))
            st += self.hyp_s(tau, bb, sqb[bb], "c")
            st.append(lambda: self.op("dve", lambda e: e.reciprocal(s0, s0),
                    reads=[s0k], writes=[s0k], tag=f"fr{tau}"))
            st.append(lambda: self.op("dve", lambda e: e.scalar_tensor_tensor(
                xin[bb], yb[bb], s0, xin[bb], op0=ALU.mult, op1=ALU.add),
                reads=[("xin", bb), s0k, ("y", bb)], writes=[("xin", bb)],
                tag=f"f5{tau}"))
            st.append(lambda: self.op("sp", lambda e: e.dma_start(
                out=outd[tau], in_=xin[bb]),
                reads=[("xin", bb)], writes=[("out", tau)], inc=16,
                tag=f"fs{tau}", sem="fs"))
            return st

        for i in range(NT + 1):
            A = chainA(i) if i < NT else []
            B = chainB(i - 1) if i >= 1 else []
            na, nb = len(A), len(B)
            ia = ib = 0
            while ia < na or ib < nb:
                if ia < na and (ib >= nb or ia * (nb + 1) <= ib * (na + 1)):
                    A[ia]()
                    ia += 1
                else:
                    B[ib]()
                    ib += 1

    def hyp_s(self, tau, bb, sqb, sub, y0k=None):
        b = self.b
        y = self.SA[:, 2048 + bb * 1024:3072 + bb * 1024]
        yk = ("y", bb)
        s0 = b[["s0", "s0b"][bb]]
        s0k = ["s0", "s0b"][bb]
        sq = b[["ssq", "ssqb"][bb]]
        sqk = ["ssq", "ssqb"][bb]
        sbwar = ([("bt", 1, 0), ("bt", 1, 1)]
                 if (tau < 2 and sub == "a") else [])
        y0read = [y0k] if y0k else []
        return [
            lambda: self.op("act", lambda e: e.activation(
                sqb[:, 0:1023], y[:, 1:1024], AF.Square, accum_out=sq),
                reads=[yk], writes=[sqk, ("sqb", bb)], war=sbwar,
                tag=f"h{sub}{tau}s"),
            lambda: self.op("dve", lambda e: e.tensor_tensor(
                s0, y[:, 0:1], y[:, 0:1], op=ALU.mult),
                reads=[yk] + y0read, writes=[s0k], tag=f"h{sub}{tau}z"),
            lambda: self.op("dve", lambda e: e.tensor_scalar(
                s0, s0, sq, EPS_HYP, op0=ALU.subtract, op1=ALU.max),
                reads=[s0k, sqk], writes=[s0k], tag=f"h{sub}{tau}d"),
            lambda: self.op("act", lambda e: e.activation(
                s0, s0, AF.Sqrt),
                reads=[s0k], writes=[s0k], tag=f"h{sub}{tau}q"),
        ]

    def emit_dump(self):
        dd = self.dump_d
        if not dd:
            return
        b = self.b

        def dram_dump(nm, src, rkeys):
            self.op("pool", lambda e, nm=nm, src=src: e.dma_start(out=dd[nm], in_=src),
                    reads=rkeys, inc=16, tag="D" + nm, sem="dmpp")

        if "h1" in dd:
            dram_dump("h1", self.h1_dram, [("h1_dram", t) for t in range(NT)])
        if "x2" in dd:
            dram_dump("x2", self.x2_dram, [("x2_dram", t) for t in range(NT)])
        if "h2" in dd:
            dram_dump("h2", self.h2_dram, [("h2_dram", t) for t in range(NT)])
        if "zsh" in dd:
            dram_dump("zsh", self.zsh_dram, [("zsh_dram", t) for t in range(NT)])
        if "ein" in dd:
            dram_dump("ein", self.ein_dram,
                      [("ein", t, s) for t in range(NT) for s in (0, 1)])
        if "eout" in dd:
            dram_dump("eout", self.eout_dram,
                      [("eout", k) for k in range(NTILE)])
        sd = {
            "imp": (b["imp_tm"][:], ["imp_tm"]),
            "sel": (b["sel"][:], ["sel"]),
            "rank": (b["rank16"][:], [("rank16",)]),
            "w0": (b["w0"][:], [("w0", t) for t in range(NT)]),
            "w1": (b["w1"][:], [("w1", t) for t in range(NT)]),
            "dest0": (b["dest0f"], [("dest0f", t) for t in range(NT)]),
            "dest1": (b["dest1f"], [("dest1f", t) for t in range(NT)]),
            "expg": (b["expg"], ["expg"]),
        }
        for nm, (src, rkeys) in sd.items():
            if nm in dd:
                self.op("sp", lambda e, nm=nm, src=src: e.dma_start(out=dd[nm], in_=src),
                        reads=rkeys, inc=16, tag="D" + nm, sem="dmps")
        if "expt" in dd:
            self.op("pool", lambda e: e.dma_start(
                out=dd["expt"], in_=self.A2[:, 8192:9216]),
                reads=[("expT", hp, q) for hp in range(2) for q in range(8)],
                inc=16, tag="Dexpt", sem="dmpp")
        if "rdenb" in dd:
            self.op("sp", lambda e: e.dma_start(out=dd["rdenb"], in_=b["rdenb"][0:16, :]),
                    reads=[("rdenb", h) for h in range(H)], inc=16, tag="Drdenb",
                    sem="dmps")
        bd = {
            "d0u": (b["dest0"][:], ["dest0"]),
            "d1u": (b["dest1"][:], ["dest1"]),
            "hit": (b["hit"], [("hit", t) for t in range(NT)]),
            "h1sel": (b["h1selT"], ["h1selT"]),
            "kself": (b["kselF"][:], ["kselF"]),
            "qself": (b["qselF"][:], ["qselF"]),
            "vselt": (b["vselT"][0:16, :], ["vselT"]),
            "ctxb": (b["ctxbT"][0:16, :], [("ctxbT", h) for h in range(H)]),
        }
        for nm, (src, rkeys) in bd.items():
            if nm in dd:
                self.op("pool", lambda e, nm=nm, src=src: e.dma_start(out=dd[nm], in_=src),
                        reads=rkeys, inc=16, tag="D" + nm, sem="dmpp")

    def emit_tail(self):
        final = dict(self.S.count)
        for eng in Sched.ENGINES:
            for tgt, cnt in final.items():
                if tgt != eng and cnt > 0:
                    self.S._wait(eng, tgt, cnt)


VFM_ALL = [("Vfm", n) for n in range(8)]
EXPB_ALL = [("expb", hp, n) for hp in range(2) for n in range(8)]
EXPT_ALL = [("expT", hp, q) for hp in range(2) for q in range(8)]


def build(dumps=(), upto="final", p2tau=NT):
    return MK(dumps, upto, p2tau).build()


def kernel(**inputs):
    prep = host_prep(inputs)
    x = np.asarray(inputs["L0_attn_x"], np.float32)
    nc = build()
    in_maps = []
    for c in range(8):
        m = {"x": np.ascontiguousarray(x[c])}
        m.update(prep)
        in_maps.append(m)
    res = run_bass_kernel_spmd(nc, in_maps, core_ids=list(range(8)))
    return np.stack([np.asarray(r["out"], np.float32) for r in res.results])


# version 39
# speedup vs baseline: 1.0574x; 1.0322x over previous
import contextlib
import numpy as np
import ml_dtypes
import concourse.bass as bass
import concourse.mybir as mybir
from concourse.bass_utils import run_bass_kernel_spmd

F32 = mybir.dt.float32
BF16 = mybir.dt.bfloat16
U32 = mybir.dt.uint32
AF = mybir.ActivationFunctionType
ALU = mybir.AluOpType
AX = mybir.AxisListType

S = 4096
D = 1024
H = 16
NE = 16
NT = 32
NTILE = 96
NG = 48
BLOBW = 14336
BO_W1, BO_W3, BO_W2 = 0, 4096, 8192
BO_B1, BO_B2, BO_B3 = 12288, 12800, 13824
BIGOOB = float(1 << 22)
EPS_HYP = 1e-6
BF = ml_dtypes.bfloat16


class Sched:

    ENGINES = ("pe", "dve", "act", "pool", "sp")

    def __init__(self):
        self.q = {e: [] for e in self.ENGINES}
        self.count = {e: 0 for e in self.ENGINES}
        self.dma_fams = set()
        self.hiw = {}
        self.writer = {}
        self.readers = {}
        self.sem = {}

    def _wait(self, eng, tgt, cnt):
        if eng == "pe" and tgt == "pe":
            return
        if tgt in self.dma_fams:
            cnt = self.count[tgt]
        if cnt <= 0:
            return
        k = (eng, tgt)
        if self.hiw.get(k, 0) >= cnt:
            return
        self.hiw[k] = cnt
        sem = self.sem
        self.q[eng].append(lambda e, t=tgt, c=cnt: e.wait_ge(sem[t], c))

    def op(self, eng, fn, reads=(), writes=(), war=(), inc=1, tag="", sem=None):
        fam = sem or eng
        if sem is not None:
            self.dma_fams.add(fam)
            self.count.setdefault(fam, 0)
        for kk in reads:
            w = self.writer.get(kk)
            if w is not None:
                self._wait(eng, w[0], w[1])
        for kk in tuple(writes) + tuple(war):
            for reng, rcnt in self.readers.get(kk, {}).items():
                self._wait(eng, reng, rcnt)
            w = self.writer.get(kk)
            if w is not None:
                self._wait(eng, w[0], w[1])
        newcnt = self.count[fam] + inc
        semd = self.sem

        def thunk(e, fn=fn, n=inc, tag=tag, fam=fam):
            ins = fn(e)
            if ins is None:
                raise RuntimeError(f"op returned no instruction (tag={tag})")
            ins.then_inc(semd[fam], n)
        self.q[eng].append(thunk)
        self.count[fam] = newcnt
        for kk in writes:
            self.writer[kk] = (fam, newcnt)
            self.readers[kk] = {}
        for kk in reads:
            self.readers.setdefault(kk, {})[fam] = newcnt


def _bf(x):
    return np.asarray(x).astype(BF)


def host_prep(inputs):
    g = {k: np.asarray(v) for k, v in inputs.items()}
    p = {}
    for nm, key in (("wq", "L0_attn_Wq"), ("wk", "L0_attn_Wk"),
                    ("wv", "L0_attn_Wv"), ("wo", "L0_attn_Wo"),
                    ("ws1a", "L0_attn_Ws1"), ("ws2a", "L0_attn_Ws2")):
        p[nm] = np.ascontiguousarray(g[key], np.float32)
    p["gatewt"] = np.ascontiguousarray(g["L0_ffn_gate_w"].T, np.float32)
    p["we1t"] = np.ascontiguousarray(g["L0_ffn_We1"].transpose(0, 2, 1), np.float32)
    p["we3t"] = np.ascontiguousarray(g["L0_ffn_We3"].transpose(0, 2, 1), np.float32)
    we2t = g["L0_ffn_We2"].transpose(0, 2, 1)
    p["we2p"] = np.ascontiguousarray(
        np.concatenate([we2t[:, 1:, :], we2t[:, 0:1, :]], axis=1), np.float32)
    p["be1r"] = np.ascontiguousarray(np.asarray(g["L0_ffn_be1"], np.float32))
    p["be3r"] = np.ascontiguousarray(np.asarray(g["L0_ffn_be3"], np.float32))
    p["be2r"] = np.ascontiguousarray(np.asarray(g["L0_ffn_be2"], np.float32))
    p["sw1t"] = np.ascontiguousarray(g["L0_ffn_Ws1"].T, np.float32)
    p["sw3t"] = np.ascontiguousarray(g["L0_ffn_Ws3"].T, np.float32)
    sw2t = g["L0_ffn_Ws2"].T
    sw2ts = np.zeros((1024, 1023), np.float32)
    sw2ts[0:1023] = sw2t[1:, :]
    p["sw2ts"] = sw2ts
    p["sw2t0"] = np.ascontiguousarray(sw2t[0:1, :], np.float32)

    def colsT(bv, m):
        return np.asarray(bv, np.float32).reshape(m, 128).T

    def pad_to(bv, n):
        out = np.zeros((n,), np.float32)
        out[: bv.shape[0]] = bv
        return out

    p["bqt"] = np.ascontiguousarray(colsT(g["L0_attn_bq"], 8))
    p["bkt"] = np.ascontiguousarray(colsT(g["L0_attn_bk"], 8))
    p["bvt"] = np.ascontiguousarray(colsT(g["L0_attn_bv"], 8))
    p["borow"] = np.ascontiguousarray(
        np.asarray(g["L0_attn_bo"], np.float32).reshape(1, 1024))
    p["bs1at"] = np.ascontiguousarray(colsT(g["L0_attn_bs1"], 4))
    p["bs2ar"] = np.ascontiguousarray(np.broadcast_to(
        np.asarray(g["L0_attn_bs2"], np.float32).reshape(1, 1), (128, 1)))
    p["bs1st"] = np.ascontiguousarray(colsT(pad_to(np.asarray(g["L0_ffn_bs1"], np.float32), 1024), 8))
    p["bs3st"] = np.ascontiguousarray(colsT(pad_to(np.asarray(g["L0_ffn_bs3"], np.float32), 1024), 8))
    p["bs2st"] = np.ascontiguousarray(colsT(pad_to(np.asarray(g["L0_ffn_bs2"], np.float32), 1024), 8))

    p["ident"] = _bf(np.eye(128, dtype=np.float32))
    p["strictu128"] = _bf(np.triu(np.ones((128, 128), np.float32), 1))
    p["strictu32"] = _bf(np.triu(np.ones((32, 32), np.float32), 1))
    p["iotaf"] = np.ascontiguousarray(
        np.arange(S, dtype=np.float32).reshape(NT, 128).T)
    p["iota16"] = np.ascontiguousarray(
        np.broadcast_to(np.arange(16, dtype=np.float32), (128, 16)))
    p["onesrow"] = np.ones((1, 128), np.float32)
    p["onesrowb"] = _bf(np.ones((1, 128), np.float32))
    p["tb80"] = np.ascontiguousarray(
        (np.arange(NG, dtype=np.float32) * 256).reshape(1, NG))
    p["onescol"] = _bf(np.ones((128, 1), np.float32))
    return p


WEIGHT_SPECS = [
    ("wq", [1024, 1024], F32), ("wk", [1024, 1024], F32),
    ("wv", [1024, 1024], F32), ("wo", [1024, 1024], F32),
    ("ws1a", [1024, 512], F32), ("ws2a", [512, 1], F32),
    ("gatewt", [1024, 16], F32),
    ("we1t", [NE, 1024, 511], F32), ("we3t", [NE, 1024, 511], F32),
    ("we2p", [NE, 512, 1023], F32), ("be1r", [NE, 511], F32),
    ("be3r", [NE, 511], F32), ("be2r", [NE, 1023], F32),
    ("sw1t", [1024, 1023], F32), ("sw3t", [1024, 1023], F32),
    ("sw2ts", [1024, 1023], F32), ("sw2t0", [1, 1023], F32),
    ("bqt", [128, 8], F32), ("bkt", [128, 8], F32), ("bvt", [128, 8], F32),
    ("borow", [1, 1024], F32), ("bs1at", [128, 4], F32), ("bs2ar", [128, 1], F32),
    ("bs1st", [128, 8], F32), ("bs3st", [128, 8], F32), ("bs2st", [128, 8], F32),
    ("ident", [128, 128], BF16), ("strictu128", [128, 128], BF16),
    ("strictu32", [32, 32], BF16), ("iotaf", [128, NT], F32),
    ("iota16", [128, 16], F32), ("onesrow", [1, 128], F32), ("onescol", [128, 1], BF16),
    ("onesrowb", [1, 128], BF16), ("tb80", [1, NG], F32),
]

CB_NAMES = {"bqt", "bkt", "bvt", "bs1at", "bs2ar",
            "bs1st", "bs3st", "bs2st", "ident", "strictu128",
            "strictu32", "iotaf", "iota16", "onesrow", "onescol",
            "onesrowb", "tb80"}

DUMP_SHAPES = {
    "h1": [S, D], "imp": [128, NT], "sel": [128, NT], "rank": [128, NT],
    "h1sel": [16, D], "kself": [128, 128], "qself": [128, 128],
    "vselt": [16, D], "ctxb": [16, D], "x2": [S, D], "h2": [S, D],
    "expg": [128, NT * 16], "hit": [128, NT * 16], "dest0": [128, NT],
    "dest1": [128, NT], "w0": [128, NT], "w1": [128, NT],
    "ein": [NTILE * 128, D], "zsh": [S, D], "eout": [NTILE * 128, D],
    "expt": [128, 1024], "rdenb": [16, 16], "d0u": [128, NT], "d1u": [128, NT],
}

PHASES = ["init", "A", "imp", "topk", "selproj", "p1", "p2", "route",
          "scatter", "shared", "experts", "final"]


class MK:
    def __init__(self, dumps=(), upto="final", p2tau=NT):
        self.nc = bass.Bass("TRN2", target_bir_lowering=False)
        self.S = Sched()
        self.dumps = tuple(dumps)
        self.upto = PHASES.index(upto)
        self.p2tau = p2tau

    def op(self, *a, **k):
        self.S.op(*a, **k)

    def build(self):
        nc = self.nc
        self.d_in = {"x": nc.dram_tensor("x", [S, D], F32, kind="ExternalInput").ap()}
        for name, shape, dt in WEIGHT_SPECS:
            self.d_in[name] = nc.dram_tensor(name, shape, dt, kind="ExternalInput").ap()
        self.out_d = nc.dram_tensor("out", [S, D], F32, kind="ExternalOutput").ap()
        self.h1_dram = nc.dram_tensor("h1_dram", [S, D], BF16, kind="Internal").ap()
        self.h2_dram = nc.dram_tensor("h2_dram", [S, D], BF16, kind="Internal").ap()
        self.x2_dram = nc.dram_tensor("x2_dram", [S, D], F32, kind="Internal").ap()
        self.zsh_dram = nc.dram_tensor("zsh_dram", [S, D], BF16, kind="Internal").ap()
        self.impflat_dram = nc.dram_tensor("impflat_dram", [1, S], F32, kind="Internal").ap()
        self.ein_dram = nc.dram_tensor("ein_dram", [NTILE * 128, D], BF16, kind="Internal").ap()
        self.eout_dram = nc.dram_tensor("eout_dram", [NTILE * 128, D], BF16, kind="Internal").ap()
        self.wblob = nc.dram_tensor("wblob", [NE * 128, BLOBW], BF16, kind="Internal").ap()
        self.dump_d = {nm: nc.dram_tensor("dump_" + nm, DUMP_SHAPES[nm], F32,
                                          kind="ExternalOutput").ap()
                       for nm in self.dumps}

        ctx = contextlib.ExitStack()
        sbuf = lambda name, shape, dt: ctx.enter_context(nc.sbuf_tensor(name, shape, dt))
        psum = lambda name, shape, dt: ctx.enter_context(nc.psum_tensor(name, shape, dt))
        with ctx:
            self.A1 = sbuf("A1", [128, 32768], BF16)
            self.A2 = sbuf("A2", [128, 16384], BF16)
            self.A3 = sbuf("A3", [128, 16384], BF16)
            self.WA = sbuf("WA", [128, 24576], BF16)
            self.SA = sbuf("SA", [128, 4096], F32)
            self.ctxf = self.WA[:, 0:1024]
            self.cb = {}
            for name, shape, dt in WEIGHT_SPECS:
                if name in CB_NAMES:
                    self.cb[name] = sbuf("c_" + name, shape, dt)
            b = {}
            B16 = sbuf("B16", [128, 5120], BF16)
            self.B16 = B16
            b["gatew"] = self.A3[:, 0:128]
            b["ws2a"] = self.A3[:, 1540:1544]
            b["PT"] = B16[0:16, 0:4096]
            b["ctxbT"] = B16[0:16, 4096:5120]
            b["vselT"] = B16[32:48, 0:1024]
            b["wTst"] = B16[32:48, 1024:1536]
            b["h1selT"] = B16[32:48, 1536:2560]
            b["borowb"] = B16[32:33, 2560:3584]
            b["onesb32"] = B16[32:33, 3584:3712]
            b["timerow"] = B16[64:65, 0:1024]
            b["sw2t0"] = B16[64:65, 3072:4095]
            b["h1selF"] = sbuf("h1selF", [128, 128], BF16)
            b["qselF"] = sbuf("qselF", [128, 128], BF16)
            b["kselF"] = sbuf("kselF", [128, 128], BF16)
            b["vselF"] = sbuf("vselF", [128, 128], BF16)
            b["denb8"] = sbuf("denb8", [16, 8], F32)
            b["denb8B"] = sbuf("denb8B", [16, 8], F32)
            b["rdenb"] = sbuf("rdenb", [16, 16], F32)
            b["imp_tm"] = sbuf("imp_tm", [128, NT], F32)
            b["topv"] = sbuf("topv", [1, 16], F32)
            b["t16b"] = sbuf("t16b", [128, 1], F32)
            b["sel"] = sbuf("sel", [128, NT], F32)
            b["notsel"] = sbuf("notsel", [128, NT], F32)
            b["hitbf"] = sbuf("hitbf", [128, NT], BF16)
            b["selT"] = sbuf("selT_", [32, 128], BF16)
            b["rfree"] = sbuf("rfree", [128, NT], F32)
            b["rowtot"] = sbuf("rowtot", [128, 1], F32)
            b["rowtotb"] = sbuf("rowtotb", [128, 1], BF16)
            b["offs"] = sbuf("offs", [128, 1], F32)
            b["rank16"] = sbuf("rank16", [128, NT], F32)
            b["Pbuf"] = self.A3[:, 1024:1536]
            b["Pf32"] = sbuf("Pf32", [128, 16], F32)
            b["idx16"] = sbuf("idx16", [16, 1], U32)
            b["idx16f"] = sbuf("idx16f", [16, 1], F32)
            b["expg"] = self.SA[:, 512:1024].bitcast(F32) if False else self.SA[:, 512:1024]
            b["hit"] = self.A2[:, 0:512]
            b["rall"] = self.SA[:, 0:512]
            b["rden"] = sbuf("rden", [128, NT], F32)
            b["w0"] = sbuf("w0", [128, NT], F32)
            b["w1"] = sbuf("w1", [128, NT], F32)
            b["dest0f"] = self.SA[:, 1024:1056]
            b["dest1f"] = self.SA[:, 1056:1088]
            b["dest0"] = sbuf("dest0", [128, NT], U32)
            b["dest1"] = sbuf("dest1", [128, NT], U32)
            b["pofsb"] = self.SA[:, 1184:1200]
            for i, nm in enumerate(("m1v", "m2v", "eav", "ebv", "ddv",
                                    "flagv", "wlov", "whiv")):
                b[nm] = self.SA[:, 1264 + i * 32:1296 + i * 32]
            b["reqall"] = self.SA[:, 0:512]

            b["req2all"] = self.SA[:, 3072:3584]
            b["cntrow"] = self.SA[0:1, 1200:1216]
            b["pcrow"] = self.SA[0:1, 1216:1232]
            b["pcs"] = self.SA[0:1, 1232:1248]
            b["pofsrow"] = self.SA[0:1, 1248:1264]
            b["texp"] = B16[32:33, 3712:3712 + NG]
            b["t80"] = B16[32:33, 3800:3800 + NG]

            b["idxmat"] = sbuf("idxmat", [128, NG], U32)
            b["pc32"] = b["dest0"][0:1, 0:16]
            b["elo"] = self.SA[:, 1120:1152]
            b["ehi"] = self.SA[:, 1152:1184]
            rs16 = sbuf("rs16", [128, 24], F32)
            for i, nm in enumerate(("m1", "m2", "dd", "ea", "eb", "flag", "wlo",
                                    "whi", "r0", "r1", "msk")):
                b[nm] = rs16[:, i:i + 1]
            b["req"] = self.SA[:, 1088:1104]
            b["req2"] = self.SA[:, 1104:1120]
            b["exps"] = self.WA[:, 1536:1792]
            b["wsc"] = self.WA[:, 1792:2048]
            b["kselP"] = self.WA[:, 6144:6400]
            b["qselP"] = self.WA[:, 24320:24576]
            b["aden"] = sbuf("aden", [128, 16], F32)
            b["arden"] = sbuf("arden", [128, 16], F32)
            b["mean"] = rs16[:, 11:12]
            b["ssq"] = rs16[:, 14:15]
            b["rstd"] = rs16[:, 12:13]
            b["s0"] = rs16[:, 13:14]
            b["c1"] = rs16[:, 15:16]
            b["ceps"] = rs16[:, 16:17]
            for i, nm in enumerate(("s0b", "ssqb", "mean0", "mean1",
                                    "rstd0", "rstd1", "ssqA0", "ssqA1")):
                b[nm] = rs16[:, i:i + 1] if i < 7 else rs16[:, 17 + i - 7:18]
            b["hitbfB"] = sbuf("hitbfB", [128, NT], BF16)
            b["selTB"] = sbuf("selTB2", [32, 128], BF16)
            b["rowtotbB"] = sbuf("rowtotbB", [128, 1], BF16)
            b["rfreeB"] = b["imp_tm"]
            b["rowtotB"] = b["r0"]
            b["offsB"] = b["r1"]

            b["zt"] = self.A3[:, 15360:16384]
            b["bt"] = [self.WA[:, 3072:4096], self.WA[:, 4096:5120]]
            self.b = b

            self.pA = psum("pA", [128, 512], F32)
            self.pB = psum("pB", [128, 512], F32)
            self.pBig = psum("pBig", [128, 1024], F32)
            self.pS = psum("pS", [128, 512], F32)
            self.pX = psum("pX", [128, 512], F32)
            self.pT = [psum("pT0", [128, 1024], BF16), psum("pT1", [128, 1024], BF16)]

            FAMS = ["blb", "cbld", "xin0", "xin1", "h1st", "impq", "sx0", "sx1",
                    "sh2", "zst", "el0", "el1", "eW", "fx0", "fx1", "fz0",
                    "fz1", "fs", "wld", "zein", "gsel", "sc", "sc0", "sc1",
                    "fo00", "fo01",
                    "fo10", "fo11", "ewa", "ewb", "st0", "st1", "dmpp", "dmps"]
            for e in list(Sched.ENGINES) + FAMS:
                self.S.sem[e] = ctx.enter_context(nc.semaphore("sem_" + e))

            import os as _os
            skip = set(_os.environ.get("SKIPPH", "").split(","))
            emits = [self.emit_init, self.emit_phaseA, self.emit_imp,
                     self.emit_topk, self.emit_selproj, self.emit_p1,
                     self.emit_p2, self.emit_route, self.emit_scatter,
                     self.emit_shared, self.emit_experts, self.emit_final]
            for i, fn in enumerate(emits):
                if i <= self.upto and PHASES[i] not in skip:
                    fn()
            self.emit_dump()
            self.emit_tail()

            with nc.Block() as block:
                qq = self.S.q

                @block.tensor
                def _(eng):
                    for t in qq["pe"]:
                        t(eng)

                @block.vector
                def _(eng):
                    for t in qq["dve"]:
                        t(eng)

                @block.scalar
                def _(eng):
                    for t in qq["act"]:
                        t(eng)

                @block.gpsimd
                def _(eng):
                    self.bcreg = eng.alloc_register("bcreg")
                    eng.reg_mov(self.bcreg, NTILE * 128 - 1)
                    for t in qq["pool"]:
                        t(eng)

                @block.sync
                def _(eng):
                    for t in qq["sp"]:
                        t(eng)
        return self.nc

    def load_w(self, name, wofs, cols, nk=8, dram_slice=None, key=None, war=(),
               sem="wld", eng="pool"):
        src = dram_slice if dram_slice is not None else self.d_in[name]
        dst = self.WA[:, wofs:wofs + nk * cols].rearrange("p (k f) -> p k f", k=nk)

        def fn(e, src=src, dst=dst):
            return e.dma_start(out=dst, in_=src.rearrange("(k p) f -> p k f", p=128))
        self.op(eng, fn, writes=[key or ("w", name)], war=war, inc=16,
                tag="ldw_" + (name if isinstance(name, str) else str(name)),
                sem=sem)

    def wck(self, wofs, cols, k, sub=None):
        c = self.WA[:, wofs + k * cols: wofs + (k + 1) * cols]
        if sub is not None:
            c = c[:, sub[0]:sub[1]]
        return c

    def tbatch(self, srcs, outw, evac_fn, bank, rkeys, wkeys, tag, war=(),
               base=0):
        pT = self.pT[bank]
        ident = self.cb["ident"]

        def tf(e, srcs=srcs, pT=pT, outw=outw, base=base):
            ins = None
            for i, sap in enumerate(srcs):
                r = sap.shape[0]
                ins = e.transpose(pT[0:sap.shape[1], i * outw:i * outw + r],
                                  sap, ident[base:base + r, base:base + r])
            return ins
        self.op("pe", tf, reads=list(rkeys) + [("c", "ident")],
                writes=[("pt", bank)], tag=tag)
        self.op("dve", evac_fn, reads=[("pt", bank)], writes=list(wkeys),
                war=war, tag=tag + "e")

    H1F_KEYS = [("h1F", tau, half) for tau in range(NT) for half in range(2)]
    H2F_KEYS = [("h2F", tau, half) for tau in range(NT) for half in range(2)]

    def emit_init(self):
        d_in, cb, b = self.d_in, self.cb, self.b
        for name in cb:
            self.op("act", lambda e, n=name: e.dma_start(out=cb[n][:], in_=d_in[n]),
                    writes=[("c", name)], inc=16, tag="ld_" + name, sem="cbld")
        self.op("dve", lambda e: e.memset(b["zt"][:], 0.0), writes=["zt"], tag="zt")
        self.op("dve", lambda e: e.memset(b["c1"], 1.0), writes=["c1"], tag="c1")
        self.op("dve", lambda e: e.memset(b["ceps"], 1e-5), writes=["ceps"], tag="ceps")
        self.op("dve", lambda e: e.memset(b["onesb32"], 1.0), writes=["onesb32"], tag="o32")
        self.op("pool", lambda e: e.dma_start(
            out=b["ws2a"], in_=d_in["ws2a"].rearrange("(m p) o -> p (m o)", p=128),
            allow_slow_non_contiguous=True),
            writes=["ws2a"], inc=16, tag="ld_ws2a", sem="wld")
        self.load_w("ws1a", 16384, 512)
        self.load_w("wk", 0, 1024)
        self.load_w("wv", 8192, 1024)
        self.op("pool", lambda e: e.dma_start(out=b["borowb"], in_=d_in["borow"]),
                writes=["borowb"], inc=16, tag="ld_borow", sem="wld")

    def emit_phaseA(self):
        b, SA, A1 = self.b, self.SA, self.A1
        h1F = A1[:].rearrange("p (m t) -> p m t", m=8)
        xin = [SA[:, 0:1024], SA[:, 1024:2048]]
        h1t = [self.A2[:, i * 1024:(i + 1) * 1024] for i in range(4)]
        sqj = self.A2[:, 4096:5120]
        xd = self.d_in["x"].rearrange("(T p) d -> T p d", p=128)
        h1d = self.h1_dram.rearrange("(T p) d -> T p d", p=128)
        def a_front(tau):
            bb = tau % 2
            return [
                lambda: self.op("sp", lambda e: e.dma_start(
                    out=xin[bb], in_=xd[tau]),
                    writes=[("xin", bb)], inc=16, tag=f"ldx{tau}",
                    sem=f"xin{bb}"),
                lambda: self.ln_head(xin[bb], [("xin", bb)], f"ln1_{tau}",
                                     mk=f"mean{bb}"),
            ]

        def a_back(tau):
            bb = tau % 2
            s4 = tau % 4
            st = [
                lambda: self.ln_tail(
                    xin[bb], [("xin", bb)], h1t[s4], [("h1t", s4)], sqj,
                    f"ln1_{tau}", mk=f"mean{bb}", rk=f"rstd{bb}",
                    qk=f"ssqA{bb}"),
                lambda: self.op("sp", lambda e: e.dma_start(
                    out=h1d[tau], in_=h1t[s4]),
                    reads=[("h1t", s4)], writes=[("h1_dram", tau)], inc=16,
                    tag=f"sth1{tau}", sem="h1st"),
            ]
            for half in range(2):
                bank = (tau * 2 + half) % 2
                srcs = [h1t[s4][:, (half * 4 + j) * 128:(half * 4 + j + 1) * 128]
                        for j in range(4)]

                def ev(e, tau=tau, half=half, bank=bank):
                    return e.tensor_copy(
                        h1F[:, half * 4:half * 4 + 4, tau * 128:(tau + 1) * 128],
                        self.pT[bank][:, 0:512].rearrange("p (m t) -> p m t", m=4))
                st.append(lambda half=half, bank=bank, srcs=srcs, ev=ev, tau=tau:
                          self.tbatch(srcs, 128, ev, bank, [("h1t", s4)],
                                      [("h1F", tau, half)], f"trA{tau}_{half}"))
            return st

        relu = [self.A3[:, 0:512], self.A3[:, 512:1024]]
        cb = self.cb
        pS = self.pS

        def imp_piece(n, m):
            rb = n % 2
            if True:
                bank = [self.pA, self.pB][(n * 4 + m) % 2]
                bkey = ["pA", "pB"][(n * 4 + m) % 2]

                def mm(e, m=m, n=n, bank=bank):
                    ins = None
                    for k in range(8):
                        ins = e.matmul(bank[:, 0:512],
                                       self.wck(16384, 512, k,
                                                (m * 128, (m + 1) * 128)),
                                       h1F[:, k, n * 512:(n + 1) * 512],
                                       start=(k == 0), stop=(k == 7))
                    return ins
                self.op("pe", mm, reads=[("w", "ws1a")] +
                        [("h1F", 4 * n + j, hh)
                         for j in range(4) for hh in range(2)],
                        writes=[bkey], tag=f"is1_{m}_{n}")
                self.op("act", lambda e, m=m, rb=rb, bank=bank: e.activation(
                    relu[rb], bank[:, 0:512], AF.Relu,
                    bias=cb["bs1at"][:, m:m + 1]),
                    reads=[bkey, ("c", "bs1at")], writes=[("relu", rb)],
                    tag=f"ir_{m}_{n}")

                def mm2(e, m=m, n=n, rb=rb):
                    ins = None
                    for j in range(4):
                        tau = n * 4 + j
                        ins = e.matmul(pS[:, tau:tau + 1],
                                       relu[rb][:, j * 128:(j + 1) * 128],
                                       b["ws2a"][:, m:m + 1],
                                       start=(n == 0 and m == 0 and j == 0),
                                       stop=(n == 7 and m == 3 and j == 3))
                    return ins
                self.op("pe", mm2, reads=[("relu", rb), "ws2a"], writes=["pS"],
                        tag=f"is2_{m}_{n}")

        for i in range(NT + 4):
            A = a_front(i) if i < NT else []
            B = a_back(i - 1) if 1 <= i <= NT else []
            na, nb = len(A), len(B)
            ia = ib = 0
            while ia < na or ib < nb:
                if ia < na and (ib >= nb or ia * (nb + 1) <= ib * (na + 1)):
                    A[ia]()
                    ia += 1
                else:
                    B[ib]()
                    ib += 1
            k = i - 4
            if 0 <= k < NT:
                imp_piece(k // 4, k % 4)

    def ln_head(self, x_ap, x_keys, tag, mk="mean"):
        b = self.b
        self.op("dve", lambda e, mk=mk: e.tensor_reduce(
            b[mk], x_ap, axis=AX.X, op=ALU.add, negate=True),
            reads=x_keys, writes=[mk], tag=tag + "m")
        self.op("dve", lambda e, mk=mk: e.tensor_scalar(
            b[mk], b[mk], 1.0 / D, None, op0=ALU.mult),
            reads=[mk], writes=[mk], tag=tag + "s")

    def ln_tail(self, x_ap, x_keys, out_ap, out_keys, sqj_ap, tag,
                mk="mean", rk="rstd", qk="ssq"):
        b = self.b
        self.op("act", lambda e, mk=mk, qk=qk: e.activation(
            sqj_ap, x_ap, AF.Square, bias=b[mk], scale=1.0,
            accum_out=b[qk]),
            reads=x_keys + [mk], writes=[qk, "sqj"], tag=tag + "q")
        self.op("act", lambda e, rk=rk, qk=qk: e.activation(
            b[rk], b[qk], AF.Sqrt, scale=1.0 / D, bias=b["ceps"]),
            reads=[qk, "ceps"], writes=[rk], tag=tag + "r")
        self.op("dve", lambda e, rk=rk: e.reciprocal(b[rk], b[rk]),
                reads=[rk], writes=[rk], tag=tag + "i")
        self.op("dve", lambda e, mk=mk, rk=rk: e.tensor_scalar(
            out_ap, x_ap, b[mk], b[rk], op0=ALU.add, op1=ALU.mult),
            reads=x_keys + [mk, rk], writes=out_keys, tag=tag + "o")

    def ln_tile(self, x_ap, x_keys, out_ap, out_keys, sqj_ap, tag):
        self.ln_head(x_ap, x_keys, tag)
        self.ln_tail(x_ap, x_keys, out_ap, out_keys, sqj_ap, tag)

    def emit_bulk(self):
        d_in, b = self.d_in, self.b
        eind = self.ein_dram.rearrange("(n p) d -> n p d", p=128)
        for i in range(NTILE):
            gate = [("h1F", NT - 1, 0), ("h1F", NT - 1, 1)] if i == 0 else []
            self.op("pool", lambda e, i=i: e.dma_start(out=eind[i], in_=b["zt"][:]),
                    reads=["zt"] + gate, writes=[("eind_z", i)], inc=16,
                    tag=f"zein{i}", sem="zein")
        wb = self.wblob.rearrange("(n p) w -> n p w", p=128)
        for ex in range(NE):
            self.op("pool", lambda e, ex=ex: e.dma_start(
                out=wb[ex][:, BO_W1:BO_W1 + 4096].rearrange(
                    "p (k f) -> p k f", k=8)[:, :, 0:511],
                in_=d_in["we1t"][ex].rearrange("(k p) f -> p k f", p=128)),
                writes=[("wblob", ex, 0)], inc=16, tag=f"wb1_{ex}", sem="blb")
            self.op("pool", lambda e, ex=ex: e.dma_start(
                out=wb[ex][:, BO_W3:BO_W3 + 4096].rearrange(
                    "p (k f) -> p k f", k=8)[:, :, 0:511],
                in_=d_in["we3t"][ex].rearrange("(k p) f -> p k f", p=128)),
                writes=[("wblob", ex, 1)], inc=16, tag=f"wb3_{ex}", sem="blb")
            self.op("pool", lambda e, ex=ex: e.dma_start(
                out=wb[ex][:, BO_W2:BO_W2 + 4096].rearrange(
                    "p (k f) -> p k f", k=4)[:, :, 0:1023],
                in_=d_in["we2p"][ex].rearrange("(k p) f -> p k f", p=128)),
                writes=[("wblob", ex, 2)], inc=16, tag=f"wb2_{ex}", sem="blb")
            self.op("pool", lambda e, ex=ex: e.dma_start(
                out=wb[ex][0:1, BO_B1:BO_B1 + 511],
                in_=d_in["be1r"][ex:ex + 1, :]),
                writes=[("wblob", ex, 3)], inc=16, tag=f"wbb1_{ex}", sem="blb")
            self.op("pool", lambda e, ex=ex: e.dma_start(
                out=wb[ex][0:1, BO_B2:BO_B2 + 1023],
                in_=d_in["be2r"][ex:ex + 1, :]),
                writes=[("wblob", ex, 4)], inc=16, tag=f"wbb2_{ex}", sem="blb")
            self.op("pool", lambda e, ex=ex: e.dma_start(
                out=wb[ex][0:1, BO_B3:BO_B3 + 511],
                in_=d_in["be3r"][ex:ex + 1, :]),
                writes=[("wblob", ex, 5)], inc=16, tag=f"wbb3_{ex}", sem="blb")

    def emit_imp(self):
        b, cb = self.b, self.cb
        pS = self.pS
        self.op("act", lambda e: e.activation(
            b["imp_tm"][:], pS[:, 0:NT], AF.Sigmoid, bias=cb["bs2ar"][:, 0:1]),
            reads=["pS", ("c", "bs2ar")], writes=["imp_tm"], tag="isig")

    def emit_topk(self):
        b, cb, SA = self.b, self.cb, self.SA
        impf = SA[0:1, 0:4096]
        scr = SA[32:33, 0:4096]
        self.op("sp", lambda e: e.dma_start(
            out=self.impflat_dram.rearrange("o (p T) -> (o p) T", p=128),
            in_=b["imp_tm"][:]),
            reads=["imp_tm"], writes=["impflat"], inc=16, tag="impo", sem="impq")
        self.op("sp", lambda e: e.dma_start(out=impf, in_=self.impflat_dram),
                reads=["impflat"], writes=["impf"], war=[("xin", 0), ("xin", 1)],
                inc=16, tag="impi", sem="impq")
        self.op("dve", lambda e: e.max(out=b["topv"][:, 0:8], in_=impf),
                reads=["impf"], writes=["topv"], tag="mx8a")
        self.op("dve", lambda e: e.match_replace(
            out=scr, in_to_replace=b["topv"][:, 0:8], in_values=impf, imm_value=-1.0),
            reads=["impf", "topv"], writes=["scr"], tag="mra")
        self.op("dve", lambda e: e.max(out=b["topv"][:, 8:16], in_=scr),
                reads=["scr", "topv"], writes=["topv"], tag="mx8b")
        self.op("pe", lambda e: e.matmul(self.pX[:, 0:1], cb["onesrow"][0:1, :],
                                         b["topv"][0:1, 15:16], start=True, stop=True),
                reads=[("c", "onesrow"), "topv"], writes=["pX"], tag="bc16")
        self.op("dve", lambda e: e.tensor_copy(b["t16b"][:], self.pX[:, 0:1]),
                reads=["pX"], writes=["t16b"], tag="t16")
        self.op("dve", lambda e: e.tensor_scalar(
            b["sel"][:], b["imp_tm"][:], b["t16b"][:, 0:1], None, op0=ALU.is_ge),
            reads=["imp_tm", "t16b"], writes=["sel"], tag="sel")
        self.op("dve", lambda e: e.tensor_scalar(
            b["notsel"][:], b["sel"][:], -1.0, 1.0, op0=ALU.mult, op1=ALU.add),
            reads=["sel"], writes=["notsel"], tag="nsel")
        self.op("dve", lambda e: e.tensor_copy(b["hitbf"][:], b["sel"][:]),
                reads=["sel"], writes=["hitbf"], tag="hbf")
        self.emit_rank(b["sel"][:], ["sel"], b["rank16"][:], [("rank16",)], "rs")

        first = [True]
        for tau in range(NT):
            self.op("dve", lambda e, tau=tau: e.tensor_scalar(
                b["Pf32"][:], cb["iota16"][:], b["rank16"][:, tau:tau + 1], None,
                op0=ALU.is_equal),
                reads=[("rank16",), ("c", "iota16")], writes=["Pf32"], tag=f"pq{tau}")
            self.op("dve", lambda e, tau=tau: e.tensor_scalar(
                b["Pf32"][:], b["Pf32"][:], b["sel"][:, tau:tau + 1], None,
                op0=ALU.mult),
                reads=["Pf32", "sel"], writes=["Pf32"], tag=f"pm{tau}")
            self.op("dve", lambda e, tau=tau: e.tensor_copy(
                b["Pbuf"][:, tau * 16:(tau + 1) * 16], b["Pf32"][:]),
                reads=["Pf32"], writes=[("Pbuf", tau)], tag=f"pb{tau}")

            def mmi(e, tau=tau, first=first):
                ins = e.matmul(self.pS[0:16, 48:49], b["Pf32"][:],
                               cb["iotaf"][:, tau:tau + 1],
                               start=first[0], stop=(tau == NT - 1))
                first[0] = False
                return ins
            self.op("pe", mmi, reads=["Pf32", ("c", "iotaf")], writes=["pS"],
                    tag=f"px{tau}")
            bank = tau % 2
            self.op("pe", lambda e, tau=tau, bank=bank: e.transpose(
                self.pT[bank][0:16, 0:128], b["Pbuf"][:, tau * 16:(tau + 1) * 16],
                cb["ident"][:]),
                reads=[("Pbuf", tau), ("c", "ident")], writes=[("pt", bank)],
                tag=f"pt{tau}")
            self.op("dve", lambda e, tau=tau, bank=bank: e.tensor_copy(
                b["PT"][:, tau * 128:(tau + 1) * 128], self.pT[bank][0:16, 0:128]),
                reads=[("pt", bank)], writes=[("PT", tau)], tag=f"pe{tau}")
        self.op("dve", lambda e: e.tensor_copy(b["idx16f"][0:16, :], self.pS[0:16, 48:49]),
                reads=["pS"], writes=["idx16f"], tag="i16f")
        self.op("dve", lambda e: e.tensor_copy(b["idx16"][0:16, :], b["idx16f"][0:16, :]),
                reads=["idx16f"], writes=["idx16"], tag="i16")
        hkeys = [("h1_dram", tau) for tau in range(NT)]
        self.op("pool", lambda e: e.indirect_dma_start(
            out=b["h1selT"], out_offset=None, in_=self.h1_dram,
            in_offset=bass.IndirectOffsetOnAxis(ap=b["idx16"][0:16, :], axis=0)),
            reads=["idx16"] + hkeys, writes=["h1selT"], inc=16, tag="gsel",
            sem="gsel")
        self.load_w("wq", 16384, 1024, key=("w", "wqsel"), war=[("w", "ws1a")])
        self.emit_bulk()

    def rank_ops(self, hit_ap, hit_keys, rank_ap, rank_keys, tag, war=(),
                 ps=0):
        b, cb = self.b, self.cb
        sfx = "B" if ps else ""
        hb, sT, rf = "hitbf" + sfx, "selT" + sfx, "rfree" + sfx
        rt, rtb, of = "rowtot" + sfx, "rowtotb" + sfx, "offs" + sfx
        pS_, kS = [(self.pS, "pS"), (self.pA, "pA")][ps]
        pX_, kX = [(self.pX, "pX"), (self.pB, "pB")][ps]
        pT_ = self.pT[ps]
        return [
            lambda: self.op("pe", lambda e: e.transpose(
                pT_[0:32, 0:128], b[hb][:, 0:NT], cb["ident"][:]),
                reads=[hb, ("c", "ident")], writes=[("pt", ps)], tag=tag + "t"),
            lambda: self.op("dve", lambda e: e.tensor_copy(
                b[sT][0:32, :], pT_[0:32, 0:128]),
                reads=[("pt", ps)], writes=[sT], tag=tag + "e"),
            lambda: self.op("pe", lambda e: e.matmul(
                pS_[:, 0:32], b[sT][0:32, :],
                cb["strictu32"][0:32, :], start=True, stop=True),
                reads=[sT, ("c", "strictu32")], writes=[kS], tag=tag + "f"),
            lambda: self.op("dve", lambda e: e.tensor_copy(
                b[rf][:], pS_[:, 0:32]),
                reads=[kS], writes=[rf], tag=tag + "g"),
            lambda: self.op("dve", lambda e: e.tensor_reduce(
                b[rt][:], hit_ap, axis=AX.X, op=ALU.add),
                reads=hit_keys, writes=[rt], tag=tag + "h"),
            lambda: self.op("dve", lambda e: e.tensor_copy(b[rtb][:], b[rt][:]),
                    reads=[rt], writes=[rtb], tag=tag + "i"),
            lambda: self.op("pe", lambda e: e.matmul(
                pX_[:, 0:1], cb["strictu128"][:],
                b[rtb][:, 0:1], start=True, stop=True),
                reads=[rtb, ("c", "strictu128")], writes=[kX], tag=tag + "j"),
            lambda: self.op("dve", lambda e: e.tensor_copy(b[of][:], pX_[:, 0:1]),
                    reads=[kX], writes=[of], tag=tag + "k"),
            lambda: self.op("dve", lambda e: e.tensor_scalar(
                rank_ap, b[rf][:], b[of][:, 0:1], None, op0=ALU.add),
                reads=[rf, of], writes=rank_keys, war=war, tag=tag + "l"),
        ]

    def emit_rank(self, hit_ap, hit_keys, rank_ap, rank_keys, tag, war=()):
        for f in self.rank_ops(hit_ap, hit_keys, rank_ap, rank_keys, tag, war):
            f()

    def emit_selproj(self):
        b, cb = self.b, self.cb
        h1selF = b["h1selF"][:].rearrange("p (m j) -> p m j", m=8)
        vselF = b["vselF"][:].rearrange("p (m j) -> p m j", m=8)
        qselF = b["qselF"][:].rearrange("p (m j) -> p m j", m=8)
        kselF = b["kselF"][:].rearrange("p (m j) -> p m j", m=8)
        for half in range(2):
            bank = half % 2
            srcs = [b["h1selT"][:, (half * 4 + j) * 128:(half * 4 + j + 1) * 128]
                    for j in range(4)]

            def ev(e, half=half, bank=bank):
                return e.tensor_copy(
                    h1selF[:, half * 4:half * 4 + 4, :],
                    self.pT[bank][:, 0:64].rearrange("p (m j) -> p m j", m=4))
            self.tbatch(srcs, 16, ev, bank, ["h1selT"], ["h1selF"], f"st{half}",
                        base=32)
        for wname, wofs, bias, dst, dkey in (
                ("wqsel", 16384, "bqt", qselF, "qselF"),
                ("wk", 0, "bkt", kselF, "kselF"),
                ("wv", 8192, "bvt", vselF, "vselF")):
            for m in range(8):
                bank = [self.pA, self.pB][m % 2]
                bkey = ["pA", "pB"][m % 2]

                def mm(e, wofs=wofs, m=m, bank=bank):
                    ins = None
                    for k in range(8):
                        ins = e.matmul(bank[:, 0:16],
                                       self.wck(wofs, 1024, k, (m * 128, (m + 1) * 128)),
                                       h1selF[:, k, :],
                                       start=(k == 0), stop=(k == 7))
                    return ins
                self.op("pe", mm, reads=[("w", wname), "h1selF"], writes=[bkey],
                        tag=f"sp_{wname}_{m}")
                scl = 0.125 if wname == "wqsel" else None
                op1 = ALU.mult if scl is not None else ALU.bypass
                self.op("dve", lambda e, m=m, dst=dst, bias=bias, bank=bank,
                        scl=scl, op1=op1: e.tensor_scalar(
                            dst[:, m, :], bank[:, 0:16], cb[bias][:, m:m + 1],
                            scl, op0=ALU.add, op1=op1),
                        reads=[bkey, ("c", bias)], writes=[dkey],
                        tag=f"se_{wname}_{m}")
        for half in range(2):
            bank = half % 2
            srcs = [vselF[:, half * 4 + j, :] for j in range(4)]

            def ev(e, half=half, bank=bank):
                return e.tensor_copy(
                    b["vselT"][:, half * 512:(half + 1) * 512],
                    self.pT[bank][0:16, 0:512])
            self.tbatch(srcs, 128, ev, bank, ["vselF"], ["vselT"], f"vt{half}")
        self.emit_selP("qselP", qselF, war=[("w", "wqsel")])

    def emit_selP(self, nm, srcf, war=()):
        b = self.b
        self.op("dve", lambda e, nm=nm: e.memset(b[nm], 0.0), writes=[nm],
                war=war, tag=nm + "z")
        for h in range(H):
            m, lo = h // 2, (h % 2) * 64
            self.op("dve", lambda e, m=m, lo=lo, h=h, nm=nm, srcf=srcf:
                    e.tensor_copy(
                        b[nm][lo:lo + 64, h * 16:(h + 1) * 16],
                        srcf[lo:lo + 64, m, :]),
                    reads=["kselF" if nm == "kselP" else "qselF"],
                    writes=[nm], tag=f"{nm}{h}")

    def emit_p1(self):
        b, cb = self.b, self.cb
        h1F = self.A1[:].rearrange("p (m t) -> p m t", m=8)
        Kg = [self.A3[:, 0:4096], self.A3[:, 4096:8192]]
        Vg = [self.A3[:, 8192:12288].rearrange("p (T d) -> p T d", T=NT),
              self.A3[:, 12288:16384].rearrange("p (T d) -> p T d", T=NT)]
        Vfm = self.A2[:, 0:4096]
        expb = [self.A2[0:16, 4096:8192], self.A2[32:48, 4096:8192]]
        expT = self.A2[:, 8192:9216].rearrange("p (T j) -> p T j", T=NT)
        qselF = b["qselF"][:].rearrange("p (m j) -> p m j", m=8)
        pBig = self.pBig
        first_ctx = [True, True]
        VFM_KEYS = [("Vfm", n) for n in range(8)]

        for g in range(8):
            gb = g % 2
            for wname, wofs, bias, dstf in (("wk", 0, "bkt", None),
                                            ("wv", 8192, "bvt", Vfm)):
                for n in range(8):
                    bank = [self.pA, self.pB][n % 2]
                    bkey = ["pA", "pB"][n % 2]

                    def mm(e, wofs=wofs, g=g, n=n, bank=bank):
                        ins = None
                        for k in range(8):
                            ins = e.matmul(bank[:, 0:512],
                                           self.wck(wofs, 1024, k,
                                                    (g * 128, (g + 1) * 128)),
                                           h1F[:, k, n * 512:(n + 1) * 512],
                                           start=(k == 0), stop=(k == 7))
                        return ins
                    self.op("pe", mm, reads=[("w", wname)] + self.H1F_KEYS,
                            writes=[bkey], tag=f"{wname}p{g}_{n}")
                    if dstf is None:
                        war = [("h1t", i) for i in range(4)] if (g == 0 and n == 0) else []
                        self.op("dve", lambda e, g=g, n=n, gb=gb, bank=bank:
                                e.tensor_scalar(
                                    Kg[gb][:, n * 512:(n + 1) * 512], bank[:, 0:512],
                                    cb["bkt"][:, g:g + 1], None, op0=ALU.add),
                                reads=[bkey, ("c", "bkt")], writes=[("Kg", gb, n)],
                                war=[("relu", 0), ("relu", 1)] if (g < 2 and n == 0) else [],
                                tag=f"kpe{g}_{n}")
                    else:
                        war = ([("h1t", i) for i in range(4)] + ["sqj"]) \
                            if (g == 0 and n == 0) else []
                        self.op("dve", lambda e, g=g, n=n, bank=bank: e.tensor_scalar(
                            Vfm[:, n * 512:(n + 1) * 512], bank[:, 0:512],
                            cb["bvt"][:, g:g + 1], None, op0=ALU.add),
                            reads=[bkey, ("c", "bvt")], writes=[("Vfm", n)],
                            war=war, tag=f"vpe{g}_{n}")
            for q in range(8):
                bank = q % 2
                srcs = [Vfm[:, (q * 4 + j) * 128:(q * 4 + j + 1) * 128]
                        for j in range(4)]

                def ev(e, gb=gb, q=q, bank=bank):
                    return e.tensor_copy(
                        Vg[gb][:, q * 4:q * 4 + 4, :],
                        self.pT[bank][:, 0:512].rearrange("p (T d) -> p T d", T=4))
                self.tbatch(srcs, 128, ev, bank, VFM_KEYS, [("Vg", gb, q)],
                            f"vt{g}_{q}",
                            war=["zt"] if (g == 1 and q >= 6) else [])
            for n in range(8):
                for hp in range(2):
                    h = 2 * g + hp
                    sbk = [self.pS, self.pX][hp]
                    sbkey = ["pS", "pX"][hp]
                    dnm = ["denb8", "denb8B"][hp]

                    def mm(e, g=g, hp=hp, n=n, sbk=sbk, gb=gb, h=h):
                        return e.matmul(sbk[0:16, 0:512],
                                        b["qselP"][:, h * 16:(h + 1) * 16],
                                        Kg[gb][:, n * 512:(n + 1) * 512],
                                        start=True, stop=True)
                    self.op("pe", mm, reads=["qselP", ("Kg", gb, n)],
                            writes=[sbkey], tag=f"sb{h}_{n}")
                    self.op("act", lambda e, hp=hp, n=n, sbk=sbk, dnm=dnm:
                            e.activation(
                        expb[hp][:, n * 512:(n + 1) * 512], sbk[0:16, 0:512], AF.Exp,
                        accum_out=b[dnm][0:16, n:n + 1]),
                        reads=[sbkey], writes=[("expb", hp, n), dnm],
                        war=["sqj"] if (g == 0 and n < 2) else [],
                        tag=f"se{h}_{n}")
            for hp in range(2):
                h = 2 * g + hp
                dnm = ["denb8", "denb8B"][hp]
                self.op("dve", lambda e, h=h, dnm=dnm: e.tensor_reduce(
                    b["rdenb"][0:16, h:h + 1], b[dnm][0:16, :], axis=AX.X,
                    op=ALU.add),
                    reads=[dnm], writes=[("rdt", h)], tag=f"dn{h}")
                self.op("dve", lambda e, h=h: e.reciprocal(
                    b["rdenb"][0:16, h:h + 1], b["rdenb"][0:16, h:h + 1]),
                    reads=[("rdt", h)], writes=[("rdenb", h), ("rdt", h)],
                    tag=f"dr{h}")
            for hp in range(2):
                ebkeys = [("expb", hp, n) for n in range(8)]
                for q in range(8):
                    bank = q % 2
                    srcs = [expb[hp][:, (q * 4 + j) * 128:(q * 4 + j + 1) * 128]
                            for j in range(4)]

                    def ev(e, hp=hp, q=q, bank=bank):
                        return e.tensor_copy(
                            expT[:, q * 4:q * 4 + 4, hp * 16:(hp + 1) * 16],
                            self.pT[bank][:, 0:64].rearrange("p (T j) -> p T j", T=4))
                    self.tbatch(srcs, 16, ev, bank, ebkeys, [("expT", hp, q)],
                                f"bt{g}_{hp}_{q}", base=32 * hp)
            for hp in range(2):
                h = 2 * g + hp
                bk01 = 0 if h < 8 else 1

                def mmc(e, gb=gb, hp=hp, h=h, first_ctx=first_ctx, bk01=bk01):
                    ins = None
                    for kt in range(NT):
                        ins = e.matmul(pBig[0:16, h * 64:(h + 1) * 64],
                                       expT[:, kt, hp * 16:(hp + 1) * 16],
                                       Vg[gb][:, kt, 64 * hp:64 * hp + 64],
                                       start=(first_ctx[bk01] and kt == 0),
                                       stop=(h in (7, 15) and kt == NT - 1))
                    first_ctx[bk01] = False
                    return ins
                self.op("pe", mmc,
                        reads=[("expT", hp, q) for q in range(8)] +
                              [("Vg", gb, q) for q in range(8)],
                        writes=[("pBig", bk01)], tag=f"cb{h}")
        for h in range(H):
            bk01 = 0 if h < 8 else 1
            self.op("dve", lambda e, h=h: e.tensor_scalar(
                b["ctxbT"][:, h * 64:(h + 1) * 64],
                self.pBig[0:16, h * 64:(h + 1) * 64],
                b["rdenb"][0:16, h:h + 1], None, op0=ALU.mult),
                reads=[("pBig", bk01), ("rdenb", h)], writes=[("ctxbT", h)],
                tag=f"ce{h}")

    def load_w_half(self, name, wofs, h, key, war, tag):
        src = self.d_in[name].rearrange("(k p) f -> p k f", p=128)[
            :, :, h * 512:(h + 1) * 512]
        dst = self.WA[:, wofs:wofs + 8192].rearrange("p (k f) -> p k f", k=8)[
            :, :, h * 512:(h + 1) * 512]

        def fn(e, src=src, dst=dst):
            return e.dma_start(out=dst, in_=src)
        self.op("pool", fn, writes=[key], war=war, inc=16, tag=tag, sem="wld")

    def emit_p2(self):
        import os as _os
        b, cb = self.b, self.cb
        h1F = self.A1[:].rearrange("p (m t) -> p m t", m=8)
        h2F = h1F
        self.load_w("wo", 8192, 1024, key=("w", "wo"), war=[("w", "wv")])
        self.load_w("wq", 16384, 1024, key=("w", "wqsel"), war=["qselP"])
        kselF = b["kselF"][:].rearrange("p (m j) -> p m j", m=8)
        self.emit_selP("kselP", kselF, war=[("w", "wk")])

        def qch(m):
            if m < 4:
                return self.A2[:, m * 4096:(m + 1) * 4096]
            return self.A3[:, (m - 4) * 4096:(m - 3) * 4096]

        for m in range(8):
            for n in range(8):
                bank = [self.pA, self.pB][(m * 8 + n) % 2]
                bkey = ["pA", "pB"][(m * 8 + n) % 2]

                def mm(e, m=m, n=n, bank=bank):
                    ins = None
                    for k in range(8):
                        ins = e.matmul(bank[:, 0:512],
                                       self.wck(16384, 1024, k, (m * 128, (m + 1) * 128)),
                                       h1F[:, k, n * 512:(n + 1) * 512],
                                       start=(k == 0), stop=(k == 7))
                    return ins
                self.op("pe", mm, reads=[("w", "wqsel")] + self.H1F_KEYS,
                        writes=[bkey], tag=f"qp{m}_{n}")
                self.op("dve", lambda e, m=m, n=n, bank=bank: e.tensor_scalar(
                    qch(m)[:, n * 512:(n + 1) * 512], bank[:, 0:512],
                    cb["bqt"][:, m:m + 1], 0.125, op0=ALU.add, op1=ALU.mult),
                    reads=[bkey, ("c", "bqt")], writes=[("Qf", m, n)],
                    tag=f"qe{m}_{n}")
        QF_KEYS = [("Qf", m, n) for m in range(8) for n in range(8)]

        kselF = b["kselF"][:].rearrange("p (m j) -> p m j", m=8)
        xin = [self.SA[:, 0:1024], self.SA[:, 1024:2048]]
        x2t = [self.SA[:, 2048:3072], self.SA[:, 3072:4096]]
        xd = self.d_in["x"].rearrange("(T p) d -> T p d", p=128)
        x2d = self.x2_dram.rearrange("(T p) d -> T p d", p=128)
        h2d = self.h2_dram.rearrange("(T p) d -> T p d", p=128)

        def p2F(tau):
            bb = tau % 2
            bank = [self.pA, self.pB][tau % 2]
            bkey = ["pA", "pB"][tau % 2]
            st = [lambda: self.op("sp", lambda e: e.dma_start(
                out=xin[bb], in_=xd[tau]),
                writes=[("xin", bb)], war=["impf", "scr"] if tau < 2 else [],
                inc=16, tag=f"px{tau}", sem=f"xin{bb}")]

            def mma(e, tau=tau, bank=bank):
                ins = None
                for h in range(H):
                    m = h // 2
                    ins = e.matmul(bank[:, h * 16:(h + 1) * 16],
                                   qch(m)[:, tau * 128:(tau + 1) * 128],
                                   b["kselP"][:, h * 16:(h + 1) * 16],
                                   start=(h == 0), stop=(h == H - 1))
                return ins
            st.append(lambda: self.op("pe", mma, reads=QF_KEYS + ["kselP"],
                                      writes=[bkey], tag=f"sa{tau}"))
            st.append(lambda: self.op("act", lambda e: e.activation(
                b["exps"][:], bank[:, 0:256], AF.Exp),
                reads=[bkey], writes=["exps"], tag=f"se{tau}"))
            st.append(lambda: self.op("dve", lambda e: e.tensor_reduce(
                b["aden"][:], b["exps"][:].rearrange("p (h j) -> p h j", h=16),
                axis=AX.X, op=ALU.add),
                reads=["exps"], writes=["aden"], tag=f"sd{tau}"))
            st.append(lambda: self.op("dve", lambda e: e.reciprocal(
                b["arden"][:], b["aden"][:]),
                reads=["aden"], writes=["arden"], tag=f"sr{tau}"))
            st.append(lambda: self.op("dve", lambda e: e.tensor_tensor(
                b["wsc"][:].rearrange("p (h j) -> p h j", h=16),
                b["exps"][:].rearrange("p (h j) -> p h j", h=16),
                b["arden"][:].rearrange("p (h j) -> p h j", h=16, j=1)
                .broadcast_to([128, 16, 16]), op=ALU.mult),
                reads=["exps", "arden"], writes=["wsc"], tag=f"sw{tau}"))
            for q in range(4):
                tb = q % 2
                srcs = [b["wsc"][:, (q * 4 + j) * 16:(q * 4 + j + 1) * 16]
                        for j in range(4)]

                def ev(e, tb=tb):
                    return e.tensor_copy(b["wTst"][:, 0:512],
                                         self.pT[tb][0:16, 0:512])
                st.append(lambda q=q, tb=tb, srcs=srcs, ev=ev:
                          self.tbatch(srcs, 128, ev, tb, ["wsc"], ["wTst"],
                                      f"wt{tau}_{q}"))

                def mmc(e, tau=tau, q=q):
                    ins = None
                    for j in range(4):
                        h = q * 4 + j
                        ins = e.matmul(self.pBig[:, h * 64:(h + 1) * 64],
                                       b["wTst"][:, j * 128:(j + 1) * 128],
                                       b["vselT"][:, h * 64:(h + 1) * 64],
                                       start=(h in (0, 8)), stop=False)
                    return ins
                st.append(lambda mmc=mmc, q=q: self.op(
                    "pe", mmc, reads=["wTst", "vselT"],
                    writes=[("pBig", q // 2)], tag=f"cx{tau}_{q}"))

            def mmp(e, tau=tau):
                e.matmul(self.pBig[:, 0:512],
                         b["PT"][:, tau * 128:(tau + 1) * 128],
                         b["ctxbT"][:, 0:512], start=False, stop=True)
                return e.matmul(self.pBig[:, 512:1024],
                                b["PT"][:, tau * 128:(tau + 1) * 128],
                                b["ctxbT"][:, 512:1024], start=False, stop=True)
            st.append(lambda: self.op("pe", mmp,
                      reads=[("PT", tau)] + [("ctxbT", h) for h in range(H)],
                      writes=[("pBig", 0), ("pBig", 1)], tag=f"cp{tau}"))
            return st

        def p2B(tau):
            bb = tau % 2
            bt = b["bt"][bb]
            sh = self.WA[:, 2048:3072]
            st = []
            for half in range(2):
                st.append(lambda half=half: self.op(
                    "dve", lambda e: e.tensor_scalar(
                        bt[:, half * 512:(half + 1) * 512],
                        self.pBig[:, half * 512:(half + 1) * 512],
                        b["notsel"][:, tau:tau + 1], None, op0=ALU.mult),
                    reads=[("pBig", half), "notsel"],
                    writes=[("bt", bb, half)], tag=f"cv{tau}_{half}"))
            for half in range(2):
                srcs = [bt[:, (half * 4 + j) * 128:(half * 4 + j + 1) * 128]
                        for j in range(4)]

                def ev(e, half=half):
                    return e.tensor_copy(
                        self.ctxf[:, half * 512:(half + 1) * 512],
                        self.pT[half][:, 0:512])
                st.append(lambda half=half, srcs=srcs, ev=ev: self.tbatch(
                    srcs, 128, ev, half, [("bt", bb, 0), ("bt", bb, 1)],
                    [("ctxf", half)], f"cf{tau}_{half}"))
            for half, pw, pwkey in ((0, self.pS, "pS"), (1, self.pX, "pX")):
                def mmw(e, half=half, pw=pw):
                    for k in range(8):
                        e.matmul(pw[:, 0:512],
                                 self.ctxf[:, k * 128:(k + 1) * 128],
                                 self.wck(8192, 1024, k,
                                          (half * 512, (half + 1) * 512)),
                                 start=(k == 0), stop=False)
                    return e.matmul(pw[:, 0:512], b["onesb32"],
                                    b["borowb"][:, half * 512:(half + 1) * 512],
                                    start=False, stop=True)
                st.append(lambda mmw=mmw, pwkey=pwkey: self.op(
                    "pe", mmw, reads=[("ctxf", 0), ("ctxf", 1), ("w", "wo"),
                                      "borowb", "onesb32"],
                    writes=[pwkey], tag=f"wo{tau}_{half}"))
                st.append(lambda half=half, pw=pw, pwkey=pwkey: self.op(
                    "dve", lambda e: e.tensor_tensor(
                        x2t[bb][:, half * 512:(half + 1) * 512], pw[:, 0:512],
                        xin[bb][:, half * 512:(half + 1) * 512], op=ALU.add),
                    reads=[pwkey, ("xin", bb)], writes=[("x2t", bb, half)],
                    tag=f"we{tau}_{half}"))
            st.append(lambda: self.op("sp", lambda e: e.dma_start(
                out=x2d[tau], in_=x2t[bb]),
                reads=[("x2t", bb, 0), ("x2t", bb, 1)],
                writes=[("x2_dram", tau)],
                inc=16, tag=f"sx{tau}", sem=f"sx{bb}"))
            st.append(lambda: self.ln_head(
                x2t[bb], [("x2t", bb, 0), ("x2t", bb, 1)], f"l2_{tau}",
                mk=f"mean{bb}"))
            st.append(lambda: self.ln_tail(
                x2t[bb], [("x2t", bb, 0), ("x2t", bb, 1)], bt,
                [("bt", bb, 0), ("bt", bb, 1)], sh, f"l2_{tau}",
                mk=f"mean{bb}", rk=f"rstd{bb}", qk=f"ssqA{bb}"))
            st.append(lambda: self.op("sp", lambda e: e.dma_start(
                out=h2d[tau], in_=bt),
                reads=[("bt", bb, 0), ("bt", bb, 1)],
                writes=[("h2_dram", tau)],
                inc=16, tag=f"sh{tau}", sem="sh2"))
            for half in range(2):
                srcs = [bt[:, (half * 4 + j) * 128:(half * 4 + j + 1) * 128]
                        for j in range(4)]

                def ev(e, tau=tau, half=half):
                    return e.tensor_copy(
                        h2F[:, half * 4:half * 4 + 4,
                            tau * 128:(tau + 1) * 128],
                        self.pT[half][:, 0:512].rearrange(
                            "p (m t) -> p m t", m=4))
                st.append(lambda half=half, srcs=srcs, ev=ev, tau=tau:
                          self.tbatch(
                              srcs, 128, ev, half,
                              [("bt", bb, 0), ("bt", bb, 1)],
                              [("h2F", tau, half)], f"t2{tau}_{half}",
                              war=[("h1F", tau, hh) for hh in range(2)]))
            return st

        for i in range(self.p2tau + 1):
            A = p2F(i) if i < self.p2tau else []
            B = p2B(i - 1) if i >= 1 else []
            na, nb = len(A), len(B)
            ia = ib = 0
            while ia < na or ib < nb:
                if ia < na and (ib >= nb or ia * (nb + 1) <= ib * (na + 1)):
                    A[ia]()
                    ia += 1
                else:
                    B[ib]()
                    ib += 1

    def emit_route(self):
        b, cb = self.b, self.cb
        d_in = self.d_in
        h2F = self.A1[:].rearrange("p (m t) -> p m t", m=8)
        gw = b["gatew"].rearrange("p (m n) -> p m n", m=8)
        self.op("pool", lambda e: e.dma_start(
            out=gw, in_=d_in["gatewt"].rearrange("(m p) n -> p m n", p=128)),
            writes=["gatew"],
            war=[("Qf", 4, n) for n in range(8)] + [("Kg", 0, n) for n in range(8)],
            inc=16, tag="ld_gatew", sem="wld")

        def mmg(e):
            ins = None
            for tau in range(NT):
                for k in range(8):
                    ins = e.matmul(self.pA[:, tau * 16:(tau + 1) * 16],
                                   h2F[:, k, tau * 128:(tau + 1) * 128],
                                   gw[:, k, :],
                                   start=(tau == 0 and k == 0),
                                   stop=(tau == NT - 1 and k == 7))
            return ins
        self.op("pe", mmg, reads=self.H2F_KEYS + ["gatew"], writes=["pA"], tag="gt")
        self.op("act", lambda e: e.activation(b["expg"], self.pA[:, 0:512], AF.Exp),
                reads=["pA"], writes=["expg"], war=[("xin", 0), ("xin", 1)],
                tag="ge")
        self.op("dve", lambda e: e.tensor_reduce(
            b["rden"][:], b["expg"].rearrange("p (T j) -> p T j", T=NT),
            axis=AX.X, op=ALU.add),
            reads=["expg"], writes=["rden"], tag="gd")
        self.op("dve", lambda e: e.reciprocal(b["rden"][:], b["rden"][:]),
                reads=["rden"], writes=["rden"], tag="gr")
        XW = [("xin", 0), ("xin", 1)]
        r3 = b["reqall"].rearrange("p (T j) -> p T j", T=NT)
        q3 = b["req2all"].rearrange("p (T j) -> p T j", T=NT)
        expg3 = b["expg"].rearrange("p (T j) -> p T j", T=NT)
        iotam = cb["iota16"][:].unsqueeze(1).broadcast_to((128, NT, 16))
        HIT_W = [("hit", tau) for tau in range(NT)]
        self.op("dve", lambda e: e.tensor_reduce(
            b["m1v"], expg3, axis=AX.X, op=ALU.max),
            reads=["expg"], writes=["m1v"], war=XW, tag="v1")
        self.op("dve", lambda e: e.tensor_tensor(
            r3, expg3, b["m1v"].unsqueeze(2).broadcast_to((128, NT, 16)),
            op=ALU.is_ge),
            reads=["expg", "m1v"], writes=["reqall"], war=XW, tag="v2")
        self.op("dve", lambda e: e.scalar_tensor_tensor(
            q3, r3, -1e30, expg3, op0=ALU.mult, op1=ALU.add),
            reads=["reqall", "expg"], writes=["req2all"],
            war=[("x2t", 1, 0), ("x2t", 1, 1)], tag="v3")
        self.op("dve", lambda e: e.tensor_reduce(
            b["m2v"], q3, axis=AX.X, op=ALU.max),
            reads=["req2all"], writes=["m2v"], war=XW, tag="v4")
        self.op("dve", lambda e: e.tensor_tensor(
            q3, q3, b["m2v"].unsqueeze(2).broadcast_to((128, NT, 16)),
            op=ALU.is_ge),
            reads=["req2all", "m2v"], writes=["req2all"], tag="v5")
        self.op("dve", lambda e: e.tensor_add(
            b["hit"][:, :], b["reqall"], b["req2all"]),
            reads=["reqall", "req2all"], writes=HIT_W,
            war=[("Qf", m, n) for m in range(4) for n in range(8)], tag="v6")
        self.op("dve", lambda e: e.tensor_tensor(
            r3, r3, iotam, op=ALU.mult),
            reads=["reqall", ("c", "iota16")], writes=["reqall"], tag="v7")
        self.op("dve", lambda e: e.tensor_reduce(
            b["eav"], r3, axis=AX.X, op=ALU.add),
            reads=["reqall"], writes=["eav"], war=XW, tag="v8")
        self.op("dve", lambda e: e.tensor_tensor(
            q3, q3, iotam, op=ALU.mult),
            reads=["req2all", ("c", "iota16")], writes=["req2all"], tag="v9")
        self.op("dve", lambda e: e.tensor_reduce(
            b["ebv"], q3, axis=AX.X, op=ALU.add),
            reads=["req2all"], writes=["ebv"], war=XW, tag="va")
        self.op("dve", lambda e: e.tensor_tensor(
            b["elo"][:, :], b["eav"], b["ebv"], op=ALU.min),
            reads=["eav", "ebv"], war=XW,
            writes=[("elo", t) for t in range(NT)], tag="vb")
        self.op("dve", lambda e: e.tensor_tensor(
            b["ehi"][:, :], b["eav"], b["ebv"], op=ALU.max),
            reads=["eav", "ebv"],
            writes=[("ehi", t) for t in range(NT)], tag="vc")
        self.op("dve", lambda e: e.tensor_tensor(
            b["flagv"], b["eav"], b["ebv"], op=ALU.is_lt),
            reads=["eav", "ebv"], writes=["flagv"], war=XW, tag="vd")
        self.op("dve", lambda e: e.tensor_tensor(
            b["ddv"], b["m1v"], b["m2v"], op=ALU.subtract),
            reads=["m1v", "m2v"], writes=["ddv"], war=XW, tag="ve")
        self.op("dve", lambda e: e.tensor_tensor(
            b["ddv"], b["ddv"], b["flagv"], op=ALU.mult),
            reads=["ddv", "flagv"], writes=["ddv"], tag="vf")
        self.op("dve", lambda e: e.tensor_tensor(
            b["wlov"], b["m2v"], b["ddv"], op=ALU.add),
            reads=["m2v", "ddv"], writes=["wlov"], war=XW, tag="vg")
        self.op("dve", lambda e: e.tensor_tensor(
            b["whiv"], b["m1v"], b["m2v"], op=ALU.add),
            reads=["m1v", "m2v"], writes=["whiv"], war=XW, tag="vh")
        self.op("dve", lambda e: e.tensor_tensor(
            b["whiv"], b["whiv"], b["wlov"], op=ALU.subtract),
            reads=["whiv", "wlov"], writes=["whiv"], tag="vi")
        self.op("dve", lambda e: e.tensor_tensor(
            b["w0"][:, :], b["wlov"], b["rden"][:], op=ALU.mult),
            reads=["wlov", "rden"], writes=[("w0", t) for t in range(NT)],
            tag="vj")
        self.op("dve", lambda e: e.tensor_tensor(
            b["w1"][:, :], b["whiv"], b["rden"][:], op=ALU.mult),
            reads=["whiv", "rden"], writes=[("w1", t) for t in range(NT)],
            tag="vk")
        HIT_KEYS = [("hit", tau) for tau in range(NT)]
        hit3 = b["hit"].rearrange("p (T j) -> p T j", T=NT)
        rall3 = b["rall"].rearrange("p (T j) -> p T j", T=NT)
        def rank_chain(ex):
            ps = ex % 2
            sfx = "B" if ps else ""
            ops = [lambda: self.op(
                "dve", lambda e, ex=ex, sfx=sfx: e.tensor_copy(
                    b["hitbf" + sfx][:], hit3[:, :, ex]),
                reads=HIT_KEYS, writes=["hitbf" + sfx], tag=f"rh{ex}")]
            ops += self.rank_ops(hit3[:, :, ex], HIT_KEYS, rall3[:, :, ex],
                                 [("rall", ex)], f"re{ex}",
                                 war=[("xin", 0), ("xin", 1), "reqall"]
                                 if ex < 2 else [], ps=ps)
            ops.append(lambda: self.op(
                "pe", lambda e, ex=ex, sfx=sfx: e.matmul(
                    self.pBig[0:1, ex:ex + 1], b["rowtotb" + sfx][:, 0:1],
                    cb["onescol"][:], start=True, stop=True),
                reads=["rowtotb" + sfx, ("c", "onescol")],
                writes=[("pBig", 0)], tag=f"rc{ex}"))
            return ops

        for p0 in range(0, NE, 2):
            for a, bl in zip(rank_chain(p0), rank_chain(p0 + 1)):
                a()
                bl()
        RALL_KEYS = [("rall", ex) for ex in range(NE)]
        self.op("dve", lambda e: e.tensor_copy(b["cntrow"][:], self.pBig[0:1, 0:16]),
                reads=[("pBig", 0)], writes=["cntrow"],
                war=[("xin", 0), ("xin", 1)], tag="pc0")
        self.op("dve", lambda e: e.tensor_scalar(
            b["pcs"][:], b["cntrow"][:], 127.49, 1.0 / 256.0,
            op0=ALU.add, op1=ALU.mult),
            reads=["cntrow"], writes=["pcs"], tag="pc1")
        self.op("dve", lambda e: e.tensor_copy(b["pc32"][:], b["pcs"][:]),
                reads=["pcs"], writes=["pc32"], tag="pc2")
        self.op("dve", lambda e: e.tensor_copy(b["pcs"][:], b["pc32"][:]),
                reads=["pc32"], writes=["pcs"], tag="pc3")
        self.op("dve", lambda e: e.tensor_scalar(
            b["pcrow"][:], b["pcs"][:], 256.0, None, op0=ALU.mult),
            reads=["pcs"], writes=["pcrow"], tag="pc4")
        self.op("dve", lambda e: e.memset(b["pofsrow"][:, 0:1], 0.0),
                writes=["pofsrow"], war=[("xin", 0), ("xin", 1)], tag="po0")
        for ex in range(1, NE):
            self.op("dve", lambda e, ex=ex: e.tensor_tensor(
                b["pofsrow"][:, ex:ex + 1], b["pofsrow"][:, ex - 1:ex],
                b["pcrow"][:, ex - 1:ex], op=ALU.add),
                reads=["pofsrow", "pcrow"], writes=["pofsrow"], tag=f"po{ex}")
        self.op("pe", lambda e: e.matmul(
            self.pS[:, 0:16], cb["onesrow"][:], b["pofsrow"][:],
            start=True, stop=True),
            reads=["pofsrow", ("c", "onesrow")], writes=["pS"], tag="pob")
        self.op("dve", lambda e: e.tensor_copy(b["pofsb"][:], self.pS[:, 0:16]),
                reads=["pS"], writes=["pofsb"],
                war=[("xin", 0), ("xin", 1)], tag="poc")
        for ex in range(NE):
            self.op("dve", lambda e, ex=ex: e.tensor_scalar(
                rall3[:, :, ex], rall3[:, :, ex], b["pofsb"][:, ex:ex + 1], None,
                op0=ALU.add),
                reads=[("rall", ex), "pofsb"], writes=[("rall", ex)], tag=f"pr{ex}")
        self.op("dve", lambda e: e.memset(b["texp"][:], -1.0),
                writes=["texp"], tag="tx_0")
        for ex in range(NE):
            self.op("dve", lambda e, ex=ex: e.tensor_scalar(
                b["t80"][:], cb["tb80"][:], b["pofsrow"][:, ex:ex + 1], None,
                op0=ALU.is_ge),
                reads=[("c", "tb80"), "pofsrow"], writes=["t80"], tag=f"tx{ex}a")
            self.op("dve", lambda e: e.tensor_tensor(
                b["texp"][:], b["texp"][:], b["t80"][:], op=ALU.add),
                reads=["texp", "t80"], writes=["texp"], tag=f"tx{ex}b")
        self.op("pe", lambda e: e.matmul(
            self.pX[:, 0:NG], b["onesb32"][:], b["texp"][:],
            start=True, stop=True),
            reads=["texp", "onesb32"], writes=["pX"], tag="txb")
        self.op("dve", lambda e: e.tensor_scalar(
            self.pX[:, 0:NG], self.pX[:, 0:NG], 128.0,
            cb["iotaf"][:, 0:1], op0=ALU.mult, op1=ALU.add),
            reads=["pX", ("c", "iotaf")], writes=["pX"], tag="txd")
        self.op("dve", lambda e: e.tensor_copy(b["idxmat"][:], self.pX[:, 0:NG]),
                reads=["pX"], writes=["idxmat"], tag="txf")
        for ekey, dkey in (("elo", "dest0f"), ("ehi", "dest1f")):
            self.op("dve", lambda e, ekey=ekey: e.tensor_tensor(
                q3, iotam,
                b[ekey][:, :].unsqueeze(2).broadcast_to((128, NT, 16)),
                op=ALU.is_equal),
                reads=[(ekey, t) for t in range(NT)] + [("c", "iota16")],
                writes=["req2all"], tag=f"dw1{dkey}")
            self.op("dve", lambda e: e.tensor_tensor(
                q3, q3, rall3, op=ALU.mult),
                reads=["req2all"] + RALL_KEYS, writes=["req2all"],
                tag=f"dw2{dkey}")
            self.op("dve", lambda e, dkey=dkey: e.tensor_reduce(
                b[dkey][:, :], q3, axis=AX.X, op=ALU.add),
                reads=["req2all"], writes=[(dkey, t) for t in range(NT)],
                tag=f"dw3{dkey}")
        self.op("dve", lambda e: e.tensor_copy(b["dest0"][:], b["dest0f"]),
                reads=[("dest0f", tau) for tau in range(NT)], writes=["dest0"],
                war=["pc32"], tag="dc0")
        self.op("dve", lambda e: e.tensor_copy(b["dest1"][:], b["dest1f"]),
                reads=[("dest1f", tau) for tau in range(NT)], writes=["dest1"],
                tag="dc1")

    def emit_scatter(self):
        import os as _os_sc
        b = self.b
        self.load_w("sw1t", 0, 1023, key=("w", "sw1t"),
                    war=[("w", "wk"), "kselP", ("ctxf", 0), ("ctxf", 1), "exps",
                         "wsc", "sqj"] +
                        [("bt", i, h) for i in range(2) for h in range(2)])
        self.load_w("sw3t", 8192, 1023, key=("w", "sw3t"), war=[("w", "wo")])
        self.load_w("sw2ts", 16384, 1023, key=("w", "sw2ts"),
                    war=[("w", "wqsel"), "qselP"])
        self.op("pool", lambda e: e.dma_start(
            out=b["sw2t0"], in_=self.d_in["sw2t0"]),
            writes=["sw2t0"], inc=16, tag="ld_sw2t0", sem="wld")
        h2d = self.h2_dram.rearrange("(T p) d -> T p d", p=128)
        zkeys = [("eind_z", i) for i in range(NTILE)]
        sst = [self.A2[:, 12288:13312], self.A2[:, 13312:14336]]
        QF_ALL = [("Qf", m, n) for m in range(8) for n in range(8)]
        for tau in range(NT):
            bb = tau % 2
            stg = sst[bb]
            self.op("sp", lambda e, tau=tau, stg=stg: e.dma_start(
                out=stg, in_=h2d[tau]),
                reads=[("h2_dram", tau)] +
                      ([("w", "sw2ts")] if tau == 0 else []),
                writes=[("sst", bb)],
                war=QF_ALL if tau < 2 else [], inc=16, tag=f"sl{tau}",
                sem=f"st{bb}")
            for slot, dest in ((0, b["dest0"]), (1, b["dest1"])):
                self.op("pool", lambda e, tau=tau, stg=stg, dest=dest:
                        e.indirect_dma_start(
                            out=self.ein_dram,
                            out_offset=bass.IndirectOffsetOnAxis(
                                ap=dest[:, tau:tau + 1], axis=0),
                            in_=stg, in_offset=None,
                            bounds_check=self.bcreg, oob_is_err=False),
                        reads=[("sst", bb),
                               "dest0" if slot == 0 else "dest1"] + zkeys,
                        writes=[("ein", tau, slot)] + (
                            ["sc_chain"] if _os_sc.environ.get("SIMSAFE")
                            else []),
                        inc=16, tag=f"sc{tau}_{slot}", sem=f"sc{tau % 2}")

    def emit_shared(self):
        b, cb = self.b, self.cb
        h2F = self.A1[:].rearrange("p (m t) -> p m t", m=8)
        QF_WAR = [("Qf", m, n) for m in range(8) for n in range(8)]
        a_sh = self.A3[:, 0:8192].rearrange("p (m t) -> p m t", m=8)
        s_sh = self.A3[:, 8192:16384].rearrange("p (m t) -> p m t", m=8)
        z_sh = self.A2[:, 0:8192].rearrange("p (m t) -> p m t", m=8)
        zsq = self.A2[:, 8192:9216]
        sqz = self.A2[:, 9216:10240]
        zshd = self.zsh_dram.rearrange("(T p) d -> T p d", p=128)

        for blk in range(4):
            t0 = blk * 1024
            for m in range(8):
                rows = 128 if m < 7 else 127
                for nh in range(2):
                    bank = [self.pA, self.pB][nh]
                    bkey = ["pA", "pB"][nh]

                    def mm1(e, m=m, nh=nh, bank=bank, t0=t0, rows=rows):
                        ins = None
                        for k in range(8):
                            ins = e.matmul(bank[0:rows, 0:512],
                                           self.wck(0, 1023, k, (m * 128, m * 128 + rows)),
                                           h2F[:, k, t0 + nh * 512:t0 + (nh + 1) * 512],
                                           start=(k == 0), stop=(k == 7))
                        return ins
                    self.op("pe", mm1, reads=[("w", "sw1t")] + self.H2F_KEYS,
                            writes=[bkey], tag=f"s1_{blk}_{m}_{nh}")
                    self.op("act", lambda e, m=m, nh=nh, bank=bank, rows=rows:
                            e.activation(
                                a_sh[0:rows, m, nh * 512:(nh + 1) * 512],
                                bank[0:rows, 0:512], AF.Silu,
                                bias=cb["bs1st"][0:rows, m:m + 1]),
                            reads=[bkey, ("c", "bs1st")], writes=[("ash", m, nh)],
                            war=QF_WAR if (blk == 0 and m == 4 and nh == 0) else [],
                            tag=f"s1e_{blk}_{m}_{nh}")

                    def mm3(e, m=m, nh=nh, bank=bank, t0=t0, rows=rows):
                        ins = None
                        for k in range(8):
                            ins = e.matmul(bank[0:rows, 0:512],
                                           self.wck(8192, 1023, k,
                                                    (m * 128, m * 128 + rows)),
                                           h2F[:, k, t0 + nh * 512:t0 + (nh + 1) * 512],
                                           start=(k == 0), stop=(k == 7))
                        return ins
                    self.op("pe", mm3, reads=[("w", "sw3t")] + self.H2F_KEYS,
                            writes=[bkey], tag=f"s3_{blk}_{m}_{nh}")
                    self.op("dve", lambda e, m=m, nh=nh, bank=bank, rows=rows:
                            e.scalar_tensor_tensor(
                                s_sh[0:rows, m, nh * 512:(nh + 1) * 512],
                                bank[0:rows, 0:512], cb["bs3st"][0:rows, m:m + 1],
                                a_sh[0:rows, m, nh * 512:(nh + 1) * 512],
                                op0=ALU.add, op1=ALU.mult),
                            reads=[bkey, ("c", "bs3st"), ("ash", m, nh)],
                            writes=[("ssh", m, nh)], tag=f"s3e_{blk}_{m}_{nh}")
            for m in range(8):
                rows = 128 if m < 7 else 127
                self.op("act", lambda e, m=m, rows=rows: e.activation(
                    zsq[0:rows, :], s_sh[0:rows, m, :], AF.Square),
                    reads=[("ssh", m, 0), ("ssh", m, 1)], writes=["zsq"],
                    war=QF_WAR if (blk == 0 and m == 0) else [],
                    tag=f"sq_{blk}_{m}")
                for nh, pw, pwkey in ((0, self.pS, "pS"), (1, self.pX, "pX")):
                    self.op("pe", lambda e, m=m, nh=nh, pw=pw, rows=rows: e.matmul(
                        pw[0:1, 0:512], cb["onescol"][0:rows, :],
                        zsq[0:rows, nh * 512:(nh + 1) * 512],
                        start=(m == 0), stop=(m == 7)),
                        reads=["zsq", ("c", "onescol")], writes=[pwkey],
                        tag=f"ts_{blk}_{m}_{nh}")
            for nh, pw, pwkey in ((0, self.pS, "pS"), (1, self.pX, "pX")):
                self.op("act", lambda e, nh=nh, pw=pw: e.activation(
                    b["timerow"][:, nh * 512:(nh + 1) * 512], pw[0:1, 0:512],
                    AF.Sqrt, bias=b["c1"][0:1, :]),
                    reads=[pwkey, "c1"], writes=[("time", nh)],
                    war=["borowb"] if blk == 0 else [], tag=f"tt_{blk}_{nh}")
            for m in range(8):
                rows_out = 128 if m < 7 else 127
                for nh in range(2):
                    def mmz(e, m=m, nh=nh, mc=rows_out):
                        ins = None
                        for k in range(8):
                            rows = 128 if k < 7 else 127
                            ins = e.matmul(self.pBig[0:mc, nh * 512:(nh + 1) * 512],
                                           self.wck(16384, 1023, k,
                                                    (m * 128, m * 128 + mc))[0:rows, :],
                                           s_sh[0:rows, k, nh * 512:(nh + 1) * 512],
                                           start=(k == 0), stop=False)
                        return e.matmul(self.pBig[0:mc, nh * 512:(nh + 1) * 512],
                                        b["sw2t0"][:, m * 128:m * 128 + mc],
                                        b["timerow"][:, nh * 512:(nh + 1) * 512],
                                        start=False, stop=True)
                    self.op("pe", mmz,
                            reads=[("w", "sw2ts"), "sw2t0", ("time", nh)] +
                                  [("ssh", mm_, nn) for mm_ in range(8)
                                   for nn in range(2)],
                            writes=[("pBig", nh)], tag=f"z_{blk}_{m}_{nh}")
                    self.op("dve", lambda e, m=m, nh=nh, ro=rows_out: e.tensor_scalar(
                        z_sh[0:ro, m, nh * 512:(nh + 1) * 512],
                        self.pBig[0:ro, nh * 512:(nh + 1) * 512],
                        cb["bs2st"][0:ro, m:m + 1], None, op0=ALU.add),
                        reads=[("pBig", nh), ("c", "bs2st")],
                        writes=[("zsh", m, nh)], tag=f"ze_{blk}_{m}_{nh}")
            for tc in range(8):
                tau = blk * 8 + tc
                sbb = tc % 2
                stg = self.A2[:, 10240 + sbb * 1024:10240 + (sbb + 1) * 1024][:]
                for half in range(2):
                    if half == 0:
                        srcs = [z_sh[:, j, tc * 128:(tc + 1) * 128] for j in range(4)]
                        dst = stg[:, 1:513]
                        pws = 512
                    else:
                        srcs = ([z_sh[:, 4 + j, tc * 128:(tc + 1) * 128]
                                 for j in range(3)] +
                                [z_sh[0:127, 7, tc * 128:(tc + 1) * 128]])
                        dst = stg[:, 513:1024]
                        pws = 511

                    def ev(e, dst=dst, half=half, pws=pws):
                        return e.tensor_copy(dst, self.pT[half][:, 0:pws])
                    self.tbatch(srcs, 128, ev, half,
                                [("zsh", half * 4 + j, nn)
                                 for j in range(4) for nn in range(2)],
                                [("shs", sbb, half)], f"zt{tau}_{half}")
                self.op("act", lambda e, stg=stg: e.activation(
                    sqz[:, 0:1023], stg[:, 1:1024], AF.Square, accum_out=b["ssq"]),
                    reads=[("shs", sbb, 0), ("shs", sbb, 1)], writes=["ssq", "sqz"],
                    tag=f"zs{tau}")
                self.op("act", lambda e, stg=stg: e.activation(
                    stg[:, 0:1], b["ssq"], AF.Sqrt, bias=b["c1"]),
                    reads=["ssq", "c1"], writes=[("shs", sbb, "t")], tag=f"zq{tau}")
                self.op("sp", lambda e, tau=tau, stg=stg: e.dma_start(
                    out=zshd[tau], in_=stg[:, 0:1024]),
                    reads=[("shs", sbb, 0), ("shs", sbb, 1), ("shs", sbb, "t")],
                    writes=[("zsh_dram", tau)], inc=16, tag=f"zst{tau}", sem="zst")

    def emit_experts(self):
        b, cb = self.b, self.cb
        A1, WA = self.A1, self.WA
        wslot = [A1[:, 0:BLOBW], A1[:, BLOBW:2 * BLOBW]]
        in_tm = [A1[:, 28672 + i * 1024:28672 + (i + 1) * 1024] for i in range(4)]
        in_fm = [WA[:, 8192:9216].rearrange("p (k c) -> p k c", k=8),
                 WA[:, 9216:10240].rearrange("p (k c) -> p k c", k=8)]
        a_tm = [WA[:, 0:512], WA[:, 512:1024]]
        s_tm = [WA[:, 1024:1536], WA[:, 1536:2048]]
        out_tm = [WA[:, 2048:3072], WA[:, 3072:4096]]
        sqs = WA[:, 4096:4608]
        sqz = WA[:, 4608:5632]
        s_fm = [WA[:, 5632:6144].rearrange("p (k c) -> p k c", k=4),
                WA[:, 6144:6656].rearrange("p (k c) -> p k c", k=4)]
        eind = self.ein_dram.rearrange("(n p) d -> n p d", p=128)
        eoutd = self.eout_dram.rearrange("(n p) d -> n p d", p=128)
        EIN_KEYS = [("ein", tau, slot) for tau in range(NT) for slot in (0, 1)]
        WB_KEYS = [("wblob", ex, j) for ex in range(NE) for j in range(6)]

        def ld_in(k):
            s4 = k % 4
            self.op("sp", lambda e, k=k, s4=s4: e.dma_start(
                out=in_tm[s4], in_=eind[k]),
                reads=EIN_KEYS + [("eind_z", k)], writes=[("etm", s4)],
                war=self.H2F_KEYS if k < 4 else [], inc=16, tag=f"el{k}",
                sem=f"el{s4 % 2}")

        def gather_w(g):
            sb = g % 2
            war = self.H2F_KEYS if g < 2 else [("ewt", g - 2)]
            self.op("pool", lambda e, sb=sb, g=g: e.indirect_dma_start(
                out=wslot[sb], out_offset=None, in_=self.wblob,
                in_offset=bass.IndirectOffsetOnAxis(
                    ap=b["idxmat"][:, g:g + 1], axis=0),
                bounds_check=self.bcreg, oob_is_err=False),
                reads=["idxmat"] + WB_KEYS, writes=[("ewt", g)], war=war,
                inc=16, tag=f"gw{g}", sem="ewa" if sb == 0 else "ewb")

        def et_t(k):
            sb, s4 = k % 2, k % 4
            f2 = k < 2
            for hf in range(2):
                srcs = [in_tm[s4][:, (hf * 4 + j) * 128:(hf * 4 + j + 1) * 128]
                        for j in range(4)]

                def ev(e, sb=sb, hf=hf):
                    return e.tensor_copy(
                        in_fm[sb][:, hf * 4:hf * 4 + 4, :],
                        self.pT[hf][:, 0:512].rearrange("p (m t) -> p m t", m=4))
                self.tbatch(srcs, 128, ev, hf, [("etm", s4)],
                            [("efm", sb, hf)], f"et{k}_{hf}",
                            war=[("w", "sw3t")] if f2 else [])

        def front(k):
            sb = k % 2
            ws = wslot[(k // 2) % 2]
            pS1 = [self.pA, self.pS][k % 2]
            pS3 = [self.pB, self.pX][k % 2]
            kS1 = ["pA", "pS"][k % 2]
            kS3 = ["pB", "pX"][k % 2]
            f2 = k < 2

            def mk13(pp, ofs, bofs):
                def mm(e, sb=sb, ws=ws, pp=pp, ofs=ofs, bofs=bofs):
                    ins = None
                    for kc in range(8):
                        ins = e.matmul(pp[:, 0:511], in_fm[sb][:, kc, :],
                                       ws[:, ofs + kc * 512:ofs + kc * 512 + 511],
                                       start=(kc == 0), stop=False)
                    return e.matmul(pp[:, 0:511], cb["onesrowb"][:],
                                    ws[0:1, bofs:bofs + 511],
                                    start=False, stop=True)
                return mm
            self.op("pe", mk13(pS1, BO_W1, BO_B1),
                    reads=[("efm", sb, 0), ("efm", sb, 1), ("ewt", k // 2),
                           ("c", "onesrowb")],
                    writes=[kS1], tag=f"e13a_{k}")
            self.op("pe", mk13(pS3, BO_W3, BO_B3),
                    reads=[("efm", sb, 0), ("efm", sb, 1), ("ewt", k // 2),
                           ("c", "onesrowb")],
                    writes=[kS3], tag=f"e13b_{k}")
        def front_act(k):
            sb = k % 2
            pS1 = [self.pA, self.pS][k % 2]
            pS3 = [self.pB, self.pX][k % 2]
            kS1 = ["pA", "pS"][k % 2]
            kS3 = ["pB", "pX"][k % 2]
            f2 = k < 2
            self.op("act", lambda e, sb=sb, pS1=pS1: e.activation(
                a_tm[sb][:, 0:511], pS1[:, 0:511], AF.Silu),
                reads=[kS1], writes=[("ea", sb)],
                war=[("w", "sw1t")] if f2 else [], tag=f"e1e_{k}")
            self.op("dve", lambda e, sb=sb, pS3=pS3: e.tensor_tensor(
                s_tm[sb][:, 0:511], pS3[:, 0:511], a_tm[sb][:, 0:511],
                op=ALU.mult),
                reads=[kS3, ("ea", sb)], writes=[("es", sb)],
                war=([("hit", t) for t in range(NT)] + ["exps", "wsc"])
                if f2 else [], tag=f"e3e_{k}")
            self.op("act", lambda e, sb=sb: e.activation(
                sqs[:, 0:511], s_tm[sb][:, 0:511], AF.Square, accum_out=b["ssq"]),
                reads=[("es", sb)], writes=["ssq", "sqs"],
                war=[("bt", 1, 0), ("bt", 1, 1)] if f2 else [], tag=f"eq_{k}")
            self.op("act", lambda e, sb=sb: e.activation(
                s_tm[sb][:, 511:512], b["ssq"][:], AF.Sqrt, bias=b["c1"][:]),
                reads=["ssq", "c1"], writes=[("es", sb, "t")], tag=f"eT_{k}")

        def back_st(k):
            sb = k % 2
            f2 = k < 2
            for q in range(2):
                srcs = [s_tm[sb][:, (q * 2 + j) * 128:(q * 2 + j + 1) * 128]
                        for j in range(2)]
                rk = [("es", sb), ("es", sb, "t")] if q == 1 else [("es", sb)]

                def ev(e, sb=sb, q=q):
                    return e.tensor_copy(
                        s_fm[sb][:, q * 2:q * 2 + 2, :],
                        self.pT[q][:, 0:256].rearrange("p (m t) -> p m t", m=2))
                self.tbatch(srcs, 128, ev, q, rk, [("esf", sb, q)],
                            f"st_{k}_{q}", war=["kselP"] if f2 else [])

        def back_mm(k):
            sb = k % 2
            ws = wslot[(k // 2) % 2]
            f2 = k < 2
            for half, (c0, cw) in enumerate(((0, 512), (512, 511))):
                def mmz(e, sb=sb, ws=ws, c0=c0, cw=cw):
                    ins = None
                    for kc in range(4):
                        ins = e.matmul(
                            self.pBig[:, c0:c0 + cw], s_fm[sb][:, kc, :],
                            ws[:, BO_W2 + kc * 1024 + c0:
                               BO_W2 + kc * 1024 + c0 + cw],
                            start=(kc == 0), stop=False)
                    return e.matmul(
                        self.pBig[:, c0:c0 + cw], cb["onesrowb"][:],
                        ws[0:1, BO_B2 + c0:BO_B2 + c0 + cw],
                        start=False, stop=True)
                self.op("pe", mmz,
                        reads=[("esf", sb, 0), ("esf", sb, 1), ("ewt", k // 2),
                               ("c", "onesrowb")],
                        writes=[("pBig", half)], tag=f"ez_{k}_{half}")
                self.op("dve", lambda e, sb=sb, c0=c0, cw=cw, half=half:
                        e.tensor_copy(out_tm[sb][:, 1 + c0:1 + c0 + cw],
                                      self.pBig[:, c0:c0 + cw]),
                        reads=[("pBig", half)], writes=[("ez", sb, half)],
                        war=(["sqj", ("bt", 0, 0), ("bt", 0, 1)] if f2 else []),
                        tag=f"eze_{k}_{half}")

        def back_fin(k):
            sb = k % 2
            self.op("act", lambda e, sb=sb: e.activation(
                sqz[:, 0:1023], out_tm[sb][:, 1:1024], AF.Square,
                accum_out=b["ssq"]),
                reads=[("ez", sb, 0), ("ez", sb, 1)], writes=["ssq", "sqz"],
                tag=f"eS_{k}")
            self.op("act", lambda e, sb=sb: e.activation(
                out_tm[sb][:, 0:1], b["ssq"][:], AF.Sqrt, bias=b["c1"][:]),
                reads=["ssq", "c1"], writes=[("ez", sb, "t")], tag=f"eQ_{k}")
            self.op("sp", lambda e, k=k, sb=sb: e.dma_start(
                out=eoutd[k], in_=out_tm[sb]),
                reads=[("ez", sb, 0), ("ez", sb, 1), ("ez", sb, "t")],
                writes=[("eout", k)], inc=16, tag=f"eW{k}", sem="eW")
            if k % 2 == 1 and k // 2 + 2 < NG:
                gather_w(k // 2 + 2)

        ld_in(0)
        ld_in(1)
        ld_in(2)
        gather_w(0)
        gather_w(1)
        for i in range(NTILE + 1):
            if i >= 1:
                back_st(i - 1)
            if i < NTILE:
                if i + 3 < NTILE:
                    ld_in(i + 3)
                et_t(i)
                front(i)
                front_act(i)
            if i >= 1:
                back_mm(i - 1)
                back_fin(i - 1)

    def emit_final(self):
        b = self.b
        A3 = self.A3
        O0 = [A3[:, 0:1024], A3[:, 1024:2048]]
        O1 = [A3[:, 2048:3072], A3[:, 3072:4096]]
        ZS = [A3[:, 4096:5120], A3[:, 5120:6144]]
        sqb = [A3[:, 6144:7168], A3[:, 7168:8192]]
        xin = [self.SA[:, 0:1024], self.SA[:, 1024:2048]]
        yb = [self.SA[:, 2048:3072], self.SA[:, 3072:4096]]
        x2d = self.x2_dram.rearrange("(T p) d -> T p d", p=128)
        zshd = self.zsh_dram.rearrange("(T p) d -> T p d", p=128)
        outd = self.out_d.rearrange("(T p) d -> T p d", p=128)
        EOUT_KEYS = [("eout", k) for k in range(NTILE)]
        ASH_WAR = ([("ash", m, nh) for m in range(8) for nh in range(2)] +
                   [("ssh", m, nh) for m in range(8) for nh in range(2)])

        def chainA(tau):
            bb = tau % 2
            fwar = ([("rall", ex) for ex in range(NE)] + ["expg"] +
                    [("dest0f", t) for t in range(NT)] +
                    [("dest1f", t) for t in range(NT)] +
                    [("elo", t) for t in range(NT)] +
                    [("ehi", t) for t in range(NT)] +
                    ["pofsb", "cntrow", "pcrow", "pcs", "pofsrow", "m1v",
                     "m2v", "eav", "ebv", "ddv", "flagv", "wlov", "whiv",
                     "reqall"]) \
                if tau < 2 else []
            ywar = (["req2all", ("x2t", 1, 0), ("x2t", 1, 1)] if bb == 1 else
                    [("x2t", 0, 0), ("x2t", 0, 1)]) if tau < 2 else []
            st = [
                lambda: self.op("sp", lambda e: e.dma_start(
                    out=xin[bb], in_=x2d[tau]),
                    reads=[("x2_dram", tau)], writes=[("xin", bb)], war=fwar,
                    inc=16, tag=f"fx{tau}", sem=f"fx{bb}"),
                lambda: self.op("sp", lambda e: e.dma_start(
                    out=ZS[bb], in_=zshd[tau]),
                    reads=[("zsh_dram", tau)], writes=[("ZS", bb)],
                    war=ASH_WAR if tau < 2 else [], inc=16, tag=f"fz{tau}",
                    sem=f"fz{bb}"),
            ]
            for slot, Obuf, dkey in ((0, O0, "dest0"), (1, O1, "dest1")):
                st.append(lambda slot=slot, Obuf=Obuf, dkey=dkey:
                          self.op("pool", lambda e: e.indirect_dma_start(
                              out=Obuf[bb], out_offset=None,
                              in_=self.eout_dram,
                              in_offset=bass.IndirectOffsetOnAxis(
                                  ap=b[dkey][:, tau:tau + 1], axis=0),
                              bounds_check=self.bcreg, oob_is_err=False),
                              reads=[dkey] + EOUT_KEYS,
                              writes=[("O", slot, bb)],
                              war=ASH_WAR if tau < 2 else [], inc=16,
                              tag=f"fo{tau}_{slot}", sem=f"fo{slot}{bb}"))
            st.append(lambda: self.op("dve", lambda e: e.tensor_scalar(
                yb[bb], O0[bb], b["w0"][:, tau:tau + 1], None, op0=ALU.mult),
                reads=[("O", 0, bb), ("w0", tau)],
                writes=[("y", bb)], war=ywar, tag=f"f1{tau}"))
            st.append(lambda: self.op("dve", lambda e: e.tensor_scalar(
                yb[bb][:, 0:1], yb[bb][:, 0:1], 1.0, None, op0=ALU.add),
                reads=[("y", bb)], writes=[("y0", bb)], tag=f"f2{tau}"))
            st += self.hyp_s(tau, bb, sqb[bb], "a", y0k=("y0", bb))
            s0 = b[["s0", "s0b"][bb]]
            s0k = ["s0", "s0b"][bb]
            w1s = b[["mean0", "mean1"][bb]]
            w1sk = ["mean0", "mean1"][bb]
            st.append(lambda: self.op("dve", lambda e: e.tensor_tensor(
                w1s, b["w1"][:, tau:tau + 1], s0, op=ALU.mult),
                reads=[("w1", tau), s0k], writes=[w1sk], tag=f"fw{tau}"))
            st.append(lambda: self.op("dve", lambda e: e.scalar_tensor_tensor(
                yb[bb], O1[bb], w1s, yb[bb],
                op0=ALU.mult, op1=ALU.add),
                reads=[("O", 1, bb), w1sk, ("y", bb), ("y0", bb)],
                writes=[("y", bb), ("y0", bb)], tag=f"f3{tau}"))
            st += self.hyp_s(tau, bb, sqb[bb], "b")
            return st

        def chainB(tau):
            bb = tau % 2
            s0 = b[["s0", "s0b"][bb]]
            s0k = ["s0", "s0b"][bb]
            st = []
            st.append(lambda: self.op("dve", lambda e: e.scalar_tensor_tensor(
                yb[bb], ZS[bb], s0, yb[bb], op0=ALU.mult, op1=ALU.add),
                reads=[("ZS", bb), s0k, ("y", bb)], writes=[("y", bb)],
                tag=f"f4{tau}"))
            st += self.hyp_s(tau, bb, sqb[bb], "c")
            st.append(lambda: self.op("dve", lambda e: e.reciprocal(s0, s0),
                    reads=[s0k], writes=[s0k], tag=f"fr{tau}"))
            st.append(lambda: self.op("dve", lambda e: e.scalar_tensor_tensor(
                xin[bb], yb[bb], s0, xin[bb], op0=ALU.mult, op1=ALU.add),
                reads=[("xin", bb), s0k, ("y", bb)], writes=[("xin", bb)],
                tag=f"f5{tau}"))
            st.append(lambda: self.op("sp", lambda e: e.dma_start(
                out=outd[tau], in_=xin[bb]),
                reads=[("xin", bb)], writes=[("out", tau)], inc=16,
                tag=f"fs{tau}", sem="fs"))
            return st

        for i in range(NT + 1):
            A = chainA(i) if i < NT else []
            B = chainB(i - 1) if i >= 1 else []
            na, nb = len(A), len(B)
            ia = ib = 0
            while ia < na or ib < nb:
                if ia < na and (ib >= nb or ia * (nb + 1) <= ib * (na + 1)):
                    A[ia]()
                    ia += 1
                else:
                    B[ib]()
                    ib += 1

    def hyp_s(self, tau, bb, sqb, sub, y0k=None):
        b = self.b
        y = self.SA[:, 2048 + bb * 1024:3072 + bb * 1024]
        yk = ("y", bb)
        s0 = b[["s0", "s0b"][bb]]
        s0k = ["s0", "s0b"][bb]
        sq = b[["ssq", "ssqb"][bb]]
        sqk = ["ssq", "ssqb"][bb]
        sbwar = ([("bt", 1, 0), ("bt", 1, 1)]
                 if (tau < 2 and sub == "a") else [])
        y0read = [y0k] if y0k else []
        return [
            lambda: self.op("act", lambda e: e.activation(
                sqb[:, 0:1023], y[:, 1:1024], AF.Square, accum_out=sq),
                reads=[yk], writes=[sqk, ("sqb", bb)], war=sbwar,
                tag=f"h{sub}{tau}s"),
            lambda: self.op("dve", lambda e: e.tensor_tensor(
                s0, y[:, 0:1], y[:, 0:1], op=ALU.mult),
                reads=[yk] + y0read, writes=[s0k], tag=f"h{sub}{tau}z"),
            lambda: self.op("dve", lambda e: e.tensor_scalar(
                s0, s0, sq, EPS_HYP, op0=ALU.subtract, op1=ALU.max),
                reads=[s0k, sqk], writes=[s0k], tag=f"h{sub}{tau}d"),
            lambda: self.op("act", lambda e: e.activation(
                s0, s0, AF.Sqrt),
                reads=[s0k], writes=[s0k], tag=f"h{sub}{tau}q"),
        ]

    def emit_dump(self):
        dd = self.dump_d
        if not dd:
            return
        b = self.b

        def dram_dump(nm, src, rkeys):
            self.op("pool", lambda e, nm=nm, src=src: e.dma_start(out=dd[nm], in_=src),
                    reads=rkeys, inc=16, tag="D" + nm, sem="dmpp")

        if "h1" in dd:
            dram_dump("h1", self.h1_dram, [("h1_dram", t) for t in range(NT)])
        if "x2" in dd:
            dram_dump("x2", self.x2_dram, [("x2_dram", t) for t in range(NT)])
        if "h2" in dd:
            dram_dump("h2", self.h2_dram, [("h2_dram", t) for t in range(NT)])
        if "zsh" in dd:
            dram_dump("zsh", self.zsh_dram, [("zsh_dram", t) for t in range(NT)])
        if "ein" in dd:
            dram_dump("ein", self.ein_dram,
                      [("ein", t, s) for t in range(NT) for s in (0, 1)])
        if "eout" in dd:
            dram_dump("eout", self.eout_dram,
                      [("eout", k) for k in range(NTILE)])
        sd = {
            "imp": (b["imp_tm"][:], ["imp_tm"]),
            "sel": (b["sel"][:], ["sel"]),
            "rank": (b["rank16"][:], [("rank16",)]),
            "w0": (b["w0"][:], [("w0", t) for t in range(NT)]),
            "w1": (b["w1"][:], [("w1", t) for t in range(NT)]),
            "dest0": (b["dest0f"], [("dest0f", t) for t in range(NT)]),
            "dest1": (b["dest1f"], [("dest1f", t) for t in range(NT)]),
            "expg": (b["expg"], ["expg"]),
        }
        for nm, (src, rkeys) in sd.items():
            if nm in dd:
                self.op("sp", lambda e, nm=nm, src=src: e.dma_start(out=dd[nm], in_=src),
                        reads=rkeys, inc=16, tag="D" + nm, sem="dmps")
        if "expt" in dd:
            self.op("pool", lambda e: e.dma_start(
                out=dd["expt"], in_=self.A2[:, 8192:9216]),
                reads=[("expT", hp, q) for hp in range(2) for q in range(8)],
                inc=16, tag="Dexpt", sem="dmpp")
        if "rdenb" in dd:
            self.op("sp", lambda e: e.dma_start(out=dd["rdenb"], in_=b["rdenb"][0:16, :]),
                    reads=[("rdenb", h) for h in range(H)], inc=16, tag="Drdenb",
                    sem="dmps")
        bd = {
            "d0u": (b["dest0"][:], ["dest0"]),
            "d1u": (b["dest1"][:], ["dest1"]),
            "hit": (b["hit"], [("hit", t) for t in range(NT)]),
            "h1sel": (b["h1selT"], ["h1selT"]),
            "kself": (b["kselF"][:], ["kselF"]),
            "qself": (b["qselF"][:], ["qselF"]),
            "vselt": (b["vselT"][0:16, :], ["vselT"]),
            "ctxb": (b["ctxbT"][0:16, :], [("ctxbT", h) for h in range(H)]),
        }
        for nm, (src, rkeys) in bd.items():
            if nm in dd:
                self.op("pool", lambda e, nm=nm, src=src: e.dma_start(out=dd[nm], in_=src),
                        reads=rkeys, inc=16, tag="D" + nm, sem="dmpp")

    def emit_tail(self):
        final = dict(self.S.count)
        for eng in Sched.ENGINES:
            for tgt, cnt in final.items():
                if tgt != eng and cnt > 0:
                    self.S._wait(eng, tgt, cnt)


VFM_ALL = [("Vfm", n) for n in range(8)]
EXPB_ALL = [("expb", hp, n) for hp in range(2) for n in range(8)]
EXPT_ALL = [("expT", hp, q) for hp in range(2) for q in range(8)]


def build(dumps=(), upto="final", p2tau=NT):
    return MK(dumps, upto, p2tau).build()


def kernel(**inputs):
    prep = host_prep(inputs)
    x = np.asarray(inputs["L0_attn_x"], np.float32)
    nc = build()
    in_maps = []
    for c in range(8):
        m = {"x": np.ascontiguousarray(x[c])}
        m.update(prep)
        in_maps.append(m)
    res = run_bass_kernel_spmd(nc, in_maps, core_ids=list(range(8)))
    return np.stack([np.asarray(r["out"], np.float32) for r in res.results])


# version 41
# speedup vs baseline: 1.0703x; 1.0122x over previous
import contextlib
import numpy as np
import ml_dtypes
import concourse.bass as bass
import concourse.mybir as mybir
from concourse.bass_utils import run_bass_kernel_spmd

F32 = mybir.dt.float32
BF16 = mybir.dt.bfloat16
U32 = mybir.dt.uint32
AF = mybir.ActivationFunctionType
ALU = mybir.AluOpType
AX = mybir.AxisListType

S = 4096
D = 1024
H = 16
NE = 16
NT = 32
NTILE = 96
NG = 48
BLOBW = 14336
BO_W1, BO_W3, BO_W2 = 0, 4096, 8192
BO_B1, BO_B2, BO_B3 = 12288, 12800, 13824
BIGOOB = float(1 << 22)
EPS_HYP = 1e-6
BF = ml_dtypes.bfloat16


class Sched:

    ENGINES = ("pe", "dve", "act", "pool", "sp")

    def __init__(self):
        self.q = {e: [] for e in self.ENGINES}
        self.count = {e: 0 for e in self.ENGINES}
        self.dma_fams = set()
        self.hiw = {}
        self.writer = {}
        self.readers = {}
        self.sem = {}

    def _wait(self, eng, tgt, cnt):
        if eng == "pe" and tgt == "pe":
            return
        if tgt in self.dma_fams:
            cnt = self.count[tgt]
        if cnt <= 0:
            return
        k = (eng, tgt)
        if self.hiw.get(k, 0) >= cnt:
            return
        self.hiw[k] = cnt
        sem = self.sem
        self.q[eng].append(lambda e, t=tgt, c=cnt: e.wait_ge(sem[t], c))

    def op(self, eng, fn, reads=(), writes=(), war=(), inc=1, tag="", sem=None):
        fam = sem or eng
        if sem is not None:
            self.dma_fams.add(fam)
            self.count.setdefault(fam, 0)
        for kk in reads:
            w = self.writer.get(kk)
            if w is not None:
                self._wait(eng, w[0], w[1])
        for kk in tuple(writes) + tuple(war):
            for reng, rcnt in self.readers.get(kk, {}).items():
                self._wait(eng, reng, rcnt)
            w = self.writer.get(kk)
            if w is not None:
                self._wait(eng, w[0], w[1])
        newcnt = self.count[fam] + inc
        semd = self.sem

        def thunk(e, fn=fn, n=inc, tag=tag, fam=fam):
            ins = fn(e)
            if ins is None:
                raise RuntimeError(f"op returned no instruction (tag={tag})")
            ins.then_inc(semd[fam], n)
        self.q[eng].append(thunk)
        self.count[fam] = newcnt
        for kk in writes:
            self.writer[kk] = (fam, newcnt)
            self.readers[kk] = {}
        for kk in reads:
            self.readers.setdefault(kk, {})[fam] = newcnt


def _bf(x):
    return np.asarray(x).astype(BF)


def host_prep(inputs):
    g = {k: np.asarray(v) for k, v in inputs.items()}
    p = {}
    for nm, key in (("wq", "L0_attn_Wq"), ("wk", "L0_attn_Wk"),
                    ("wv", "L0_attn_Wv"), ("wo", "L0_attn_Wo"),
                    ("ws1a", "L0_attn_Ws1"), ("ws2a", "L0_attn_Ws2")):
        p[nm] = np.ascontiguousarray(g[key], np.float32)
    p["gatewt"] = np.ascontiguousarray(g["L0_ffn_gate_w"].T, np.float32)
    p["we1t"] = np.ascontiguousarray(g["L0_ffn_We1"].transpose(0, 2, 1), np.float32)
    p["we3t"] = np.ascontiguousarray(g["L0_ffn_We3"].transpose(0, 2, 1), np.float32)
    we2t = g["L0_ffn_We2"].transpose(0, 2, 1)
    p["we2p"] = np.ascontiguousarray(
        np.concatenate([we2t[:, 1:, :], we2t[:, 0:1, :]], axis=1), np.float32)
    p["be1r"] = np.ascontiguousarray(np.asarray(g["L0_ffn_be1"], np.float32))
    p["be3r"] = np.ascontiguousarray(np.asarray(g["L0_ffn_be3"], np.float32))
    p["be2r"] = np.ascontiguousarray(np.asarray(g["L0_ffn_be2"], np.float32))
    p["sw1t"] = np.ascontiguousarray(g["L0_ffn_Ws1"].T, np.float32)
    p["sw3t"] = np.ascontiguousarray(g["L0_ffn_Ws3"].T, np.float32)
    sw2t = g["L0_ffn_Ws2"].T
    sw2ts = np.zeros((1024, 1023), np.float32)
    sw2ts[0:1023] = sw2t[1:, :]
    p["sw2ts"] = sw2ts
    p["sw2t0"] = np.ascontiguousarray(sw2t[0:1, :], np.float32)

    def colsT(bv, m):
        return np.asarray(bv, np.float32).reshape(m, 128).T

    def pad_to(bv, n):
        out = np.zeros((n,), np.float32)
        out[: bv.shape[0]] = bv
        return out

    p["bqt"] = np.ascontiguousarray(colsT(g["L0_attn_bq"], 8))
    p["bkt"] = np.ascontiguousarray(colsT(g["L0_attn_bk"], 8))
    p["bvt"] = np.ascontiguousarray(colsT(g["L0_attn_bv"], 8))
    p["borow"] = np.ascontiguousarray(
        np.asarray(g["L0_attn_bo"], np.float32).reshape(1, 1024))
    p["bs1at"] = np.ascontiguousarray(colsT(g["L0_attn_bs1"], 4))
    p["bs2ar"] = np.ascontiguousarray(np.broadcast_to(
        np.asarray(g["L0_attn_bs2"], np.float32).reshape(1, 1), (128, 1)))
    p["bs1st"] = np.ascontiguousarray(colsT(pad_to(np.asarray(g["L0_ffn_bs1"], np.float32), 1024), 8))
    p["bs3st"] = np.ascontiguousarray(colsT(pad_to(np.asarray(g["L0_ffn_bs3"], np.float32), 1024), 8))
    p["bs2st"] = np.ascontiguousarray(colsT(pad_to(np.asarray(g["L0_ffn_bs2"], np.float32), 1024), 8))

    p["ident"] = _bf(np.eye(128, dtype=np.float32))
    p["strictu128"] = _bf(np.triu(np.ones((128, 128), np.float32), 1))
    p["strictu32"] = _bf(np.triu(np.ones((32, 32), np.float32), 1))
    p["iotaf"] = np.ascontiguousarray(
        np.arange(S, dtype=np.float32).reshape(NT, 128).T)
    p["iota16"] = np.ascontiguousarray(
        np.broadcast_to(np.arange(16, dtype=np.float32), (128, 16)))
    p["onesrow"] = np.ones((1, 128), np.float32)
    p["onesrowb"] = _bf(np.ones((1, 128), np.float32))
    p["tb80"] = np.ascontiguousarray(
        (np.arange(NG, dtype=np.float32) * 256).reshape(1, NG))
    p["onescol"] = _bf(np.ones((128, 1), np.float32))
    return p


WEIGHT_SPECS = [
    ("wq", [1024, 1024], F32), ("wk", [1024, 1024], F32),
    ("wv", [1024, 1024], F32), ("wo", [1024, 1024], F32),
    ("ws1a", [1024, 512], F32), ("ws2a", [512, 1], F32),
    ("gatewt", [1024, 16], F32),
    ("we1t", [NE, 1024, 511], F32), ("we3t", [NE, 1024, 511], F32),
    ("we2p", [NE, 512, 1023], F32), ("be1r", [NE, 511], F32),
    ("be3r", [NE, 511], F32), ("be2r", [NE, 1023], F32),
    ("sw1t", [1024, 1023], F32), ("sw3t", [1024, 1023], F32),
    ("sw2ts", [1024, 1023], F32), ("sw2t0", [1, 1023], F32),
    ("bqt", [128, 8], F32), ("bkt", [128, 8], F32), ("bvt", [128, 8], F32),
    ("borow", [1, 1024], F32), ("bs1at", [128, 4], F32), ("bs2ar", [128, 1], F32),
    ("bs1st", [128, 8], F32), ("bs3st", [128, 8], F32), ("bs2st", [128, 8], F32),
    ("ident", [128, 128], BF16), ("strictu128", [128, 128], BF16),
    ("strictu32", [32, 32], BF16), ("iotaf", [128, NT], F32),
    ("iota16", [128, 16], F32), ("onesrow", [1, 128], F32), ("onescol", [128, 1], BF16),
    ("onesrowb", [1, 128], BF16), ("tb80", [1, NG], F32),
]

CB_NAMES = {"bqt", "bkt", "bvt", "bs1at", "bs2ar",
            "bs1st", "bs3st", "bs2st", "ident", "strictu128",
            "strictu32", "iotaf", "iota16", "onesrow", "onescol",
            "onesrowb", "tb80"}

DUMP_SHAPES = {
    "h1": [S, D], "imp": [128, NT], "sel": [128, NT], "rank": [128, NT],
    "h1sel": [16, D], "kself": [128, 128], "qself": [128, 128],
    "vselt": [16, D], "ctxb": [16, D], "x2": [S, D], "h2": [S, D],
    "expg": [128, NT * 16], "hit": [128, NT * 16], "dest0": [128, NT],
    "dest1": [128, NT], "w0": [128, NT], "w1": [128, NT],
    "ein": [NTILE * 128, D], "zsh": [S, D], "eout": [NTILE * 128, D],
    "expt": [128, 1024], "rdenb": [16, 16], "d0u": [128, NT], "d1u": [128, NT],
}

PHASES = ["init", "A", "imp", "topk", "selproj", "p1", "p2", "route",
          "scatter", "shared", "experts", "final"]


class MK:
    def __init__(self, dumps=(), upto="final", p2tau=NT):
        self.nc = bass.Bass("TRN2", target_bir_lowering=False)
        self.S = Sched()
        self.dumps = tuple(dumps)
        self.upto = PHASES.index(upto)
        self.p2tau = p2tau

    def op(self, *a, **k):
        self.S.op(*a, **k)

    def build(self):
        nc = self.nc
        self.d_in = {"x": nc.dram_tensor("x", [S, D], F32, kind="ExternalInput").ap()}
        for name, shape, dt in WEIGHT_SPECS:
            self.d_in[name] = nc.dram_tensor(name, shape, dt, kind="ExternalInput").ap()
        self.out_d = nc.dram_tensor("out", [S, D], F32, kind="ExternalOutput").ap()
        self.h1_dram = nc.dram_tensor("h1_dram", [S, D], BF16, kind="Internal").ap()
        self.h2_dram = nc.dram_tensor("h2_dram", [S, D], BF16, kind="Internal").ap()
        self.x2_dram = nc.dram_tensor("x2_dram", [S, D], F32, kind="Internal").ap()
        self.zsh_dram = nc.dram_tensor("zsh_dram", [S, D], BF16, kind="Internal").ap()
        self.impflat_dram = nc.dram_tensor("impflat_dram", [1, S], F32, kind="Internal").ap()
        self.ein_dram = nc.dram_tensor("ein_dram", [NTILE * 128, D], BF16, kind="Internal").ap()
        self.eout_dram = nc.dram_tensor("eout_dram", [NTILE * 128, D], BF16, kind="Internal").ap()
        self.wblob = nc.dram_tensor("wblob", [NE * 128, BLOBW], BF16, kind="Internal").ap()
        self.dump_d = {nm: nc.dram_tensor("dump_" + nm, DUMP_SHAPES[nm], F32,
                                          kind="ExternalOutput").ap()
                       for nm in self.dumps}

        ctx = contextlib.ExitStack()
        sbuf = lambda name, shape, dt: ctx.enter_context(nc.sbuf_tensor(name, shape, dt))
        psum = lambda name, shape, dt: ctx.enter_context(nc.psum_tensor(name, shape, dt))
        with ctx:
            self.A1 = sbuf("A1", [128, 32768], BF16)
            self.A2 = sbuf("A2", [128, 16384], BF16)
            self.A3 = sbuf("A3", [128, 16384], BF16)
            self.WA = sbuf("WA", [128, 24576], BF16)
            self.SA = sbuf("SA", [128, 4096], F32)
            self.ctxf = self.WA[:, 0:1024]
            self.cb = {}
            for name, shape, dt in WEIGHT_SPECS:
                if name in CB_NAMES:
                    self.cb[name] = sbuf("c_" + name, shape, dt)
            b = {}
            B16 = sbuf("B16", [128, 5120], BF16)
            self.B16 = B16
            b["gatew"] = self.A3[:, 0:128]
            b["ws2a"] = self.A3[:, 1540:1544]
            b["PT"] = B16[0:16, 0:4096]
            b["ctxbT"] = B16[0:16, 4096:5120]
            b["vselT"] = B16[32:48, 0:1024]
            b["wTst"] = B16[32:48, 1024:1536]
            b["h1selT"] = B16[32:48, 1536:2560]
            b["borowb"] = B16[32:33, 2560:3584]
            b["onesb32"] = B16[32:33, 3584:3712]
            b["timerow"] = B16[64:65, 0:1024]
            b["sw2t0"] = B16[64:65, 3072:4095]
            b["h1selF"] = sbuf("h1selF", [128, 128], BF16)
            b["qselF"] = sbuf("qselF", [128, 128], BF16)
            b["kselF"] = sbuf("kselF", [128, 128], BF16)
            b["vselF"] = sbuf("vselF", [128, 128], BF16)
            b["denb8"] = sbuf("denb8", [16, 8], F32)
            b["denb8B"] = sbuf("denb8B", [16, 8], F32)
            b["rdenb"] = sbuf("rdenb", [16, 16], F32)
            b["imp_tm"] = sbuf("imp_tm", [128, NT], F32)
            b["topv"] = sbuf("topv", [1, 16], F32)
            b["t16b"] = sbuf("t16b", [128, 1], F32)
            b["sel"] = sbuf("sel", [128, NT], F32)
            b["notsel"] = sbuf("notsel", [128, NT], F32)
            b["hitbf"] = sbuf("hitbf", [128, NT], BF16)
            b["selT"] = sbuf("selT_", [32, 128], BF16)
            b["rfree"] = sbuf("rfree", [128, NT], F32)
            b["rowtot"] = sbuf("rowtot", [128, 1], F32)
            b["rowtotb"] = sbuf("rowtotb", [128, 1], BF16)
            b["offs"] = sbuf("offs", [128, 1], F32)
            b["rank16"] = sbuf("rank16", [128, NT], F32)
            b["Pbuf"] = self.A3[:, 1024:1536]
            b["Pf32"] = sbuf("Pf32", [128, 16], F32)
            b["idx16"] = sbuf("idx16", [16, 1], U32)
            b["idx16f"] = sbuf("idx16f", [16, 1], F32)
            b["expg"] = self.SA[:, 512:1024].bitcast(F32) if False else self.SA[:, 512:1024]
            b["hit"] = self.A2[:, 0:512]
            b["rall"] = self.SA[:, 0:512]
            b["rden"] = sbuf("rden", [128, NT], F32)
            b["w0"] = sbuf("w0", [128, NT], F32)
            b["w1"] = sbuf("w1", [128, NT], F32)
            b["dest0f"] = self.SA[:, 1024:1056]
            b["dest1f"] = self.SA[:, 1056:1088]
            b["dest0"] = sbuf("dest0", [128, NT], U32)
            b["dest1"] = sbuf("dest1", [128, NT], U32)
            b["pofsb"] = self.SA[:, 1184:1200]
            for i, nm in enumerate(("m1v", "m2v", "eav", "ebv", "ddv",
                                    "flagv", "wlov", "whiv")):
                b[nm] = self.SA[:, 1264 + i * 32:1296 + i * 32]
            b["reqall"] = self.SA[:, 0:512]

            b["req2all"] = self.SA[:, 3072:3584]
            b["cntrow"] = self.SA[0:1, 1200:1216]
            b["pcrow"] = self.SA[0:1, 1216:1232]
            b["pcs"] = self.SA[0:1, 1232:1248]
            b["pofsrow"] = self.SA[0:1, 1248:1264]
            b["texp"] = B16[32:33, 3712:3712 + NG]
            b["t80"] = B16[32:33, 3800:3800 + NG]

            b["idxmat"] = sbuf("idxmat", [128, NG], U32)
            b["pc32"] = b["dest0"][0:1, 0:16]
            b["elo"] = self.SA[:, 1120:1152]
            b["ehi"] = self.SA[:, 1152:1184]
            rs16 = sbuf("rs16", [128, 24], F32)
            for i, nm in enumerate(("m1", "m2", "dd", "ea", "eb", "flag", "wlo",
                                    "whi", "r0", "r1", "msk")):
                b[nm] = rs16[:, i:i + 1]
            b["req"] = self.SA[:, 1088:1104]
            b["req2"] = self.SA[:, 1104:1120]
            b["exps"] = self.WA[:, 1536:1792]
            b["wsc"] = self.WA[:, 1792:2048]
            b["kselP"] = self.WA[:, 6144:6400]
            b["qselP"] = self.WA[:, 24320:24576]
            b["aden"] = sbuf("aden", [128, 16], F32)
            b["arden"] = sbuf("arden", [128, 16], F32)
            b["mean"] = rs16[:, 11:12]
            b["ssq"] = rs16[:, 14:15]
            b["rstd"] = rs16[:, 12:13]
            b["s0"] = rs16[:, 13:14]
            b["c1"] = rs16[:, 15:16]
            b["ceps"] = rs16[:, 16:17]
            for i, nm in enumerate(("s0b", "ssqb", "mean0", "mean1",
                                    "rstd0", "rstd1", "ssqA0", "ssqA1")):
                b[nm] = rs16[:, i:i + 1] if i < 7 else rs16[:, 17 + i - 7:18]
            b["hitbfB"] = sbuf("hitbfB", [128, NT], BF16)
            b["selTB"] = sbuf("selTB2", [32, 128], BF16)
            b["rowtotbB"] = sbuf("rowtotbB", [128, 1], BF16)
            b["rfreeB"] = b["imp_tm"]
            b["rowtotB"] = b["r0"]
            b["offsB"] = b["r1"]

            b["zt"] = self.A3[:, 15360:16384]
            b["bt"] = [self.WA[:, 3072:4096], self.WA[:, 4096:5120]]
            self.b = b

            self.pA = psum("pA", [128, 512], F32)
            self.pB = psum("pB", [128, 512], F32)
            self.pBig = psum("pBig", [128, 1024], F32)
            self.pS = psum("pS", [128, 512], F32)
            self.pX = psum("pX", [128, 512], F32)
            self.pT = [psum("pT0", [128, 1024], BF16), psum("pT1", [128, 1024], BF16)]

            FAMS = ["blb", "cbld", "xin0", "xin1", "h1st", "impq", "sx0", "sx1",
                    "sh2", "zst", "el0", "el1", "eW", "fx0", "fx1", "fz0",
                    "fz1", "fs", "wld", "zein", "gsel", "sc", "sc0", "sc1",
                    "fo00", "fo01",
                    "fo10", "fo11", "ewa", "ewb", "st0", "st1", "dmpp", "dmps"]
            for e in list(Sched.ENGINES) + FAMS:
                self.S.sem[e] = ctx.enter_context(nc.semaphore("sem_" + e))

            import os as _os
            skip = set(_os.environ.get("SKIPPH", "").split(","))
            emits = [self.emit_init, self.emit_phaseA, self.emit_imp,
                     self.emit_topk, self.emit_selproj, self.emit_p1,
                     self.emit_p2, self.emit_route, self.emit_scatter,
                     self.emit_shared, self.emit_experts, self.emit_final]
            for i, fn in enumerate(emits):
                if i <= self.upto and PHASES[i] not in skip:
                    fn()
            self.emit_dump()
            self.emit_tail()

            with nc.Block() as block:
                qq = self.S.q

                @block.tensor
                def _(eng):
                    for t in qq["pe"]:
                        t(eng)

                @block.vector
                def _(eng):
                    for t in qq["dve"]:
                        t(eng)

                @block.scalar
                def _(eng):
                    for t in qq["act"]:
                        t(eng)

                @block.gpsimd
                def _(eng):
                    self.bcreg = eng.alloc_register("bcreg")
                    eng.reg_mov(self.bcreg, NTILE * 128 - 1)
                    for t in qq["pool"]:
                        t(eng)

                @block.sync
                def _(eng):
                    for t in qq["sp"]:
                        t(eng)
        return self.nc

    def load_w(self, name, wofs, cols, nk=8, dram_slice=None, key=None, war=(),
               sem="wld", eng="pool"):
        src = dram_slice if dram_slice is not None else self.d_in[name]
        dst = self.WA[:, wofs:wofs + nk * cols].rearrange("p (k f) -> p k f", k=nk)

        def fn(e, src=src, dst=dst):
            return e.dma_start(out=dst, in_=src.rearrange("(k p) f -> p k f", p=128))
        self.op(eng, fn, writes=[key or ("w", name)], war=war, inc=16,
                tag="ldw_" + (name if isinstance(name, str) else str(name)),
                sem=sem)

    def wck(self, wofs, cols, k, sub=None):
        c = self.WA[:, wofs + k * cols: wofs + (k + 1) * cols]
        if sub is not None:
            c = c[:, sub[0]:sub[1]]
        return c

    def tbatch(self, srcs, outw, evac_fn, bank, rkeys, wkeys, tag, war=(),
               base=0, pbank=None):
        if pbank is None:
            pT, pkey = self.pT[bank], ("pt", bank)
        else:
            pT, pkey = pbank
        ident = self.cb["ident"]

        def tf(e, srcs=srcs, pT=pT, outw=outw, base=base):
            ins = None
            for i, sap in enumerate(srcs):
                r = sap.shape[0]
                ins = e.transpose(pT[0:sap.shape[1], i * outw:i * outw + r],
                                  sap, ident[base:base + r, base:base + r])
            return ins
        self.op("pe", tf, reads=list(rkeys) + [("c", "ident")],
                writes=[pkey], tag=tag)
        self.op("dve", evac_fn, reads=[pkey], writes=list(wkeys),
                war=war, tag=tag + "e")

    H1F_KEYS = [("h1F", tau, half) for tau in range(NT) for half in range(2)]
    H2F_KEYS = [("h2F", tau, half) for tau in range(NT) for half in range(2)]

    def emit_init(self):
        d_in, cb, b = self.d_in, self.cb, self.b
        for name in cb:
            self.op("act", lambda e, n=name: e.dma_start(out=cb[n][:], in_=d_in[n]),
                    writes=[("c", name)], inc=16, tag="ld_" + name, sem="cbld")
        self.op("dve", lambda e: e.memset(b["zt"][:], 0.0), writes=["zt"], tag="zt")
        self.op("dve", lambda e: e.memset(b["c1"], 1.0), writes=["c1"], tag="c1")
        self.op("dve", lambda e: e.memset(b["ceps"], 1e-5), writes=["ceps"], tag="ceps")
        self.op("dve", lambda e: e.memset(b["onesb32"], 1.0), writes=["onesb32"], tag="o32")
        self.op("pool", lambda e: e.dma_start(
            out=b["ws2a"], in_=d_in["ws2a"].rearrange("(m p) o -> p (m o)", p=128),
            allow_slow_non_contiguous=True),
            writes=["ws2a"], inc=16, tag="ld_ws2a", sem="wld")
        self.load_w("ws1a", 16384, 512)
        self.load_w("wk", 0, 1024)
        self.load_w("wv", 8192, 1024)
        self.op("pool", lambda e: e.dma_start(out=b["borowb"], in_=d_in["borow"]),
                writes=["borowb"], inc=16, tag="ld_borow", sem="wld")

    def emit_phaseA(self):
        b, SA, A1 = self.b, self.SA, self.A1
        h1F = A1[:].rearrange("p (m t) -> p m t", m=8)
        xin = [SA[:, 0:1024], SA[:, 1024:2048]]
        h1t = [self.A2[:, i * 1024:(i + 1) * 1024] for i in range(4)]
        sqj = self.A2[:, 4096:5120]
        xd = self.d_in["x"].rearrange("(T p) d -> T p d", p=128)
        h1d = self.h1_dram.rearrange("(T p) d -> T p d", p=128)
        def a_front(tau):
            bb = tau % 2
            return [
                lambda: self.op("sp", lambda e: e.dma_start(
                    out=xin[bb], in_=xd[tau]),
                    writes=[("xin", bb)], inc=16, tag=f"ldx{tau}",
                    sem=f"xin{bb}"),
                lambda: self.ln_head(xin[bb], [("xin", bb)], f"ln1_{tau}",
                                     mk=f"mean{bb}"),
            ]

        def a_back(tau):
            bb = tau % 2
            s4 = tau % 4
            st = [
                lambda: self.ln_tail(
                    xin[bb], [("xin", bb)], h1t[s4], [("h1t", s4)], sqj,
                    f"ln1_{tau}", mk=f"mean{bb}", rk=f"rstd{bb}",
                    qk=f"ssqA{bb}"),
                lambda: self.op("sp", lambda e: e.dma_start(
                    out=h1d[tau], in_=h1t[s4]),
                    reads=[("h1t", s4)], writes=[("h1_dram", tau)], inc=16,
                    tag=f"sth1{tau}", sem="h1st"),
            ]
            for half in range(2):
                bank = (tau * 2 + half) % 2
                srcs = [h1t[s4][:, (half * 4 + j) * 128:(half * 4 + j + 1) * 128]
                        for j in range(4)]

                def ev(e, tau=tau, half=half, bank=bank):
                    return e.tensor_copy(
                        h1F[:, half * 4:half * 4 + 4, tau * 128:(tau + 1) * 128],
                        self.pT[bank][:, 0:512].rearrange("p (m t) -> p m t", m=4))
                st.append(lambda half=half, bank=bank, srcs=srcs, ev=ev, tau=tau:
                          self.tbatch(srcs, 128, ev, bank, [("h1t", s4)],
                                      [("h1F", tau, half)], f"trA{tau}_{half}"))
            return st

        relu = [self.A3[:, 0:512], self.A3[:, 512:1024]]
        cb = self.cb
        pS = self.pS

        def imp_piece(n, m):
            rb = n % 2
            if True:
                bank = [self.pA, self.pB][(n * 4 + m) % 2]
                bkey = ["pA", "pB"][(n * 4 + m) % 2]

                def mm(e, m=m, n=n, bank=bank):
                    ins = None
                    for k in range(8):
                        ins = e.matmul(bank[:, 0:512],
                                       self.wck(16384, 512, k,
                                                (m * 128, (m + 1) * 128)),
                                       h1F[:, k, n * 512:(n + 1) * 512],
                                       start=(k == 0), stop=(k == 7))
                    return ins
                self.op("pe", mm, reads=[("w", "ws1a")] +
                        [("h1F", 4 * n + j, hh)
                         for j in range(4) for hh in range(2)],
                        writes=[bkey], tag=f"is1_{m}_{n}")
                self.op("act", lambda e, m=m, rb=rb, bank=bank: e.activation(
                    relu[rb], bank[:, 0:512], AF.Relu,
                    bias=cb["bs1at"][:, m:m + 1]),
                    reads=[bkey, ("c", "bs1at")], writes=[("relu", rb)],
                    tag=f"ir_{m}_{n}")

                def mm2(e, m=m, n=n, rb=rb):
                    ins = None
                    for j in range(4):
                        tau = n * 4 + j
                        ins = e.matmul(pS[:, tau:tau + 1],
                                       relu[rb][:, j * 128:(j + 1) * 128],
                                       b["ws2a"][:, m:m + 1],
                                       start=(n == 0 and m == 0 and j == 0),
                                       stop=(n == 7 and m == 3 and j == 3))
                    return ins
                self.op("pe", mm2, reads=[("relu", rb), "ws2a"], writes=["pS"],
                        tag=f"is2_{m}_{n}")

        for i in range(NT + 4):
            A = a_front(i) if i < NT else []
            B = a_back(i - 1) if 1 <= i <= NT else []
            na, nb = len(A), len(B)
            ia = ib = 0
            while ia < na or ib < nb:
                if ia < na and (ib >= nb or ia * (nb + 1) <= ib * (na + 1)):
                    A[ia]()
                    ia += 1
                else:
                    B[ib]()
                    ib += 1
            k = i - 4
            if 0 <= k < NT:
                imp_piece(k // 4, k % 4)

    def ln_head(self, x_ap, x_keys, tag, mk="mean"):
        b = self.b
        self.op("dve", lambda e, mk=mk: e.tensor_reduce(
            b[mk], x_ap, axis=AX.X, op=ALU.add, negate=True),
            reads=x_keys, writes=[mk], tag=tag + "m")
        self.op("dve", lambda e, mk=mk: e.tensor_scalar(
            b[mk], b[mk], 1.0 / D, None, op0=ALU.mult),
            reads=[mk], writes=[mk], tag=tag + "s")

    def ln_tail(self, x_ap, x_keys, out_ap, out_keys, sqj_ap, tag,
                mk="mean", rk="rstd", qk="ssq"):
        b = self.b
        self.op("act", lambda e, mk=mk, qk=qk: e.activation(
            sqj_ap, x_ap, AF.Square, bias=b[mk], scale=1.0,
            accum_out=b[qk]),
            reads=x_keys + [mk], writes=[qk, "sqj"], tag=tag + "q")
        self.op("act", lambda e, rk=rk, qk=qk: e.activation(
            b[rk], b[qk], AF.Sqrt, scale=1.0 / D, bias=b["ceps"]),
            reads=[qk, "ceps"], writes=[rk], tag=tag + "r")
        self.op("dve", lambda e, rk=rk: e.reciprocal(b[rk], b[rk]),
                reads=[rk], writes=[rk], tag=tag + "i")
        self.op("dve", lambda e, mk=mk, rk=rk: e.tensor_scalar(
            out_ap, x_ap, b[mk], b[rk], op0=ALU.add, op1=ALU.mult),
            reads=x_keys + [mk, rk], writes=out_keys, tag=tag + "o")

    def ln_tile(self, x_ap, x_keys, out_ap, out_keys, sqj_ap, tag):
        self.ln_head(x_ap, x_keys, tag)
        self.ln_tail(x_ap, x_keys, out_ap, out_keys, sqj_ap, tag)

    def emit_bulk(self):
        d_in, b = self.d_in, self.b
        eind = self.ein_dram.rearrange("(n p) d -> n p d", p=128)
        for i in range(NTILE):
            gate = [("h1F", NT - 1, 0), ("h1F", NT - 1, 1)] if i == 0 else []
            self.op("pool", lambda e, i=i: e.dma_start(out=eind[i], in_=b["zt"][:]),
                    reads=["zt"] + gate, writes=[("eind_z", i)], inc=16,
                    tag=f"zein{i}", sem="zein")
        wb = self.wblob.rearrange("(n p) w -> n p w", p=128)
        for ex in range(NE):
            self.op("pool", lambda e, ex=ex: e.dma_start(
                out=wb[ex][:, BO_W1:BO_W1 + 4096].rearrange(
                    "p (k f) -> p k f", k=8)[:, :, 0:511],
                in_=d_in["we1t"][ex].rearrange("(k p) f -> p k f", p=128)),
                writes=[("wblob", ex, 0)], inc=16, tag=f"wb1_{ex}", sem="blb")
            self.op("pool", lambda e, ex=ex: e.dma_start(
                out=wb[ex][:, BO_W3:BO_W3 + 4096].rearrange(
                    "p (k f) -> p k f", k=8)[:, :, 0:511],
                in_=d_in["we3t"][ex].rearrange("(k p) f -> p k f", p=128)),
                writes=[("wblob", ex, 1)], inc=16, tag=f"wb3_{ex}", sem="blb")
            self.op("pool", lambda e, ex=ex: e.dma_start(
                out=wb[ex][:, BO_W2:BO_W2 + 4096].rearrange(
                    "p (k f) -> p k f", k=4)[:, :, 0:1023],
                in_=d_in["we2p"][ex].rearrange("(k p) f -> p k f", p=128)),
                writes=[("wblob", ex, 2)], inc=16, tag=f"wb2_{ex}", sem="blb")
            self.op("pool", lambda e, ex=ex: e.dma_start(
                out=wb[ex][0:1, BO_B1:BO_B1 + 511],
                in_=d_in["be1r"][ex:ex + 1, :]),
                writes=[("wblob", ex, 3)], inc=16, tag=f"wbb1_{ex}", sem="blb")
            self.op("pool", lambda e, ex=ex: e.dma_start(
                out=wb[ex][0:1, BO_B2:BO_B2 + 1023],
                in_=d_in["be2r"][ex:ex + 1, :]),
                writes=[("wblob", ex, 4)], inc=16, tag=f"wbb2_{ex}", sem="blb")
            self.op("pool", lambda e, ex=ex: e.dma_start(
                out=wb[ex][0:1, BO_B3:BO_B3 + 511],
                in_=d_in["be3r"][ex:ex + 1, :]),
                writes=[("wblob", ex, 5)], inc=16, tag=f"wbb3_{ex}", sem="blb")

    def emit_imp(self):
        b, cb = self.b, self.cb
        pS = self.pS
        self.op("act", lambda e: e.activation(
            b["imp_tm"][:], pS[:, 0:NT], AF.Sigmoid, bias=cb["bs2ar"][:, 0:1]),
            reads=["pS", ("c", "bs2ar")], writes=["imp_tm"], tag="isig")

    def emit_topk(self):
        b, cb, SA = self.b, self.cb, self.SA
        impf = SA[0:1, 0:4096]
        scr = SA[32:33, 0:4096]
        self.op("sp", lambda e: e.dma_start(
            out=self.impflat_dram.rearrange("o (p T) -> (o p) T", p=128),
            in_=b["imp_tm"][:]),
            reads=["imp_tm"], writes=["impflat"], inc=16, tag="impo", sem="impq")
        self.op("sp", lambda e: e.dma_start(out=impf, in_=self.impflat_dram),
                reads=["impflat"], writes=["impf"], war=[("xin", 0), ("xin", 1)],
                inc=16, tag="impi", sem="impq")
        self.op("dve", lambda e: e.max(out=b["topv"][:, 0:8], in_=impf),
                reads=["impf"], writes=["topv"], tag="mx8a")
        self.op("dve", lambda e: e.match_replace(
            out=scr, in_to_replace=b["topv"][:, 0:8], in_values=impf, imm_value=-1.0),
            reads=["impf", "topv"], writes=["scr"], tag="mra")
        self.op("dve", lambda e: e.max(out=b["topv"][:, 8:16], in_=scr),
                reads=["scr", "topv"], writes=["topv"], tag="mx8b")
        self.op("pe", lambda e: e.matmul(self.pX[:, 0:1], cb["onesrow"][0:1, :],
                                         b["topv"][0:1, 15:16], start=True, stop=True),
                reads=[("c", "onesrow"), "topv"], writes=["pX"], tag="bc16")
        self.op("dve", lambda e: e.tensor_copy(b["t16b"][:], self.pX[:, 0:1]),
                reads=["pX"], writes=["t16b"], tag="t16")
        self.op("dve", lambda e: e.tensor_scalar(
            b["sel"][:], b["imp_tm"][:], b["t16b"][:, 0:1], None, op0=ALU.is_ge),
            reads=["imp_tm", "t16b"], writes=["sel"], tag="sel")
        self.op("dve", lambda e: e.tensor_scalar(
            b["notsel"][:], b["sel"][:], -1.0, 1.0, op0=ALU.mult, op1=ALU.add),
            reads=["sel"], writes=["notsel"], tag="nsel")
        self.op("dve", lambda e: e.tensor_copy(b["hitbf"][:], b["sel"][:]),
                reads=["sel"], writes=["hitbf"], tag="hbf")
        self.emit_rank(b["sel"][:], ["sel"], b["rank16"][:], [("rank16",)], "rs")

        first = [True]
        for tau in range(NT):
            self.op("dve", lambda e, tau=tau: e.tensor_scalar(
                b["Pf32"][:], cb["iota16"][:], b["rank16"][:, tau:tau + 1], None,
                op0=ALU.is_equal),
                reads=[("rank16",), ("c", "iota16")], writes=["Pf32"], tag=f"pq{tau}")
            self.op("dve", lambda e, tau=tau: e.tensor_scalar(
                b["Pf32"][:], b["Pf32"][:], b["sel"][:, tau:tau + 1], None,
                op0=ALU.mult),
                reads=["Pf32", "sel"], writes=["Pf32"], tag=f"pm{tau}")
            self.op("dve", lambda e, tau=tau: e.tensor_copy(
                b["Pbuf"][:, tau * 16:(tau + 1) * 16], b["Pf32"][:]),
                reads=["Pf32"], writes=[("Pbuf", tau)], tag=f"pb{tau}")

            def mmi(e, tau=tau, first=first):
                ins = e.matmul(self.pS[0:16, 48:49], b["Pf32"][:],
                               cb["iotaf"][:, tau:tau + 1],
                               start=first[0], stop=(tau == NT - 1))
                first[0] = False
                return ins
            self.op("pe", mmi, reads=["Pf32", ("c", "iotaf")], writes=["pS"],
                    tag=f"px{tau}")
            bank = tau % 2
            self.op("pe", lambda e, tau=tau, bank=bank: e.transpose(
                self.pT[bank][0:16, 0:128], b["Pbuf"][:, tau * 16:(tau + 1) * 16],
                cb["ident"][:]),
                reads=[("Pbuf", tau), ("c", "ident")], writes=[("pt", bank)],
                tag=f"pt{tau}")
            self.op("dve", lambda e, tau=tau, bank=bank: e.tensor_copy(
                b["PT"][:, tau * 128:(tau + 1) * 128], self.pT[bank][0:16, 0:128]),
                reads=[("pt", bank)], writes=[("PT", tau)], tag=f"pe{tau}")
        self.op("dve", lambda e: e.tensor_copy(b["idx16f"][0:16, :], self.pS[0:16, 48:49]),
                reads=["pS"], writes=["idx16f"], tag="i16f")
        self.op("dve", lambda e: e.tensor_copy(b["idx16"][0:16, :], b["idx16f"][0:16, :]),
                reads=["idx16f"], writes=["idx16"], tag="i16")
        hkeys = [("h1_dram", tau) for tau in range(NT)]
        self.op("pool", lambda e: e.indirect_dma_start(
            out=b["h1selT"], out_offset=None, in_=self.h1_dram,
            in_offset=bass.IndirectOffsetOnAxis(ap=b["idx16"][0:16, :], axis=0)),
            reads=["idx16"] + hkeys, writes=["h1selT"], inc=16, tag="gsel",
            sem="gsel")
        self.load_w("wq", 16384, 1024, key=("w", "wqsel"), war=[("w", "ws1a")])
        self.emit_bulk()

    def rank_ops(self, hit_ap, hit_keys, rank_ap, rank_keys, tag, war=(),
                 ps=0):
        b, cb = self.b, self.cb
        sfx = "B" if ps else ""
        hb, sT, rf = "hitbf" + sfx, "selT" + sfx, "rfree" + sfx
        rt, rtb, of = "rowtot" + sfx, "rowtotb" + sfx, "offs" + sfx
        pS_, kS = [(self.pS, "pS"), (self.pA, "pA")][ps]
        pX_, kX = [(self.pX, "pX"), (self.pB, "pB")][ps]
        pT_ = self.pT[ps]
        return [
            lambda: self.op("pe", lambda e: e.transpose(
                pT_[0:32, 0:128], b[hb][:, 0:NT], cb["ident"][:]),
                reads=[hb, ("c", "ident")], writes=[("pt", ps)], tag=tag + "t"),
            lambda: self.op("dve", lambda e: e.tensor_copy(
                b[sT][0:32, :], pT_[0:32, 0:128]),
                reads=[("pt", ps)], writes=[sT], tag=tag + "e"),
            lambda: self.op("pe", lambda e: e.matmul(
                pS_[:, 0:32], b[sT][0:32, :],
                cb["strictu32"][0:32, :], start=True, stop=True),
                reads=[sT, ("c", "strictu32")], writes=[kS], tag=tag + "f"),
            lambda: self.op("dve", lambda e: e.tensor_copy(
                b[rf][:], pS_[:, 0:32]),
                reads=[kS], writes=[rf], tag=tag + "g"),
            lambda: self.op("dve", lambda e: e.tensor_reduce(
                b[rt][:], hit_ap, axis=AX.X, op=ALU.add),
                reads=hit_keys, writes=[rt], tag=tag + "h"),
            lambda: self.op("dve", lambda e: e.tensor_copy(b[rtb][:], b[rt][:]),
                    reads=[rt], writes=[rtb], tag=tag + "i"),
            lambda: self.op("pe", lambda e: e.matmul(
                pX_[:, 0:1], cb["strictu128"][:],
                b[rtb][:, 0:1], start=True, stop=True),
                reads=[rtb, ("c", "strictu128")], writes=[kX], tag=tag + "j"),
            lambda: self.op("dve", lambda e: e.tensor_copy(b[of][:], pX_[:, 0:1]),
                    reads=[kX], writes=[of], tag=tag + "k"),
            lambda: self.op("dve", lambda e: e.tensor_scalar(
                rank_ap, b[rf][:], b[of][:, 0:1], None, op0=ALU.add),
                reads=[rf, of], writes=rank_keys, war=war, tag=tag + "l"),
        ]

    def emit_rank(self, hit_ap, hit_keys, rank_ap, rank_keys, tag, war=()):
        for f in self.rank_ops(hit_ap, hit_keys, rank_ap, rank_keys, tag, war):
            f()

    def emit_selproj(self):
        b, cb = self.b, self.cb
        h1selF = b["h1selF"][:].rearrange("p (m j) -> p m j", m=8)
        vselF = b["vselF"][:].rearrange("p (m j) -> p m j", m=8)
        qselF = b["qselF"][:].rearrange("p (m j) -> p m j", m=8)
        kselF = b["kselF"][:].rearrange("p (m j) -> p m j", m=8)
        for half in range(2):
            bank = half % 2
            srcs = [b["h1selT"][:, (half * 4 + j) * 128:(half * 4 + j + 1) * 128]
                    for j in range(4)]

            def ev(e, half=half, bank=bank):
                return e.tensor_copy(
                    h1selF[:, half * 4:half * 4 + 4, :],
                    self.pT[bank][:, 0:64].rearrange("p (m j) -> p m j", m=4))
            self.tbatch(srcs, 16, ev, bank, ["h1selT"], ["h1selF"], f"st{half}",
                        base=32)
        for wname, wofs, bias, dst, dkey in (
                ("wqsel", 16384, "bqt", qselF, "qselF"),
                ("wk", 0, "bkt", kselF, "kselF"),
                ("wv", 8192, "bvt", vselF, "vselF")):
            for m in range(8):
                bank = [self.pA, self.pB][m % 2]
                bkey = ["pA", "pB"][m % 2]

                def mm(e, wofs=wofs, m=m, bank=bank):
                    ins = None
                    for k in range(8):
                        ins = e.matmul(bank[:, 0:16],
                                       self.wck(wofs, 1024, k, (m * 128, (m + 1) * 128)),
                                       h1selF[:, k, :],
                                       start=(k == 0), stop=(k == 7))
                    return ins
                self.op("pe", mm, reads=[("w", wname), "h1selF"], writes=[bkey],
                        tag=f"sp_{wname}_{m}")
                scl = 0.125 if wname == "wqsel" else None
                op1 = ALU.mult if scl is not None else ALU.bypass
                self.op("dve", lambda e, m=m, dst=dst, bias=bias, bank=bank,
                        scl=scl, op1=op1: e.tensor_scalar(
                            dst[:, m, :], bank[:, 0:16], cb[bias][:, m:m + 1],
                            scl, op0=ALU.add, op1=op1),
                        reads=[bkey, ("c", bias)], writes=[dkey],
                        tag=f"se_{wname}_{m}")
        for half in range(2):
            bank = half % 2
            srcs = [vselF[:, half * 4 + j, :] for j in range(4)]

            def ev(e, half=half, bank=bank):
                return e.tensor_copy(
                    b["vselT"][:, half * 512:(half + 1) * 512],
                    self.pT[bank][0:16, 0:512])
            self.tbatch(srcs, 128, ev, bank, ["vselF"], ["vselT"], f"vt{half}")
        self.emit_selP("qselP", qselF, war=[("w", "wqsel")])

    def emit_selP(self, nm, srcf, war=()):
        b = self.b
        self.op("dve", lambda e, nm=nm: e.memset(b[nm], 0.0), writes=[nm],
                war=war, tag=nm + "z")
        for h in range(H):
            m, lo = h // 2, (h % 2) * 64
            self.op("dve", lambda e, m=m, lo=lo, h=h, nm=nm, srcf=srcf:
                    e.tensor_copy(
                        b[nm][lo:lo + 64, h * 16:(h + 1) * 16],
                        srcf[lo:lo + 64, m, :]),
                    reads=["kselF" if nm == "kselP" else "qselF"],
                    writes=[nm], tag=f"{nm}{h}")

    def emit_p1(self):
        b, cb = self.b, self.cb
        h1F = self.A1[:].rearrange("p (m t) -> p m t", m=8)
        Kg = [self.A3[:, 0:4096], self.A3[:, 4096:8192]]
        Vg = [self.A3[:, 8192:12288].rearrange("p (T d) -> p T d", T=NT),
              self.A3[:, 12288:16384].rearrange("p (T d) -> p T d", T=NT)]
        Vfm = self.A2[:, 0:4096]
        expb = [self.A2[0:16, 4096:8192], self.A2[32:48, 4096:8192]]
        expT = self.A2[:, 8192:9216].rearrange("p (T j) -> p T j", T=NT)
        qselF = b["qselF"][:].rearrange("p (m j) -> p m j", m=8)
        pBig = self.pBig
        first_ctx = [True, True]
        VFM_KEYS = [("Vfm", n) for n in range(8)]

        for g in range(8):
            gb = g % 2
            for wname, wofs, bias, dstf in (("wk", 0, "bkt", None),
                                            ("wv", 8192, "bvt", Vfm)):
                for n in range(8):
                    bank = [self.pA, self.pB][n % 2]
                    bkey = ["pA", "pB"][n % 2]

                    def mm(e, wofs=wofs, g=g, n=n, bank=bank):
                        ins = None
                        for k in range(8):
                            ins = e.matmul(bank[:, 0:512],
                                           self.wck(wofs, 1024, k,
                                                    (g * 128, (g + 1) * 128)),
                                           h1F[:, k, n * 512:(n + 1) * 512],
                                           start=(k == 0), stop=(k == 7))
                        return ins
                    self.op("pe", mm, reads=[("w", wname)] + self.H1F_KEYS,
                            writes=[bkey], tag=f"{wname}p{g}_{n}")
                    if dstf is None:
                        war = [("h1t", i) for i in range(4)] if (g == 0 and n == 0) else []
                        self.op("dve", lambda e, g=g, n=n, gb=gb, bank=bank:
                                e.tensor_scalar(
                                    Kg[gb][:, n * 512:(n + 1) * 512], bank[:, 0:512],
                                    cb["bkt"][:, g:g + 1], None, op0=ALU.add),
                                reads=[bkey, ("c", "bkt")], writes=[("Kg", gb, n)],
                                war=[("relu", 0), ("relu", 1)] if (g < 2 and n == 0) else [],
                                tag=f"kpe{g}_{n}")
                    else:
                        war = ([("h1t", i) for i in range(4)] + ["sqj"]) \
                            if (g == 0 and n == 0) else []
                        self.op("dve", lambda e, g=g, n=n, bank=bank: e.tensor_scalar(
                            Vfm[:, n * 512:(n + 1) * 512], bank[:, 0:512],
                            cb["bvt"][:, g:g + 1], None, op0=ALU.add),
                            reads=[bkey, ("c", "bvt")], writes=[("Vfm", n)],
                            war=war, tag=f"vpe{g}_{n}")
            for q in range(8):
                bank = q % 2
                srcs = [Vfm[:, (q * 4 + j) * 128:(q * 4 + j + 1) * 128]
                        for j in range(4)]

                def ev(e, gb=gb, q=q, bank=bank):
                    return e.tensor_copy(
                        Vg[gb][:, q * 4:q * 4 + 4, :],
                        self.pT[bank][:, 0:512].rearrange("p (T d) -> p T d", T=4))
                self.tbatch(srcs, 128, ev, bank, VFM_KEYS, [("Vg", gb, q)],
                            f"vt{g}_{q}",
                            war=["zt"] if (g == 1 and q >= 6) else [])
            for n in range(8):
                for hp in range(2):
                    h = 2 * g + hp
                    sbk = [self.pS, self.pX][hp]
                    sbkey = ["pS", "pX"][hp]
                    dnm = ["denb8", "denb8B"][hp]

                    def mm(e, g=g, hp=hp, n=n, sbk=sbk, gb=gb, h=h):
                        return e.matmul(sbk[0:16, 0:512],
                                        b["qselP"][:, h * 16:(h + 1) * 16],
                                        Kg[gb][:, n * 512:(n + 1) * 512],
                                        start=True, stop=True)
                    self.op("pe", mm, reads=["qselP", ("Kg", gb, n)],
                            writes=[sbkey], tag=f"sb{h}_{n}")
                    self.op("act", lambda e, hp=hp, n=n, sbk=sbk, dnm=dnm:
                            e.activation(
                        expb[hp][:, n * 512:(n + 1) * 512], sbk[0:16, 0:512], AF.Exp,
                        accum_out=b[dnm][0:16, n:n + 1]),
                        reads=[sbkey], writes=[("expb", hp, n), dnm],
                        war=["sqj"] if (g == 0 and n < 2) else [],
                        tag=f"se{h}_{n}")
            for hp in range(2):
                h = 2 * g + hp
                dnm = ["denb8", "denb8B"][hp]
                self.op("dve", lambda e, h=h, dnm=dnm: e.tensor_reduce(
                    b["rdenb"][0:16, h:h + 1], b[dnm][0:16, :], axis=AX.X,
                    op=ALU.add),
                    reads=[dnm], writes=[("rdt", h)], tag=f"dn{h}")
                self.op("dve", lambda e, h=h: e.reciprocal(
                    b["rdenb"][0:16, h:h + 1], b["rdenb"][0:16, h:h + 1]),
                    reads=[("rdt", h)], writes=[("rdenb", h), ("rdt", h)],
                    tag=f"dr{h}")
            pb4 = [self.pT[0][:, 0:64], self.pT[1][:, 0:64],
                   self.pS[:, 0:32].bitcast(BF16),
                   self.pX[:, 0:32].bitcast(BF16)]
            pk4 = [("pt", 0), ("pt", 1), "pS", "pX"]
            for q in range(8):
                for hp in range(2):
                    idx = (q * 2 + hp) % 4
                    pb, pk = pb4[idx], pk4[idx]
                    ebkeys = [("expb", hp, n) for n in range(8)]
                    srcs = [expb[hp][:, (q * 4 + j) * 128:(q * 4 + j + 1) * 128]
                            for j in range(4)]

                    def ev(e, hp=hp, q=q, pb=pb):
                        return e.tensor_copy(
                            expT[:, q * 4:q * 4 + 4, hp * 16:(hp + 1) * 16],
                            pb.rearrange("p (T j) -> p T j", T=4))
                    self.tbatch(srcs, 16, ev, 0, ebkeys, [("expT", hp, q)],
                                f"bt{g}_{hp}_{q}", base=32 * hp, pbank=(pb, pk))
            for hp in range(2):
                h = 2 * g + hp
                bk01 = 0 if h < 8 else 1

                def mmc(e, gb=gb, hp=hp, h=h, first_ctx=first_ctx, bk01=bk01):
                    ins = None
                    for kt in range(NT):
                        ins = e.matmul(pBig[0:16, h * 64:(h + 1) * 64],
                                       expT[:, kt, hp * 16:(hp + 1) * 16],
                                       Vg[gb][:, kt, 64 * hp:64 * hp + 64],
                                       start=(first_ctx[bk01] and kt == 0),
                                       stop=(h in (7, 15) and kt == NT - 1))
                    first_ctx[bk01] = False
                    return ins
                self.op("pe", mmc,
                        reads=[("expT", hp, q) for q in range(8)] +
                              [("Vg", gb, q) for q in range(8)],
                        writes=[("pBig", bk01)], tag=f"cb{h}")
        for h in range(H):
            bk01 = 0 if h < 8 else 1
            self.op("dve", lambda e, h=h: e.tensor_scalar(
                b["ctxbT"][:, h * 64:(h + 1) * 64],
                self.pBig[0:16, h * 64:(h + 1) * 64],
                b["rdenb"][0:16, h:h + 1], None, op0=ALU.mult),
                reads=[("pBig", bk01), ("rdenb", h)], writes=[("ctxbT", h)],
                tag=f"ce{h}")

    def load_w_half(self, name, wofs, h, key, war, tag):
        src = self.d_in[name].rearrange("(k p) f -> p k f", p=128)[
            :, :, h * 512:(h + 1) * 512]
        dst = self.WA[:, wofs:wofs + 8192].rearrange("p (k f) -> p k f", k=8)[
            :, :, h * 512:(h + 1) * 512]

        def fn(e, src=src, dst=dst):
            return e.dma_start(out=dst, in_=src)
        self.op("pool", fn, writes=[key], war=war, inc=16, tag=tag, sem="wld")

    def emit_p2(self):
        import os as _os
        b, cb = self.b, self.cb
        h1F = self.A1[:].rearrange("p (m t) -> p m t", m=8)
        h2F = h1F
        self.load_w("wo", 8192, 1024, key=("w", "wo"), war=[("w", "wv")])
        self.load_w("wq", 16384, 1024, key=("w", "wqsel"), war=["qselP"])
        kselF = b["kselF"][:].rearrange("p (m j) -> p m j", m=8)
        self.emit_selP("kselP", kselF, war=[("w", "wk")])

        def qch(m):
            if m < 4:
                return self.A2[:, m * 4096:(m + 1) * 4096]
            return self.A3[:, (m - 4) * 4096:(m - 3) * 4096]

        for m in range(8):
            for n in range(8):
                bank = [self.pA, self.pB][(m * 8 + n) % 2]
                bkey = ["pA", "pB"][(m * 8 + n) % 2]

                def mm(e, m=m, n=n, bank=bank):
                    ins = None
                    for k in range(8):
                        ins = e.matmul(bank[:, 0:512],
                                       self.wck(16384, 1024, k, (m * 128, (m + 1) * 128)),
                                       h1F[:, k, n * 512:(n + 1) * 512],
                                       start=(k == 0), stop=(k == 7))
                    return ins
                self.op("pe", mm, reads=[("w", "wqsel")] + self.H1F_KEYS,
                        writes=[bkey], tag=f"qp{m}_{n}")
                self.op("dve", lambda e, m=m, n=n, bank=bank: e.tensor_scalar(
                    qch(m)[:, n * 512:(n + 1) * 512], bank[:, 0:512],
                    cb["bqt"][:, m:m + 1], 0.125, op0=ALU.add, op1=ALU.mult),
                    reads=[bkey, ("c", "bqt")], writes=[("Qf", m, n)],
                    tag=f"qe{m}_{n}")
        QF_KEYS = [("Qf", m, n) for m in range(8) for n in range(8)]

        kselF = b["kselF"][:].rearrange("p (m j) -> p m j", m=8)
        xin = [self.SA[:, 0:1024], self.SA[:, 1024:2048]]
        x2t = [self.SA[:, 2048:3072], self.SA[:, 3072:4096]]
        xd = self.d_in["x"].rearrange("(T p) d -> T p d", p=128)
        x2d = self.x2_dram.rearrange("(T p) d -> T p d", p=128)
        h2d = self.h2_dram.rearrange("(T p) d -> T p d", p=128)

        def p2F(tau):
            bb = tau % 2
            bank = [self.pA, self.pB][tau % 2]
            bkey = ["pA", "pB"][tau % 2]
            st = [lambda: self.op("sp", lambda e: e.dma_start(
                out=xin[bb], in_=xd[tau]),
                writes=[("xin", bb)], war=["impf", "scr"] if tau < 2 else [],
                inc=16, tag=f"px{tau}", sem=f"xin{bb}")]

            def mma(e, tau=tau, bank=bank):
                ins = None
                for h in range(H):
                    m = h // 2
                    ins = e.matmul(bank[:, h * 16:(h + 1) * 16],
                                   qch(m)[:, tau * 128:(tau + 1) * 128],
                                   b["kselP"][:, h * 16:(h + 1) * 16],
                                   start=(h == 0), stop=(h == H - 1))
                return ins
            st.append(lambda: self.op("pe", mma, reads=QF_KEYS + ["kselP"],
                                      writes=[bkey], tag=f"sa{tau}"))
            st.append(lambda: self.op("act", lambda e: e.activation(
                b["exps"][:], bank[:, 0:256], AF.Exp),
                reads=[bkey], writes=["exps"], tag=f"se{tau}"))
            st.append(lambda: self.op("dve", lambda e: e.tensor_reduce(
                b["aden"][:], b["exps"][:].rearrange("p (h j) -> p h j", h=16),
                axis=AX.X, op=ALU.add),
                reads=["exps"], writes=["aden"], tag=f"sd{tau}"))
            st.append(lambda: self.op("dve", lambda e: e.reciprocal(
                b["arden"][:], b["aden"][:]),
                reads=["aden"], writes=["arden"], tag=f"sr{tau}"))
            st.append(lambda: self.op("dve", lambda e: e.tensor_tensor(
                b["wsc"][:].rearrange("p (h j) -> p h j", h=16),
                b["exps"][:].rearrange("p (h j) -> p h j", h=16),
                b["arden"][:].rearrange("p (h j) -> p h j", h=16, j=1)
                .broadcast_to([128, 16, 16]), op=ALU.mult),
                reads=["exps", "arden"], writes=["wsc"], tag=f"sw{tau}"))
            for q in range(4):
                tb = q % 2
                srcs = [b["wsc"][:, (q * 4 + j) * 16:(q * 4 + j + 1) * 16]
                        for j in range(4)]

                def ev(e, tb=tb):
                    return e.tensor_copy(b["wTst"][:, 0:512],
                                         self.pT[tb][0:16, 0:512])
                st.append(lambda q=q, tb=tb, srcs=srcs, ev=ev:
                          self.tbatch(srcs, 128, ev, tb, ["wsc"], ["wTst"],
                                      f"wt{tau}_{q}"))

                def mmc(e, tau=tau, q=q):
                    ins = None
                    for j in range(4):
                        h = q * 4 + j
                        ins = e.matmul(self.pBig[:, h * 64:(h + 1) * 64],
                                       b["wTst"][:, j * 128:(j + 1) * 128],
                                       b["vselT"][:, h * 64:(h + 1) * 64],
                                       start=(h in (0, 8)), stop=False)
                    return ins
                st.append(lambda mmc=mmc, q=q: self.op(
                    "pe", mmc, reads=["wTst", "vselT"],
                    writes=[("pBig", q // 2)], tag=f"cx{tau}_{q}"))

            def mmp(e, tau=tau):
                e.matmul(self.pBig[:, 0:512],
                         b["PT"][:, tau * 128:(tau + 1) * 128],
                         b["ctxbT"][:, 0:512], start=False, stop=True)
                return e.matmul(self.pBig[:, 512:1024],
                                b["PT"][:, tau * 128:(tau + 1) * 128],
                                b["ctxbT"][:, 512:1024], start=False, stop=True)
            st.append(lambda: self.op("pe", mmp,
                      reads=[("PT", tau)] + [("ctxbT", h) for h in range(H)],
                      writes=[("pBig", 0), ("pBig", 1)], tag=f"cp{tau}"))
            return st

        def p2B(tau):
            bb = tau % 2
            bt = b["bt"][bb]
            sh = self.WA[:, 2048:3072]
            st = []
            for half in range(2):
                st.append(lambda half=half: self.op(
                    "dve", lambda e: e.tensor_scalar(
                        bt[:, half * 512:(half + 1) * 512],
                        self.pBig[:, half * 512:(half + 1) * 512],
                        b["notsel"][:, tau:tau + 1], None, op0=ALU.mult),
                    reads=[("pBig", half), "notsel"],
                    writes=[("bt", bb, half)], tag=f"cv{tau}_{half}"))
            for half in range(2):
                srcs = [bt[:, (half * 4 + j) * 128:(half * 4 + j + 1) * 128]
                        for j in range(4)]

                def ev(e, half=half):
                    return e.tensor_copy(
                        self.ctxf[:, half * 512:(half + 1) * 512],
                        self.pT[half][:, 0:512])
                st.append(lambda half=half, srcs=srcs, ev=ev: self.tbatch(
                    srcs, 128, ev, half, [("bt", bb, 0), ("bt", bb, 1)],
                    [("ctxf", half)], f"cf{tau}_{half}"))
            for half, pw, pwkey in ((0, self.pS, "pS"), (1, self.pX, "pX")):
                def mmw(e, half=half, pw=pw):
                    for k in range(8):
                        e.matmul(pw[:, 0:512],
                                 self.ctxf[:, k * 128:(k + 1) * 128],
                                 self.wck(8192, 1024, k,
                                          (half * 512, (half + 1) * 512)),
                                 start=(k == 0), stop=False)
                    return e.matmul(pw[:, 0:512], b["onesb32"],
                                    b["borowb"][:, half * 512:(half + 1) * 512],
                                    start=False, stop=True)
                st.append(lambda mmw=mmw, pwkey=pwkey: self.op(
                    "pe", mmw, reads=[("ctxf", 0), ("ctxf", 1), ("w", "wo"),
                                      "borowb", "onesb32"],
                    writes=[pwkey], tag=f"wo{tau}_{half}"))
                st.append(lambda half=half, pw=pw, pwkey=pwkey: self.op(
                    "dve", lambda e: e.tensor_tensor(
                        x2t[bb][:, half * 512:(half + 1) * 512], pw[:, 0:512],
                        xin[bb][:, half * 512:(half + 1) * 512], op=ALU.add),
                    reads=[pwkey, ("xin", bb)], writes=[("x2t", bb, half)],
                    tag=f"we{tau}_{half}"))
            st.append(lambda: self.op("sp", lambda e: e.dma_start(
                out=x2d[tau], in_=x2t[bb]),
                reads=[("x2t", bb, 0), ("x2t", bb, 1)],
                writes=[("x2_dram", tau)],
                inc=16, tag=f"sx{tau}", sem=f"sx{bb}"))
            st.append(lambda: self.ln_head(
                x2t[bb], [("x2t", bb, 0), ("x2t", bb, 1)], f"l2_{tau}",
                mk=f"mean{bb}"))
            st.append(lambda: self.ln_tail(
                x2t[bb], [("x2t", bb, 0), ("x2t", bb, 1)], bt,
                [("bt", bb, 0), ("bt", bb, 1)], sh, f"l2_{tau}",
                mk=f"mean{bb}", rk=f"rstd{bb}", qk=f"ssqA{bb}"))
            st.append(lambda: self.op("sp", lambda e: e.dma_start(
                out=h2d[tau], in_=bt),
                reads=[("bt", bb, 0), ("bt", bb, 1)],
                writes=[("h2_dram", tau)],
                inc=16, tag=f"sh{tau}", sem="sh2"))
            for half in range(2):
                srcs = [bt[:, (half * 4 + j) * 128:(half * 4 + j + 1) * 128]
                        for j in range(4)]

                def ev(e, tau=tau, half=half):
                    return e.tensor_copy(
                        h2F[:, half * 4:half * 4 + 4,
                            tau * 128:(tau + 1) * 128],
                        self.pT[half][:, 0:512].rearrange(
                            "p (m t) -> p m t", m=4))
                st.append(lambda half=half, srcs=srcs, ev=ev, tau=tau:
                          self.tbatch(
                              srcs, 128, ev, half,
                              [("bt", bb, 0), ("bt", bb, 1)],
                              [("h2F", tau, half)], f"t2{tau}_{half}",
                              war=[("h1F", tau, hh) for hh in range(2)]))
            return st

        for i in range(self.p2tau + 1):
            A = p2F(i) if i < self.p2tau else []
            B = p2B(i - 1) if i >= 1 else []
            na, nb = len(A), len(B)
            ia = ib = 0
            while ia < na or ib < nb:
                if ia < na and (ib >= nb or ia * (nb + 1) <= ib * (na + 1)):
                    A[ia]()
                    ia += 1
                else:
                    B[ib]()
                    ib += 1

    def emit_route(self):
        b, cb = self.b, self.cb
        d_in = self.d_in
        h2F = self.A1[:].rearrange("p (m t) -> p m t", m=8)
        gw = b["gatew"].rearrange("p (m n) -> p m n", m=8)
        self.op("pool", lambda e: e.dma_start(
            out=gw, in_=d_in["gatewt"].rearrange("(m p) n -> p m n", p=128)),
            writes=["gatew"],
            war=[("Qf", 4, n) for n in range(8)] + [("Kg", 0, n) for n in range(8)],
            inc=16, tag="ld_gatew", sem="wld")

        def mmg(e):
            ins = None
            for tau in range(NT):
                for k in range(8):
                    ins = e.matmul(self.pA[:, tau * 16:(tau + 1) * 16],
                                   h2F[:, k, tau * 128:(tau + 1) * 128],
                                   gw[:, k, :],
                                   start=(tau == 0 and k == 0),
                                   stop=(tau == NT - 1 and k == 7))
            return ins
        self.op("pe", mmg, reads=self.H2F_KEYS + ["gatew"], writes=["pA"], tag="gt")
        self.op("act", lambda e: e.activation(b["expg"], self.pA[:, 0:512], AF.Exp),
                reads=["pA"], writes=["expg"], war=[("xin", 0), ("xin", 1)],
                tag="ge")
        self.op("dve", lambda e: e.tensor_reduce(
            b["rden"][:], b["expg"].rearrange("p (T j) -> p T j", T=NT),
            axis=AX.X, op=ALU.add),
            reads=["expg"], writes=["rden"], tag="gd")
        self.op("dve", lambda e: e.reciprocal(b["rden"][:], b["rden"][:]),
                reads=["rden"], writes=["rden"], tag="gr")
        XW = [("xin", 0), ("xin", 1)]
        r3 = b["reqall"].rearrange("p (T j) -> p T j", T=NT)
        q3 = b["req2all"].rearrange("p (T j) -> p T j", T=NT)
        expg3 = b["expg"].rearrange("p (T j) -> p T j", T=NT)
        iotam = cb["iota16"][:].unsqueeze(1).broadcast_to((128, NT, 16))
        HIT_W = [("hit", tau) for tau in range(NT)]
        self.op("dve", lambda e: e.tensor_reduce(
            b["m1v"], expg3, axis=AX.X, op=ALU.max),
            reads=["expg"], writes=["m1v"], war=XW, tag="v1")
        self.op("dve", lambda e: e.tensor_tensor(
            r3, expg3, b["m1v"].unsqueeze(2).broadcast_to((128, NT, 16)),
            op=ALU.is_ge),
            reads=["expg", "m1v"], writes=["reqall"], war=XW, tag="v2")
        self.op("dve", lambda e: e.scalar_tensor_tensor(
            q3, r3, -1e30, expg3, op0=ALU.mult, op1=ALU.add),
            reads=["reqall", "expg"], writes=["req2all"],
            war=[("x2t", 1, 0), ("x2t", 1, 1)], tag="v3")
        self.op("dve", lambda e: e.tensor_reduce(
            b["m2v"], q3, axis=AX.X, op=ALU.max),
            reads=["req2all"], writes=["m2v"], war=XW, tag="v4")
        self.op("dve", lambda e: e.tensor_tensor(
            q3, q3, b["m2v"].unsqueeze(2).broadcast_to((128, NT, 16)),
            op=ALU.is_ge),
            reads=["req2all", "m2v"], writes=["req2all"], tag="v5")
        self.op("dve", lambda e: e.tensor_add(
            b["hit"][:, :], b["reqall"], b["req2all"]),
            reads=["reqall", "req2all"], writes=HIT_W,
            war=[("Qf", m, n) for m in range(4) for n in range(8)], tag="v6")
        self.op("dve", lambda e: e.tensor_tensor(
            r3, r3, iotam, op=ALU.mult),
            reads=["reqall", ("c", "iota16")], writes=["reqall"], tag="v7")
        self.op("dve", lambda e: e.tensor_reduce(
            b["eav"], r3, axis=AX.X, op=ALU.add),
            reads=["reqall"], writes=["eav"], war=XW, tag="v8")
        self.op("dve", lambda e: e.tensor_tensor(
            q3, q3, iotam, op=ALU.mult),
            reads=["req2all", ("c", "iota16")], writes=["req2all"], tag="v9")
        self.op("dve", lambda e: e.tensor_reduce(
            b["ebv"], q3, axis=AX.X, op=ALU.add),
            reads=["req2all"], writes=["ebv"], war=XW, tag="va")
        self.op("dve", lambda e: e.tensor_tensor(
            b["elo"][:, :], b["eav"], b["ebv"], op=ALU.min),
            reads=["eav", "ebv"], war=XW,
            writes=[("elo", t) for t in range(NT)], tag="vb")
        self.op("dve", lambda e: e.tensor_tensor(
            b["ehi"][:, :], b["eav"], b["ebv"], op=ALU.max),
            reads=["eav", "ebv"],
            writes=[("ehi", t) for t in range(NT)], tag="vc")
        self.op("dve", lambda e: e.tensor_tensor(
            b["flagv"], b["eav"], b["ebv"], op=ALU.is_lt),
            reads=["eav", "ebv"], writes=["flagv"], war=XW, tag="vd")
        self.op("dve", lambda e: e.tensor_tensor(
            b["ddv"], b["m1v"], b["m2v"], op=ALU.subtract),
            reads=["m1v", "m2v"], writes=["ddv"], war=XW, tag="ve")
        self.op("dve", lambda e: e.tensor_tensor(
            b["ddv"], b["ddv"], b["flagv"], op=ALU.mult),
            reads=["ddv", "flagv"], writes=["ddv"], tag="vf")
        self.op("dve", lambda e: e.tensor_tensor(
            b["wlov"], b["m2v"], b["ddv"], op=ALU.add),
            reads=["m2v", "ddv"], writes=["wlov"], war=XW, tag="vg")
        self.op("dve", lambda e: e.tensor_tensor(
            b["whiv"], b["m1v"], b["m2v"], op=ALU.add),
            reads=["m1v", "m2v"], writes=["whiv"], war=XW, tag="vh")
        self.op("dve", lambda e: e.tensor_tensor(
            b["whiv"], b["whiv"], b["wlov"], op=ALU.subtract),
            reads=["whiv", "wlov"], writes=["whiv"], tag="vi")
        self.op("dve", lambda e: e.tensor_tensor(
            b["w0"][:, :], b["wlov"], b["rden"][:], op=ALU.mult),
            reads=["wlov", "rden"], writes=[("w0", t) for t in range(NT)],
            tag="vj")
        self.op("dve", lambda e: e.tensor_tensor(
            b["w1"][:, :], b["whiv"], b["rden"][:], op=ALU.mult),
            reads=["whiv", "rden"], writes=[("w1", t) for t in range(NT)],
            tag="vk")
        HIT_KEYS = [("hit", tau) for tau in range(NT)]
        hit3 = b["hit"].rearrange("p (T j) -> p T j", T=NT)
        rall3 = b["rall"].rearrange("p (T j) -> p T j", T=NT)
        def rank_chain(ex):
            ps = ex % 2
            sfx = "B" if ps else ""
            ops = [lambda: self.op(
                "dve", lambda e, ex=ex, sfx=sfx: e.tensor_copy(
                    b["hitbf" + sfx][:], hit3[:, :, ex]),
                reads=HIT_KEYS, writes=["hitbf" + sfx], tag=f"rh{ex}")]
            ops += self.rank_ops(hit3[:, :, ex], HIT_KEYS, rall3[:, :, ex],
                                 [("rall", ex)], f"re{ex}",
                                 war=[("xin", 0), ("xin", 1), "reqall"]
                                 if ex < 2 else [], ps=ps)
            ops.append(lambda: self.op(
                "pe", lambda e, ex=ex, sfx=sfx: e.matmul(
                    self.pBig[0:1, ex:ex + 1], b["rowtotb" + sfx][:, 0:1],
                    cb["onescol"][:], start=True, stop=True),
                reads=["rowtotb" + sfx, ("c", "onescol")],
                writes=[("pBig", 0)], tag=f"rc{ex}"))
            return ops

        for p0 in range(0, NE, 2):
            for a, bl in zip(rank_chain(p0), rank_chain(p0 + 1)):
                a()
                bl()
        RALL_KEYS = [("rall", ex) for ex in range(NE)]
        self.op("dve", lambda e: e.tensor_copy(b["cntrow"][:], self.pBig[0:1, 0:16]),
                reads=[("pBig", 0)], writes=["cntrow"],
                war=[("xin", 0), ("xin", 1)], tag="pc0")
        self.op("dve", lambda e: e.tensor_scalar(
            b["pcs"][:], b["cntrow"][:], 127.49, 1.0 / 256.0,
            op0=ALU.add, op1=ALU.mult),
            reads=["cntrow"], writes=["pcs"], tag="pc1")
        self.op("dve", lambda e: e.tensor_copy(b["pc32"][:], b["pcs"][:]),
                reads=["pcs"], writes=["pc32"], tag="pc2")
        self.op("dve", lambda e: e.tensor_copy(b["pcs"][:], b["pc32"][:]),
                reads=["pc32"], writes=["pcs"], tag="pc3")
        self.op("dve", lambda e: e.tensor_scalar(
            b["pcrow"][:], b["pcs"][:], 256.0, None, op0=ALU.mult),
            reads=["pcs"], writes=["pcrow"], tag="pc4")
        self.op("dve", lambda e: e.memset(b["pofsrow"][:, 0:1], 0.0),
                writes=["pofsrow"], war=[("xin", 0), ("xin", 1)], tag="po0")
        for ex in range(1, NE):
            self.op("dve", lambda e, ex=ex: e.tensor_tensor(
                b["pofsrow"][:, ex:ex + 1], b["pofsrow"][:, ex - 1:ex],
                b["pcrow"][:, ex - 1:ex], op=ALU.add),
                reads=["pofsrow", "pcrow"], writes=["pofsrow"], tag=f"po{ex}")
        self.op("pe", lambda e: e.matmul(
            self.pS[:, 0:16], cb["onesrow"][:], b["pofsrow"][:],
            start=True, stop=True),
            reads=["pofsrow", ("c", "onesrow")], writes=["pS"], tag="pob")
        self.op("dve", lambda e: e.tensor_copy(b["pofsb"][:], self.pS[:, 0:16]),
                reads=["pS"], writes=["pofsb"],
                war=[("xin", 0), ("xin", 1)], tag="poc")
        for ex in range(NE):
            self.op("dve", lambda e, ex=ex: e.tensor_scalar(
                rall3[:, :, ex], rall3[:, :, ex], b["pofsb"][:, ex:ex + 1], None,
                op0=ALU.add),
                reads=[("rall", ex), "pofsb"], writes=[("rall", ex)], tag=f"pr{ex}")
        self.op("dve", lambda e: e.memset(b["texp"][:], -1.0),
                writes=["texp"], tag="tx_0")
        for ex in range(NE):
            self.op("dve", lambda e, ex=ex: e.tensor_scalar(
                b["t80"][:], cb["tb80"][:], b["pofsrow"][:, ex:ex + 1], None,
                op0=ALU.is_ge),
                reads=[("c", "tb80"), "pofsrow"], writes=["t80"], tag=f"tx{ex}a")
            self.op("dve", lambda e: e.tensor_tensor(
                b["texp"][:], b["texp"][:], b["t80"][:], op=ALU.add),
                reads=["texp", "t80"], writes=["texp"], tag=f"tx{ex}b")
        self.op("pe", lambda e: e.matmul(
            self.pX[:, 0:NG], b["onesb32"][:], b["texp"][:],
            start=True, stop=True),
            reads=["texp", "onesb32"], writes=["pX"], tag="txb")
        self.op("dve", lambda e: e.tensor_scalar(
            self.pX[:, 0:NG], self.pX[:, 0:NG], 128.0,
            cb["iotaf"][:, 0:1], op0=ALU.mult, op1=ALU.add),
            reads=["pX", ("c", "iotaf")], writes=["pX"], tag="txd")
        self.op("dve", lambda e: e.tensor_copy(b["idxmat"][:], self.pX[:, 0:NG]),
                reads=["pX"], writes=["idxmat"], tag="txf")
        for ekey, dkey in (("elo", "dest0f"), ("ehi", "dest1f")):
            self.op("dve", lambda e, ekey=ekey: e.tensor_tensor(
                q3, iotam,
                b[ekey][:, :].unsqueeze(2).broadcast_to((128, NT, 16)),
                op=ALU.is_equal),
                reads=[(ekey, t) for t in range(NT)] + [("c", "iota16")],
                writes=["req2all"], tag=f"dw1{dkey}")
            self.op("dve", lambda e: e.tensor_tensor(
                q3, q3, rall3, op=ALU.mult),
                reads=["req2all"] + RALL_KEYS, writes=["req2all"],
                tag=f"dw2{dkey}")
            self.op("dve", lambda e, dkey=dkey: e.tensor_reduce(
                b[dkey][:, :], q3, axis=AX.X, op=ALU.add),
                reads=["req2all"], writes=[(dkey, t) for t in range(NT)],
                tag=f"dw3{dkey}")
        self.op("dve", lambda e: e.tensor_copy(b["dest0"][:], b["dest0f"]),
                reads=[("dest0f", tau) for tau in range(NT)], writes=["dest0"],
                war=["pc32"], tag="dc0")
        self.op("dve", lambda e: e.tensor_copy(b["dest1"][:], b["dest1f"]),
                reads=[("dest1f", tau) for tau in range(NT)], writes=["dest1"],
                tag="dc1")

    def emit_scatter(self):
        import os as _os_sc
        b = self.b
        self.load_w("sw1t", 0, 1023, key=("w", "sw1t"),
                    war=[("w", "wk"), "kselP", ("ctxf", 0), ("ctxf", 1), "exps",
                         "wsc", "sqj"] +
                        [("bt", i, h) for i in range(2) for h in range(2)])
        self.load_w("sw3t", 8192, 1023, key=("w", "sw3t"), war=[("w", "wo")])
        self.load_w("sw2ts", 16384, 1023, key=("w", "sw2ts"),
                    war=[("w", "wqsel"), "qselP"])
        self.op("pool", lambda e: e.dma_start(
            out=b["sw2t0"], in_=self.d_in["sw2t0"]),
            writes=["sw2t0"], inc=16, tag="ld_sw2t0", sem="wld")
        h2d = self.h2_dram.rearrange("(T p) d -> T p d", p=128)
        zkeys = [("eind_z", i) for i in range(NTILE)]
        sst = [self.A2[:, 12288:13312], self.A2[:, 13312:14336]]
        QF_ALL = [("Qf", m, n) for m in range(8) for n in range(8)]
        for tau in range(NT):
            bb = tau % 2
            stg = sst[bb]
            self.op("sp", lambda e, tau=tau, stg=stg: e.dma_start(
                out=stg, in_=h2d[tau]),
                reads=[("h2_dram", tau)] +
                      ([("w", "sw2ts")] if tau == 0 else []),
                writes=[("sst", bb)],
                war=QF_ALL if tau < 2 else [], inc=16, tag=f"sl{tau}",
                sem=f"st{bb}")
            for slot, dest in ((0, b["dest0"]), (1, b["dest1"])):
                self.op("pool", lambda e, tau=tau, stg=stg, dest=dest:
                        e.indirect_dma_start(
                            out=self.ein_dram,
                            out_offset=bass.IndirectOffsetOnAxis(
                                ap=dest[:, tau:tau + 1], axis=0),
                            in_=stg, in_offset=None,
                            bounds_check=self.bcreg, oob_is_err=False),
                        reads=[("sst", bb),
                               "dest0" if slot == 0 else "dest1"] + zkeys,
                        writes=[("ein", tau, slot)] + (
                            ["sc_chain"] if _os_sc.environ.get("SIMSAFE")
                            else []),
                        inc=16, tag=f"sc{tau}_{slot}", sem=f"sc{tau % 2}")

    def emit_shared(self):
        b, cb = self.b, self.cb
        h2F = self.A1[:].rearrange("p (m t) -> p m t", m=8)
        QF_WAR = [("Qf", m, n) for m in range(8) for n in range(8)]
        a_sh = self.A3[:, 0:8192].rearrange("p (m t) -> p m t", m=8)
        s_sh = self.A3[:, 8192:16384].rearrange("p (m t) -> p m t", m=8)
        z_sh = self.A2[:, 0:8192].rearrange("p (m t) -> p m t", m=8)
        zsq = self.A2[:, 8192:9216]
        sqz = self.A2[:, 9216:10240]
        zshd = self.zsh_dram.rearrange("(T p) d -> T p d", p=128)

        for blk in range(4):
            t0 = blk * 1024
            for m in range(8):
                rows = 128 if m < 7 else 127
                for nh in range(2):
                    bank = [self.pA, self.pB][nh]
                    bkey = ["pA", "pB"][nh]

                    def mm1(e, m=m, nh=nh, bank=bank, t0=t0, rows=rows):
                        ins = None
                        for k in range(8):
                            ins = e.matmul(bank[0:rows, 0:512],
                                           self.wck(0, 1023, k, (m * 128, m * 128 + rows)),
                                           h2F[:, k, t0 + nh * 512:t0 + (nh + 1) * 512],
                                           start=(k == 0), stop=(k == 7))
                        return ins
                    self.op("pe", mm1, reads=[("w", "sw1t")] + self.H2F_KEYS,
                            writes=[bkey], tag=f"s1_{blk}_{m}_{nh}")
                    self.op("act", lambda e, m=m, nh=nh, bank=bank, rows=rows:
                            e.activation(
                                a_sh[0:rows, m, nh * 512:(nh + 1) * 512],
                                bank[0:rows, 0:512], AF.Silu,
                                bias=cb["bs1st"][0:rows, m:m + 1]),
                            reads=[bkey, ("c", "bs1st")], writes=[("ash", m, nh)],
                            war=QF_WAR if (blk == 0 and m == 4 and nh == 0) else [],
                            tag=f"s1e_{blk}_{m}_{nh}")

                    def mm3(e, m=m, nh=nh, bank=bank, t0=t0, rows=rows):
                        ins = None
                        for k in range(8):
                            ins = e.matmul(bank[0:rows, 0:512],
                                           self.wck(8192, 1023, k,
                                                    (m * 128, m * 128 + rows)),
                                           h2F[:, k, t0 + nh * 512:t0 + (nh + 1) * 512],
                                           start=(k == 0), stop=(k == 7))
                        return ins
                    self.op("pe", mm3, reads=[("w", "sw3t")] + self.H2F_KEYS,
                            writes=[bkey], tag=f"s3_{blk}_{m}_{nh}")
                    self.op("dve", lambda e, m=m, nh=nh, bank=bank, rows=rows:
                            e.scalar_tensor_tensor(
                                s_sh[0:rows, m, nh * 512:(nh + 1) * 512],
                                bank[0:rows, 0:512], cb["bs3st"][0:rows, m:m + 1],
                                a_sh[0:rows, m, nh * 512:(nh + 1) * 512],
                                op0=ALU.add, op1=ALU.mult),
                            reads=[bkey, ("c", "bs3st"), ("ash", m, nh)],
                            writes=[("ssh", m, nh)], tag=f"s3e_{blk}_{m}_{nh}")
            for m in range(8):
                rows = 128 if m < 7 else 127
                self.op("act", lambda e, m=m, rows=rows: e.activation(
                    zsq[0:rows, :], s_sh[0:rows, m, :], AF.Square),
                    reads=[("ssh", m, 0), ("ssh", m, 1)], writes=["zsq"],
                    war=QF_WAR if (blk == 0 and m == 0) else [],
                    tag=f"sq_{blk}_{m}")
                for nh, pw, pwkey in ((0, self.pS, "pS"), (1, self.pX, "pX")):
                    self.op("pe", lambda e, m=m, nh=nh, pw=pw, rows=rows: e.matmul(
                        pw[0:1, 0:512], cb["onescol"][0:rows, :],
                        zsq[0:rows, nh * 512:(nh + 1) * 512],
                        start=(m == 0), stop=(m == 7)),
                        reads=["zsq", ("c", "onescol")], writes=[pwkey],
                        tag=f"ts_{blk}_{m}_{nh}")
            for nh, pw, pwkey in ((0, self.pS, "pS"), (1, self.pX, "pX")):
                self.op("act", lambda e, nh=nh, pw=pw: e.activation(
                    b["timerow"][:, nh * 512:(nh + 1) * 512], pw[0:1, 0:512],
                    AF.Sqrt, bias=b["c1"][0:1, :]),
                    reads=[pwkey, "c1"], writes=[("time", nh)],
                    war=["borowb"] if blk == 0 else [], tag=f"tt_{blk}_{nh}")
            for m in range(8):
                rows_out = 128 if m < 7 else 127
                for nh in range(2):
                    def mmz(e, m=m, nh=nh, mc=rows_out):
                        ins = None
                        for k in range(8):
                            rows = 128 if k < 7 else 127
                            ins = e.matmul(self.pBig[0:mc, nh * 512:(nh + 1) * 512],
                                           self.wck(16384, 1023, k,
                                                    (m * 128, m * 128 + mc))[0:rows, :],
                                           s_sh[0:rows, k, nh * 512:(nh + 1) * 512],
                                           start=(k == 0), stop=False)
                        return e.matmul(self.pBig[0:mc, nh * 512:(nh + 1) * 512],
                                        b["sw2t0"][:, m * 128:m * 128 + mc],
                                        b["timerow"][:, nh * 512:(nh + 1) * 512],
                                        start=False, stop=True)
                    self.op("pe", mmz,
                            reads=[("w", "sw2ts"), "sw2t0", ("time", nh)] +
                                  [("ssh", mm_, nn) for mm_ in range(8)
                                   for nn in range(2)],
                            writes=[("pBig", nh)], tag=f"z_{blk}_{m}_{nh}")
                    self.op("dve", lambda e, m=m, nh=nh, ro=rows_out: e.tensor_scalar(
                        z_sh[0:ro, m, nh * 512:(nh + 1) * 512],
                        self.pBig[0:ro, nh * 512:(nh + 1) * 512],
                        cb["bs2st"][0:ro, m:m + 1], None, op0=ALU.add),
                        reads=[("pBig", nh), ("c", "bs2st")],
                        writes=[("zsh", m, nh)], tag=f"ze_{blk}_{m}_{nh}")
            for tc in range(8):
                tau = blk * 8 + tc
                sbb = tc % 2
                stg = self.A2[:, 10240 + sbb * 1024:10240 + (sbb + 1) * 1024][:]
                for half in range(2):
                    if half == 0:
                        srcs = [z_sh[:, j, tc * 128:(tc + 1) * 128] for j in range(4)]
                        dst = stg[:, 1:513]
                        pws = 512
                    else:
                        srcs = ([z_sh[:, 4 + j, tc * 128:(tc + 1) * 128]
                                 for j in range(3)] +
                                [z_sh[0:127, 7, tc * 128:(tc + 1) * 128]])
                        dst = stg[:, 513:1024]
                        pws = 511

                    def ev(e, dst=dst, half=half, pws=pws):
                        return e.tensor_copy(dst, self.pT[half][:, 0:pws])
                    self.tbatch(srcs, 128, ev, half,
                                [("zsh", half * 4 + j, nn)
                                 for j in range(4) for nn in range(2)],
                                [("shs", sbb, half)], f"zt{tau}_{half}")
                self.op("act", lambda e, stg=stg: e.activation(
                    sqz[:, 0:1023], stg[:, 1:1024], AF.Square, accum_out=b["ssq"]),
                    reads=[("shs", sbb, 0), ("shs", sbb, 1)], writes=["ssq", "sqz"],
                    tag=f"zs{tau}")
                self.op("act", lambda e, stg=stg: e.activation(
                    stg[:, 0:1], b["ssq"], AF.Sqrt, bias=b["c1"]),
                    reads=["ssq", "c1"], writes=[("shs", sbb, "t")], tag=f"zq{tau}")
                self.op("sp", lambda e, tau=tau, stg=stg: e.dma_start(
                    out=zshd[tau], in_=stg[:, 0:1024]),
                    reads=[("shs", sbb, 0), ("shs", sbb, 1), ("shs", sbb, "t")],
                    writes=[("zsh_dram", tau)], inc=16, tag=f"zst{tau}", sem="zst")

    def emit_experts(self):
        b, cb = self.b, self.cb
        A1, WA = self.A1, self.WA
        wslot = [A1[:, 0:BLOBW], A1[:, BLOBW:2 * BLOBW]]
        in_tm = [A1[:, 28672 + i * 1024:28672 + (i + 1) * 1024] for i in range(4)]
        in_fm = [WA[:, 8192:9216].rearrange("p (k c) -> p k c", k=8),
                 WA[:, 9216:10240].rearrange("p (k c) -> p k c", k=8)]
        a_tm = [WA[:, 0:512], WA[:, 512:1024]]
        s_tm = [WA[:, 1024:1536], WA[:, 1536:2048]]
        out_tm = [WA[:, 2048:3072], WA[:, 3072:4096]]
        sqs = WA[:, 4096:4608]
        sqz = WA[:, 4608:5632]
        s_fm = [WA[:, 5632:6144].rearrange("p (k c) -> p k c", k=4),
                WA[:, 6144:6656].rearrange("p (k c) -> p k c", k=4)]
        eind = self.ein_dram.rearrange("(n p) d -> n p d", p=128)
        eoutd = self.eout_dram.rearrange("(n p) d -> n p d", p=128)
        EIN_KEYS = [("ein", tau, slot) for tau in range(NT) for slot in (0, 1)]
        WB_KEYS = [("wblob", ex, j) for ex in range(NE) for j in range(6)]

        def ld_in(k):
            s4 = k % 4
            self.op("sp", lambda e, k=k, s4=s4: e.dma_start(
                out=in_tm[s4], in_=eind[k]),
                reads=EIN_KEYS + [("eind_z", k)], writes=[("etm", s4)],
                war=self.H2F_KEYS if k < 4 else [], inc=16, tag=f"el{k}",
                sem=f"el{s4 % 2}")

        def gather_w(g):
            sb = g % 2
            war = self.H2F_KEYS if g < 2 else [("ewt", g - 2)]
            self.op("pool", lambda e, sb=sb, g=g: e.indirect_dma_start(
                out=wslot[sb], out_offset=None, in_=self.wblob,
                in_offset=bass.IndirectOffsetOnAxis(
                    ap=b["idxmat"][:, g:g + 1], axis=0),
                bounds_check=self.bcreg, oob_is_err=False),
                reads=["idxmat"] + WB_KEYS, writes=[("ewt", g)], war=war,
                inc=16, tag=f"gw{g}", sem="ewa" if sb == 0 else "ewb")

        def et_t(k):
            sb, s4 = k % 2, k % 4
            f2 = k < 2
            for hf in range(2):
                srcs = [in_tm[s4][:, (hf * 4 + j) * 128:(hf * 4 + j + 1) * 128]
                        for j in range(4)]

                def ev(e, sb=sb, hf=hf):
                    return e.tensor_copy(
                        in_fm[sb][:, hf * 4:hf * 4 + 4, :],
                        self.pT[hf][:, 0:512].rearrange("p (m t) -> p m t", m=4))
                self.tbatch(srcs, 128, ev, hf, [("etm", s4)],
                            [("efm", sb, hf)], f"et{k}_{hf}",
                            war=[("w", "sw3t")] if f2 else [])

        def front(k):
            sb = k % 2
            ws = wslot[(k // 2) % 2]
            pS1 = [self.pA, self.pS][k % 2]
            pS3 = [self.pB, self.pX][k % 2]
            kS1 = ["pA", "pS"][k % 2]
            kS3 = ["pB", "pX"][k % 2]
            f2 = k < 2

            def mk13(pp, ofs, bofs):
                def mm(e, sb=sb, ws=ws, pp=pp, ofs=ofs, bofs=bofs):
                    ins = None
                    for kc in range(8):
                        ins = e.matmul(pp[:, 0:511], in_fm[sb][:, kc, :],
                                       ws[:, ofs + kc * 512:ofs + kc * 512 + 511],
                                       start=(kc == 0), stop=False)
                    return e.matmul(pp[:, 0:511], cb["onesrowb"][:],
                                    ws[0:1, bofs:bofs + 511],
                                    start=False, stop=True)
                return mm
            self.op("pe", mk13(pS1, BO_W1, BO_B1),
                    reads=[("efm", sb, 0), ("efm", sb, 1), ("ewt", k // 2),
                           ("c", "onesrowb")],
                    writes=[kS1], tag=f"e13a_{k}")
            self.op("pe", mk13(pS3, BO_W3, BO_B3),
                    reads=[("efm", sb, 0), ("efm", sb, 1), ("ewt", k // 2),
                           ("c", "onesrowb")],
                    writes=[kS3], tag=f"e13b_{k}")
        def front_act(k):
            sb = k % 2
            pS1 = [self.pA, self.pS][k % 2]
            pS3 = [self.pB, self.pX][k % 2]
            kS1 = ["pA", "pS"][k % 2]
            kS3 = ["pB", "pX"][k % 2]
            f2 = k < 2
            self.op("act", lambda e, sb=sb, pS1=pS1: e.activation(
                a_tm[sb][:, 0:511], pS1[:, 0:511], AF.Silu),
                reads=[kS1], writes=[("ea", sb)],
                war=[("w", "sw1t")] if f2 else [], tag=f"e1e_{k}")
            self.op("dve", lambda e, sb=sb, pS3=pS3: e.tensor_tensor(
                s_tm[sb][:, 0:511], pS3[:, 0:511], a_tm[sb][:, 0:511],
                op=ALU.mult),
                reads=[kS3, ("ea", sb)], writes=[("es", sb)],
                war=([("hit", t) for t in range(NT)] + ["exps", "wsc"])
                if f2 else [], tag=f"e3e_{k}")
            self.op("act", lambda e, sb=sb: e.activation(
                sqs[:, 0:511], s_tm[sb][:, 0:511], AF.Square, accum_out=b["ssq"]),
                reads=[("es", sb)], writes=["ssq", "sqs"],
                war=[("bt", 1, 0), ("bt", 1, 1)] if f2 else [], tag=f"eq_{k}")
            self.op("act", lambda e, sb=sb: e.activation(
                s_tm[sb][:, 511:512], b["ssq"][:], AF.Sqrt, bias=b["c1"][:]),
                reads=["ssq", "c1"], writes=[("es", sb, "t")], tag=f"eT_{k}")

        def back_st(k):
            sb = k % 2
            f2 = k < 2
            for q in range(2):
                srcs = [s_tm[sb][:, (q * 2 + j) * 128:(q * 2 + j + 1) * 128]
                        for j in range(2)]
                rk = [("es", sb), ("es", sb, "t")] if q == 1 else [("es", sb)]

                def ev(e, sb=sb, q=q):
                    return e.tensor_copy(
                        s_fm[sb][:, q * 2:q * 2 + 2, :],
                        self.pT[q][:, 0:256].rearrange("p (m t) -> p m t", m=2))
                self.tbatch(srcs, 128, ev, q, rk, [("esf", sb, q)],
                            f"st_{k}_{q}", war=["kselP"] if f2 else [])

        def back_mm(k):
            sb = k % 2
            ws = wslot[(k // 2) % 2]
            f2 = k < 2
            for half, (c0, cw) in enumerate(((0, 512), (512, 511))):
                def mmz(e, sb=sb, ws=ws, c0=c0, cw=cw):
                    ins = None
                    for kc in range(4):
                        ins = e.matmul(
                            self.pBig[:, c0:c0 + cw], s_fm[sb][:, kc, :],
                            ws[:, BO_W2 + kc * 1024 + c0:
                               BO_W2 + kc * 1024 + c0 + cw],
                            start=(kc == 0), stop=False)
                    return e.matmul(
                        self.pBig[:, c0:c0 + cw], cb["onesrowb"][:],
                        ws[0:1, BO_B2 + c0:BO_B2 + c0 + cw],
                        start=False, stop=True)
                self.op("pe", mmz,
                        reads=[("esf", sb, 0), ("esf", sb, 1), ("ewt", k // 2),
                               ("c", "onesrowb")],
                        writes=[("pBig", half)], tag=f"ez_{k}_{half}")
                self.op("dve", lambda e, sb=sb, c0=c0, cw=cw, half=half:
                        e.tensor_copy(out_tm[sb][:, 1 + c0:1 + c0 + cw],
                                      self.pBig[:, c0:c0 + cw]),
                        reads=[("pBig", half)], writes=[("ez", sb, half)],
                        war=(["sqj", ("bt", 0, 0), ("bt", 0, 1)] if f2 else []),
                        tag=f"eze_{k}_{half}")

        def back_fin(k):
            sb = k % 2
            self.op("act", lambda e, sb=sb: e.activation(
                sqz[:, 0:1023], out_tm[sb][:, 1:1024], AF.Square,
                accum_out=b["ssq"]),
                reads=[("ez", sb, 0), ("ez", sb, 1)], writes=["ssq", "sqz"],
                tag=f"eS_{k}")
            self.op("act", lambda e, sb=sb: e.activation(
                out_tm[sb][:, 0:1], b["ssq"][:], AF.Sqrt, bias=b["c1"][:]),
                reads=["ssq", "c1"], writes=[("ez", sb, "t")], tag=f"eQ_{k}")
            self.op("sp", lambda e, k=k, sb=sb: e.dma_start(
                out=eoutd[k], in_=out_tm[sb]),
                reads=[("ez", sb, 0), ("ez", sb, 1), ("ez", sb, "t")],
                writes=[("eout", k)], inc=16, tag=f"eW{k}", sem="eW")
            if k % 2 == 1 and k // 2 + 2 < NG:
                gather_w(k // 2 + 2)

        ld_in(0)
        ld_in(1)
        ld_in(2)
        gather_w(0)
        gather_w(1)
        for i in range(NTILE + 1):
            if i >= 1:
                back_st(i - 1)
            if i < NTILE:
                if i + 3 < NTILE:
                    ld_in(i + 3)
                et_t(i)
                front(i)
                front_act(i)
            if i >= 1:
                back_mm(i - 1)
                back_fin(i - 1)

    def emit_final(self):
        b = self.b
        A3 = self.A3
        O0 = [A3[:, 0:1024], A3[:, 1024:2048]]
        O1 = [A3[:, 2048:3072], A3[:, 3072:4096]]
        ZS = [A3[:, 4096:5120], A3[:, 5120:6144]]
        sqb = [A3[:, 6144:7168], A3[:, 7168:8192]]
        xin = [self.SA[:, 0:1024], self.SA[:, 1024:2048]]
        yb = [self.SA[:, 2048:3072], self.SA[:, 3072:4096]]
        x2d = self.x2_dram.rearrange("(T p) d -> T p d", p=128)
        zshd = self.zsh_dram.rearrange("(T p) d -> T p d", p=128)
        outd = self.out_d.rearrange("(T p) d -> T p d", p=128)
        EOUT_KEYS = [("eout", k) for k in range(NTILE)]
        ASH_WAR = ([("ash", m, nh) for m in range(8) for nh in range(2)] +
                   [("ssh", m, nh) for m in range(8) for nh in range(2)])

        def chainA(tau):
            bb = tau % 2
            fwar = ([("rall", ex) for ex in range(NE)] + ["expg"] +
                    [("dest0f", t) for t in range(NT)] +
                    [("dest1f", t) for t in range(NT)] +
                    [("elo", t) for t in range(NT)] +
                    [("ehi", t) for t in range(NT)] +
                    ["pofsb", "cntrow", "pcrow", "pcs", "pofsrow", "m1v",
                     "m2v", "eav", "ebv", "ddv", "flagv", "wlov", "whiv",
                     "reqall"]) \
                if tau < 2 else []
            ywar = (["req2all", ("x2t", 1, 0), ("x2t", 1, 1)] if bb == 1 else
                    [("x2t", 0, 0), ("x2t", 0, 1)]) if tau < 2 else []
            st = [
                lambda: self.op("sp", lambda e: e.dma_start(
                    out=xin[bb], in_=x2d[tau]),
                    reads=[("x2_dram", tau)], writes=[("xin", bb)], war=fwar,
                    inc=16, tag=f"fx{tau}", sem=f"fx{bb}"),
                lambda: self.op("sp", lambda e: e.dma_start(
                    out=ZS[bb], in_=zshd[tau]),
                    reads=[("zsh_dram", tau)], writes=[("ZS", bb)],
                    war=ASH_WAR if tau < 2 else [], inc=16, tag=f"fz{tau}",
                    sem=f"fz{bb}"),
            ]
            for slot, Obuf, dkey in ((0, O0, "dest0"), (1, O1, "dest1")):
                st.append(lambda slot=slot, Obuf=Obuf, dkey=dkey:
                          self.op("pool", lambda e: e.indirect_dma_start(
                              out=Obuf[bb], out_offset=None,
                              in_=self.eout_dram,
                              in_offset=bass.IndirectOffsetOnAxis(
                                  ap=b[dkey][:, tau:tau + 1], axis=0),
                              bounds_check=self.bcreg, oob_is_err=False),
                              reads=[dkey] + EOUT_KEYS,
                              writes=[("O", slot, bb)],
                              war=ASH_WAR if tau < 2 else [], inc=16,
                              tag=f"fo{tau}_{slot}", sem=f"fo{slot}{bb}"))
            st.append(lambda: self.op("dve", lambda e: e.tensor_scalar(
                yb[bb], O0[bb], b["w0"][:, tau:tau + 1], None, op0=ALU.mult),
                reads=[("O", 0, bb), ("w0", tau)],
                writes=[("y", bb)], war=ywar, tag=f"f1{tau}"))
            st.append(lambda: self.op("dve", lambda e: e.tensor_scalar(
                yb[bb][:, 0:1], yb[bb][:, 0:1], 1.0, None, op0=ALU.add),
                reads=[("y", bb)], writes=[("y0", bb)], tag=f"f2{tau}"))
            st += self.hyp_s(tau, bb, sqb[bb], "a", y0k=("y0", bb))
            s0 = b[["s0", "s0b"][bb]]
            s0k = ["s0", "s0b"][bb]
            w1s = b[["mean0", "mean1"][bb]]
            w1sk = ["mean0", "mean1"][bb]
            st.append(lambda: self.op("dve", lambda e: e.tensor_tensor(
                w1s, b["w1"][:, tau:tau + 1], s0, op=ALU.mult),
                reads=[("w1", tau), s0k], writes=[w1sk], tag=f"fw{tau}"))
            st.append(lambda: self.op("dve", lambda e: e.scalar_tensor_tensor(
                yb[bb], O1[bb], w1s, yb[bb],
                op0=ALU.mult, op1=ALU.add),
                reads=[("O", 1, bb), w1sk, ("y", bb), ("y0", bb)],
                writes=[("y", bb), ("y0", bb)], tag=f"f3{tau}"))
            st += self.hyp_s(tau, bb, sqb[bb], "b")
            return st

        def chainB(tau):
            bb = tau % 2
            s0 = b[["s0", "s0b"][bb]]
            s0k = ["s0", "s0b"][bb]
            st = []
            st.append(lambda: self.op("dve", lambda e: e.scalar_tensor_tensor(
                yb[bb], ZS[bb], s0, yb[bb], op0=ALU.mult, op1=ALU.add),
                reads=[("ZS", bb), s0k, ("y", bb)], writes=[("y", bb)],
                tag=f"f4{tau}"))
            st += self.hyp_s(tau, bb, sqb[bb], "c")
            st.append(lambda: self.op("dve", lambda e: e.reciprocal(s0, s0),
                    reads=[s0k], writes=[s0k], tag=f"fr{tau}"))
            st.append(lambda: self.op("dve", lambda e: e.scalar_tensor_tensor(
                xin[bb], yb[bb], s0, xin[bb], op0=ALU.mult, op1=ALU.add),
                reads=[("xin", bb), s0k, ("y", bb)], writes=[("xin", bb)],
                tag=f"f5{tau}"))
            st.append(lambda: self.op("sp", lambda e: e.dma_start(
                out=outd[tau], in_=xin[bb]),
                reads=[("xin", bb)], writes=[("out", tau)], inc=16,
                tag=f"fs{tau}", sem="fs"))
            return st

        for i in range(NT + 1):
            A = chainA(i) if i < NT else []
            B = chainB(i - 1) if i >= 1 else []
            na, nb = len(A), len(B)
            ia = ib = 0
            while ia < na or ib < nb:
                if ia < na and (ib >= nb or ia * (nb + 1) <= ib * (na + 1)):
                    A[ia]()
                    ia += 1
                else:
                    B[ib]()
                    ib += 1

    def hyp_s(self, tau, bb, sqb, sub, y0k=None):
        b = self.b
        y = self.SA[:, 2048 + bb * 1024:3072 + bb * 1024]
        yk = ("y", bb)
        s0 = b[["s0", "s0b"][bb]]
        s0k = ["s0", "s0b"][bb]
        sq = b[["ssq", "ssqb"][bb]]
        sqk = ["ssq", "ssqb"][bb]
        sbwar = ([("bt", 1, 0), ("bt", 1, 1)]
                 if (tau < 2 and sub == "a") else [])
        y0read = [y0k] if y0k else []
        return [
            lambda: self.op("act", lambda e: e.activation(
                sqb[:, 0:1023], y[:, 1:1024], AF.Square, accum_out=sq),
                reads=[yk], writes=[sqk, ("sqb", bb)], war=sbwar,
                tag=f"h{sub}{tau}s"),
            lambda: self.op("dve", lambda e: e.tensor_tensor(
                s0, y[:, 0:1], y[:, 0:1], op=ALU.mult),
                reads=[yk] + y0read, writes=[s0k], tag=f"h{sub}{tau}z"),
            lambda: self.op("dve", lambda e: e.tensor_scalar(
                s0, s0, sq, EPS_HYP, op0=ALU.subtract, op1=ALU.max),
                reads=[s0k, sqk], writes=[s0k], tag=f"h{sub}{tau}d"),
            lambda: self.op("act", lambda e: e.activation(
                s0, s0, AF.Sqrt),
                reads=[s0k], writes=[s0k], tag=f"h{sub}{tau}q"),
        ]

    def emit_dump(self):
        dd = self.dump_d
        if not dd:
            return
        b = self.b

        def dram_dump(nm, src, rkeys):
            self.op("pool", lambda e, nm=nm, src=src: e.dma_start(out=dd[nm], in_=src),
                    reads=rkeys, inc=16, tag="D" + nm, sem="dmpp")

        if "h1" in dd:
            dram_dump("h1", self.h1_dram, [("h1_dram", t) for t in range(NT)])
        if "x2" in dd:
            dram_dump("x2", self.x2_dram, [("x2_dram", t) for t in range(NT)])
        if "h2" in dd:
            dram_dump("h2", self.h2_dram, [("h2_dram", t) for t in range(NT)])
        if "zsh" in dd:
            dram_dump("zsh", self.zsh_dram, [("zsh_dram", t) for t in range(NT)])
        if "ein" in dd:
            dram_dump("ein", self.ein_dram,
                      [("ein", t, s) for t in range(NT) for s in (0, 1)])
        if "eout" in dd:
            dram_dump("eout", self.eout_dram,
                      [("eout", k) for k in range(NTILE)])
        sd = {
            "imp": (b["imp_tm"][:], ["imp_tm"]),
            "sel": (b["sel"][:], ["sel"]),
            "rank": (b["rank16"][:], [("rank16",)]),
            "w0": (b["w0"][:], [("w0", t) for t in range(NT)]),
            "w1": (b["w1"][:], [("w1", t) for t in range(NT)]),
            "dest0": (b["dest0f"], [("dest0f", t) for t in range(NT)]),
            "dest1": (b["dest1f"], [("dest1f", t) for t in range(NT)]),
            "expg": (b["expg"], ["expg"]),
        }
        for nm, (src, rkeys) in sd.items():
            if nm in dd:
                self.op("sp", lambda e, nm=nm, src=src: e.dma_start(out=dd[nm], in_=src),
                        reads=rkeys, inc=16, tag="D" + nm, sem="dmps")
        if "expt" in dd:
            self.op("pool", lambda e: e.dma_start(
                out=dd["expt"], in_=self.A2[:, 8192:9216]),
                reads=[("expT", hp, q) for hp in range(2) for q in range(8)],
                inc=16, tag="Dexpt", sem="dmpp")
        if "rdenb" in dd:
            self.op("sp", lambda e: e.dma_start(out=dd["rdenb"], in_=b["rdenb"][0:16, :]),
                    reads=[("rdenb", h) for h in range(H)], inc=16, tag="Drdenb",
                    sem="dmps")
        bd = {
            "d0u": (b["dest0"][:], ["dest0"]),
            "d1u": (b["dest1"][:], ["dest1"]),
            "hit": (b["hit"], [("hit", t) for t in range(NT)]),
            "h1sel": (b["h1selT"], ["h1selT"]),
            "kself": (b["kselF"][:], ["kselF"]),
            "qself": (b["qselF"][:], ["qselF"]),
            "vselt": (b["vselT"][0:16, :], ["vselT"]),
            "ctxb": (b["ctxbT"][0:16, :], [("ctxbT", h) for h in range(H)]),
        }
        for nm, (src, rkeys) in bd.items():
            if nm in dd:
                self.op("pool", lambda e, nm=nm, src=src: e.dma_start(out=dd[nm], in_=src),
                        reads=rkeys, inc=16, tag="D" + nm, sem="dmpp")

    def emit_tail(self):
        final = dict(self.S.count)
        for eng in Sched.ENGINES:
            for tgt, cnt in final.items():
                if tgt != eng and cnt > 0:
                    self.S._wait(eng, tgt, cnt)


VFM_ALL = [("Vfm", n) for n in range(8)]
EXPB_ALL = [("expb", hp, n) for hp in range(2) for n in range(8)]
EXPT_ALL = [("expT", hp, q) for hp in range(2) for q in range(8)]


def build(dumps=(), upto="final", p2tau=NT):
    return MK(dumps, upto, p2tau).build()


def kernel(**inputs):
    prep = host_prep(inputs)
    x = np.asarray(inputs["L0_attn_x"], np.float32)
    nc = build()
    in_maps = []
    for c in range(8):
        m = {"x": np.ascontiguousarray(x[c])}
        m.update(prep)
        in_maps.append(m)
    res = run_bass_kernel_spmd(nc, in_maps, core_ids=list(range(8)))
    return np.stack([np.asarray(r["out"], np.float32) for r in res.results])
